# version 01
# baseline (speedup 1.0000x reference)
import numpy as np
import ml_dtypes
import concourse.bass as bass
import concourse.mybir as mybir
from concourse.bass_utils import run_bass_kernel_spmd
from contextlib import ExitStack

F32 = mybir.dt.float32
BF16 = mybir.dt.bfloat16
AF = mybir.ActivationFunctionType
ALU = mybir.AluOpType
ET = mybir.EngineType
bfdt = ml_dtypes.bfloat16
ts = bass.ts

P = 128
D = 1024
DC = D // P
T = 4096
TOWN = 2048
NB = 4
NH = 16
HD = 64
NHC = 8
NHP = NHC // 2
NTT = T // P
NQB = T // 512
LN_EPS = 1e-5
ROPE_THETA = 10000.0
PAIR_GROUPS = [[0, 1], [2, 3], [4, 5], [6, 7]]
ALL_GROUP = [[0, 1, 2, 3, 4, 5, 6, 7]]

ENGS = ("sync", "tensor", "vector", "scalar", "gpsimd")



def _rope_tables_np():
    pos = np.arange(T, dtype=np.float64)[:, None]
    k = np.arange(HD // 2, dtype=np.float64)[None, :]
    ang = pos / ROPE_THETA ** (2.0 * k / HD)

    def f(a):
        a = a.astype(np.float32).reshape(T // P, P, HD // 2).transpose(1, 0, 2)
        return np.ascontiguousarray(a.reshape(P, (T // P) * (HD // 2)))
    return f(np.sin(ang)), f(np.cos(ang))


def _stair_mask_np():
    w = np.zeros((P, 896), dtype=np.float32)
    kk = np.arange(P)[:, None]
    cc = np.arange(896)[None, :]
    w[cc < 384 + kk] = -1e30
    return w



class Op:
    __slots__ = ("eng", "fn", "deps", "hdeps", "inc", "seq", "cum", "needed",
                 "name")

    def __init__(self, eng, fn, deps, hdeps, inc, name):
        self.eng = eng
        self.fn = fn
        self.deps = deps
        self.hdeps = hdeps
        self.inc = inc
        self.needed = False
        self.name = name


class Phase:

    def __init__(self, pr, name):
        self.pr = pr
        self.name = name
        self.ops = []

    def op(self, eng, deps=(), hdeps=(), inc=1, name=""):
        def reg(fn):
            o = Op(eng, fn, [d for d in deps if d is not None],
                   [d for d in hdeps if d is not None], inc, name)
            self.ops.append(o)
            return o
        return reg

    def emit(self):
        pr = self.pr
        nc = pr.nc
        for o in self.ops:
            if o.inc != 1:
                o.needed = True
            for d in o.deps:
                d.needed = True
            for d in o.hdeps:
                d.needed = True
        for o in self.ops:
            sem = pr.esem[o.eng]
            o.seq = sem.n + (o.inc if o.needed else 0)
            if o.needed:
                sem.n += o.inc
            o.cum = o.seq

        by_eng = {e: [o for o in self.ops if o.eng == e] for e in ENGS}

        def make_body(eng):
            def body(e):
                hwm = {}
                for o in by_eng[eng]:
                    for d in list(o.hdeps) + list(o.deps):
                        if not d.needed:
                            raise RuntimeError(f"dep {d.name} not marked needed")
                        sem = self.pr.esem[d.eng]
                        if hwm.get(d.eng, -1) < d.cum:
                            e.wait_ge(sem.h, d.cum)
                            hwm[d.eng] = d.cum
                    ins = o.fn(e)
                    if o.needed:
                        if ins is None:
                            raise RuntimeError(f"op {o.name} returned no instr")
                        ins.then_inc(self.pr.esem[o.eng].h, o.inc)
            return body

        with nc.Block() as block:
            for eng in ENGS:
                if by_eng[eng]:
                    getattr(block, eng)(make_body(eng))


class Sem:
    def __init__(self, nc, name):
        self.h = nc.alloc_semaphore(name)
        self.n = 0


class Prog:
    def __init__(self, num_devices=8):
        self.nc = bass.Bass("TRN2", target_bir_lowering=False,
                            num_devices=num_devices)
        self.es = ExitStack()
        self.esem = {e: Sem(self.nc, f"se_{e}") for e in ENGS}
        self.s_cc = Sem(self.nc, "s_cc")

    def sb(self, name, shape, dtype):
        return self.es.enter_context(self.nc.sbuf_tensor(name, shape, dtype))

    def close(self):
        self.es.close()

    def load_consts(self, rope=True):
        nc = self.nc
        self.ident = self.sb("c_ident", [P, P], BF16)
        self.ones_bf = self.sb("c_ones", [P, P], BF16)
        self.ones_f32r = self.sb("c_onesr", [P, P], F32)
        self.ones_f32c = self.sb("c_onesc", [P, 2], F32)
        self.stair = self.sb("c_stair", [P, 896], F32)
        self.ident_f32 = self.sb("c_idf", [P, P], F32)
        self.eye64 = self.sb("c_eye64", [64, 64], F32)
        loads = [
            (self.ident, nc.inline_tensor(np.eye(P, dtype=bfdt), "h_ident")),
            (self.ones_bf, nc.inline_tensor(np.ones((P, P), dtype=bfdt), "h_ones")),
            (self.ones_f32r,
             nc.inline_tensor(np.ones((P, P), dtype=np.float32), "h_onesr")),
            (self.ones_f32c,
             nc.inline_tensor(np.ones((P, 2), dtype=np.float32), "h_onesc")),
            (self.stair, nc.inline_tensor(_stair_mask_np(), "h_stair")),
            (self.ident_f32,
             nc.inline_tensor(np.eye(P, dtype=np.float32), "h_idf")),
            (self.eye64,
             nc.inline_tensor(np.eye(64, dtype=np.float32), "h_eye64")),
        ]
        if rope:
            sin_np, cos_np = _rope_tables_np()
            self.sin = self.sb("c_sin", [P, sin_np.shape[1]], F32)
            self.cos = self.sb("c_cos", [P, cos_np.shape[1]], F32)
            loads += [(self.sin, nc.inline_tensor(sin_np, "h_sin")),
                      (self.cos, nc.inline_tensor(cos_np, "h_cos"))]
        self.cbias = self.sb("c_bias", [P, 4], F32)
        ph = Phase(self, "consts")
        for i, (dst, h) in enumerate(loads):
            @ph.op("sync", inc=16, name=f"c{i}")
            def _(e, dst=dst, h=h):
                return e.dma_start(dst[:], h.ap()[:, :])

        @ph.op("vector", name="cbias")
        def _(e):
            e.memset(self.cbias[:, 0:1], float(LN_EPS))
            e.memset(self.cbias[:, 1:2], 1e-3)
            e.memset(self.cbias[:, 2:3], 1e-6)
            return e.memset(self.cbias[:, 3:4], 0.0)
        ph.emit()
        self.eps_ln = self.cbias[:, 0:1]
        self.eps_bl = self.cbias[:, 1:2]
        self.eps_rms = self.cbias[:, 2:3]

    def pair_allgather(self, src_dram, dst_dram):
        nc = self.nc
        s = self.s_cc
        with nc.Block() as block:
            @block.gpsimd
            def _(g):
                g.collective_compute(
                    "AllGather", ALU.bypass, replica_groups=PAIR_GROUPS,
                    ins=[src_dram.ap().opt()], outs=[dst_dram.ap().opt()],
                ).then_inc(s.h, 1)
                s.n += 1
                g.wait_ge(s.h, s.n)

    def allgather8(self, src_dram, dst_dram):
        nc = self.nc
        s = self.s_cc
        with nc.Block() as block:
            @block.gpsimd
            def _(g):
                g.collective_compute(
                    "AllGather", ALU.bypass, replica_groups=ALL_GROUP,
                    ins=[src_dram.ap().opt()], outs=[dst_dram.ap().opt()],
                ).then_inc(s.h, 1)
                s.n += 1
                g.wait_ge(s.h, s.n)

    def allreduce_max(self, buf_dram):
        nc = self.nc
        s = self.s_cc
        with nc.Block() as block:
            @block.gpsimd
            def _(g):
                g.collective_compute(
                    "AllReduce", ALU.max, replica_groups=ALL_GROUP,
                    ins=[buf_dram.ap().opt()], outs=[buf_dram.ap().opt()],
                ).then_inc(s.h, 1)
                s.n += 1
                g.wait_ge(s.h, s.n)


def bap(t, off, dims):
    return bass.AP(t, off, dims)



def emit_ln_to_dram(pr, x_in, a_out, tag, n_tok=TOWN, out_dtype=BF16):
    nc = pr.nc
    NCH = n_tok // 512
    es = ExitStack()
    with es:
        sb = lambda n, s, d: es.enter_context(nc.sbuf_tensor(f"{tag}_{n}", s, d))
        xf = sb("xf", [P, 2 * DC * 512], F32)
        xb = sb("xb", [P, DC * 512], BF16)
        xq = sb("xq", [P, DC * 512], BF16)
        nb = sb("nb", [P, 512], F32)
        mr = sb("mr", [P, 512], F32)
        rr = sb("rr", [P, 512], F32)
        ab = sb("ab", [P, 2 * DC * 512], out_dtype)
        ps_m = es.enter_context(nc.psum_tensor(f"{tag}_pm", [P, 512], F32))
        ps_s = es.enter_context(nc.psum_tensor(f"{tag}_ps", [P, 512], F32))

        x3 = x_in.rearrange("(c p) t -> p c t", p=P)
        a3 = a_out.rearrange("(c p) t -> p c t", p=P)
        xfv = xf[:].rearrange("p (b c t) -> p b c t", b=2, c=DC)
        abv = ab[:].rearrange("p (b c t) -> p b c t", b=2, c=DC)

        ph = Phase(pr, tag)
        ld = {}; cast = {}; smm = {}; comb = {}; rstd = {}; norm = {}; st = {}
        for i in range(NCH):
            bf = i % 2

            @ph.op("sync", deps=(norm.get(i - 2),), inc=16, name=f"ld{i}")
            def _(e, i=i, bf=bf):
                return e.dma_start(xfv[:, bf], x3[:, :, ts(i, 512)])
            ld[i] = ph.ops[-1]

            @ph.op("vector", deps=(ld[i], smm.get(i - 1)), name=f"cast{i}")
            def _(e, i=i, bf=bf):
                e.tensor_copy(xb[:], xfv[:, bf].rearrange("p c t -> p (c t)"))
                return e.tensor_mul(xq[:], xb[:], xb[:])
            cast[i] = ph.ops[-1]

            @ph.op("tensor", deps=(cast[i], comb.get(i - 1)), name=f"smm{i}")
            def _(e, i=i):
                for dc in range(DC):
                    e.matmul(ps_m[:], pr.ones_bf[:], xb[:, ts(dc, 512)],
                             start=(dc == 0), stop=(dc == DC - 1),
                             skip_group_check=True)
                for dc in range(DC):
                    ins = e.matmul(ps_s[:], pr.ones_bf[:], xq[:, ts(dc, 512)],
                                   start=(dc == 0), stop=(dc == DC - 1),
                                   skip_group_check=True)
                return ins
            smm[i] = ph.ops[-1]

            @ph.op("vector", deps=(smm[i], rstd.get(i - 1)), name=f"comb{i}")
            def _(e, i=i):
                e.tensor_scalar_mul(mr[:], ps_m[:], 1.0 / D)
                e.tensor_mul(rr[:], mr[:], mr[:])
                return e.scalar_tensor_tensor(rr[:], ps_s[:], 1.0 / D, rr[:],
                                              ALU.mult, ALU.subtract)
            comb[i] = ph.ops[-1]

            @ph.op("scalar", deps=(comb[i],), name=f"rstd{i}")
            def _(e, i=i):
                e.activation(rr[:], rr[:], AF.Ln, bias=pr.eps_ln)
                return e.activation(rr[:], rr[:], AF.Exp, scale=-0.5)
            rstd[i] = ph.ops[-1]

            @ph.op("vector", deps=(rstd[i], st.get(i - 2)), name=f"norm{i}")
            def _(e, i=i, bf=bf):
                for dc in range(DC):
                    e.tensor_sub(nb[:], xfv[:, bf, dc], mr[:])
                    ins = e.tensor_mul(abv[:, bf, dc], nb[:], rr[:])
                return ins
            norm[i] = ph.ops[-1]

            @ph.op("gpsimd", deps=(norm[i],), inc=16, name=f"st{i}")
            def _(e, i=i, bf=bf):
                return e.dma_start(a3[:, :, ts(i, 512)], abv[:, bf])
            st[i] = ph.ops[-1]

        @ph.op("gpsimd", deps=(st[NCH - 1],), name="fin")
        def _(e):
            return e.memset(nb[0:1, 0:1], 0.0)
        ph.emit()



def emit_qkv_rope(pr, ag_a_out, wq_sb, wk_sb, wv_sb, QT, KT, VT,
                  tag, rope=True):
    nc = pr.nc
    es = ExitStack()
    with es:
        sb = lambda n, s, d: es.enter_context(nc.sbuf_tensor(f"{tag}_{n}", s, d))
        at = sb("at", [P, 2 * DC * P], BF16)
        qs = sb("qs", [P, 512], F32)
        qs2 = sb("qs2", [P, 512], F32)
        qro = sb("qro", [P, 512], BF16)
        kro = sb("kro", [P, 512], BF16)
        ps_q = es.enter_context(nc.psum_tensor(f"{tag}_pq", [P, 512], F32))
        ps_k = es.enter_context(nc.psum_tensor(f"{tag}_pk", [P, 512], F32))
        ps_v = es.enter_context(nc.psum_tensor(f"{tag}_pv", [P, 512], F32))
        ps_tq = es.enter_context(nc.psum_tensor(f"{tag}_ptq", [P, 8 * P], BF16))
        ps_tk = es.enter_context(nc.psum_tensor(f"{tag}_ptk", [P, 8 * P], BF16))

        agv = ag_a_out.ap().rearrange("(h c p) t -> p h c t", p=P, c=DC)
        atv = at[:].rearrange("p (b c q) -> p b c q", b=2, c=DC)
        QTv = QT[:].rearrange("p (hp t) -> p hp t", hp=NHP)
        KTv = KT[:].rearrange("p (hp t) -> p hp t", hp=NHP)
        VTv = VT[:].rearrange("p (tt h e) -> p tt h e", tt=NTT, h=NHC)
        wqv = wq_sb[:].rearrange("p (c n) -> p c n", c=DC)
        wkv = wk_sb[:].rearrange("p (c n) -> p c n", c=DC)
        wvv = wv_sb[:].rearrange("p (c n) -> p c n", c=DC)

        ph = Phase(pr, tag)
        lda = {}; mmq = {}; mmk = {}; mmv = {}
        rpq = {}; rpk = {}; evv = {}; trq = {}; trk = {}; evq = {}; evk = {}
        for tt in range(NTT):
            bf = tt % 2
            h = (tt * P) // TOWN
            co = (tt * P) % TOWN

            @ph.op("sync", deps=(mmv.get(tt - 2),), inc=16, name=f"lda{tt}")
            def _(e, h=h, co=co, bf=bf):
                return e.dma_start(atv[:, bf], agv[:, h, :, co:co + P])
            lda[tt] = ph.ops[-1]

            @ph.op("tensor", deps=(lda[tt], rpq.get(tt - 1)), name=f"mmq{tt}")
            def _(e, bf=bf):
                for dc in range(DC):
                    ins = e.matmul(ps_q[:], atv[:, bf, dc], wqv[:, dc],
                                   start=(dc == 0), stop=(dc == DC - 1))
                return ins
            mmq[tt] = ph.ops[-1]

            @ph.op("tensor", deps=(rpk.get(tt - 1),), name=f"mmk{tt}")
            def _(e, bf=bf):
                for dc in range(DC):
                    ins = e.matmul(ps_k[:], atv[:, bf, dc], wkv[:, dc],
                                   start=(dc == 0), stop=(dc == DC - 1))
                return ins
            mmk[tt] = ph.ops[-1]

            @ph.op("tensor", deps=(evv.get(tt - 1),), name=f"mmv{tt}")
            def _(e, bf=bf):
                for dc in range(DC):
                    ins = e.matmul(ps_v[:], atv[:, bf, dc], wvv[:, dc],
                                   start=(dc == 0), stop=(dc == DC - 1))
                return ins
            mmv[tt] = ph.ops[-1]

            @ph.op("vector", deps=(mmq[tt], trq.get(tt - 1)), name=f"rpq{tt}")
            def _(e, tt=tt):
                if rope:
                    _rope_rm(nc, e, ps_q, qs, qs2, qro, pr.cos, pr.sin, tt)
                    return e.tensor_copy(qro[:, 0:8], qro[:, 0:8])
                return e.tensor_copy(qro[:], ps_q[:])
            rpq[tt] = ph.ops[-1]

            @ph.op("vector", deps=(mmk[tt], trk.get(tt - 1)), name=f"rpk{tt}")
            def _(e, tt=tt):
                if rope:
                    _rope_rm(nc, e, ps_k, qs, qs2, kro, pr.cos, pr.sin, tt)
                    return e.tensor_copy(kro[:, 0:8], kro[:, 0:8])
                return e.tensor_copy(kro[:], ps_k[:])
            rpk[tt] = ph.ops[-1]

            @ph.op("vector", deps=(mmv[tt],), name=f"evv{tt}")
            def _(e, tt=tt):
                return e.tensor_copy(
                    VTv[:, tt].rearrange("p h e -> p (h e)"), ps_v[:])
            evv[tt] = ph.ops[-1]

            @ph.op("tensor", deps=(rpq[tt], evq.get(tt - 1)), name=f"trq{tt}")
            def _(e):
                for hp in range(NHP):
                    ins = e.transpose(ps_tq[:, ts(hp, P)], qro[:, ts(hp, P)],
                                      pr.ident[:])
                return ins
            trq[tt] = ph.ops[-1]

            @ph.op("tensor", deps=(rpk[tt], evk.get(tt - 1)), name=f"trk{tt}")
            def _(e):
                for hp in range(NHP):
                    ins = e.transpose(ps_tk[:, ts(hp, P)], kro[:, ts(hp, P)],
                                      pr.ident[:])
                return ins
            trk[tt] = ph.ops[-1]

            @ph.op("scalar", deps=(trq[tt],), name=f"evq{tt}")
            def _(e, tt=tt):
                for hp in range(NHP):
                    ins = e.copy(QTv[:, hp, ts(tt, P)], ps_tq[:, ts(hp, P)])
                return ins
            evq[tt] = ph.ops[-1]

            @ph.op("scalar", deps=(trk[tt],), name=f"evk{tt}")
            def _(e, tt=tt):
                for hp in range(NHP):
                    ins = e.copy(KTv[:, hp, ts(tt, P)], ps_tk[:, ts(hp, P)])
                return ins
            evk[tt] = ph.ops[-1]
        ph.emit()


def _rope_rm(nc, e, ps_in, t1s, t2s, out_bf, cosT, sinT, tt):
    ev = bap(ps_in, 0, [[512, P], [64, NHC], [1, 32]])
    ov = bap(ps_in, 32, [[512, P], [64, NHC], [1, 32]])
    t1 = bap(t1s, 0, [[512, P], [32, NHC], [1, 32]])
    t2 = bap(t2s, 0, [[512, P], [32, NHC], [1, 32]])
    oe = bap(out_bf, 0, [[512, P], [64, NHC], [1, 32]])
    oo = bap(out_bf, 32, [[512, P], [64, NHC], [1, 32]])
    cb = bap(cosT, tt * 32, [[1024, P], [0, NHC], [1, 32]])
    sn = bap(sinT, tt * 32, [[1024, P], [0, NHC], [1, 32]])
    e.tensor_mul(t1, ev, cb)
    e.tensor_mul(t2, ov, sn)
    e.tensor_sub(oe, t1, t2)
    e.tensor_mul(t1, ev, sn)
    e.tensor_mul(t2, ov, cb)
    e.tensor_add(oo, t1, t2)



def emit_causal_attention(pr, QT, KT, VT, ag_t_in, scale, tag,
                          k_fn=None, v_fn=None):
    nc = pr.nc
    es = ExitStack()
    with es:
        sb = lambda n, s, d: es.enter_context(nc.sbuf_tensor(f"{tag}_{n}", s, d))
        PA = sb("PA", [P, 2 * 512], BF16)
        PB = sb("PB", [P, 2 * 512], BF16)
        att = sb("att", [P, 512], BF16)
        attf = sb("attf", [P, 2 * 512], BF16)
        rec = sb("rec", [P, 512], F32)
        ps_sA = es.enter_context(nc.psum_tensor(f"{tag}_sA", [P, 2 * 512], F32))
        ps_sB = es.enter_context(nc.psum_tensor(f"{tag}_sB", [P, 2 * 512], F32))
        ps_pv = es.enter_context(nc.psum_tensor(f"{tag}_pv", [P, 512], F32))
        ps_dn = es.enter_context(nc.psum_tensor(f"{tag}_dn", [P, 512], F32))
        ps_bc = es.enter_context(nc.psum_tensor(f"{tag}_bc", [P, 2 * 512], F32))

        QTv = QT[:].rearrange("p (hp t) -> p hp t", hp=NHP)
        if k_fn is None:
            KTv = KT[:].rearrange("p (hp t) -> p hp t", hp=NHP)
            k_fn = lambda KTv, hp, half, kt: \
                KTv[half * 64:(half + 1) * 64, hp, ts(kt, P)]
        else:
            KTv = None
        if v_fn is None:
            VTv = VT[:].rearrange("p (tt h e) -> p tt h e", tt=NTT, h=NHC)
            v_fn = lambda VTv, hp, half, kt: VTv[:, kt, 2 * hp + half, :]
        else:
            VTv = None
        agv = ag_t_in.ap().rearrange("(hp q) t -> q hp t", q=P)

        ph = Phase(pr, tag)
        S = {}; MK = {}; EX = {}; PV = {}
        RC = {}; BC = {}; EV = {}; SCL = {}; STO = {}
        blocks = [(hp, qb) for hp in range(NHP) for qb in range(NQB)]

        def reg_S(hp, qb, kt):
            nkt = 4 * (qb + 1)
            bf = kt % 2
            deps = [EX.get((hp, qb, kt - 2))]
            if kt < 2:
                L = hp * NQB + qb
                if L > 0:
                    php, pqb = divmod(L - 1, NQB)
                    deps.append(EX[(php, pqb, 4 * (pqb + 1) - 1)])

            @ph.op("tensor", deps=tuple(deps), name=f"S{hp}_{qb}_{kt}")
            def _(e, hp=hp, qb=qb, kt=kt, bf=bf):
                e.matmul(ps_sA[:, ts(bf, 512)],
                         k_fn(KTv, hp, 0, kt),
                         QTv[0:64, hp, ts(qb, 512)],
                         start=True, stop=True, tile_position=(0, 0))
                return e.matmul(ps_sB[:, ts(bf, 512)],
                                k_fn(KTv, hp, 1, kt),
                                QTv[64:128, hp, ts(qb, 512)],
                                start=True, stop=True, tile_position=(64, 0))
            S[(hp, qb, kt)] = ph.ops[-1]

            if kt >= 4 * qb:
                j = kt - 4 * qb

                @ph.op("vector", deps=(S[(hp, qb, kt)],), name=f"MK{hp}_{qb}_{kt}")
                def _(e, bf=bf, j=j):
                    off = 384 - j * P
                    e.tensor_add(ps_sA[:, ts(bf, 512)], ps_sA[:, ts(bf, 512)],
                                 pr.stair[:, off:off + 512])
                    return e.tensor_add(ps_sB[:, ts(bf, 512)],
                                        ps_sB[:, ts(bf, 512)],
                                        pr.stair[:, off:off + 512])
                MK[(hp, qb, kt)] = ph.ops[-1]

            deps = [MK.get((hp, qb, kt)) if kt >= 4 * qb else S[(hp, qb, kt)],
                    PV.get((hp, qb, kt - 2))]
            if kt < 2:
                L = hp * NQB + qb
                if L > 0:
                    php, pqb = divmod(L - 1, NQB)
                    deps.append(PV[(php, pqb, 4 * (pqb + 1) - 1)])

            @ph.op("scalar", deps=tuple(deps), name=f"EX{hp}_{qb}_{kt}")
            def _(e, bf=bf):
                e.activation(PA[:, ts(bf, 512)], ps_sA[:, ts(bf, 512)],
                             AF.Exp, scale=float(scale))
                return e.activation(PB[:, ts(bf, 512)], ps_sB[:, ts(bf, 512)],
                                    AF.Exp, scale=float(scale))
            EX[(hp, qb, kt)] = ph.ops[-1]

        def reg_PV(hp, qb, kt):
            nkt = 4 * (qb + 1)
            bf = kt % 2
            deps = [EX[(hp, qb, kt)]]
            if kt == 0:
                L = hp * NQB + qb
                if L > 0:
                    php, pqb = divmod(L - 1, NQB)
                    deps.append(SCL[(php, pqb)])

            @ph.op("tensor", deps=tuple(deps), name=f"PV{hp}_{qb}_{kt}")
            def _(e, hp=hp, kt=kt, bf=bf, st=(kt == 0), sp=(kt == nkt - 1)):
                e.matmul(ps_pv[0:64, :], v_fn(VTv, hp, 0, kt),
                         PA[:, ts(bf, 512)], start=st, stop=sp,
                         tile_position=(0, 0), skip_group_check=True)
                e.matmul(ps_pv[64:128, :], v_fn(VTv, hp, 1, kt),
                         PB[:, ts(bf, 512)], start=st, stop=sp,
                         tile_position=(0, 64), skip_group_check=True)
                e.matmul(ps_dn[0:32, :], pr.ones_bf[:, 0:32],
                         PA[:, ts(bf, 512)], start=st, stop=sp,
                         tile_position=(0, 0), skip_group_check=True)
                return e.matmul(ps_dn[32:64, :], pr.ones_bf[:, 0:32],
                                PB[:, ts(bf, 512)], start=st, stop=sp,
                                tile_position=(0, 32), skip_group_check=True)
            PV[(hp, qb, kt)] = ph.ops[-1]

        for hp, qb in blocks:
            nkt = 4 * (qb + 1)
            L = hp * NQB + qb
            for kt in range(nkt):
                reg_S(hp, qb, kt)
                if kt >= 1:
                    reg_PV(hp, qb, kt - 1)
            reg_PV(hp, qb, nkt - 1)
            last = PV[(hp, qb, nkt - 1)]

            @ph.op("vector", deps=(last,), name=f"RC{hp}_{qb}")
            def _(e):
                return e.reciprocal(rec[0:64, :], ps_dn[0:64, :])
            RC[(hp, qb)] = ph.ops[-1]

            deps = [RC[(hp, qb)]]
            if L > 0:
                php, pqb = divmod(L - 1, NQB)
                deps.append(SCL[(php, pqb)])

            @ph.op("tensor", deps=tuple(deps), name=f"BC{hp}_{qb}")
            def _(e):
                e.matmul(ps_bc[:, 0:512], pr.ones_f32r[0:1, :], rec[0:1, :],
                         start=True, stop=True)
                return e.matmul(ps_bc[:, 512:1024], pr.ones_f32r[32:33, :],
                                rec[32:33, :], start=True, stop=True)
            BC[(hp, qb)] = ph.ops[-1]

            deps = [last]
            if L > 0:
                php, pqb = divmod(L - 1, NQB)
                deps.append(SCL[(php, pqb)])

            @ph.op("scalar", deps=tuple(deps), name=f"EV{hp}_{qb}")
            def _(e):
                return e.copy(att[:], ps_pv[:])
            EV[(hp, qb)] = ph.ops[-1]

            deps = [BC[(hp, qb)], EV[(hp, qb)]]
            if L >= 2:
                php, pqb = divmod(L - 2, NQB)
                deps.append(STO[(php, pqb)])

            @ph.op("vector", deps=tuple(deps), name=f"SCL{hp}_{qb}")
            def _(e, bf=L % 2):
                e.tensor_mul(attf[0:64, ts(bf, 512)], att[0:64, :],
                             ps_bc[0:64, 0:512])
                return e.tensor_mul(attf[64:128, ts(bf, 512)], att[64:128, :],
                                    ps_bc[64:128, 512:1024])
            SCL[(hp, qb)] = ph.ops[-1]

            @ph.op("gpsimd", deps=(SCL[(hp, qb)],), inc=16, name=f"STO{hp}_{qb}")
            def _(e, hp=hp, qb=qb, bf=L % 2):
                return e.dma_start(agv[:, hp, ts(qb, 512)], attf[:, ts(bf, 512)])
            STO[(hp, qb)] = ph.ops[-1]

        @ph.op("gpsimd", deps=(STO[(NHP - 1, NQB - 1)],), name="fin")
        def _(e):
            return e.memset(rec[0:1, 0:1], 0.0)
        ph.emit()



def emit_oproj_ln_residual(pr, ag_t_out, wo_sb, wo_ld_op, x_in, x_out, tag,
                           rhs_view_fn=None, out_scale_ap=None,
                           rhs_dep=None):
    nc = pr.nc
    NST = TOWN // 512
    es = ExitStack()
    with es:
        sb = lambda n, s, d: es.enter_context(nc.sbuf_tensor(f"{tag}_{n}", s, d))
        ats = sb("ats", [P, 2 * DC * 512], BF16)
        yb = sb("yb", [P, DC * 512], BF16)
        yq = sb("yq", [P, DC * 512], BF16)
        xf = sb("xf", [P, 2 * DC * 512], F32)
        mr = sb("mr", [P, 512], F32)
        rr = sb("rr", [P, 512], F32)
        xo = sb("xo", [P, 2 * DC * 512], F32)
        t1 = sb("t1", [P, 512], F32)
        ps_y = es.enter_context(nc.psum_tensor(f"{tag}_py", [P, 2 * 512], F32))
        ps_m = es.enter_context(nc.psum_tensor(f"{tag}_pm", [P, 512], F32))
        ps_s = es.enter_context(nc.psum_tensor(f"{tag}_ps", [P, 512], F32))

        agv = (ag_t_out.ap().rearrange("(c p) t -> p c t", p=P)
               if ag_t_out is not None else None)
        x3 = x_in.rearrange("(c p) t -> p c t", p=P)
        o3 = x_out.rearrange("(c p) t -> p c t", p=P)
        atsv = ats[:].rearrange("p (b c t) -> p b c t", b=2, c=DC)
        xfv = xf[:].rearrange("p (b c t) -> p b c t", b=2, c=DC)
        xov = xo[:].rearrange("p (b c t) -> p b c t", b=2, c=DC)
        wov = wo_sb[:].rearrange("p (c n) -> p c n", c=DC)

        ph = Phase(pr, tag)
        lda = {}; ldx = {}; mmy = {}; evy = {}; sqy = {}; smm = {}
        comb = {}; rstd = {}; add = {}; sto = {}
        for i in range(NST):
            bf = i % 2

            if rhs_view_fn is None:
                @ph.op("sync", deps=(mmy.get((i - 2, DC - 1)),), inc=16,
                       name=f"lda{i}")
                def _(e, i=i, bf=bf):
                    pid = nc.partition_id(engines=[ET.SP])
                    return e.dma_start(atsv[:, bf],
                                       agv[:, :, ts((pid % 2) * NST + i, 512)])
                lda[i] = ph.ops[-1]
            else:
                lda[i] = rhs_dep

            @ph.op("sync", deps=(add.get(i - 2),), inc=16, name=f"ldx{i}")
            def _(e, i=i, bf=bf):
                return e.dma_start(xfv[:, bf], x3[:, :, ts(i, 512)])
            ldx[i] = ph.ops[-1]

            def reg_mmy(i, oc, bf):
                yb2 = oc % 2

                @ph.op("tensor",
                       deps=(lda[i], evy.get((i, oc - 2)),
                             evy.get((i - 1, DC - 1)) if oc < 2 else None,
                             (wo_ld_op if (i == 0 and oc == 0) else None)),
                       name=f"mmy{i}_{oc}")
                def _(e, i=i, oc=oc, bf=bf, yb2=yb2):
                    for ec in range(DC):
                        rhs = (atsv[:, bf, ec] if rhs_view_fn is None
                               else rhs_view_fn(i, ec))
                        ins = e.matmul(ps_y[:, ts(yb2, 512)],
                                       wov[:, ec, ts(oc, P)], rhs,
                                       start=(ec == 0), stop=(ec == DC - 1),
                                       skip_group_check=True)
                    return ins
                mmy[(i, oc)] = ph.ops[-1]

                @ph.op("scalar", deps=(mmy[(i, oc)], add.get(i - 1)),
                       name=f"evy{i}_{oc}")
                def _(e, oc=oc, yb2=yb2):
                    if out_scale_ap is not None:
                        return e.activation(yb[:, ts(oc, 512)],
                                            ps_y[:, ts(yb2, 512)], AF.Copy,
                                            scale=out_scale_ap)
                    return e.copy(yb[:, ts(oc, 512)], ps_y[:, ts(yb2, 512)])
                evy[(i, oc)] = ph.ops[-1]

                @ph.op("vector", deps=(evy[(i, oc)],), name=f"sqy{i}_{oc}")
                def _(e, oc=oc):
                    return e.tensor_mul(yq[:, ts(oc, 512)], yb[:, ts(oc, 512)],
                                        yb[:, ts(oc, 512)])
                sqy[(i, oc)] = ph.ops[-1]

            def reg_smm(i, oc):
                @ph.op("tensor", deps=(sqy[(i, oc)], comb.get(i - 1)),
                       name=f"smm{i}_{oc}")
                def _(e, oc=oc):
                    e.matmul(ps_m[:], pr.ones_bf[:], yb[:, ts(oc, 512)],
                             start=(oc == 0), stop=(oc == DC - 1),
                             skip_group_check=True)
                    return e.matmul(ps_s[:], pr.ones_bf[:], yq[:, ts(oc, 512)],
                                    start=(oc == 0), stop=(oc == DC - 1),
                                    skip_group_check=True)
                smm[(i, oc)] = ph.ops[-1]

            for oc in range(DC):
                reg_mmy(i, oc, bf)
                if oc >= 1:
                    reg_smm(i, oc - 1)
            reg_smm(i, DC - 1)

            @ph.op("vector", deps=(smm[(i, DC - 1)], rstd.get(i - 1)),
                   name=f"comb{i}")
            def _(e):
                e.tensor_scalar_mul(mr[:], ps_m[:], 1.0 / D)
                e.tensor_mul(rr[:], mr[:], mr[:])
                return e.scalar_tensor_tensor(rr[:], ps_s[:], 1.0 / D, rr[:],
                                              ALU.mult, ALU.subtract)
            comb[i] = ph.ops[-1]

            @ph.op("scalar", deps=(comb[i],), name=f"rstd{i}")
            def _(e):
                e.activation(rr[:], rr[:], AF.Ln, bias=pr.eps_ln)
                return e.activation(rr[:], rr[:], AF.Exp, scale=-0.5)
            rstd[i] = ph.ops[-1]

            @ph.op("vector", deps=(rstd[i], ldx[i], sto.get(i - 2)),
                   name=f"add{i}")
            def _(e, i=i, bf=bf):
                for oc in range(DC):
                    e.tensor_sub(t1[:], yb[:, ts(oc, 512)], mr[:])
                    e.tensor_mul(t1[:], t1[:], rr[:])
                    ins = e.tensor_add(xov[:, bf, oc], xfv[:, bf, oc], t1[:])
                return ins
            add[i] = ph.ops[-1]

            @ph.op("gpsimd", deps=(add[i],), inc=16, name=f"sto{i}")
            def _(e, i=i, bf=bf):
                return e.dma_start(o3[:, :, ts(i, 512)], xov[:, bf])
            sto[i] = ph.ops[-1]

        @ph.op("gpsimd", deps=(sto[NST - 1],), name="fin")
        def _(e):
            return e.memset(t1[0:1, 0:1], 0.0)
        ph.emit()



def build_l0_attn():
    pr = Prog()
    nc = pr.nc
    x_in = nc.dram_tensor("x_in", [D, TOWN], F32, kind="ExternalInput").ap()
    wq = nc.dram_tensor("l0_wq", [D, NHC * HD], F32, kind="ExternalInput").ap()
    wk = nc.dram_tensor("l0_wk", [D, NHC * HD], F32, kind="ExternalInput").ap()
    wv = nc.dram_tensor("l0_wv", [D, NHC * HD], F32, kind="ExternalInput").ap()
    wo = nc.dram_tensor("l0_wo", [D, D], F32, kind="ExternalInput").ap()
    x_out = nc.dram_tensor("x_out", [D, TOWN], F32, kind="ExternalOutput").ap()
    ag_a_in = nc.dram_tensor("ag_a_in", [D, TOWN], BF16)
    ag_a_out = nc.dram_tensor("ag_a_out", [2 * D, TOWN], BF16)
    ag_t_in = nc.dram_tensor("ag_t_in", [NHC * HD, T], BF16)
    ag_t_out = nc.dram_tensor("ag_t_out", [2 * NHC * HD, T], BF16)

    pr.load_consts(rope=True)
    emit_ln_to_dram(pr, x_in, ag_a_in.ap(), "ln1")
    pr.pair_allgather(ag_a_in, ag_a_out)
    emit_attn_mid_l0(pr, ag_a_out, wq, wk, wv, ag_t_in)
    pr.pair_allgather(ag_t_in, ag_t_out)
    emit_oproj_block(pr, ag_t_out, wo, x_in, x_out, "opj")
    pr.close()
    return pr.nc


def load_weight_dmajor(pr, ph, w_dram, w_sb, name):
    @ph.op("gpsimd", inc=16, name=name)
    def _(e):
        return e.dma_start(
            w_sb[:].rearrange("p (c n) -> p c n", c=DC),
            w_dram.rearrange("(c p) n -> p c n", p=P))
    return ph.ops[-1]


def emit_attn_mid_l0(pr, ag_a_out, wq, wk, wv, ag_t_in):
    nc = pr.nc
    with ExitStack() as es:
        sbl = lambda n, s, d: es.enter_context(nc.sbuf_tensor(n, s, d))
        wq_sb = sbl("wq_sb", [P, DC * NHC * HD], BF16)
        wk_sb = sbl("wk_sb", [P, DC * NHC * HD], BF16)
        wv_sb = sbl("wv_sb", [P, DC * NHC * HD], BF16)
        QT = sbl("QT", [P, NHP * T], BF16)
        KT = sbl("KT", [P, NHP * T], BF16)
        VT = sbl("VT", [P, NTT * NHC * HD], BF16)

        phw = Phase(pr, "wload")
        wops = [load_weight_dmajor(pr, phw, wq, wq_sb, "wq"),
                load_weight_dmajor(pr, phw, wk, wk_sb, "wk"),
                load_weight_dmajor(pr, phw, wv, wv_sb, "wv")]

        @phw.op("gpsimd", deps=tuple(wops), name="wfin")
        def _(e):
            return e.memset(QT[0:1, 0:1], 0.0)
        phw.emit()

        emit_qkv_rope(pr, ag_a_out, wq_sb, wk_sb, wv_sb, QT, KT, VT, "qkv",
                      rope=True)
        emit_causal_attention(pr, QT, KT, VT, ag_t_in, scale=1.0 / 8.0,
                              tag="att")


def emit_oproj_block(pr, ag_t_out, wo, x_in, x_out, tag):
    nc = pr.nc
    with ExitStack() as es:
        wo_sb = es.enter_context(nc.sbuf_tensor("wo_sb", [P, DC * D], BF16))
        phw = Phase(pr, "woload")
        wo_op = load_weight_dmajor(pr, phw, wo, wo_sb, "wo")

        @phw.op("gpsimd", deps=(wo_op,), name="wofin")
        def _(e):
            return e.memset(wo_sb[0:1, 0:1], 0.0)
        phw.emit()
        emit_oproj_ln_residual(pr, ag_t_out, wo_sb, None, x_in, x_out, tag)



def emit_bitlinear_wq(pr, w_dram, wq_sb, beta_rep, tag):
    nc = pr.nc
    with ExitStack() as es:
        sb = lambda n, s, d: es.enter_context(nc.sbuf_tensor(f"{tag}_{n}", s, d))
        wf = sb("wf", [P, DC * D], F32)
        aw = sb("aw", [P, DC * D], F32)
        acc = sb("acc", [P, DC], F32)
        acs = sb("acs", [P, 1], F32)
        sgn = sb("sgn", [P, DC * D], BF16)
        ps_b = es.enter_context(nc.psum_tensor(f"{tag}_pb", [P, 512], F32))

        ph = Phase(pr, tag)

        @ph.op("sync", inc=16, name="ldw")
        def _(e):
            return e.dma_start(wf[:].rearrange("p (c n) -> p c n", c=DC),
                               w_dram.rearrange("(c p) n -> p c n", p=P))
        ldw = ph.ops[-1]

        @ph.op("scalar", deps=(ldw,), name="absw")
        def _(e):
            for dc in range(DC):
                ins = e.activation(aw[:, ts(dc, D)], wf[:, ts(dc, D)], AF.Abs,
                                   accum_out=acc[:, dc:dc + 1])
            return ins
        absw = ph.ops[-1]

        @ph.op("vector", deps=(absw,), name="accsum")
        def _(e):
            return e.tensor_reduce(acs[:, 0:1], acc[:],
                                   mybir.AxisListType.X, ALU.add)
        accs = ph.ops[-1]

        @ph.op("tensor", deps=(accs,), name="betamm")
        def _(e):
            return e.matmul(ps_b[0:1, 0:1], pr.ones_f32c[:, 0:1], acs[:, 0:1],
                            start=True, stop=True)
        betm = ph.ops[-1]

        @ph.op("vector", deps=(betm,), name="betasc")
        def _(e):
            return e.tensor_scalar_mul(acs[0:1, 0:1], ps_b[0:1, 0:1],
                                       1.0 / (D * D))
        betsc = ph.ops[-1]

        @ph.op("tensor", deps=(betsc,), name="betabc")
        def _(e):
            return e.matmul(ps_b[:, 2:3], pr.ones_f32r[0:1, :], acs[0:1, 0:1],
                            start=True, stop=True)
        betbc = ph.ops[-1]

        @ph.op("vector", deps=(betbc,), name="betarep")
        def _(e):
            return e.tensor_copy(beta_rep[:, 0:1], ps_b[:, 2:3])
        brep = ph.ops[-1]

        @ph.op("scalar", deps=(ldw,), name="sgn")
        def _(e):
            return e.sign(sgn[:], wf[:])
        sg = ph.ops[-1]

        @ph.op("vector", deps=(brep,), name="thr")
        def _(e):
            return e.tensor_scalar(acs[:, 0:1], beta_rep[:, 0:1], 1e-5, 0.5,
                                   ALU.add, ALU.mult)
        throp = ph.ops[-1]

        @ph.op("vector", deps=(sg,), hdeps=(throp,), name="wq")
        def _(e):
            thr_bc = bap(acs, 0, [[1, P], [0, DC * D]])
            e.tensor_tensor(aw[:], aw[:], thr_bc, ALU.is_gt)
            return e.tensor_mul(wq_sb[:], sgn[:], aw[:])
        wqop = ph.ops[-1]

        @ph.op("gpsimd", deps=(wqop,), name="fin")
        def _(e):
            return e.memset(acc[0:1, 0:1], 0.0)
        ph.emit()


def emit_double_ln_absmax(pr, x_in, gam_dram, xn_sb, g_sb, b_sb, tag):
    nc = pr.nc
    NCH = TOWN // 512
    with ExitStack() as es:
        sb = lambda n, s, d: es.enter_context(nc.sbuf_tensor(f"{tag}_{n}", s, d))
        xf = sb("xf", [P, 2 * DC * 512], F32)
        z1 = sb("z1", [P, DC * 512], F32)
        zb = sb("zb", [P, DC * 512], BF16)
        zq = sb("zq", [P, DC * 512], BF16)
        nb = sb("nb", [P, 512], F32)
        mr = sb("mr", [P, 512], F32)
        rr = sb("rr", [P, 512], F32)
        m2 = sb("m2", [P, 512], F32)
        r2 = sb("r2", [P, 512], F32)
        mx = sb("mx", [P, 2 + DC], F32)
        mxd = nc.dram_tensor(f"{tag}_mxd", [P, 1], F32)
        ps_m = es.enter_context(nc.psum_tensor(f"{tag}_pm", [P, 512], F32))
        ps_s = es.enter_context(nc.psum_tensor(f"{tag}_ps", [P, 512], F32))

        x3 = x_in.rearrange("(c p) t -> p c t", p=P)
        xfv = xf[:].rearrange("p (b c t) -> p b c t", b=2, c=DC)
        xnv = xn_sb[:].rearrange("p (c n t) -> p c n t", c=DC, n=NCH)

        ph = Phase(pr, tag)
        ld = {}; cast = {}; smm = {}; comb = {}; rstd = {}; z1o = {}
        smm2 = {}; comb2 = {}; rstd2 = {}; aff = {}; mxo = {}
        for i in range(NCH):
            bf = i % 2

            @ph.op("sync", deps=(z1o.get(i - 2),), inc=16, name=f"ld{i}")
            def _(e, i=i, bf=bf):
                return e.dma_start(xfv[:, bf], x3[:, :, ts(i, 512)])
            ld[i] = ph.ops[-1]

            @ph.op("vector", deps=(ld[i], smm.get(i - 1)), name=f"cast{i}")
            def _(e, bf=bf):
                e.tensor_copy(zb[:], xfv[:, bf].rearrange("p c t -> p (c t)"))
                return e.tensor_mul(zq[:], zb[:], zb[:])
            cast[i] = ph.ops[-1]

            @ph.op("tensor", deps=(cast[i], comb.get(i - 1)), name=f"smm{i}")
            def _(e):
                for dc in range(DC):
                    e.matmul(ps_m[:], pr.ones_bf[:], zb[:, ts(dc, 512)],
                             start=(dc == 0), stop=(dc == DC - 1),
                             skip_group_check=True)
                for dc in range(DC):
                    ins = e.matmul(ps_s[:], pr.ones_bf[:], zq[:, ts(dc, 512)],
                                   start=(dc == 0), stop=(dc == DC - 1),
                                   skip_group_check=True)
                return ins
            smm[i] = ph.ops[-1]

            @ph.op("vector", deps=(smm[i],), name=f"comb{i}")
            def _(e):
                e.tensor_scalar_mul(mr[:], ps_m[:], 1.0 / D)
                e.tensor_mul(rr[:], mr[:], mr[:])
                return e.scalar_tensor_tensor(rr[:], ps_s[:], 1.0 / D, rr[:],
                                              ALU.mult, ALU.subtract)
            comb[i] = ph.ops[-1]

            @ph.op("scalar", deps=(comb[i],), name=f"rstd{i}")
            def _(e):
                e.activation(rr[:], rr[:], AF.Ln, bias=pr.eps_ln)
                return e.activation(rr[:], rr[:], AF.Exp, scale=-0.5)
            rstd[i] = ph.ops[-1]

            @ph.op("vector", deps=(rstd[i], smm2.get(i - 1)), name=f"z1o{i}")
            def _(e, bf=bf):
                for dc in range(DC):
                    e.tensor_sub(nb[:], xfv[:, bf, dc], mr[:])
                    e.tensor_mul(z1[:, ts(dc, 512)], nb[:], rr[:])
                e.tensor_copy(zb[:], z1[:])
                return e.tensor_mul(zq[:], zb[:], zb[:])
            z1o[i] = ph.ops[-1]

            @ph.op("tensor", deps=(z1o[i], comb2.get(i - 1)), name=f"smm2{i}")
            def _(e):
                for dc in range(DC):
                    e.matmul(ps_m[:], pr.ones_bf[:], zb[:, ts(dc, 512)],
                             start=(dc == 0), stop=(dc == DC - 1),
                             skip_group_check=True)
                for dc in range(DC):
                    ins = e.matmul(ps_s[:], pr.ones_bf[:], zq[:, ts(dc, 512)],
                                   start=(dc == 0), stop=(dc == DC - 1),
                                   skip_group_check=True)
                return ins
            smm2[i] = ph.ops[-1]

            @ph.op("vector", deps=(smm2[i],), name=f"comb2{i}")
            def _(e):
                e.tensor_scalar_mul(m2[:], ps_m[:], 1.0 / D)
                e.tensor_mul(r2[:], m2[:], m2[:])
                return e.scalar_tensor_tensor(r2[:], ps_s[:], 1.0 / D, r2[:],
                                              ALU.mult, ALU.subtract)
            comb2[i] = ph.ops[-1]

            @ph.op("scalar", deps=(comb2[i],), name=f"rstd2{i}")
            def _(e):
                e.activation(r2[:], r2[:], AF.Ln, bias=pr.eps_bl)
                return e.activation(r2[:], r2[:], AF.Exp, scale=-0.5)
            rstd2[i] = ph.ops[-1]

            @ph.op("vector", deps=(rstd2[i],), name=f"z2o{i}")
            def _(e):
                for dc in range(DC):
                    e.tensor_sub(nb[:], z1[:, ts(dc, 512)], m2[:])
                    ins = e.tensor_mul(z1[:, ts(dc, 512)], nb[:], r2[:])
                return ins
            z2o = ph.ops[-1]

            @ph.op("vector", deps=(z2o,), name=f"aff{i}")
            def _(e, i=i):
                for dc in range(DC):
                    b_bc = bap(b_sb, dc, [[DC, P], [0, 512]])
                    ins = e.scalar_tensor_tensor(
                        xnv[:, dc, i], z1[:, ts(dc, 512)],
                        g_sb[:, dc:dc + 1], b_bc, ALU.mult, ALU.add)
                return ins
            aff[i] = ph.ops[-1]

            @ph.op("vector", deps=(aff[i], mxo.get(i - 1)), name=f"mxa{i}")
            def _(e, i=i):
                return e.tensor_reduce(mx[:, 2:2 + DC], xnv[:, :, i],
                                       mybir.AxisListType.X, ALU.max,
                                       apply_absolute_value=True)
            mxa = ph.ops[-1]

            @ph.op("vector", hdeps=(mxa,), name=f"mxb{i}")
            def _(e, i=i):
                return e.tensor_reduce(mx[:, 1:2], mx[:, 2:2 + DC],
                                       mybir.AxisListType.X, ALU.max)
            mxb = ph.ops[-1]

            @ph.op("vector", hdeps=(mxb,), name=f"mx{i}")
            def _(e, i=i):
                if i == 0:
                    return e.tensor_copy(mx[:, 0:1], mx[:, 1:2])
                return e.tensor_max(mx[:, 0:1], mx[:, 0:1], mx[:, 1:2])
            mxo[i] = ph.ops[-1]

        mxr = sb("mxr", [1, P], F32)
        g8 = sb("g8", [1, 8], F32)

        @ph.op("gpsimd", deps=(mxo[NCH - 1],), inc=16, name="mxst")
        def _(e):
            return e.dma_start(mxd.ap()[:, :], mx[:, 0:1])
        mxst = ph.ops[-1]

        @ph.op("sync", deps=(mxst,), inc=16, name="mxld")
        def _(e):
            return e.dma_start(mxr[0:1, :], mxd.ap().rearrange("p o -> o p"))
        mxld = ph.ops[-1]

        @ph.op("vector", deps=(mxld,), name="mxred")
        def _(e):
            e.memset(g8[:], 0.0)
            return e.tensor_reduce(g8[0:1, 0:1], mxr[0:1, :],
                                   mybir.AxisListType.X, ALU.max)
        mxred = ph.ops[-1]

        @ph.op("gpsimd", deps=(mxred,), inc=16, name="gst")
        def _(e):
            return e.dma_start(gam_dram.ap()[:, :], g8[0:1, :])
        gst = ph.ops[-1]

        @ph.op("gpsimd", deps=(gst,), name="fin")
        def _(e):
            return e.memset(mx[0:1, 1:2], 0.0)
        ph.emit()



def emit_gamma_scales(pr, gam_dram, beta_rep, sc1, s_out, tag):
    nc = pr.nc
    with ExitStack() as es:
        sb = lambda n, s, d: es.enter_context(nc.sbuf_tensor(f"{tag}_{n}", s, d))
        g8 = sb("g8", [1, 8], F32)
        ps_g = es.enter_context(nc.psum_tensor(f"{tag}_pg", [P, 2], F32))
        ph = Phase(pr, tag)

        @ph.op("sync", inc=16, name="gld")
        def _(e):
            return e.dma_start(g8[0:1, :], gam_dram.ap()[:, :])
        gld = ph.ops[-1]

        @ph.op("tensor", deps=(gld,), name="gbc")
        def _(e):
            return e.matmul(ps_g[:, 0:1], pr.ones_f32r[0:1, :], g8[0:1, 0:1],
                            start=True, stop=True)
        gbc = ph.ops[-1]

        @ph.op("vector", deps=(gbc,), name="grecip")
        def _(e):
            return e.reciprocal(sc1[:, 0:1], ps_g[:, 0:1])
        grc = ph.ops[-1]

        @ph.op("vector", deps=(gbc,), hdeps=(grc,), name="gscales")
        def _(e):
            return e.tensor_scalar_mul(sc1[:, 0:1], sc1[:, 0:1], 128.0)
        gs1 = ph.ops[-1]

        @ph.op("vector", deps=(gbc,), hdeps=(gs1,), name="gsout")
        def _(e):
            return e.tensor_scalar(s_out[:, 0:1], ps_g[:, 0:1],
                                   beta_rep[:, 0:1], 1.0 / 128.0,
                                   ALU.mult, ALU.mult)
        gs = ph.ops[-1]

        @ph.op("gpsimd", deps=(gs,), name="fin")
        def _(e):
            return e.memset(g8[0:1, 0:1], 0.0)
        ph.emit()


def build_l0_ffn():
    pr = Prog()
    nc = pr.nc
    x_in = nc.dram_tensor("x_in", [D, TOWN], F32, kind="ExternalInput").ap()
    w = nc.dram_tensor("l0f_w", [D, D], F32, kind="ExternalInput").ap()
    g_in = nc.dram_tensor("l0f_g", [P, DC], F32, kind="ExternalInput").ap()
    b_in = nc.dram_tensor("l0f_b", [P, DC], F32, kind="ExternalInput").ap()
    x_out = nc.dram_tensor("x_out", [D, TOWN], F32, kind="ExternalOutput").ap()
    gam_dram = nc.dram_tensor("gam", [1, 8], F32)

    pr.load_consts(rope=False)
    wq_sb = pr.sb("blw", [P, DC * D], BF16)
    beta_rep = pr.sb("beta", [P, 1], F32)
    xn_sb = pr.sb("xn", [P, DC * TOWN], BF16)
    g_sb = pr.sb("gg", [P, DC], F32)
    b_sb = pr.sb("bb", [P, DC], F32)
    sc1 = pr.sb("sc1", [P, 1], F32)
    s_out = pr.sb("souts", [P, 1], F32)

    ph0 = Phase(pr, "gbld")

    @ph0.op("sync", inc=16, name="gld")
    def _(e):
        return e.dma_start(g_sb[:], g_in[:, :])

    @ph0.op("sync", inc=16, name="bld")
    def _(e):
        return e.dma_start(b_sb[:], b_in[:, :])
    ph0.emit()

    emit_bitlinear_wq(pr, w, wq_sb, beta_rep, "blq")
    emit_double_ln_absmax(pr, x_in, gam_dram, xn_sb, g_sb, b_sb, "dln")
    pr.allreduce_max(gam_dram)
    emit_gamma_scales(pr, gam_dram, beta_rep, sc1, s_out, "gsc")

    NCH = TOWN // 512
    xnv = xn_sb[:].rearrange("p (c n t) -> p c n t", c=DC, n=NCH)
    phq = Phase(pr, "xq")

    @phq.op("vector", name="clip")
    def _(e):
        hi = 128.0 - 1e-5
        e.tensor_scalar(xn_sb[:], xn_sb[:], sc1[:, 0:1], -hi, ALU.mult, ALU.max)
        return e.tensor_scalar_min(xn_sb[:], xn_sb[:], hi)
    clip_op = phq.ops[-1]
    phq.emit()

    emit_oproj_ln_residual(
        pr, None, wq_sb, None, x_in, x_out, "bmm",
        rhs_view_fn=lambda i, ec: xnv[:, ec, i], out_scale_ap=s_out[:, 0:1])
    pr.close()
    return pr.nc


def stage_l0_ffn(inputs, x_maps):
    w = np.ascontiguousarray(np.asarray(inputs["L0_ffn_weight"], np.float32))
    g = np.asarray(inputs["L0_ffn_ln_gamma"], np.float32)
    b = np.asarray(inputs["L0_ffn_ln_beta"], np.float32)
    gt = np.ascontiguousarray(g.reshape(DC, P).T)
    bt = np.ascontiguousarray(b.reshape(DC, P).T)
    maps = []
    for c in range(8):
        maps.append({"x_in": x_maps[c], "l0f_w": w, "l0f_g": gt, "l0f_b": bt})
    return maps



NL = 64
TPL = T // NL


def emit_nys_qkv(pr, ag_a_out, wqkv_sb, QT, KT, VT, tag):
    nc = pr.nc
    NCB = T // 512
    with ExitStack() as es:
        sb = lambda n, s, d: es.enter_context(nc.sbuf_tensor(f"{tag}_{n}", s, d))
        at = sb("at", [P, 2 * DC * 512], BF16)
        ps_q = es.enter_context(nc.psum_tensor(f"{tag}_pq", [P, 512], F32))
        ps_k = es.enter_context(nc.psum_tensor(f"{tag}_pk", [P, 512], F32))
        ps_v = es.enter_context(nc.psum_tensor(f"{tag}_pv", [P, 512], F32))

        agv = ag_a_out.ap().rearrange("(h c p) t -> p h c t", p=P, c=DC)
        atv = at[:].rearrange("p (b c q) -> p b c q", b=2, c=DC)
        QTv = QT[:].rearrange("p (hp t) -> p hp t", hp=NHP)
        KTv = KT[:].rearrange("p (hp t) -> p hp t", hp=NHP)
        VTv = VT[:].rearrange("p (tt h e) -> p tt h e", tt=NTT, h=NHC)
        wv = wqkv_sb[:].rearrange("p (c n) -> p c n", c=DC)

        ph = Phase(pr, tag)

        @ph.op("vector", name="vones")
        def _(e):
            return e.memset(bap(VT, 64, [[NTT * NHC * 65, P],
                                         [65, NTT * NHC], [1, 1]]), 1.0)
        vones = ph.ops[-1]
        lda = {}; mmq = {}; mmk = {}; mmv = {}; evq = {}; evk = {}; evv = {}
        for cb in range(NCB):
            bf = cb % 2
            h = (cb * 512) // TOWN
            co = (cb * 512) % TOWN

            @ph.op("sync", deps=(mmv.get(cb - 2),), inc=16, name=f"lda{cb}")
            def _(e, h=h, co=co, bf=bf):
                return e.dma_start(atv[:, bf], agv[:, h, :, co:co + 512])
            lda[cb] = ph.ops[-1]

            for hp in range(NHP):
                prevq = (cb, hp - 1) if hp > 0 else (cb - 1, NHP - 1)

                @ph.op("tensor", deps=(lda[cb], evq.get(prevq)),
                       name=f"mmq{cb}_{hp}")
                def _(e, hp=hp, bf=bf):
                    for dc in range(DC):
                        ins = e.matmul(ps_q[:], wv[:, dc, hp * P:(hp + 1) * P],
                                       atv[:, bf, dc],
                                       start=(dc == 0), stop=(dc == DC - 1),
                                       skip_group_check=True)
                    return ins
                mmq[(cb, hp)] = ph.ops[-1]

                @ph.op("vector", deps=(mmq[(cb, hp)],), name=f"evq{cb}_{hp}")
                def _(e, hp=hp, cb=cb):
                    return e.tensor_copy(QTv[:, hp, ts(cb, 512)], ps_q[:])
                evq[(cb, hp)] = ph.ops[-1]

                @ph.op("tensor", deps=(lda[cb], evk.get(prevq)),
                       name=f"mmk{cb}_{hp}")
                def _(e, hp=hp, bf=bf):
                    for dc in range(DC):
                        ins = e.matmul(ps_k[:],
                                       wv[:, dc, 512 + hp * P:512 + (hp + 1) * P],
                                       atv[:, bf, dc],
                                       start=(dc == 0), stop=(dc == DC - 1),
                                       skip_group_check=True)
                    return ins
                mmk[(cb, hp)] = ph.ops[-1]

                @ph.op("vector", deps=(mmk[(cb, hp)],), name=f"evk{cb}_{hp}")
                def _(e, hp=hp, cb=cb):
                    return e.tensor_copy(KTv[:, hp, ts(cb, 512)], ps_k[:])
                evk[(cb, hp)] = ph.ops[-1]

            for st4 in range(4):
                tt = cb * 4 + st4

                @ph.op("tensor", deps=(lda[cb], evv.get(tt - 1)),
                       name=f"mmv{tt}")
                def _(e, bf=bf, st4=st4):
                    for dc in range(DC):
                        ins = e.matmul(
                            ps_v[:],
                            atv[:, bf, dc, st4 * P:(st4 + 1) * P],
                            wv[:, dc, 1024:1536],
                            start=(dc == 0), stop=(dc == DC - 1),
                            skip_group_check=True)
                    return ins
                mmvo = ph.ops[-1]

                @ph.op("vector", deps=(mmvo, vones), name=f"evv{tt}")
                def _(e, tt=tt):
                    return e.tensor_copy(VTv[:, tt, :, 0:64], ps_v[:].rearrange(
                        "p (h e) -> p h e", h=NHC))
                evv[tt] = ph.ops[-1]
            mmv[cb] = mmvo
        ph.emit()


def emit_nystrom_attn(pr, QT, KT, VT, ag_t_in, tag, dbg=None):
    nc = pr.nc
    with ExitStack() as es:
        sb = lambda n, s, d: es.enter_context(nc.sbuf_tensor(f"{tag}_{n}", s, d))
        klT = sb("klT", [P, NHP * NL], BF16)
        qlT = sb("qlT", [P, NHP * NL], BF16)
        klF = sb("klF", [P, NL], F32)
        A = sb("A", [64, NHC * 64], F32)
        AT = sb("AT", [64, NHC * 64], F32)
        Vi = sb("Vi", [64, NHC * 64], F32)
        kv = sb("kv", [64, NHC * 64], F32)
        kvt = sb("kvt", [64, NHC * 64], F32)
        tm = sb("tm", [64, NHC * 64], F32)
        vt = sb("vt", [64, NHC * 64], F32)
        rs = sb("rs", [64, NHC], F32)
        Pn = sb("Pn", [P, 2 * 512], BF16)
        Zn = sb("Zn", [64, NHC * 64], F32)
        W2 = sb("W2", [64, NHC * 65], BF16)
        EL = sb("EL", [64, 2 * 512], BF16)
        rcl = sb("rcl", [P, 512], F32)
        att = sb("att", [64, 512], BF16)
        attf = sb("attf", [64, 2 * 512], BF16)
        ps_a = es.enter_context(nc.psum_tensor(f"{tag}_pa", [P, 512], F32))
        ps_b = es.enter_context(nc.psum_tensor(f"{tag}_pb", [P, 512], F32))
        ps_n = es.enter_context(nc.psum_tensor(f"{tag}_pn", [P, 2 * 512], F32))
        ps_z = es.enter_context(nc.psum_tensor(f"{tag}_pz", [64, 512], F32))
        ps_y = es.enter_context(nc.psum_tensor(f"{tag}_py", [P, 2 * 512], F32))
        ps_bc = es.enter_context(nc.psum_tensor(f"{tag}_pbc", [P, 512], F32))

        QTv = QT[:].rearrange("p (hp t) -> p hp t", hp=NHP)
        KTv = KT[:].rearrange("p (hp t) -> p hp t", hp=NHP)
        VTv = VT[:].rearrange("p (tt h e) -> p tt h e", tt=NTT, h=NHC)
        klTv = klT[:].rearrange("p (hp l) -> p hp l", hp=NHP)
        qlTv = qlT[:].rearrange("p (hp l) -> p hp l", hp=NHP)
        agv = ag_t_in.ap().rearrange("(h e) t -> e h t", e=HD)

        ph = Phase(pr, tag)

        lmk = {}
        lprev = None
        for hp in range(NHP):
            @ph.op("vector", deps=(lprev,), name=f"lmkr{hp}")
            def _(e, hp=hp):
                kk = KTv[:, hp].rearrange("p (l g) -> p l g", l=NL)
                return e.tensor_reduce(klF[:, :], kk, mybir.AxisListType.X,
                                       ALU.add)
            o1 = ph.ops[-1]

            @ph.op("vector", hdeps=(o1,), name=f"lmks{hp}")
            def _(e, hp=hp):
                return e.tensor_scalar_mul(klTv[:, hp], klF[:, :], 1.0 / TPL)
            o2 = ph.ops[-1]

            @ph.op("vector", hdeps=(o2,), name=f"lmqr{hp}")
            def _(e, hp=hp):
                qq = QTv[:, hp].rearrange("p (l g) -> p l g", l=NL)
                return e.tensor_reduce(klF[:, :], qq, mybir.AxisListType.X,
                                       ALU.add)
            o3 = ph.ops[-1]

            @ph.op("vector", hdeps=(o3,), name=f"lmqs{hp}")
            def _(e, hp=hp):
                return e.tensor_scalar_mul(qlTv[:, hp], klF[:, :], 1.0 / TPL)
            lmk[hp] = ph.ops[-1]
            lprev = lmk[hp]

        AS = {}; INV = {}
        aprev = None
        for h in range(NHC):
            hp, hh = divmod(h, 2)
            base = hh * 64

            @ph.op("tensor", deps=(lmk[hp], aprev), name=f"As{h}")
            def _(e, hp=hp, base=base, h=h):
                return e.matmul(ps_a[0:64, 0:64],
                                qlTv[base:base + 64, hp],
                                klTv[base:base + 64, hp],
                                start=True, stop=True)
            aso = ph.ops[-1]

            @ph.op("scalar", deps=(aso,), name=f"Ae{h}")
            def _(e, h=h):
                return e.activation(A[:, ts(h, 64)], ps_a[0:64, 0:64], AF.Exp)
            aeo = ph.ops[-1]

            @ph.op("vector", deps=(aeo,), name=f"Ars{h}")
            def _(e, h=h):
                return e.tensor_reduce(rs[:, h:h + 1], A[:, ts(h, 64)],
                                       mybir.AxisListType.X, ALU.add)
            arso = ph.ops[-1]

            @ph.op("vector", deps=(arso,), name=f"Arc{h}")
            def _(e, h=h):
                return e.reciprocal(rs[:, h:h + 1], rs[:, h:h + 1])
            arco = ph.ops[-1]

            @ph.op("vector", deps=(arco,), name=f"Anm{h}")
            def _(e, h=h):
                rbc = bap(rs, h, [[NHC, 64], [0, 64]])
                return e.tensor_tensor(A[:, ts(h, 64)], A[:, ts(h, 64)], rbc,
                                       ALU.mult)
            AS[h] = ph.ops[-1]
            aprev = AS[h]

        DN = {}; TR = {}
        dprev = AS[NHC - 1]
        for h in range(NHC):
            @ph.op("tensor", deps=(AS[h], dprev), name=f"Acs{h}")
            def _(e, h=h):
                e.matmul(ps_a[0:1, 0:64], pr.ones_f32c[0:64, 0:1],
                         A[:, ts(h, 64)], start=True, stop=True)
                return e.transpose(ps_b[0:64, 0:64], A[:, ts(h, 64)],
                                   pr.ident_f32[0:64, 0:64])
            acso = ph.ops[-1]

            @ph.op("vector", deps=(acso,), name=f"Amx{h}")
            def _(e, h=h):
                e.tensor_copy(AT[:, ts(h, 64)], ps_b[0:64, 0:64])
                return e.tensor_reduce(rs[0:1, h:h + 1], ps_a[0:1, 0:64],
                                       mybir.AxisListType.X, ALU.max)
            amxo = ph.ops[-1]

            @ph.op("vector", deps=(amxo,), name=f"Adn{h}")
            def _(e, h=h):
                return e.reciprocal(rs[0:1, h:h + 1], rs[0:1, h:h + 1])
            adno = ph.ops[-1]

            @ph.op("tensor", deps=(adno,), name=f"Adb{h}")
            def _(e, h=h):
                return e.matmul(ps_a[0:64, 64:65], pr.ones_f32r[0:1, 0:64],
                                rs[0:1, h:h + 1], start=True, stop=True)
            adbo = ph.ops[-1]

            @ph.op("vector", deps=(adbo,), name=f"V0{h}")
            def _(e, h=h):
                rbc = bap(ps_a, 64, [[512, 64], [0, 64]])
                return e.tensor_tensor(Vi[:, ts(h, 64)], AT[:, ts(h, 64)],
                                       rbc, ALU.mult)
            DN[h] = ph.ops[-1]
            dprev = DN[h]

        prev = {h: DN[h] for h in range(NHC)}
        chain = DN[NHC - 1]
        for it in range(6):
            for h in range(NHC):
                @ph.op("tensor", deps=(prev[h], chain), name=f"kv{it}_{h}")
                def _(e, h=h):
                    return e.matmul(ps_a[0:64, 0:64], AT[:, ts(h, 64)],
                                    Vi[:, ts(h, 64)], start=True, stop=True)
                o1 = ph.ops[-1]

                @ph.op("vector", deps=(o1,), name=f"kve{it}_{h}")
                def _(e, h=h):
                    return e.tensor_copy(kv[:, ts(h, 64)], ps_a[0:64, 0:64])
                o2 = ph.ops[-1]

                @ph.op("tensor", deps=(o2,), name=f"kvt{it}_{h}")
                def _(e, h=h):
                    return e.transpose(ps_b[0:64, 0:64], kv[:, ts(h, 64)],
                                       pr.ident_f32[0:64, 0:64])
                o3 = ph.ops[-1]

                @ph.op("vector", deps=(o3,), name=f"kvte{it}_{h}")
                def _(e, h=h):
                    e.tensor_copy(kvt[:, ts(h, 64)], ps_b[0:64, 0:64])
                    return e.scalar_tensor_tensor(
                        tm[:, ts(h, 64)], pr.eye64[:, :], 7.0, kv[:, ts(h, 64)],
                        ALU.mult, ALU.subtract)
                o4 = ph.ops[-1]

                @ph.op("tensor", deps=(o4,), name=f"u1{it}_{h}")
                def _(e, h=h):
                    return e.matmul(ps_a[0:64, 0:64], kvt[:, ts(h, 64)],
                                    tm[:, ts(h, 64)], start=True, stop=True)
                o5 = ph.ops[-1]

                @ph.op("vector", deps=(o5,), name=f"t2{it}_{h}")
                def _(e, h=h):
                    return e.scalar_tensor_tensor(
                        tm[:, ts(h, 64)], pr.eye64[:, :], 15.0,
                        ps_a[0:64, 0:64], ALU.mult, ALU.subtract)
                o6 = ph.ops[-1]

                @ph.op("tensor", deps=(o6,), name=f"u2{it}_{h}")
                def _(e, h=h):
                    return e.matmul(ps_a[0:64, 0:64], kvt[:, ts(h, 64)],
                                    tm[:, ts(h, 64)], start=True, stop=True)
                o7 = ph.ops[-1]

                @ph.op("vector", deps=(o7,), name=f"t3{it}_{h}")
                def _(e, h=h):
                    return e.scalar_tensor_tensor(
                        tm[:, ts(h, 64)], pr.eye64[:, :], 13.0,
                        ps_a[0:64, 0:64], ALU.mult, ALU.subtract)
                o8 = ph.ops[-1]

                @ph.op("tensor", deps=(o8,), name=f"vt{it}_{h}")
                def _(e, h=h):
                    return e.transpose(ps_b[0:64, 0:64], Vi[:, ts(h, 64)],
                                       pr.ident_f32[0:64, 0:64])
                o9 = ph.ops[-1]

                @ph.op("vector", deps=(o9,), name=f"vte{it}_{h}")
                def _(e, h=h):
                    return e.tensor_copy(vt[:, ts(h, 64)], ps_b[0:64, 0:64])
                o10 = ph.ops[-1]

                @ph.op("tensor", deps=(o10,), name=f"vn{it}_{h}")
                def _(e, h=h):
                    return e.matmul(ps_a[0:64, 0:64], vt[:, ts(h, 64)],
                                    tm[:, ts(h, 64)], start=True, stop=True)
                o11 = ph.ops[-1]

                @ph.op("vector", deps=(o11,), name=f"vne{it}_{h}")
                def _(e, h=h):
                    return e.tensor_scalar_mul(Vi[:, ts(h, 64)],
                                               ps_a[0:64, 0:64], 0.25)
                prev[h] = ph.ops[-1]
                chain = prev[h]

        ZO = {}
        for h in range(NHC):
            hp, hh = divmod(h, 2)
            base = hh * 64
            nexp = {}
            NZd = {}
            for g in range(NTT // 8):
                bf = g % 2

                @ph.op("tensor",
                       deps=(lmk[hp], nexp.get(g - 2) if g >= 2 else None,
                             (ZO.get(h - 1) if g < 2 else None)),
                       name=f"ns{h}_{g}")
                def _(e, h=h, hp=hp, base=base, g=g, bf=bf):
                    for s8 in range(8):
                        tt = g * 8 + s8
                        ins = e.matmul(ps_n[:, bf * 512 + s8 * 64:
                                            bf * 512 + (s8 + 1) * 64],
                                       KTv[base:base + 64, hp, ts(tt, P)],
                                       qlTv[base:base + 64, hp],
                                       start=True, stop=True,
                                       skip_group_check=True)
                    return ins
                nso = ph.ops[-1]

                @ph.op("scalar", deps=(nso, NZd.get(g - 2)),
                       name=f"ne{h}_{g}")
                def _(e, g=g, bf=bf):
                    return e.activation(Pn[:, ts(bf, 512)],
                                        ps_n[:, ts(bf, 512)], AF.Exp)
                nexp[g] = ph.ops[-1]

                @ph.op("tensor", deps=(nexp[g],), name=f"nz{h}_{g}")
                def _(e, h=h, g=g, bf=bf, first=(g == 0)):
                    for s8 in range(8):
                        tt = g * 8 + s8
                        st = first and s8 == 0
                        sp = (g == 3) and (s8 == 7)
                        ins = e.matmul(
                            ps_z[0:64, 0:65],
                            Pn[:, bf * 512 + s8 * 64:bf * 512 + (s8 + 1) * 64],
                            VTv[:, tt, h, :],
                            start=st, stop=sp, skip_group_check=True)
                    return ins
                zo = ph.ops[-1]
                NZd[g] = zo
            ZO[h] = zo

            @ph.op("vector", deps=(ZO[h],), name=f"zr{h}")
            def _(e, h=h):
                return e.reciprocal(rs[:, h:h + 1], ps_z[0:64, 64:65])
            zro = ph.ops[-1]

            @ph.op("vector", deps=(zro,), name=f"zn{h}")
            def _(e, h=h):
                rbc = bap(rs, h, [[NHC, 64], [0, 64]])
                return e.tensor_tensor(Zn[:, ts(h, 64)], ps_z[0:64, 0:64],
                                       rbc, ALU.mult)
            ZO[h] = ph.ops[-1]

        W2O = {}
        wprev = prev[NHC - 1]
        for h in range(NHC):
            @ph.op("tensor", deps=(prev[h], ZO[h], wprev), name=f"pht{h}")
            def _(e, h=h):
                return e.transpose(ps_b[0:64, 0:64], Vi[:, ts(h, 64)],
                                   pr.ident_f32[0:64, 0:64])
            o1 = ph.ops[-1]

            @ph.op("vector", deps=(o1,), name=f"phe{h}")
            def _(e, h=h):
                return e.tensor_copy(vt[:, ts(h, 64)], ps_b[0:64, 0:64])
            o2 = ph.ops[-1]

            @ph.op("tensor", deps=(o2,), name=f"w2m{h}")
            def _(e, h=h):
                return e.matmul(ps_a[0:64, 0:64], vt[:, ts(h, 64)],
                                Zn[:, ts(h, 64)], start=True, stop=True)
            o3 = ph.ops[-1]

            @ph.op("vector", deps=(o3,), name=f"w2e{h}")
            def _(e, h=h):
                e.tensor_copy(W2[:, h * 65:h * 65 + 64], ps_a[0:64, 0:64])
                return e.memset(W2[:, h * 65 + 64:h * 65 + 65], 1.0)
            W2O[h] = ph.ops[-1]
            wprev = W2O[h]

        if dbg is not None:
            phd_prev = wprev

            @ph.op("gpsimd", deps=(wprev, ZO[NHC - 1], prev[NHC - 1]),
                   inc=16, name="dbgA")
            def _(e):
                return e.dma_start(dbg["A"].ap()[:, :], A[:])
            d1 = ph.ops[-1]

            @ph.op("gpsimd", deps=(d1,), inc=16, name="dbgVi")
            def _(e):
                return e.dma_start(dbg["Vi"].ap()[:, :], Vi[:])
            d2 = ph.ops[-1]

            @ph.op("gpsimd", deps=(d2,), inc=16, name="dbgZn")
            def _(e):
                return e.dma_start(dbg["Zn"].ap()[:, :], Zn[:])
            d3 = ph.ops[-1]

            @ph.op("gpsimd", deps=(d3,), inc=16, name="dbgW2")
            def _(e):
                return e.dma_start(dbg["W2"].ap()[:, :], W2[:])
            d4 = ph.ops[-1]

            @ph.op("gpsimd", deps=(d4,), inc=16, name="dbgkl")
            def _(e):
                return e.dma_start(dbg["klT"].ap()[:, :], klT[:])
            d5 = ph.ops[-1]

            @ph.op("gpsimd", deps=(d5,), inc=16, name="dbgql")
            def _(e):
                return e.dma_start(dbg["qlT"].ap()[:, :], qlT[:])

        STO = {}
        prev_scl = None
        last_nexp_all = ZO[NHC - 1]
        for h in range(NHC):
            hp, hh = divmod(h, 2)
            base = hh * 64
            for qb in range(NQB):
                L = h * NQB + qb
                bf = L % 2

                @ph.op("tensor",
                       deps=(lmk[hp],) +
                            ((STO.get(("e", L - 2)),) if L >= 2 else
                             (last_nexp_all, wprev)),
                       name=f"es{h}_{qb}")
                def _(e, h=h, hp=hp, base=base, qb=qb, bf=bf):
                    return e.matmul(ps_n[0:64, ts(bf, 512)],
                                    klTv[base:base + 64, hp],
                                    QTv[base:base + 64, hp, ts(qb, 512)],
                                    start=True, stop=True)
                eso = ph.ops[-1]

                @ph.op("scalar", deps=(eso,) +
                       ((STO.get(("y", L - 2)),) if L >= 2 else ()),
                       name=f"ee{h}_{qb}")
                def _(e, bf=bf):
                    return e.activation(EL[0:64, ts(bf, 512)],
                                        ps_n[0:64, ts(bf, 512)], AF.Exp)
                eeo = ph.ops[-1]
                STO[("e", L)] = eeo

                @ph.op("tensor", deps=(W2O[h], eeo) +
                       ((prev_scl,) if prev_scl is not None else ()),
                       name=f"ym{h}_{qb}")
                def _(e, h=h, bf=bf):
                    return e.matmul(ps_y[0:65, ts(bf, 512)],
                                    W2[:, h * 65:(h + 1) * 65],
                                    EL[0:64, ts(bf, 512)],
                                    start=True, stop=True)
                ymo = ph.ops[-1]
                STO[("y", L)] = ymo

                @ph.op("vector", deps=(ymo,), name=f"yr{h}_{qb}")
                def _(e, bf=bf):
                    return e.reciprocal(rcl[64:65, 0:512],
                                        ps_y[64:65, ts(bf, 512)])
                yro = ph.ops[-1]

                @ph.op("tensor", deps=(yro,), name=f"yb{h}_{qb}")
                def _(e, bf=bf):
                    return e.matmul(ps_bc[0:64, 0:512],
                                    pr.ones_f32r[64:65, 0:64],
                                    rcl[64:65, 0:512], start=True, stop=True)
                ybo = ph.ops[-1]

                @ph.op("scalar", deps=(ymo,), name=f"ye{h}_{qb}")
                def _(e, bf=bf):
                    return e.copy(att[:, :], ps_y[0:64, ts(bf, 512)])
                yeo = ph.ops[-1]

                @ph.op("vector", deps=(ybo, yeo) +
                       ((STO.get(("s", L - 2)),) if L >= 2 else ()),
                       name=f"ys{h}_{qb}")
                def _(e, bf=bf):
                    return e.tensor_mul(attf[:, ts(bf, 512)], att[:, :],
                                        ps_bc[0:64, 0:512])
                yso = ph.ops[-1]
                prev_scl = yso

                @ph.op("gpsimd", deps=(yso,), inc=16, name=f"yst{h}_{qb}")
                def _(e, h=h, qb=qb, bf=bf):
                    return e.dma_start(agv[:, h, ts(qb, 512)],
                                       attf[:, ts(bf, 512)])
                STO[("s", L)] = ph.ops[-1]

        @ph.op("gpsimd", deps=(STO[("s", NHC * NQB - 1)],), name="fin")
        def _(e):
            return e.memset(rcl[0:1, 0:1], 0.0)
        ph.emit()


def build_l1_attn(debug=False):
    pr = Prog()
    nc = pr.nc
    x_in = nc.dram_tensor("x_in", [D, TOWN], F32, kind="ExternalInput").ap()
    wqkv = nc.dram_tensor("l1_wqkv", [D, 3 * NHC * HD], F32,
                          kind="ExternalInput").ap()
    wproj = nc.dram_tensor("l1_wproj", [D, D], F32, kind="ExternalInput").ap()
    x_out = nc.dram_tensor("x_out", [D, TOWN], F32, kind="ExternalOutput").ap()
    ag_a_in = nc.dram_tensor("ag_a_in", [D, TOWN], BF16)
    ag_a_out = nc.dram_tensor("ag_a_out", [2 * D, TOWN], BF16)
    ag_t_in = nc.dram_tensor("ag_t_in", [NHC * HD, T], BF16)
    ag_t_out = nc.dram_tensor("ag_t_out", [2 * NHC * HD, T], BF16)
    dbg = None
    if debug:
        dbg = {"A": nc.dram_tensor("dbg_A", [64, NHC * 64], F32, kind="ExternalOutput"),
               "Vi": nc.dram_tensor("dbg_Vi", [64, NHC * 64], F32, kind="ExternalOutput"),
               "Zn": nc.dram_tensor("dbg_Zn", [64, NHC * 64], F32, kind="ExternalOutput"),
               "W2": nc.dram_tensor("dbg_W2", [64, NHC * 65], BF16, kind="ExternalOutput"),
               "klT": nc.dram_tensor("dbg_klT", [P, NHP * NL], BF16, kind="ExternalOutput"),
               "qlT": nc.dram_tensor("dbg_qlT", [P, NHP * NL], BF16, kind="ExternalOutput"),
               "agt": nc.dram_tensor("dbg_agt", [2 * NHC * HD, T], BF16, kind="ExternalOutput")}

    pr.load_consts(rope=False)
    emit_ln_to_dram(pr, x_in, ag_a_in.ap(), "ln1")
    pr.pair_allgather(ag_a_in, ag_a_out)
    with ExitStack() as es:
        wqkv_sb = es.enter_context(nc.sbuf_tensor("wqkv_sb", [P, DC * 1536], BF16))
        QT = es.enter_context(nc.sbuf_tensor("QT", [P, NHP * T], BF16))
        KT = es.enter_context(nc.sbuf_tensor("KT", [P, NHP * T], BF16))
        VT = es.enter_context(nc.sbuf_tensor("VT", [P, NTT * NHC * 65], BF16))
        phw = Phase(pr, "wload")
        load_weight_dmajor(pr, phw, wqkv, wqkv_sb, "wqkv")

        @phw.op("gpsimd", deps=(phw.ops[-1],), name="wfin")
        def _(e):
            return e.memset(QT[0:1, 0:1], 0.0)
        phw.emit()
        emit_nys_qkv(pr, ag_a_out, wqkv_sb, QT, KT, VT, "nqkv")
        emit_nystrom_attn(pr, QT, KT, VT, ag_t_in, "nys", dbg=dbg)
    pr.pair_allgather(ag_t_in, ag_t_out)
    if debug:
        with pr.nc.Block() as block:
            @block.gpsimd
            def _(g):
                g.dma_start(dbg["agt"].ap()[:, :], ag_t_out.ap()[:, :]
                            ).then_inc(pr.s_cc.h, 16)
                pr.s_cc.n += 16
                g.wait_ge(pr.s_cc.h, pr.s_cc.n)
    emit_oproj_block(pr, ag_t_out, wproj, x_in, x_out, "opj")
    pr.close()
    return pr.nc


def stage_l1_attn(inputs, x_maps):
    wa = np.asarray(inputs["L1_attn_W_attn"], np.float32)
    wp = np.ascontiguousarray(np.asarray(inputs["L1_attn_W_proj"], np.float32))
    maps = []
    for c in range(8):
        p = c % 2
        cols = []
        for part in range(3):
            lo = part * D + p * NHC * HD
            cols.append(wa[:, lo:lo + NHC * HD])
        wslice = np.ascontiguousarray(np.concatenate(cols, axis=1))
        maps.append({"x_in": x_maps[c], "l1_wqkv": wslice, "l1_wproj": wp})
    return maps



def emit_rms_to_dram(pr, src_dram, dst_dram, tag, src_f32=True):
    nc = pr.nc
    NCH = TOWN // 512
    with ExitStack() as es:
        sb = lambda n, s, d: es.enter_context(nc.sbuf_tensor(f"{tag}_{n}", s, d))
        xb = sb("xb", [P, 2 * DC * 512], BF16)
        xq = sb("xq", [P, DC * 512], BF16)
        rr = sb("rr", [P, 512], F32)
        ab = sb("ab", [P, 2 * DC * 512], BF16)
        ps_s = es.enter_context(nc.psum_tensor(f"{tag}_ps", [P, 512], F32))

        x3 = src_dram.rearrange("(c p) t -> p c t", p=P)
        a3 = dst_dram.rearrange("(c p) t -> p c t", p=P)
        xbv = xb[:].rearrange("p (b c t) -> p b c t", b=2, c=DC)
        abv = ab[:].rearrange("p (b c t) -> p b c t", b=2, c=DC)

        ph = Phase(pr, tag)
        ld = {}; sq = {}; smm = {}; comb = {}; rstd = {}; norm = {}; st = {}
        for i in range(NCH):
            bf = i % 2

            @ph.op("gpsimd", deps=(norm.get(i - 2),), inc=16, name=f"ld{i}")
            def _(e, i=i, bf=bf):
                return e.dma_start(xbv[:, bf], x3[:, :, ts(i, 512)])
            ld[i] = ph.ops[-1]

            @ph.op("vector", deps=(ld[i], smm.get(i - 1)), name=f"sq{i}")
            def _(e, bf=bf):
                return e.tensor_mul(xq[:], xbv[:, bf].rearrange("p c t -> p (c t)"),
                                    xbv[:, bf].rearrange("p c t -> p (c t)"))
            sq[i] = ph.ops[-1]

            @ph.op("tensor", deps=(sq[i], rstd.get(i - 1)), name=f"smm{i}")
            def _(e):
                for dc in range(DC):
                    ins = e.matmul(ps_s[:], pr.ones_bf[:], xq[:, ts(dc, 512)],
                                   start=(dc == 0), stop=(dc == DC - 1),
                                   skip_group_check=True)
                return ins
            smm[i] = ph.ops[-1]

            @ph.op("scalar", deps=(smm[i],), name=f"rstd{i}")
            def _(e):
                e.activation(rr[:], ps_s[:], AF.Ln, bias=pr.eps_rms,
                             scale=1.0 / D)
                return e.activation(rr[:], rr[:], AF.Exp, scale=-0.5)
            rstd[i] = ph.ops[-1]

            @ph.op("vector", deps=(rstd[i], st.get(i - 2)), name=f"norm{i}")
            def _(e, bf=bf):
                for dc in range(DC):
                    ins = e.tensor_mul(abv[:, bf, dc], xbv[:, bf, dc], rr[:])
                return ins
            norm[i] = ph.ops[-1]

            @ph.op("gpsimd", deps=(norm[i],), inc=16, name=f"st{i}")
            def _(e, i=i, bf=bf):
                return e.dma_start(a3[:, :, ts(i, 512)], abv[:, bf])
            st[i] = ph.ops[-1]

        @ph.op("gpsimd", deps=(st[NCH - 1],), name="fin")
        def _(e):
            return e.memset(rr[0:1, 0:1], 0.0)
        ph.emit()


def _rope_l1(nc, e, src, scratch, out_bf, cosT, sinT, tt, nhead):
    n = nhead
    x1 = bap(src, 0, [[512, P], [64, n], [1, 32]])
    x2 = bap(src, 32, [[512, P], [64, n], [1, 32]])
    t1 = bap(scratch, 0, [[512, P], [32, n], [1, 32]])
    t2 = bap(scratch, n * 32, [[512, P], [32, n], [1, 32]])
    oe = bap(out_bf, 0, [[n * 64, P], [64, n], [1, 32]])
    oo = bap(out_bf, 32, [[n * 64, P], [64, n], [1, 32]])
    cb = bap(cosT, tt * 32, [[1024, P], [0, n], [1, 32]])
    sn = bap(sinT, tt * 32, [[1024, P], [0, n], [1, 32]])
    e.tensor_mul(t1, x1, cb)
    e.tensor_mul(t2, x2, sn)
    e.tensor_add(oe, t1, t2)
    e.tensor_mul(t1, x1, sn)
    e.tensor_mul(t2, x2, cb)
    return e.tensor_sub(oo, t2, t1)


def emit_gqa_qkv(pr, ag_a_out, wq_sb, wkv_sb, qg_rep, kg_rep, QT, KTg, VTg,
                 tag):
    nc = pr.nc
    with ExitStack() as es:
        sb = lambda n, s, d: es.enter_context(nc.sbuf_tensor(f"{tag}_{n}", s, d))
        at = sb("at", [P, 2 * DC * P], BF16)
        qf = sb("qf", [P, 512], F32)
        sf = sb("sf", [P, 512], F32)
        qms = sb("qms", [P, NHC + 2], F32)
        qrob = sb("qrob", [P, 512], BF16)
        krob = sb("krob", [P, P], BF16)
        ps_q = es.enter_context(nc.psum_tensor(f"{tag}_pq", [P, 512], F32))
        ps_kv = es.enter_context(nc.psum_tensor(f"{tag}_pkv", [P, 256], F32))
        ps_tq = es.enter_context(nc.psum_tensor(f"{tag}_ptq", [P, 8 * P], BF16))
        ps_tk = es.enter_context(nc.psum_tensor(f"{tag}_ptk", [P, 8 * P], BF16))

        agv = ag_a_out.ap().rearrange("(h c p) t -> p h c t", p=P, c=DC)
        atv = at[:].rearrange("p (b c q) -> p b c q", b=2, c=DC)
        QTv = QT[:].rearrange("p (hp t) -> p hp t", hp=NHP)
        VTv = VTg[:].rearrange("p (tt h e) -> p tt h e", tt=NTT, h=2)
        wqv = wq_sb[:].rearrange("p (c n) -> p c n", c=DC)
        wkvv = wkv_sb[:].rearrange("p (c n) -> p c n", c=DC)

        ph = Phase(pr, tag)
        lda = {}; mmq = {}; mmkv = {}; qev = {}; kev = {}
        trq = {}; trk = {}; evq = {}; evk = {}; evv = {}; roq = {}; rok = {}
        for tt in range(NTT):
            bf = tt % 2
            h = (tt * P) // TOWN
            co = (tt * P) % TOWN

            @ph.op("sync", deps=(mmkv.get(tt - 2),), inc=16, name=f"lda{tt}")
            def _(e, h=h, co=co, bf=bf):
                return e.dma_start(atv[:, bf], agv[:, h, :, co:co + P])
            lda[tt] = ph.ops[-1]

            @ph.op("tensor", deps=(lda[tt], qev.get(tt - 1)), name=f"mmq{tt}")
            def _(e, bf=bf):
                for dc in range(DC):
                    ins = e.matmul(ps_q[:], atv[:, bf, dc], wqv[:, dc],
                                   start=(dc == 0), stop=(dc == DC - 1))
                return ins
            mmq[tt] = ph.ops[-1]

            @ph.op("tensor", deps=(kev.get(tt - 1), evv.get(tt - 1)),
                   name=f"mmkv{tt}")
            def _(e, bf=bf):
                for dc in range(DC):
                    ins = e.matmul(ps_kv[:], atv[:, bf, dc], wkvv[:, dc],
                                   start=(dc == 0), stop=(dc == DC - 1))
                return ins
            mmkv[tt] = ph.ops[-1]

            @ph.op("vector", deps=(mmq[tt],), name=f"qev{tt}")
            def _(e):
                return e.tensor_copy(qf[:], ps_q[:])
            qev[tt] = ph.ops[-1]

            @ph.op("vector", deps=(qev[tt],), name=f"qsq{tt}")
            def _(e):
                return e.tensor_mul(sf[:], qf[:], qf[:])
            o = ph.ops[-1]

            @ph.op("vector", deps=(o,), name=f"qrd{tt}")
            def _(e):
                return e.tensor_reduce(qms[:, 0:NHC],
                                       sf[:].rearrange("p (h e) -> p h e", h=NHC),
                                       mybir.AxisListType.X, ALU.add)
            o = ph.ops[-1]

            @ph.op("scalar", deps=(o,), name=f"qrs{tt}")
            def _(e):
                e.activation(qms[:, 0:NHC], qms[:, 0:NHC], AF.Ln,
                             bias=pr.eps_rms, scale=1.0 / HD)
                return e.activation(qms[:, 0:NHC], qms[:, 0:NHC], AF.Exp,
                                    scale=-0.5)
            o = ph.ops[-1]

            @ph.op("vector", deps=(o,), name=f"qrg{tt}")
            def _(e):
                return e.tensor_mul(qms[:, 0:NHC], qms[:, 0:NHC], qg_rep[:])
            o = ph.ops[-1]

            @ph.op("vector", deps=(o,), name=f"qap{tt}")
            def _(e):
                rbc = bap(qms, 0, [[NHC + 2, P], [1, NHC], [0, HD]])
                q3 = qf[:].rearrange("p (h e) -> p h e", h=NHC)
                return e.tensor_mul(q3, q3, rbc)
            o = ph.ops[-1]

            @ph.op("vector", deps=(o,), name=f"roq{tt}")
            def _(e, tt=tt):
                return _rope_l1(nc, e, qf, sf, qrob, pr.cos, pr.sin, tt, NHC)
            roq[tt] = ph.ops[-1]

            @ph.op("tensor", deps=(roq[tt], evq.get(tt - 1)), name=f"trq{tt}")
            def _(e):
                for j in range(NHP):
                    e.transpose(ps_tq[0:64, ts(j, P)],
                                qrob[:, j * 64:(j + 1) * 64],
                                pr.ident[:], tile_position=(0, 0))
                    ins = e.transpose(ps_tq[64:128, ts(j, P)],
                                      qrob[:, (j + 4) * 64:(j + 5) * 64],
                                      pr.ident[:], tile_position=(0, 64))
                return ins
            trq[tt] = ph.ops[-1]

            @ph.op("scalar", deps=(trq[tt],), name=f"evq{tt}")
            def _(e, tt=tt):
                for j in range(NHP):
                    ins = e.copy(QTv[:, j, ts(tt, P)], ps_tq[:, ts(j, P)])
                return ins
            evq[tt] = ph.ops[-1]

            @ph.op("vector", deps=(mmkv[tt], roq[tt]), name=f"kev{tt}")
            def _(e):
                return e.tensor_copy(qf[:, 0:128], ps_kv[:, 0:128])
            kev[tt] = ph.ops[-1]

            @ph.op("vector", deps=(kev[tt],), name=f"ksq{tt}")
            def _(e):
                return e.tensor_mul(sf[:, 0:128], qf[:, 0:128], qf[:, 0:128])
            o = ph.ops[-1]

            @ph.op("vector", deps=(o,), name=f"krd{tt}")
            def _(e):
                return e.tensor_reduce(qms[:, NHC:NHC + 2],
                                       bap(sf, 0, [[512, P], [64, 2], [1, HD]]),
                                       mybir.AxisListType.X, ALU.add)
            o = ph.ops[-1]

            @ph.op("scalar", deps=(o,), name=f"krs{tt}")
            def _(e):
                e.activation(qms[:, NHC:NHC + 2], qms[:, NHC:NHC + 2], AF.Ln,
                             bias=pr.eps_rms, scale=1.0 / HD)
                return e.activation(qms[:, NHC:NHC + 2], qms[:, NHC:NHC + 2],
                                    AF.Exp, scale=-0.5)
            o = ph.ops[-1]

            @ph.op("vector", deps=(o,), name=f"krg{tt}")
            def _(e):
                return e.tensor_mul(qms[:, NHC:NHC + 2], qms[:, NHC:NHC + 2],
                                    kg_rep[:])
            o = ph.ops[-1]

            @ph.op("vector", deps=(o,), name=f"kap{tt}")
            def _(e):
                rbc = bap(qms, NHC, [[NHC + 2, P], [1, 2], [0, HD]])
                k3 = bap(qf, 0, [[512, P], [64, 2], [1, HD]])
                return e.tensor_mul(k3, k3, rbc)
            o = ph.ops[-1]

            @ph.op("vector", deps=(o,), name=f"rok{tt}")
            def _(e, tt=tt):
                return _rope_l1(nc, e, qf, sf, krob, pr.cos, pr.sin, tt, 2)
            rok[tt] = ph.ops[-1]

            @ph.op("tensor", deps=(rok[tt], evk.get(tt - 1)), name=f"trk{tt}")
            def _(e):
                return e.transpose(ps_tk[:, 0:P], krob[:], pr.ident[:])
            trk[tt] = ph.ops[-1]

            @ph.op("scalar", deps=(trk[tt],), name=f"evk{tt}")
            def _(e, tt=tt):
                return e.copy(KTg[:, ts(tt, P)], ps_tk[:, 0:P])
            evk[tt] = ph.ops[-1]

            @ph.op("vector", deps=(mmkv[tt],), name=f"evv{tt}")
            def _(e, tt=tt):
                return e.tensor_copy(
                    VTv[:, tt].rearrange("p h e -> p (h e)"), ps_kv[:, 128:256])
            evv[tt] = ph.ops[-1]
        ph.emit()


def emit_oproj_residual_plain(pr, ag_t_out, wo_sb, g_dram, h1_dram, tag):
    nc = pr.nc
    NST = TOWN // 512
    with ExitStack() as es:
        sb = lambda n, s, d: es.enter_context(nc.sbuf_tensor(f"{tag}_{n}", s, d))
        ats = sb("ats", [P, 2 * DC * 512], BF16)
        gf = sb("gf", [P, 2 * DC * 512], F32)
        ho = sb("ho", [P, 2 * DC * 512], F32)
        ps_y = es.enter_context(nc.psum_tensor(f"{tag}_py", [P, 2 * 512], F32))

        agv = ag_t_out.ap().rearrange("(c p) t -> p c t", p=P)
        g3 = g_dram.rearrange("(c p) t -> p c t", p=P)
        o3 = h1_dram.rearrange("(c p) t -> p c t", p=P)
        atsv = ats[:].rearrange("p (b c t) -> p b c t", b=2, c=DC)
        gfv = gf[:].rearrange("p (b c t) -> p b c t", b=2, c=DC)
        hov = ho[:].rearrange("p (b c t) -> p b c t", b=2, c=DC)
        wov = wo_sb[:].rearrange("p (c n) -> p c n", c=DC)

        ph = Phase(pr, tag)
        lda = {}; ldg = {}; mmy = {}; addo = {}; sto = {}
        for i in range(NST):
            bf = i % 2

            @ph.op("sync", deps=(mmy.get((i - 2, DC - 1)),), inc=16,
                   name=f"lda{i}")
            def _(e, i=i, bf=bf):
                pid = nc.partition_id(engines=[ET.SP])
                return e.dma_start(atsv[:, bf],
                                   agv[:, :, ts((pid % 2) * NST + i, 512)])
            lda[i] = ph.ops[-1]

            @ph.op("sync", deps=(addo.get((i - 2, DC - 1)),), inc=16,
                   name=f"ldg{i}")
            def _(e, i=i, bf=bf):
                return e.dma_start(gfv[:, bf], g3[:, :, ts(i, 512)])
            ldg[i] = ph.ops[-1]

            for oc in range(DC):
                yb2 = oc % 2

                @ph.op("tensor",
                       deps=(lda[i], addo.get((i, oc - 2)),
                             addo.get((i - 1, DC - 1)) if oc < 2 else None),
                       name=f"mmy{i}_{oc}")
                def _(e, i=i, oc=oc, bf=bf, yb2=yb2):
                    for ec in range(DC):
                        ins = e.matmul(ps_y[:, ts(yb2, 512)],
                                       wov[:, ec, ts(oc, P)], atsv[:, bf, ec],
                                       start=(ec == 0), stop=(ec == DC - 1),
                                       skip_group_check=True)
                    return ins
                mmy[(i, oc)] = ph.ops[-1]

                @ph.op("vector", deps=(mmy[(i, oc)], ldg[i], sto.get(i - 2)),
                       name=f"add{i}_{oc}")
                def _(e, i=i, oc=oc, bf=bf, yb2=yb2):
                    return e.tensor_add(hov[:, bf, oc], gfv[:, bf, oc],
                                        ps_y[:, ts(yb2, 512)])
                addo[(i, oc)] = ph.ops[-1]

            @ph.op("gpsimd", deps=(addo[(i, DC - 1)],), inc=16, name=f"sto{i}")
            def _(e, i=i, bf=bf):
                return e.dma_start(o3[:, :, ts(i, 512)], hov[:, bf])
            sto[i] = ph.ops[-1]

        @ph.op("gpsimd", deps=(sto[NST - 1],), name="fin")
        def _(e):
            return e.memset(gf[0:1, 0:1], 0.0)
        ph.emit()


def emit_gqa(pr, g_dram, h1_dram, wq, wkv, wo, gains, _unused, tag):
    nc = pr.nc
    h0_dram = nc.dram_tensor(f"{tag}_h0", [D, TOWN], BF16)
    ag_a_out = nc.dram_tensor(f"{tag}_aga", [2 * D, TOWN], BF16)
    ag_t_in = nc.dram_tensor(f"{tag}_agt", [NHC * HD, T], BF16)
    ag_t_out = nc.dram_tensor(f"{tag}_agto", [2 * NHC * HD, T], BF16)

    emit_rms_to_dram(pr, g_dram.ap(), h0_dram.ap(), f"{tag}_rms")
    pr.pair_allgather(h0_dram, ag_a_out)
    with ExitStack() as es:
        wq_sb = es.enter_context(nc.sbuf_tensor(f"{tag}_wq", [P, DC * NHC * HD], BF16))
        wkv_sb = es.enter_context(nc.sbuf_tensor(f"{tag}_wkv", [P, DC * 256], BF16))
        qg_rep = es.enter_context(nc.sbuf_tensor(f"{tag}_qg", [P, NHC], F32))
        kg_rep = es.enter_context(nc.sbuf_tensor(f"{tag}_kg", [P, 2], F32))
        gl = es.enter_context(nc.sbuf_tensor(f"{tag}_gl", [1, NHC + 2], F32))
        QT = es.enter_context(nc.sbuf_tensor(f"{tag}_QT", [P, NHP * T], BF16))
        KTg = es.enter_context(nc.sbuf_tensor(f"{tag}_KT", [P, T], BF16))
        VTg = es.enter_context(nc.sbuf_tensor(f"{tag}_VT", [P, NTT * 2 * HD], BF16))

        with nc.psum_tensor(f"{tag}_pg", [P, NHC + 2], F32) as ps_g:
            phw = Phase(pr, "gqw")
            w1 = load_weight_dmajor(pr, phw, wq, wq_sb, "wq")
            w2 = load_weight_dmajor(pr, phw, wkv, wkv_sb, "wkv")

            @phw.op("sync", inc=16, name="gains")
            def _(e):
                return e.dma_start(gl[0:1, :], gains[:, :])
            w3 = phw.ops[-1]

            @phw.op("tensor", deps=(w3,), name="gbc")
            def _(e):
                return e.matmul(ps_g[:, :], pr.ones_f32r[0:1, :], gl[0:1, :],
                                start=True, stop=True)
            w4 = phw.ops[-1]

            @phw.op("vector", deps=(w4,), name="gev")
            def _(e):
                e.tensor_copy(qg_rep[:], ps_g[:, 0:NHC])
                return e.tensor_copy(kg_rep[:], ps_g[:, NHC:NHC + 2])
            w5 = phw.ops[-1]

            @phw.op("gpsimd", deps=(w1, w2, w5), name="wfin")
            def _(e):
                return e.memset(gl[0:1, 0:1], 0.0)
            phw.emit()

        emit_gqa_qkv(pr, ag_a_out, wq_sb, wkv_sb, qg_rep, kg_rep,
                     QT, KTg, VTg, f"{tag}_qkv")
        KTgv = KTg[:]
        VTvv = VTg[:].rearrange("p (tt h e) -> p tt h e", tt=NTT, h=2)
        emit_causal_attention(
            pr, QT, QT, VTg, ag_t_in, scale=1.0 / 8.0, tag=f"{tag}_att",
            k_fn=lambda KTv, hp, half, kt: KTgv[half * 64:(half + 1) * 64,
                                                ts(kt, P)],
            v_fn=lambda VTv, hp, half, kt: VTvv[:, kt, half, :])
    pr.pair_allgather(ag_t_in, ag_t_out)
    emit_oproj_plain_block(pr, ag_t_out, wo, g_dram.ap(), h1_dram.ap(),
                           f"{tag}_opj")


def emit_oproj_plain_block(pr, ag_t_out, wo, g_dram, h1_dram, tag):
    nc = pr.nc
    with ExitStack() as es:
        wo_sb = es.enter_context(nc.sbuf_tensor(f"{tag}_wo", [P, DC * D], BF16))
        phw = Phase(pr, f"{tag}_wl")
        wo_op = load_weight_dmajor(pr, phw, wo, wo_sb, "wo")

        @phw.op("gpsimd", deps=(wo_op,), name="wofin")
        def _(e):
            return e.memset(wo_sb[0:1, 0:1], 0.0)
        phw.emit()
        emit_oproj_residual_plain(pr, ag_t_out, wo_sb, g_dram, h1_dram, tag)



def build_l1_ffn_gqa():
    pr = Prog()
    nc = pr.nc
    x_in = nc.dram_tensor("x_in", [D, TOWN], F32, kind="ExternalInput").ap()
    wq = nc.dram_tensor("gq_wq", [D, NHC * HD], F32, kind="ExternalInput").ap()
    wkv = nc.dram_tensor("gq_wkv", [D, 256], F32, kind="ExternalInput").ap()
    wo = nc.dram_tensor("gq_wo", [D, D], F32, kind="ExternalInput").ap()
    gains = nc.dram_tensor("gq_gains", [1, NHC + 2], F32, kind="ExternalInput").ap()
    h1_out = nc.dram_tensor("h1_out", [D, TOWN], F32, kind="ExternalOutput")
    g_dram = nc.dram_tensor("g_dram", [D, TOWN], F32)

    pr.load_consts(rope=True)
    emit_ln_to_dram(pr, x_in, g_dram.ap(), "ln7", out_dtype=F32)
    emit_gqa(pr, g_dram, h1_out, wq, wkv, wo, gains, None, "gqa")
    pr.close()
    return pr.nc


def stage_l1_ffn_gqa(inputs, x_maps):
    wq = np.asarray(inputs["L1_ffn_wq"], np.float32)
    wk = np.asarray(inputs["L1_ffn_wk"], np.float32)
    wv = np.asarray(inputs["L1_ffn_wv"], np.float32)
    wo = np.asarray(inputs["L1_ffn_wo"], np.float32)
    qg = np.asarray(inputs["L1_ffn_q_gain"], np.float32)
    kg = np.asarray(inputs["L1_ffn_k_gain"], np.float32)
    maps = []
    r = np.arange(D)
    pglob = r // 512
    j = (r % 512) // 128
    half = (r % 128) // 64
    dv = r % 64
    ghead = pglob * NHC + j + 4 * half
    perm = ghead * 64 + dv
    woT = np.ascontiguousarray(wo.T)
    woT_perm = np.ascontiguousarray(woT[perm])
    for c in range(8):
        p = c % 2
        heads = range(p * NHC, (p + 1) * NHC)
        rows_q = np.concatenate([np.arange(h * HD, (h + 1) * HD) for h in heads])
        kvrows = np.arange(2 * p * HD, (2 * p + 2) * HD)
        wkvT = np.concatenate([wk[kvrows].T, wv[kvrows].T], axis=1)
        g = np.concatenate([qg[p * NHC:(p + 1) * NHC], kg[2 * p:2 * p + 2]])
        maps.append({
            "x_in": x_maps[c],
            "gq_wq": np.ascontiguousarray(wq[rows_q].T),
            "gq_wkv": np.ascontiguousarray(wkvT),
            "gq_wo": woT_perm,
            "gq_gains": np.ascontiguousarray(g[None, :]),
        })
    return maps



BANK = 8


def emit_top2(pr, logits_dram, c_rep, tag):
    nc = pr.nc
    with ExitStack() as es:
        sb = lambda n, s, d: es.enter_context(nc.sbuf_tensor(f"{tag}_{n}", s, d))
        lg = sb("lg", [1, BANK], F32)
        is1 = sb("is1", [1, BANK], F32)
        is2 = sb("is2", [1, BANK], F32)
        t2 = sb("t2", [1, BANK], F32)
        ex = sb("ex", [1, BANK], F32)
        m1 = sb("m1", [1, 2], F32)
        m2 = sb("m2", [1, 2], F32)
        cs = sb("cs", [1, BANK], F32)
        ps = es.enter_context(nc.psum_tensor(f"{tag}_ps", [P, BANK], F32))
        ph = Phase(pr, tag)

        def step(eng, deps, name, fn):
            o = ph.op(eng, hdeps=deps, name=name)(fn)
            return o

        o0 = step("sync", (), "ld",
                  lambda e: e.dma_start(lg[0:1, :], logits_dram[:, :]))
        o0.inc = 16
        o1 = step("vector", (o0,), "m1",
                  lambda e: e.tensor_reduce(m1[0:1, 0:1], lg[0:1, :],
                                            mybir.AxisListType.X, ALU.max))
        o2 = step("vector", (o1,), "is1",
                  lambda e: e.tensor_tensor(is1[0:1, :], lg[0:1, :],
                                            m1[0:1, 0:1].broadcast_to([1, BANK]),
                                            ALU.is_ge))
        o3 = step("vector", (o2,), "big",
                  lambda e: e.tensor_scalar_mul(t2[0:1, :], is1[0:1, :], 1e30))
        o4 = step("vector", (o3,), "l2",
                  lambda e: e.tensor_sub(t2[0:1, :], lg[0:1, :], t2[0:1, :]))
        o5 = step("vector", (o4,), "m2",
                  lambda e: e.tensor_reduce(m2[0:1, 0:1], t2[0:1, :],
                                            mybir.AxisListType.X, ALU.max))
        o6 = step("vector", (o5,), "is2",
                  lambda e: e.tensor_tensor(is2[0:1, :], t2[0:1, :],
                                            m2[0:1, 0:1].broadcast_to([1, BANK]),
                                            ALU.is_ge))
        o7 = step("vector", (o6,), "msk",
                  lambda e: e.tensor_add(is1[0:1, :], is1[0:1, :], is2[0:1, :]))
        o8 = step("vector", (o1,), "negm1",
                  lambda e: e.tensor_scalar_mul(m1[0:1, 1:2], m1[0:1, 0:1], -1.0))
        o9 = step("scalar", (o8, o0), "exp",
                  lambda e: e.activation(ex[0:1, :], lg[0:1, :], AF.Exp,
                                         bias=m1[0:1, 1:2]))
        o10 = step("vector", (o9, o7), "emask",
                   lambda e: e.tensor_mul(ex[0:1, :], ex[0:1, :], is1[0:1, :]))
        o11 = step("vector", (o10,), "zsum",
                   lambda e: e.tensor_reduce(m2[0:1, 1:2], ex[0:1, :],
                                             mybir.AxisListType.X, ALU.add))
        o12 = step("vector", (o11,), "zrec",
                   lambda e: e.reciprocal(m2[0:1, 1:2], m2[0:1, 1:2]))
        o13 = step("vector", (o12,), "cvec",
                   lambda e: e.tensor_mul(cs[0:1, :], ex[0:1, :],
                                          m2[0:1, 1:2].broadcast_to([1, BANK])))
        o14 = step("tensor", (o13,), "cbc",
                   lambda e: e.matmul(ps[:, :], pr.ones_f32r[0:1, :],
                                      cs[0:1, :], start=True, stop=True))
        o15 = step("vector", (o14,), "crep",
                   lambda e: e.tensor_copy(c_rep[:], ps[:, :]))
        step("gpsimd", (o15,), "fin",
             lambda e: e.memset(lg[0:1, 0:1], 0.0))
        ph.emit()


def emit_compose(pr, c_rep, mix_dram, s1T_dram, s2_dram, msh_dram, nin, ngl,
                 jw, tag):
    nc = pr.nc
    na = nin * 64
    nmchunk = (ngl * 64) // P
    g2n = nin // 2
    nbc = na // P
    with ExitStack() as es:
        sb = lambda n, s, d: es.enter_context(nc.sbuf_tensor(f"{tag}_{n}", s, d))
        mix_sb = sb("mix", [P, BANK * nmchunk * na], BF16)
        s1T = sb("s1T", [P, BANK * g2n * 64], BF16)
        s1s = sb("s1s", [P, BANK * g2n * 64], BF16)
        s2 = sb("s2", [P, BANK * nmchunk * 64], BF16)
        vv = sb("vv", [P, nbc * BANK * jw], BF16)
        mo = sb("mo", [P, 512], BF16)
        ps_d = es.enter_context(nc.psum_tensor(f"{tag}_pd", [P, 512], F32))
        ps_2 = es.enter_context(nc.psum_tensor(f"{tag}_p2", [P, 512], F32))

        mix_pitch = BANK * nmchunk * na
        vv_pitch = nbc * BANK * jw
        s1v = s1s[:].rearrange("p (i g c) -> p i g c", i=BANK, g=g2n)
        s2v = s2[:].rearrange("p (i c n) -> p i c n", i=BANK, c=nmchunk)
        mshv = msh_dram.ap().rearrange("(c p) j -> p c j", p=P)

        ph = Phase(pr, tag)

        @ph.op("gpsimd", inc=16, name="ldmix")
        def _(e):
            return e.dma_start(
                mix_sb[:].rearrange("p (i c n) -> p i c n", i=BANK, c=nmchunk),
                mix_dram.rearrange("i (c p) n -> p i c n", p=P))
        ldm = ph.ops[-1]

        @ph.op("gpsimd", inc=16, name="lds1")
        def _(e):
            return e.dma_start(s1T[:], s1T_dram)
        ld1 = ph.ops[-1]

        @ph.op("gpsimd", inc=16, name="lds2")
        def _(e):
            return e.dma_start(s2[:], s2_dram)
        ld2 = ph.ops[-1]

        @ph.op("vector", deps=(ld1,), name="s1scale")
        def _(e):
            cb = bap(c_rep, 0, [[BANK, P], [1, BANK], [0, g2n], [0, 64]])
            return e.tensor_tensor(
                s1s[:].rearrange("p (i g c) -> p i g c", i=BANK, g=g2n),
                s1T[:].rearrange("p (i g c) -> p i g c", i=BANK, g=g2n),
                cb, ALU.mult)
        s1sc = ph.ops[-1]

        prev = None
        for i in range(BANK):
            for G in range(ngl):
                gg, Gh = G // 2, (G % 2) * 64
                for fill in range(nin // 16) or [0]:
                    pass
                nfill = max(1, nin // 16)
                for fill in range(nfill):
                    g0 = fill * 16
                    gcnt = min(16, nin - g0)

                    @ph.op("tensor", deps=(ldm, ld2, prev),
                           name=f"vd{i}_{G}_{fill}")
                    def _(e, i=i, gg=gg, Gh=Gh, g0=g0, gcnt=gcnt):
                        for gi in range(gcnt):
                            g = g0 + gi
                            off = ((i * nmchunk + gg) * na) + (g + i) % nin \
                                + Gh * mix_pitch
                            lhsT = bap(mix_sb, off,
                                       [[mix_pitch, 64], [nin, 64]])
                            ins = e.matmul(
                                ps_d[(g % 2) * 64:(g % 2) * 64 + 64,
                                     ((g // 2) % 8) * 64:((g // 2) % 8 + 1) * 64],
                                lhsT, s2v[Gh:Gh + 64, i, gg],
                                start=True, stop=True,
                                tile_position=(Gh, (g % 2) * 64),
                                skip_group_check=True)
                        return ins
                    mmop = ph.ops[-1]

                    @ph.op("vector", deps=(mmop,), name=f"ve{i}_{G}_{fill}")
                    def _(e, i=i, G=G, fill=fill, gcnt=gcnt):
                        nslot = gcnt // 2
                        src = bap(ps_d, 0, [[512, P], [64, nslot], [1, 64]])
                        dst = bap(vv,
                                  (fill * 8) * (BANK * jw) + i * jw + G * 64,
                                  [[vv_pitch, P], [BANK * jw, nslot], [1, 64]])
                        return e.tensor_copy(dst, src)
                    prev = ph.ops[-1]
        last_ev = prev

        prevst = None
        for gp in range(nin // 2):
            first_deps = (s1sc, last_ev) if gp == 0 else (prevst,)
            for half in range(2):
                g = gp * 2 + half
                Gh = half * 64

                @ph.op("tensor",
                       deps=first_deps if half == 0 else (),
                       name=f"m2_{g}")
                def _(e, g=g, gp=gp, Gh=Gh):
                    for i in range(BANK):
                        rhs = bap(vv, gp * (BANK * jw) + i * jw + Gh * vv_pitch,
                                  [[vv_pitch, 64], [1, jw]])
                        ins = e.matmul(
                            ps_2[Gh:Gh + 64, 0:jw],
                            s1v[Gh:Gh + 64, i, gp], rhs,
                            start=(i == 0), stop=(i == BANK - 1),
                            tile_position=(Gh, Gh),
                            skip_group_check=True)
                    return ins
                mm = ph.ops[-1]

            @ph.op("vector", deps=(mm,), name=f"me{gp}")
            def _(e):
                return e.tensor_copy(mo[:, 0:jw], ps_2[:, 0:jw])
            meo = ph.ops[-1]

            @ph.op("gpsimd", deps=(meo,), inc=16, name=f"ms{gp}")
            def _(e, gp=gp):
                return e.dma_start(mshv[:, gp, 0:jw], mo[:, 0:jw])
            prevst = ph.ops[-1]

        @ph.op("gpsimd", deps=(prevst,), name="fin")
        def _(e):
            return e.memset(mo[0:1, 0:1], 0.0)
        ph.emit()


def _part_slice(ap, base, count):
    dims = [list(d) for d in ap.ap]
    step = dims[0][0]
    dims[0] = [step, count]
    return bass.AP(ap.tensor, ap.offset + base * step, dims)



def emit_load_mfull(pr, ag_out_dram, w_sb, ncols, tag):
    nc = pr.nc
    jw = ncols // 8
    ph = Phase(pr, tag)
    na8, _ = ag_out_dram.ap().shape
    na = na8 // 8
    nbc = na // P
    ops = []
    for r in range(8):
        @ph.op("sync", inc=16, name=f"ld{r}")
        def _(e, r=r):
            src = ag_out_dram.ap()[r * na:(r + 1) * na, :].rearrange(
                "(c p) j -> p c j", p=P)
            dst = bap(w_sb, r * jw, [[nbc * ncols, P], [ncols, nbc], [1, jw]])
            return e.dma_start(dst, src)
        ops.append(ph.ops[-1])

    @ph.op("gpsimd", deps=tuple(ops), name="fin")
    def _(e):
        return e.memset(w_sb[0:1, 0:1], 0.0)
    ph.emit()


def emit_fc_stream(pr, wfc_sb, h2_dram, h3_dram, tag):
    nc = pr.nc
    NST = TOWN // 512
    NOC = 4096 // P
    with ExitStack() as es:
        sb = lambda n, s, d: es.enter_context(nc.sbuf_tensor(f"{tag}_{n}", s, d))
        h2s = sb("h2s", [P, 2 * DC * 512], BF16)
        t1 = sb("t1", [P, 512], F32)
        h3b = sb("h3b", [P, 2 * 512], BF16)
        ps_y = es.enter_context(nc.psum_tensor(f"{tag}_py", [P, 2 * 512], F32))

        h23 = h2_dram.rearrange("(c p) t -> p c t", p=P)
        h33 = h3_dram.rearrange("(c p) t -> p c t", p=P)
        h2v = h2s[:].rearrange("p (b c t) -> p b c t", b=2, c=DC)
        wv = wfc_sb[:].rearrange("p (c n) -> p c n", c=DC)

        ph = Phase(pr, tag)
        lda = {}; mmy = {}; rel = {}; sto = {}
        for i in range(NST):
            bf = i % 2

            @ph.op("sync", deps=(mmy.get((i - 2, NOC - 1)),), inc=16,
                   name=f"lda{i}")
            def _(e, i=i, bf=bf):
                return e.dma_start(h2v[:, bf], h23[:, :, ts(i, 512)])
            lda[i] = ph.ops[-1]

            for oc in range(NOC):
                yb2 = oc % 2

                @ph.op("tensor",
                       deps=(lda[i], rel.get((i, oc - 2)),
                             rel.get((i - 1, NOC - 1)) if oc < 2 else None),
                       name=f"mmy{i}_{oc}")
                def _(e, i=i, oc=oc, bf=bf, yb2=yb2):
                    for ec in range(DC):
                        ins = e.matmul(ps_y[:, ts(yb2, 512)],
                                       wv[:, ec, ts(oc, P)], h2v[:, bf, ec],
                                       start=(ec == 0), stop=(ec == DC - 1),
                                       skip_group_check=True)
                    return ins
                mmy[(i, oc)] = ph.ops[-1]

                @ph.op("vector", deps=(mmy[(i, oc)], sto.get((i, oc - 2)),
                                       sto.get((i - 1, NOC - 1)) if oc < 2 else None),
                       name=f"rel{i}_{oc}")
                def _(e, oc=oc, yb2=yb2):
                    e.tensor_scalar_max(t1[:], ps_y[:, ts(yb2, 512)], 0.0)
                    return e.tensor_mul(h3b[:, ts(yb2, 512)], t1[:], t1[:])
                rel[(i, oc)] = ph.ops[-1]

                @ph.op("gpsimd", deps=(rel[(i, oc)],), inc=16,
                       name=f"sto{i}_{oc}")
                def _(e, i=i, oc=oc, yb2=yb2):
                    return e.dma_start(h33[:, oc, ts(i, 512)],
                                       h3b[:, ts(yb2, 512)])
                sto[(i, oc)] = ph.ops[-1]

        @ph.op("gpsimd", deps=(sto[(NST - 1, NOC - 1)],), name="fin")
        def _(e):
            return e.memset(t1[0:1, 0:1], 0.0)
        ph.emit()


def emit_pj_final(pr, wpj_sb, h3_dram, h1_dram, x_in, x_out, tag):
    nc = pr.nc
    NST = TOWN // 512
    NKC = 4096 // P
    with ExitStack() as es:
        sb = lambda n, s, d: es.enter_context(nc.sbuf_tensor(f"{tag}_{n}", s, d))
        h3s = sb("h3s", [P, NKC * 512], BF16)
        h1s = sb("h1s", [P, DC * 512], F32)
        xf = sb("xf", [P, DC * 512], F32)
        h4f = sb("h4f", [P, DC * 512], F32)
        yb = sb("yb", [P, DC * 512], BF16)
        yq = sb("yq", [P, DC * 512], BF16)
        mr = sb("mr", [P, 512], F32)
        rr = sb("rr", [P, 512], F32)
        xo = sb("xo", [P, DC * 512], F32)
        t1 = sb("t1", [P, 512], F32)
        ps_y = es.enter_context(nc.psum_tensor(f"{tag}_py", [P, 2 * 512], F32))
        ps_m = es.enter_context(nc.psum_tensor(f"{tag}_pm", [P, 512], F32))
        ps_s = es.enter_context(nc.psum_tensor(f"{tag}_ps", [P, 512], F32))

        h33 = h3_dram.rearrange("(c p) t -> p c t", p=P)
        h13 = h1_dram.rearrange("(c p) t -> p c t", p=P)
        x3 = x_in.rearrange("(c p) t -> p c t", p=P)
        o3 = x_out.rearrange("(c p) t -> p c t", p=P)
        h3v = h3s[:].rearrange("p (c t) -> p c t", c=NKC)
        h1v = h1s[:].rearrange("p (c t) -> p c t", c=DC)
        xfv = xf[:].rearrange("p (c t) -> p c t", c=DC)
        xov = xo[:].rearrange("p (c t) -> p c t", c=DC)
        wv = wpj_sb[:].rearrange("p (c n) -> p c n", c=NKC)

        ph = Phase(pr, tag)
        ldh3 = {}; ldh1 = {}; ldx = {}; mmy = {}; hev = {}; sqy = {}; smm = {}
        comb = {}; rstd = {}; add = {}; sto = {}
        for i in range(NST):
            bf = i % 2

            @ph.op("sync", deps=(mmy.get((i - 1, DC - 1)),), inc=16,
                   name=f"ldh3{i}")
            def _(e, i=i):
                return e.dma_start(h3v, h33[:, :, ts(i, 512)])
            ldh3[i] = ph.ops[-1]

            @ph.op("sync", deps=(hev.get((i - 1, DC - 1)),), inc=16,
                   name=f"ldh1{i}")
            def _(e, i=i):
                return e.dma_start(h1v, h13[:, :, ts(i, 512)])
            ldh1[i] = ph.ops[-1]

            @ph.op("sync", deps=(add.get(i - 1),), inc=16, name=f"ldx{i}")
            def _(e, i=i):
                return e.dma_start(xfv, x3[:, :, ts(i, 512)])
            ldx[i] = ph.ops[-1]

            for oc in range(DC):
                yb2 = oc % 2

                @ph.op("tensor",
                       deps=(ldh3[i], hev.get((i, oc - 2)),
                             hev.get((i - 1, DC - 1)) if oc < 2 else None),
                       name=f"mmy{i}_{oc}")
                def _(e, oc=oc, yb2=yb2):
                    for kc in range(NKC):
                        ins = e.matmul(ps_y[:, ts(yb2, 512)],
                                       wv[:, kc, ts(oc, P)], h3v[:, kc],
                                       start=(kc == 0), stop=(kc == NKC - 1),
                                       skip_group_check=True)
                    return ins
                mmy[(i, oc)] = ph.ops[-1]

                @ph.op("vector", deps=(mmy[(i, oc)], ldh1[i], add.get(i - 1)),
                       name=f"hev{i}_{oc}")
                def _(e, oc=oc, yb2=yb2):
                    e.tensor_add(h4f[:, ts(oc, 512)], h1v[:, oc],
                                 ps_y[:, ts(yb2, 512)])
                    e.tensor_copy(yb[:, ts(oc, 512)], h4f[:, ts(oc, 512)])
                    return e.tensor_mul(yq[:, ts(oc, 512)], yb[:, ts(oc, 512)],
                                        yb[:, ts(oc, 512)])
                hev[(i, oc)] = ph.ops[-1]

                @ph.op("tensor", deps=(hev[(i, oc)], comb.get(i - 1)),
                       name=f"smm{i}_{oc}")
                def _(e, oc=oc):
                    e.matmul(ps_m[:], pr.ones_bf[:], yb[:, ts(oc, 512)],
                             start=(oc == 0), stop=(oc == DC - 1),
                             skip_group_check=True)
                    return e.matmul(ps_s[:], pr.ones_bf[:], yq[:, ts(oc, 512)],
                                    start=(oc == 0), stop=(oc == DC - 1),
                                    skip_group_check=True)
                smm[(i, oc)] = ph.ops[-1]

            @ph.op("vector", deps=(smm[(i, DC - 1)], rstd.get(i - 1)),
                   name=f"comb{i}")
            def _(e):
                e.tensor_scalar_mul(mr[:], ps_m[:], 1.0 / D)
                e.tensor_mul(rr[:], mr[:], mr[:])
                return e.scalar_tensor_tensor(rr[:], ps_s[:], 1.0 / D, rr[:],
                                              ALU.mult, ALU.subtract)
            comb[i] = ph.ops[-1]

            @ph.op("scalar", deps=(comb[i],), name=f"rstd{i}")
            def _(e):
                e.activation(rr[:], rr[:], AF.Ln, bias=pr.eps_ln)
                return e.activation(rr[:], rr[:], AF.Exp, scale=-0.5)
            rstd[i] = ph.ops[-1]

            @ph.op("vector", deps=(rstd[i], ldx[i], sto.get(i - 1)),
                   name=f"add{i}")
            def _(e, i=i):
                for oc in range(DC):
                    e.tensor_sub(t1[:], h4f[:, ts(oc, 512)], mr[:])
                    e.tensor_mul(t1[:], t1[:], rr[:])
                    ins = e.tensor_add(xov[:, oc], xfv[:, oc], t1[:])
                return ins
            add[i] = ph.ops[-1]

            @ph.op("gpsimd", deps=(add[i],), inc=16, name=f"sto{i}")
            def _(e, i=i):
                return e.dma_start(o3[:, :, ts(i, 512)], xov)
            sto[i] = ph.ops[-1]

        @ph.op("gpsimd", deps=(sto[NST - 1],), name="fin")
        def _(e):
            return e.memset(t1[0:1, 0:1], 0.0)
        ph.emit()


def emit_bank(pr, h1_dram, x_in, x_out, fc_mix, fc_s1T, fc_s2, fc_lg,
              pj_mix, pj_s1T, pj_s2, pj_lg, tag):
    nc = pr.nc
    h2_dram = nc.dram_tensor(f"{tag}_h2", [D, TOWN], BF16)
    h3_dram = nc.dram_tensor(f"{tag}_h3", [4096, TOWN], BF16)
    mfc_sh = nc.dram_tensor(f"{tag}_mfcs", [D, 512], BF16)
    mfc_ag = nc.dram_tensor(f"{tag}_mfca", [8 * D, 512], BF16)
    mpj_sh = nc.dram_tensor(f"{tag}_mpjs", [4096, P], BF16)
    mpj_ag = nc.dram_tensor(f"{tag}_mpja", [8 * 4096, P], BF16)

    emit_rms_to_dram(pr, h1_dram.ap(), h2_dram.ap(), f"{tag}_rms")

    with ExitStack() as es:
        c_fc = es.enter_context(nc.sbuf_tensor(f"{tag}_cfc", [P, BANK], F32))
        c_pj = es.enter_context(nc.sbuf_tensor(f"{tag}_cpj", [P, BANK], F32))
        emit_top2(pr, fc_lg, c_fc, f"{tag}_t2f")
        emit_top2(pr, pj_lg, c_pj, f"{tag}_t2p")
        emit_compose(pr, c_fc, fc_mix, fc_s1T, fc_s2, mfc_sh, 16, 8, 512,
                     f"{tag}_cmf")
        pr.allgather8(mfc_sh, mfc_ag)
        emit_compose(pr, c_pj, pj_mix, pj_s1T, pj_s2, mpj_sh, 64, 2, 128,
                     f"{tag}_cmp")
        pr.allgather8(mpj_sh, mpj_ag)

    with ExitStack() as es:
        wfc = es.enter_context(nc.sbuf_tensor(f"{tag}_wfc", [P, DC * 4096], BF16))
        emit_load_mfull(pr, mfc_ag, wfc, 4096, f"{tag}_lmf")
        emit_fc_stream(pr, wfc, h2_dram.ap(), h3_dram.ap(), f"{tag}_fc")

    with ExitStack() as es:
        wpj = es.enter_context(nc.sbuf_tensor(f"{tag}_wpj", [P, 32 * D], BF16))
        emit_load_mfull(pr, mpj_ag, wpj, D, f"{tag}_lmp")
        emit_pj_final(pr, wpj, h3_dram.ap(), h1_dram.ap(), x_in, x_out,
                      f"{tag}_pj")



def build_l1_ffn():
    pr = Prog()
    nc = pr.nc
    x_in = nc.dram_tensor("x_in", [D, TOWN], F32, kind="ExternalInput").ap()
    wq = nc.dram_tensor("gq_wq", [D, NHC * HD], F32, kind="ExternalInput").ap()
    wkv = nc.dram_tensor("gq_wkv", [D, 256], F32, kind="ExternalInput").ap()
    wo = nc.dram_tensor("gq_wo", [D, D], F32, kind="ExternalInput").ap()
    gains = nc.dram_tensor("gq_gains", [1, NHC + 2], F32, kind="ExternalInput").ap()
    fc_mix = nc.dram_tensor("b_fcmix", [BANK, 512, 1024], F32, kind="ExternalInput").ap()
    fc_s1T = nc.dram_tensor("b_fcs1", [P, BANK * 8 * 64], F32, kind="ExternalInput").ap()
    fc_s2 = nc.dram_tensor("b_fcs2", [P, BANK * 4 * 64], F32, kind="ExternalInput").ap()
    fc_lg = nc.dram_tensor("b_fclg", [1, BANK], F32, kind="ExternalInput").ap()
    pj_mix = nc.dram_tensor("b_pjmix", [BANK, P, 4096], F32, kind="ExternalInput").ap()
    pj_s1T = nc.dram_tensor("b_pjs1", [P, BANK * 32 * 64], F32, kind="ExternalInput").ap()
    pj_s2 = nc.dram_tensor("b_pjs2", [P, BANK * 1 * 64], F32, kind="ExternalInput").ap()
    pj_lg = nc.dram_tensor("b_pjlg", [1, BANK], F32, kind="ExternalInput").ap()
    x_out = nc.dram_tensor("x_out", [D, TOWN], F32, kind="ExternalOutput").ap()
    g_dram = nc.dram_tensor("g_dram", [D, TOWN], F32)
    h1_dram = nc.dram_tensor("h1_dram", [D, TOWN], F32)

    pr.load_consts(rope=True)
    emit_ln_to_dram(pr, x_in, g_dram.ap(), "ln7", out_dtype=F32)
    emit_gqa(pr, g_dram, h1_dram, wq, wkv, wo, gains, None, "gqa")
    emit_bank(pr, h1_dram, x_in, x_out, fc_mix, fc_s1T, fc_s2, fc_lg,
              pj_mix, pj_s1T, pj_s2, pj_lg, "bk")
    pr.close()
    return pr.nc


def _stage_s1T(s1, nin):
    s1T = s1.reshape(BANK, nin, 64, 64).transpose(0, 1, 3, 2)
    out = np.empty((P, BANK, nin // 2, 64), np.float32)
    for g in range(nin):
        g2, gh = g // 2, g % 2
        out[gh * 64:(gh + 1) * 64, :, g2, :] = s1T[:, g].transpose(1, 0, 2)
    return np.ascontiguousarray(out.reshape(P, BANK * (nin // 2) * 64))


def stage_l1_ffn(inputs, x_maps):
    base = stage_l1_ffn_gqa(inputs, x_maps)
    fc_mix = np.asarray(inputs["L1_ffn_fc_mix"], np.float32)
    fc_s1 = np.asarray(inputs["L1_ffn_fc_s1"], np.float32)
    fc_s2 = np.asarray(inputs["L1_ffn_fc_s2"], np.float32)
    fc_lg = np.asarray(inputs["L1_ffn_fc_logits"], np.float32)
    pj_mix = np.asarray(inputs["L1_ffn_pj_mix"], np.float32)
    pj_s1 = np.asarray(inputs["L1_ffn_pj_s1"], np.float32)
    pj_s2 = np.asarray(inputs["L1_ffn_pj_s2"], np.float32)
    pj_lg = np.asarray(inputs["L1_ffn_pj_logits"], np.float32)
    fc_s1T_h = _stage_s1T(fc_s1, 16)
    pj_s1T_h = _stage_s1T(pj_s1, 64)
    for c in range(8):
        m = base[c]
        m["b_fcmix"] = np.ascontiguousarray(fc_mix[:, c * 512:(c + 1) * 512, :])
        m["b_fcs1"] = fc_s1T_h
        s2sl = fc_s2[:, c * 512:(c + 1) * 512, :]
        m["b_fcs2"] = np.ascontiguousarray(
            s2sl.reshape(BANK, 4, P, 64).transpose(2, 0, 1, 3).reshape(P, -1))
        m["b_fclg"] = np.ascontiguousarray(fc_lg[None, :])
        m["b_pjmix"] = np.ascontiguousarray(pj_mix[:, c * P:(c + 1) * P, :])
        m["b_pjs1"] = pj_s1T_h
        s2sl = pj_s2[:, c * P:(c + 1) * P, :]
        m["b_pjs2"] = np.ascontiguousarray(
            s2sl.transpose(1, 0, 2).reshape(P, -1))
        m["b_pjlg"] = np.ascontiguousarray(pj_lg[None, :])
    return base



def _l0_head_perm():
    return np.concatenate([np.arange(0, HD, 2), np.arange(1, HD, 2)])


def stage_l0_attn(inputs):
    x = np.asarray(inputs["L0_attn_x"])
    wq = np.asarray(inputs["L0_attn_wq"])
    wk = np.asarray(inputs["L0_attn_wk"])
    wv = np.asarray(inputs["L0_attn_wv"])
    wo = np.asarray(inputs["L0_attn_wo"])
    perm = _l0_head_perm()
    maps = []
    for c in range(8):
        b, p = c // 2, c % 2
        xs = np.ascontiguousarray(x[b, p * TOWN:(p + 1) * TOWN, :].T)
        heads = range(p * NHC, (p + 1) * NHC)
        rows_qk = np.concatenate([h * HD + perm for h in heads])
        rows_v = np.concatenate([np.arange(h * HD, (h + 1) * HD) for h in heads])
        maps.append({
            "x_in": xs,
            "l0_wq": np.ascontiguousarray(wq[rows_qk].T),
            "l0_wk": np.ascontiguousarray(wk[rows_qk].T),
            "l0_wv": np.ascontiguousarray(wv[rows_v].T),
            "l0_wo": np.ascontiguousarray(wo.T),
        })
    return maps


def unstage_x(results, key="x_out"):
    out = np.empty((NB_OUT := 4, T, D), np.float32)
    for c in range(8):
        b, p = c // 2, c % 2
        out[b, p * TOWN:(p + 1) * TOWN, :] = results[c][key].T
    return out


def stage_x(x_full, key="x_in"):
    maps = []
    for c in range(8):
        b, p = c // 2, c % 2
        maps.append({key: np.ascontiguousarray(x_full[b, p * TOWN:(p + 1) * TOWN, :].T)})
    return maps



_NC_CACHE = {}


def _get_nc(name, builder):
    if name not in _NC_CACHE:
        _NC_CACHE[name] = builder()
    return _NC_CACHE[name]


def kernel(**inputs):
    cores = list(range(8))

    nc1 = _get_nc("l0_attn", build_l0_attn)
    maps1 = stage_l0_attn(inputs)
    res1 = run_bass_kernel_spmd(nc1, maps1, core_ids=cores)
    x1 = [res1.results[c]["x_out"] for c in range(8)]

    nc2 = _get_nc("l0_ffn", build_l0_ffn)
    maps2 = stage_l0_ffn(inputs, x1)
    res2 = run_bass_kernel_spmd(nc2, maps2, core_ids=cores)
    x2 = [res2.results[c]["x_out"] for c in range(8)]

    nc3 = _get_nc("l1_attn", build_l1_attn)
    maps3 = stage_l1_attn(inputs, x2)
    res3 = run_bass_kernel_spmd(nc3, maps3, core_ids=cores)
    x3 = [res3.results[c]["x_out"] for c in range(8)]

    nc4 = _get_nc("l1_ffn", build_l1_ffn)
    maps4 = stage_l1_ffn(inputs, x3)
    res4 = run_bass_kernel_spmd(nc4, maps4, core_ids=cores)

    out = np.empty((NB, T, D), np.float32)
    for c in range(8):
        b, p = c // 2, c % 2
        out[b, p * TOWN:(p + 1) * TOWN, :] = res4.results[c]["x_out"].T
    return out


# version 02
# speedup vs baseline: 1.0038x; 1.0038x over previous
import numpy as np
import ml_dtypes
import concourse.bass as bass
import concourse.mybir as mybir
from concourse.bass_utils import run_bass_kernel_spmd
from contextlib import ExitStack
from concourse.expressions_rust import AnnotatedLowered

F32 = mybir.dt.float32
BF16 = mybir.dt.bfloat16
AF = mybir.ActivationFunctionType
ALU = mybir.AluOpType
ET = mybir.EngineType
bfdt = ml_dtypes.bfloat16
ts = bass.ts

P = 128
D = 1024
DC = D // P
T = 4096
TOWN = 2048
NB = 4
NH = 16
HD = 64
NHC = 8
NHP = NHC // 2
NTT = T // P
NQB = T // 512
LN_EPS = 1e-5
ROPE_THETA = 10000.0
PAIR_GROUPS = [[0, 1], [2, 3], [4, 5], [6, 7]]
ALL_GROUP = [[0, 1, 2, 3, 4, 5, 6, 7]]

ENGS = ("sync", "tensor", "vector", "scalar", "gpsimd")



def _rope_tables_np():
    pos = np.arange(T, dtype=np.float64)[:, None]
    k = np.arange(HD // 2, dtype=np.float64)[None, :]
    ang = pos / ROPE_THETA ** (2.0 * k / HD)

    def f(a):
        a = a.astype(np.float32).reshape(T // P, P, HD // 2).transpose(1, 0, 2)
        return np.ascontiguousarray(a.reshape(P, (T // P) * (HD // 2)))
    return f(np.sin(ang)), f(np.cos(ang))


def _stair_mask_np():
    w = np.zeros((P, 896), dtype=np.float32)
    kk = np.arange(P)[:, None]
    cc = np.arange(896)[None, :]
    w[cc < 384 + kk] = -1e30
    return w



class Op:
    __slots__ = ("eng", "fn", "deps", "hdeps", "inc", "seq", "cum", "needed",
                 "name")

    def __init__(self, eng, fn, deps, hdeps, inc, name):
        self.eng = eng
        self.fn = fn
        self.deps = deps
        self.hdeps = hdeps
        self.inc = inc
        self.needed = False
        self.name = name


class Phase:

    def __init__(self, pr, name):
        self.pr = pr
        self.name = name
        self.ops = []

    def op(self, eng, deps=(), hdeps=(), inc=1, name=""):
        def reg(fn):
            o = Op(eng, fn, [d for d in deps if d is not None],
                   [d for d in hdeps if d is not None], inc, name)
            self.ops.append(o)
            return o
        return reg

    def emit(self):
        pr = self.pr
        nc = pr.nc
        for o in self.ops:
            if o.inc != 1:
                o.needed = True
            for d in o.deps:
                d.needed = True
            for d in o.hdeps:
                d.needed = True
        for o in self.ops:
            sem = pr.esem[o.eng]
            o.seq = sem.n + (o.inc if o.needed else 0)
            if o.needed:
                sem.n += o.inc
            o.cum = o.seq

        by_eng = {e: [o for o in self.ops if o.eng == e] for e in ENGS}

        def make_body(eng):
            def body(e):
                hwm = {}
                for o in by_eng[eng]:
                    for d in list(o.hdeps) + list(o.deps):
                        if not d.needed:
                            raise RuntimeError(f"dep {d.name} not marked needed")
                        sem = self.pr.esem[d.eng]
                        if hwm.get(d.eng, -1) < d.cum:
                            e.wait_ge(sem.h, d.cum)
                            hwm[d.eng] = d.cum
                    ins = o.fn(e)
                    if o.needed:
                        if ins is None:
                            raise RuntimeError(f"op {o.name} returned no instr")
                        ins.then_inc(self.pr.esem[o.eng].h, o.inc)
            return body

        with nc.Block() as block:
            for eng in ENGS:
                if by_eng[eng]:
                    getattr(block, eng)(make_body(eng))


class Sem:
    def __init__(self, nc, name):
        self.h = nc.alloc_semaphore(name)
        self.n = 0


class Prog:
    def __init__(self, num_devices=8):
        self.nc = bass.Bass("TRN2", target_bir_lowering=False,
                            num_devices=num_devices)
        self.es = ExitStack()
        self.esem = {e: Sem(self.nc, f"se_{e}") for e in ENGS}
        self.s_cc = Sem(self.nc, "s_cc")

    def sb(self, name, shape, dtype):
        return self.es.enter_context(self.nc.sbuf_tensor(name, shape, dtype))

    def close(self):
        self.es.close()

    def load_consts(self, rope=True):
        nc = self.nc
        self.toff_in = nc.dram_tensor("c_toff", [1, 1], mybir.dt.int32,
                                      kind="ExternalInput").ap()
        self.toff_sb = self.sb("c_toffs", [1, 1], mybir.dt.int32)
        self.ident = self.sb("c_ident", [P, P], BF16)
        self.ones_bf = self.sb("c_ones", [P, P], BF16)
        self.ones_f32r = self.sb("c_onesr", [P, P], F32)
        self.ones_f32c = self.sb("c_onesc", [P, 2], F32)
        self.stair = self.sb("c_stair", [P, 896], F32)
        self.ident_f32 = self.sb("c_idf", [P, P], F32)
        self.eye64 = self.sb("c_eye64", [64, 64], F32)
        loads = [
            (self.ident, nc.inline_tensor(np.eye(P, dtype=bfdt), "h_ident")),
            (self.ones_bf, nc.inline_tensor(np.ones((P, P), dtype=bfdt), "h_ones")),
            (self.ones_f32r,
             nc.inline_tensor(np.ones((P, P), dtype=np.float32), "h_onesr")),
            (self.ones_f32c,
             nc.inline_tensor(np.ones((P, 2), dtype=np.float32), "h_onesc")),
            (self.stair, nc.inline_tensor(_stair_mask_np(), "h_stair")),
            (self.ident_f32,
             nc.inline_tensor(np.eye(P, dtype=np.float32), "h_idf")),
            (self.eye64,
             nc.inline_tensor(np.eye(64, dtype=np.float32), "h_eye64")),
        ]
        if rope:
            sin_np, cos_np = _rope_tables_np()
            self.sin = self.sb("c_sin", [P, sin_np.shape[1]], F32)
            self.cos = self.sb("c_cos", [P, cos_np.shape[1]], F32)
            loads += [(self.sin, nc.inline_tensor(sin_np, "h_sin")),
                      (self.cos, nc.inline_tensor(cos_np, "h_cos"))]
        self.cbias = self.sb("c_bias", [P, 4], F32)
        ph = Phase(self, "consts")

        @ph.op("sync", inc=16, name="toff")
        def _(e):
            return e.dma_start(self.toff_sb[0:1, 0:1], self.toff_in[:, :])

        for i, (dst, h) in enumerate(loads):
            @ph.op("sync", inc=16, name=f"c{i}")
            def _(e, dst=dst, h=h):
                return e.dma_start(dst[:], h.ap()[:, :])

        @ph.op("vector", name="cbias")
        def _(e):
            e.memset(self.cbias[:, 0:1], float(LN_EPS))
            e.memset(self.cbias[:, 1:2], 1e-3)
            e.memset(self.cbias[:, 2:3], 1e-6)
            return e.memset(self.cbias[:, 3:4], 0.0)
        ph.emit()
        self.eps_ln = self.cbias[:, 0:1]
        self.eps_bl = self.cbias[:, 1:2]
        self.eps_rms = self.cbias[:, 2:3]

    def pair_allgather(self, src_dram, dst_dram):
        nc = self.nc
        s = self.s_cc
        with nc.Block() as block:
            @block.gpsimd
            def _(g):
                g.collective_compute(
                    "AllGather", ALU.bypass, replica_groups=PAIR_GROUPS,
                    ins=[src_dram.ap().opt()], outs=[dst_dram.ap().opt()],
                ).then_inc(s.h, 1)
                s.n += 1
                g.wait_ge(s.h, s.n)

    def allgather8(self, src_dram, dst_dram):
        nc = self.nc
        s = self.s_cc
        with nc.Block() as block:
            @block.gpsimd
            def _(g):
                g.collective_compute(
                    "AllGather", ALU.bypass, replica_groups=ALL_GROUP,
                    ins=[src_dram.ap().opt()], outs=[dst_dram.ap().opt()],
                ).then_inc(s.h, 1)
                s.n += 1
                g.wait_ge(s.h, s.n)

    def allreduce_max(self, buf_dram):
        nc = self.nc
        s = self.s_cc
        with nc.Block() as block:
            @block.gpsimd
            def _(g):
                g.collective_compute(
                    "AllReduce", ALU.max, replica_groups=ALL_GROUP,
                    ins=[buf_dram.ap().opt()], outs=[buf_dram.ap().opt()],
                ).then_inc(s.h, 1)
                s.n += 1
                g.wait_ge(s.h, s.n)


def bap(t, off, dims):
    return bass.AP(t, off, dims)



def emit_ln_to_dram(pr, x_in, a_out, tag, n_tok=TOWN, out_dtype=BF16):
    nc = pr.nc
    NCH = n_tok // 512
    es = ExitStack()
    with es:
        sb = lambda n, s, d: es.enter_context(nc.sbuf_tensor(f"{tag}_{n}", s, d))
        xf = sb("xf", [P, 2 * DC * 512], F32)
        xb = sb("xb", [P, DC * 512], BF16)
        xq = sb("xq", [P, DC * 512], BF16)
        nb = sb("nb", [P, 512], F32)
        mr = sb("mr", [P, 512], F32)
        rr = sb("rr", [P, 512], F32)
        ab = sb("ab", [P, 2 * DC * 512], out_dtype)
        ps_m = es.enter_context(nc.psum_tensor(f"{tag}_pm", [P, 512], F32))
        ps_s = es.enter_context(nc.psum_tensor(f"{tag}_ps", [P, 512], F32))

        x3 = x_in.rearrange("(c p) t -> p c t", p=P)
        a3 = a_out.rearrange("(c p) t -> p c t", p=P)
        xfv = xf[:].rearrange("p (b c t) -> p b c t", b=2, c=DC)
        abv = ab[:].rearrange("p (b c t) -> p b c t", b=2, c=DC)

        ph = Phase(pr, tag)
        ld = {}; cast = {}; smm = {}; comb = {}; rstd = {}; norm = {}; st = {}
        for i in range(NCH):
            bf = i % 2

            @ph.op("sync", deps=(norm.get(i - 2),), inc=16, name=f"ld{i}")
            def _(e, i=i, bf=bf):
                return e.dma_start(xfv[:, bf], x3[:, :, ts(i, 512)])
            ld[i] = ph.ops[-1]

            @ph.op("vector", deps=(ld[i], smm.get(i - 1)), name=f"cast{i}")
            def _(e, i=i, bf=bf):
                e.tensor_copy(xb[:], xfv[:, bf].rearrange("p c t -> p (c t)"))
                return e.tensor_mul(xq[:], xb[:], xb[:])
            cast[i] = ph.ops[-1]

            @ph.op("tensor", deps=(cast[i], comb.get(i - 1)), name=f"smm{i}")
            def _(e, i=i):
                for dc in range(DC):
                    e.matmul(ps_m[:], pr.ones_bf[:], xb[:, ts(dc, 512)],
                             start=(dc == 0), stop=(dc == DC - 1),
                             skip_group_check=True)
                for dc in range(DC):
                    ins = e.matmul(ps_s[:], pr.ones_bf[:], xq[:, ts(dc, 512)],
                                   start=(dc == 0), stop=(dc == DC - 1),
                                   skip_group_check=True)
                return ins
            smm[i] = ph.ops[-1]

            @ph.op("vector", deps=(smm[i], rstd.get(i - 1)), name=f"comb{i}")
            def _(e, i=i):
                e.tensor_scalar_mul(mr[:], ps_m[:], 1.0 / D)
                e.tensor_mul(rr[:], mr[:], mr[:])
                return e.scalar_tensor_tensor(rr[:], ps_s[:], 1.0 / D, rr[:],
                                              ALU.mult, ALU.subtract)
            comb[i] = ph.ops[-1]

            @ph.op("scalar", deps=(comb[i],), name=f"rstd{i}")
            def _(e, i=i):
                e.activation(rr[:], rr[:], AF.Ln, bias=pr.eps_ln)
                return e.activation(rr[:], rr[:], AF.Exp, scale=-0.5)
            rstd[i] = ph.ops[-1]

            @ph.op("vector", deps=(rstd[i], st.get(i - 2)), name=f"norm{i}")
            def _(e, i=i, bf=bf):
                for dc in range(DC):
                    e.tensor_sub(nb[:], xfv[:, bf, dc], mr[:])
                    ins = e.tensor_mul(abv[:, bf, dc], nb[:], rr[:])
                return ins
            norm[i] = ph.ops[-1]

            @ph.op("gpsimd", deps=(norm[i],), inc=16, name=f"st{i}")
            def _(e, i=i, bf=bf):
                return e.dma_start(a3[:, :, ts(i, 512)], abv[:, bf])
            st[i] = ph.ops[-1]

        @ph.op("gpsimd", deps=(st[NCH - 1],), name="fin")
        def _(e):
            return e.memset(nb[0:1, 0:1], 0.0)
        ph.emit()



def emit_qkv_rope(pr, ag_a_out, wq_sb, wk_sb, wv_sb, QT, KT, VT,
                  tag, rope=True):
    nc = pr.nc
    es = ExitStack()
    with es:
        sb = lambda n, s, d: es.enter_context(nc.sbuf_tensor(f"{tag}_{n}", s, d))
        at = sb("at", [P, 2 * DC * P], BF16)
        qs = sb("qs", [P, 512], F32)
        qs2 = sb("qs2", [P, 512], F32)
        qro = sb("qro", [P, 512], BF16)
        kro = sb("kro", [P, 512], BF16)
        ps_q = es.enter_context(nc.psum_tensor(f"{tag}_pq", [P, 512], F32))
        ps_k = es.enter_context(nc.psum_tensor(f"{tag}_pk", [P, 512], F32))
        ps_v = es.enter_context(nc.psum_tensor(f"{tag}_pv", [P, 512], F32))
        ps_tq = es.enter_context(nc.psum_tensor(f"{tag}_ptq", [P, 8 * P], BF16))
        ps_tk = es.enter_context(nc.psum_tensor(f"{tag}_ptk", [P, 8 * P], BF16))

        agv = ag_a_out.ap().rearrange("(h c p) t -> p h c t", p=P, c=DC)
        atv = at[:].rearrange("p (b c q) -> p b c q", b=2, c=DC)
        QTv = QT[:].rearrange("p (hp t) -> p hp t", hp=NHP)
        KTv = KT[:].rearrange("p (hp t) -> p hp t", hp=NHP)
        VTv = VT[:].rearrange("p (tt h e) -> p tt h e", tt=NTT, h=NHC)
        wqv = wq_sb[:].rearrange("p (c n) -> p c n", c=DC)
        wkv = wk_sb[:].rearrange("p (c n) -> p c n", c=DC)
        wvv = wv_sb[:].rearrange("p (c n) -> p c n", c=DC)

        ph = Phase(pr, tag)
        lda = {}; mmq = {}; mmk = {}; mmv = {}
        rpq = {}; rpk = {}; evv = {}; trq = {}; trk = {}; evq = {}; evk = {}
        for tt in range(NTT):
            bf = tt % 2
            h = (tt * P) // TOWN
            co = (tt * P) % TOWN

            @ph.op("sync", deps=(mmv.get(tt - 2),), inc=16, name=f"lda{tt}")
            def _(e, h=h, co=co, bf=bf):
                return e.dma_start(atv[:, bf], agv[:, h, :, co:co + P])
            lda[tt] = ph.ops[-1]

            @ph.op("tensor", deps=(lda[tt], rpq.get(tt - 1)), name=f"mmq{tt}")
            def _(e, bf=bf):
                for dc in range(DC):
                    ins = e.matmul(ps_q[:], atv[:, bf, dc], wqv[:, dc],
                                   start=(dc == 0), stop=(dc == DC - 1))
                return ins
            mmq[tt] = ph.ops[-1]

            @ph.op("tensor", deps=(rpk.get(tt - 1),), name=f"mmk{tt}")
            def _(e, bf=bf):
                for dc in range(DC):
                    ins = e.matmul(ps_k[:], atv[:, bf, dc], wkv[:, dc],
                                   start=(dc == 0), stop=(dc == DC - 1))
                return ins
            mmk[tt] = ph.ops[-1]

            @ph.op("tensor", deps=(evv.get(tt - 1),), name=f"mmv{tt}")
            def _(e, bf=bf):
                for dc in range(DC):
                    ins = e.matmul(ps_v[:], atv[:, bf, dc], wvv[:, dc],
                                   start=(dc == 0), stop=(dc == DC - 1))
                return ins
            mmv[tt] = ph.ops[-1]

            @ph.op("vector", deps=(mmq[tt], trq.get(tt - 1)), name=f"rpq{tt}")
            def _(e, tt=tt):
                if rope:
                    _rope_rm(nc, e, ps_q, qs, qs2, qro, pr.cos, pr.sin, tt)
                    return e.tensor_copy(qro[:, 0:8], qro[:, 0:8])
                return e.tensor_copy(qro[:], ps_q[:])
            rpq[tt] = ph.ops[-1]

            @ph.op("vector", deps=(mmk[tt], trk.get(tt - 1)), name=f"rpk{tt}")
            def _(e, tt=tt):
                if rope:
                    _rope_rm(nc, e, ps_k, qs, qs2, kro, pr.cos, pr.sin, tt)
                    return e.tensor_copy(kro[:, 0:8], kro[:, 0:8])
                return e.tensor_copy(kro[:], ps_k[:])
            rpk[tt] = ph.ops[-1]

            @ph.op("vector", deps=(mmv[tt],), name=f"evv{tt}")
            def _(e, tt=tt):
                return e.tensor_copy(
                    VTv[:, tt].rearrange("p h e -> p (h e)"), ps_v[:])
            evv[tt] = ph.ops[-1]

            @ph.op("tensor", deps=(rpq[tt], evq.get(tt - 1)), name=f"trq{tt}")
            def _(e):
                for hp in range(NHP):
                    ins = e.transpose(ps_tq[:, ts(hp, P)], qro[:, ts(hp, P)],
                                      pr.ident[:])
                return ins
            trq[tt] = ph.ops[-1]

            @ph.op("tensor", deps=(rpk[tt], evk.get(tt - 1)), name=f"trk{tt}")
            def _(e):
                for hp in range(NHP):
                    ins = e.transpose(ps_tk[:, ts(hp, P)], kro[:, ts(hp, P)],
                                      pr.ident[:])
                return ins
            trk[tt] = ph.ops[-1]

            @ph.op("scalar", deps=(trq[tt],), name=f"evq{tt}")
            def _(e, tt=tt):
                for hp in range(NHP):
                    ins = e.copy(QTv[:, hp, ts(tt, P)], ps_tq[:, ts(hp, P)])
                return ins
            evq[tt] = ph.ops[-1]

            @ph.op("scalar", deps=(trk[tt],), name=f"evk{tt}")
            def _(e, tt=tt):
                for hp in range(NHP):
                    ins = e.copy(KTv[:, hp, ts(tt, P)], ps_tk[:, ts(hp, P)])
                return ins
            evk[tt] = ph.ops[-1]
        ph.emit()


def _rope_rm(nc, e, ps_in, t1s, t2s, out_bf, cosT, sinT, tt):
    ev = bap(ps_in, 0, [[512, P], [64, NHC], [1, 32]])
    ov = bap(ps_in, 32, [[512, P], [64, NHC], [1, 32]])
    t1 = bap(t1s, 0, [[512, P], [32, NHC], [1, 32]])
    t2 = bap(t2s, 0, [[512, P], [32, NHC], [1, 32]])
    oe = bap(out_bf, 0, [[512, P], [64, NHC], [1, 32]])
    oo = bap(out_bf, 32, [[512, P], [64, NHC], [1, 32]])
    cb = bap(cosT, tt * 32, [[1024, P], [0, NHC], [1, 32]])
    sn = bap(sinT, tt * 32, [[1024, P], [0, NHC], [1, 32]])
    e.tensor_mul(t1, ev, cb)
    e.tensor_mul(t2, ov, sn)
    e.tensor_sub(oe, t1, t2)
    e.tensor_mul(t1, ev, sn)
    e.tensor_mul(t2, ov, cb)
    e.tensor_add(oo, t1, t2)



def emit_causal_attention(pr, QT, KT, VT, ag_t_in, scale, tag,
                          k_fn=None, v_fn=None):
    nc = pr.nc
    es = ExitStack()
    with es:
        sb = lambda n, s, d: es.enter_context(nc.sbuf_tensor(f"{tag}_{n}", s, d))
        PA = sb("PA", [P, 2 * 512], BF16)
        PB = sb("PB", [P, 2 * 512], BF16)
        att = sb("att", [P, 512], BF16)
        attf = sb("attf", [P, 2 * 512], BF16)
        rec = sb("rec", [P, 512], F32)
        ps_sA = es.enter_context(nc.psum_tensor(f"{tag}_sA", [P, 2 * 512], F32))
        ps_sB = es.enter_context(nc.psum_tensor(f"{tag}_sB", [P, 2 * 512], F32))
        ps_pv = es.enter_context(nc.psum_tensor(f"{tag}_pv", [P, 512], F32))
        ps_dn = es.enter_context(nc.psum_tensor(f"{tag}_dn", [P, 512], F32))
        ps_bc = es.enter_context(nc.psum_tensor(f"{tag}_bc", [P, 2 * 512], F32))

        QTv = QT[:].rearrange("p (hp t) -> p hp t", hp=NHP)
        if k_fn is None:
            KTv = KT[:].rearrange("p (hp t) -> p hp t", hp=NHP)
            k_fn = lambda KTv, hp, half, kt: \
                KTv[half * 64:(half + 1) * 64, hp, ts(kt, P)]
        else:
            KTv = None
        if v_fn is None:
            VTv = VT[:].rearrange("p (tt h e) -> p tt h e", tt=NTT, h=NHC)
            v_fn = lambda VTv, hp, half, kt: VTv[:, kt, 2 * hp + half, :]
        else:
            VTv = None
        agv = ag_t_in.ap().rearrange("(hp q) t -> q hp t", q=P)

        ph = Phase(pr, tag)
        S = {}; MK = {}; EX = {}; PV = {}
        RC = {}; BC = {}; EV = {}; SCL = {}; STO = {}
        blocks = [(hp, qb) for hp in range(NHP) for qb in range(NQB)]

        def reg_S(hp, qb, kt):
            nkt = 4 * (qb + 1)
            bf = kt % 2
            deps = [EX.get((hp, qb, kt - 2))]
            if kt < 2:
                L = hp * NQB + qb
                if L > 0:
                    php, pqb = divmod(L - 1, NQB)
                    deps.append(EX[(php, pqb, 4 * (pqb + 1) - 1)])

            @ph.op("tensor", deps=tuple(deps), name=f"S{hp}_{qb}_{kt}")
            def _(e, hp=hp, qb=qb, kt=kt, bf=bf):
                e.matmul(ps_sA[:, ts(bf, 512)],
                         k_fn(KTv, hp, 0, kt),
                         QTv[0:64, hp, ts(qb, 512)],
                         start=True, stop=True, tile_position=(0, 0))
                return e.matmul(ps_sB[:, ts(bf, 512)],
                                k_fn(KTv, hp, 1, kt),
                                QTv[64:128, hp, ts(qb, 512)],
                                start=True, stop=True, tile_position=(64, 0))
            S[(hp, qb, kt)] = ph.ops[-1]

            if kt >= 4 * qb:
                j = kt - 4 * qb

                @ph.op("vector", deps=(S[(hp, qb, kt)],), name=f"MK{hp}_{qb}_{kt}")
                def _(e, bf=bf, j=j):
                    off = 384 - j * P
                    e.tensor_add(ps_sA[:, ts(bf, 512)], ps_sA[:, ts(bf, 512)],
                                 pr.stair[:, off:off + 512])
                    return e.tensor_add(ps_sB[:, ts(bf, 512)],
                                        ps_sB[:, ts(bf, 512)],
                                        pr.stair[:, off:off + 512])
                MK[(hp, qb, kt)] = ph.ops[-1]

            deps = [MK.get((hp, qb, kt)) if kt >= 4 * qb else S[(hp, qb, kt)],
                    PV.get((hp, qb, kt - 2))]
            if kt < 2:
                L = hp * NQB + qb
                if L > 0:
                    php, pqb = divmod(L - 1, NQB)
                    deps.append(PV[(php, pqb, 4 * (pqb + 1) - 1)])

            @ph.op("scalar", deps=tuple(deps), name=f"EX{hp}_{qb}_{kt}")
            def _(e, bf=bf):
                e.activation(PA[:, ts(bf, 512)], ps_sA[:, ts(bf, 512)],
                             AF.Exp, scale=float(scale))
                return e.activation(PB[:, ts(bf, 512)], ps_sB[:, ts(bf, 512)],
                                    AF.Exp, scale=float(scale))
            EX[(hp, qb, kt)] = ph.ops[-1]

        def reg_PV(hp, qb, kt):
            nkt = 4 * (qb + 1)
            bf = kt % 2
            deps = [EX[(hp, qb, kt)]]
            if kt == 0:
                L = hp * NQB + qb
                if L > 0:
                    php, pqb = divmod(L - 1, NQB)
                    deps.append(SCL[(php, pqb)])

            @ph.op("tensor", deps=tuple(deps), name=f"PV{hp}_{qb}_{kt}")
            def _(e, hp=hp, kt=kt, bf=bf, st=(kt == 0), sp=(kt == nkt - 1)):
                e.matmul(ps_pv[0:64, :], v_fn(VTv, hp, 0, kt),
                         PA[:, ts(bf, 512)], start=st, stop=sp,
                         tile_position=(0, 0), skip_group_check=True)
                e.matmul(ps_pv[64:128, :], v_fn(VTv, hp, 1, kt),
                         PB[:, ts(bf, 512)], start=st, stop=sp,
                         tile_position=(0, 64), skip_group_check=True)
                e.matmul(ps_dn[0:32, :], pr.ones_bf[:, 0:32],
                         PA[:, ts(bf, 512)], start=st, stop=sp,
                         tile_position=(0, 0), skip_group_check=True)
                return e.matmul(ps_dn[32:64, :], pr.ones_bf[:, 0:32],
                                PB[:, ts(bf, 512)], start=st, stop=sp,
                                tile_position=(0, 32), skip_group_check=True)
            PV[(hp, qb, kt)] = ph.ops[-1]

        for hp, qb in blocks:
            nkt = 4 * (qb + 1)
            L = hp * NQB + qb
            for kt in range(nkt):
                reg_S(hp, qb, kt)
                if kt >= 1:
                    reg_PV(hp, qb, kt - 1)
            reg_PV(hp, qb, nkt - 1)
            last = PV[(hp, qb, nkt - 1)]

            @ph.op("vector", deps=(last,), name=f"RC{hp}_{qb}")
            def _(e):
                return e.reciprocal(rec[0:64, :], ps_dn[0:64, :])
            RC[(hp, qb)] = ph.ops[-1]

            deps = [RC[(hp, qb)]]
            if L > 0:
                php, pqb = divmod(L - 1, NQB)
                deps.append(SCL[(php, pqb)])

            @ph.op("tensor", deps=tuple(deps), name=f"BC{hp}_{qb}")
            def _(e):
                e.matmul(ps_bc[:, 0:512], pr.ones_f32r[0:1, :], rec[0:1, :],
                         start=True, stop=True)
                return e.matmul(ps_bc[:, 512:1024], pr.ones_f32r[32:33, :],
                                rec[32:33, :], start=True, stop=True)
            BC[(hp, qb)] = ph.ops[-1]

            deps = [last]
            if L > 0:
                php, pqb = divmod(L - 1, NQB)
                deps.append(SCL[(php, pqb)])

            @ph.op("scalar", deps=tuple(deps), name=f"EV{hp}_{qb}")
            def _(e):
                return e.copy(att[:], ps_pv[:])
            EV[(hp, qb)] = ph.ops[-1]

            deps = [BC[(hp, qb)], EV[(hp, qb)]]
            if L >= 2:
                php, pqb = divmod(L - 2, NQB)
                deps.append(STO[(php, pqb)])

            @ph.op("vector", deps=tuple(deps), name=f"SCL{hp}_{qb}")
            def _(e, bf=L % 2):
                e.tensor_mul(attf[0:64, ts(bf, 512)], att[0:64, :],
                             ps_bc[0:64, 0:512])
                return e.tensor_mul(attf[64:128, ts(bf, 512)], att[64:128, :],
                                    ps_bc[64:128, 512:1024])
            SCL[(hp, qb)] = ph.ops[-1]

            @ph.op("gpsimd", deps=(SCL[(hp, qb)],), inc=16, name=f"STO{hp}_{qb}")
            def _(e, hp=hp, qb=qb, bf=L % 2):
                return e.dma_start(agv[:, hp, ts(qb, 512)], attf[:, ts(bf, 512)])
            STO[(hp, qb)] = ph.ops[-1]

        @ph.op("gpsimd", deps=(STO[(NHP - 1, NQB - 1)],), name="fin")
        def _(e):
            return e.memset(rec[0:1, 0:1], 0.0)
        ph.emit()



def emit_oproj_ln_residual(pr, ag_t_out, wo_sb, wo_ld_op, x_in, x_out, tag,
                           rhs_view_fn=None, out_scale_ap=None,
                           rhs_dep=None):
    nc = pr.nc
    NST = TOWN // 512
    es = ExitStack()
    with es:
        sb = lambda n, s, d: es.enter_context(nc.sbuf_tensor(f"{tag}_{n}", s, d))
        ats = sb("ats", [P, 2 * DC * 512], BF16)
        yb = sb("yb", [P, DC * 512], BF16)
        yq = sb("yq", [P, DC * 512], BF16)
        xf = sb("xf", [P, 2 * DC * 512], F32)
        mr = sb("mr", [P, 512], F32)
        rr = sb("rr", [P, 512], F32)
        xo = sb("xo", [P, 2 * DC * 512], F32)
        t1 = sb("t1", [P, 512], F32)
        ps_y = es.enter_context(nc.psum_tensor(f"{tag}_py", [P, 2 * 512], F32))
        ps_m = es.enter_context(nc.psum_tensor(f"{tag}_pm", [P, 512], F32))
        ps_s = es.enter_context(nc.psum_tensor(f"{tag}_ps", [P, 512], F32))

        agv = (ag_t_out.ap().rearrange("(c p) t -> p c t", p=P)
               if ag_t_out is not None else None)
        x3 = x_in.rearrange("(c p) t -> p c t", p=P)
        o3 = x_out.rearrange("(c p) t -> p c t", p=P)
        atsv = ats[:].rearrange("p (b c t) -> p b c t", b=2, c=DC)
        xfv = xf[:].rearrange("p (b c t) -> p b c t", b=2, c=DC)
        xov = xo[:].rearrange("p (b c t) -> p b c t", b=2, c=DC)
        wov = wo_sb[:].rearrange("p (c n) -> p c n", c=DC)

        ph = Phase(pr, tag)
        lda = {}; ldx = {}; mmy = {}; evy = {}; sqy = {}; smm = {}
        comb = {}; rstd = {}; add = {}; sto = {}
        for i in range(NST):
            bf = i % 2

            if rhs_view_fn is None:
                @ph.op("sync", deps=(mmy.get((i - 2, DC - 1)),), inc=16,
                       name=f"lda{i}")
                def _(e, i=i, bf=bf):
                    with e.register(f"{tag}r{i}") as r:
                        e.load(r, pr.toff_sb[0:1, 0:1])
                        e.reg_add(r, r, i)
                        e.reg_mul(r, r, 512)
                        v = AnnotatedLowered(r, 0, T - 512).to_runtime()
                        return e.dma_start(atsv[:, bf],
                                           agv[:, :, bass.DynSlice(v, 512)])
                lda[i] = ph.ops[-1]
            else:
                lda[i] = rhs_dep

            @ph.op("sync", deps=(add.get(i - 2),), inc=16, name=f"ldx{i}")
            def _(e, i=i, bf=bf):
                return e.dma_start(xfv[:, bf], x3[:, :, ts(i, 512)])
            ldx[i] = ph.ops[-1]

            def reg_mmy(i, oc, bf):
                yb2 = oc % 2

                @ph.op("tensor",
                       deps=(lda[i], evy.get((i, oc - 2)),
                             evy.get((i - 1, DC - 1)) if oc < 2 else None,
                             (wo_ld_op if (i == 0 and oc == 0) else None)),
                       name=f"mmy{i}_{oc}")
                def _(e, i=i, oc=oc, bf=bf, yb2=yb2):
                    for ec in range(DC):
                        rhs = (atsv[:, bf, ec] if rhs_view_fn is None
                               else rhs_view_fn(i, ec))
                        ins = e.matmul(ps_y[:, ts(yb2, 512)],
                                       wov[:, ec, ts(oc, P)], rhs,
                                       start=(ec == 0), stop=(ec == DC - 1),
                                       skip_group_check=True)
                    return ins
                mmy[(i, oc)] = ph.ops[-1]

                @ph.op("scalar", deps=(mmy[(i, oc)], add.get(i - 1)),
                       name=f"evy{i}_{oc}")
                def _(e, oc=oc, yb2=yb2):
                    if out_scale_ap is not None:
                        return e.activation(yb[:, ts(oc, 512)],
                                            ps_y[:, ts(yb2, 512)], AF.Copy,
                                            scale=out_scale_ap)
                    return e.copy(yb[:, ts(oc, 512)], ps_y[:, ts(yb2, 512)])
                evy[(i, oc)] = ph.ops[-1]

                @ph.op("vector", deps=(evy[(i, oc)],), name=f"sqy{i}_{oc}")
                def _(e, oc=oc):
                    return e.tensor_mul(yq[:, ts(oc, 512)], yb[:, ts(oc, 512)],
                                        yb[:, ts(oc, 512)])
                sqy[(i, oc)] = ph.ops[-1]

            def reg_smm(i, oc):
                @ph.op("tensor", deps=(sqy[(i, oc)], comb.get(i - 1)),
                       name=f"smm{i}_{oc}")
                def _(e, oc=oc):
                    e.matmul(ps_m[:], pr.ones_bf[:], yb[:, ts(oc, 512)],
                             start=(oc == 0), stop=(oc == DC - 1),
                             skip_group_check=True)
                    return e.matmul(ps_s[:], pr.ones_bf[:], yq[:, ts(oc, 512)],
                                    start=(oc == 0), stop=(oc == DC - 1),
                                    skip_group_check=True)
                smm[(i, oc)] = ph.ops[-1]

            for oc in range(DC):
                reg_mmy(i, oc, bf)
                if oc >= 1:
                    reg_smm(i, oc - 1)
            reg_smm(i, DC - 1)

            @ph.op("vector", deps=(smm[(i, DC - 1)], rstd.get(i - 1)),
                   name=f"comb{i}")
            def _(e):
                e.tensor_scalar_mul(mr[:], ps_m[:], 1.0 / D)
                e.tensor_mul(rr[:], mr[:], mr[:])
                return e.scalar_tensor_tensor(rr[:], ps_s[:], 1.0 / D, rr[:],
                                              ALU.mult, ALU.subtract)
            comb[i] = ph.ops[-1]

            @ph.op("scalar", deps=(comb[i],), name=f"rstd{i}")
            def _(e):
                e.activation(rr[:], rr[:], AF.Ln, bias=pr.eps_ln)
                return e.activation(rr[:], rr[:], AF.Exp, scale=-0.5)
            rstd[i] = ph.ops[-1]

            @ph.op("vector", deps=(rstd[i], ldx[i], sto.get(i - 2)),
                   name=f"add{i}")
            def _(e, i=i, bf=bf):
                for oc in range(DC):
                    e.tensor_sub(t1[:], yb[:, ts(oc, 512)], mr[:])
                    e.tensor_mul(t1[:], t1[:], rr[:])
                    ins = e.tensor_add(xov[:, bf, oc], xfv[:, bf, oc], t1[:])
                return ins
            add[i] = ph.ops[-1]

            @ph.op("gpsimd", deps=(add[i],), inc=16, name=f"sto{i}")
            def _(e, i=i, bf=bf):
                return e.dma_start(o3[:, :, ts(i, 512)], xov[:, bf])
            sto[i] = ph.ops[-1]

        @ph.op("gpsimd", deps=(sto[NST - 1],), name="fin")
        def _(e):
            return e.memset(t1[0:1, 0:1], 0.0)
        ph.emit()



def emit_slot_l0_attn(pr, x_in, x_out, wq, wk, wv, wo, pfx="s0"):
    nc = pr.nc
    ag_a_in = nc.dram_tensor(f"{pfx}_agai", [D, TOWN], BF16)
    ag_a_out = nc.dram_tensor(f"{pfx}_agao", [2 * D, TOWN], BF16)
    ag_t_in = nc.dram_tensor(f"{pfx}_agti", [NHC * HD, T], BF16)
    ag_t_out = nc.dram_tensor(f"{pfx}_agto", [2 * NHC * HD, T], BF16)
    emit_ln_to_dram(pr, x_in, ag_a_in.ap(), f"{pfx}_ln1")
    pr.pair_allgather(ag_a_in, ag_a_out)
    emit_attn_mid_l0(pr, ag_a_out, wq, wk, wv, ag_t_in)
    pr.pair_allgather(ag_t_in, ag_t_out)
    emit_oproj_block(pr, ag_t_out, wo, x_in, x_out, f"{pfx}_opj")


def build_l0_attn():
    pr = Prog()
    nc = pr.nc
    x_in = nc.dram_tensor("x_in", [D, TOWN], F32, kind="ExternalInput").ap()
    wq = nc.dram_tensor("l0_wq", [D, NHC * HD], F32, kind="ExternalInput").ap()
    wk = nc.dram_tensor("l0_wk", [D, NHC * HD], F32, kind="ExternalInput").ap()
    wv = nc.dram_tensor("l0_wv", [D, NHC * HD], F32, kind="ExternalInput").ap()
    wo = nc.dram_tensor("l0_wo", [D, D], F32, kind="ExternalInput").ap()
    x_out = nc.dram_tensor("x_out", [D, TOWN], F32, kind="ExternalOutput").ap()
    pr.load_consts(rope=True)
    emit_slot_l0_attn(pr, x_in, x_out, wq, wk, wv, wo)
    pr.close()
    return pr.nc


def load_weight_dmajor(pr, ph, w_dram, w_sb, name):
    @ph.op("gpsimd", inc=16, name=name)
    def _(e):
        return e.dma_start(
            w_sb[:].rearrange("p (c n) -> p c n", c=DC),
            w_dram.rearrange("(c p) n -> p c n", p=P))
    return ph.ops[-1]


def emit_attn_mid_l0(pr, ag_a_out, wq, wk, wv, ag_t_in):
    nc = pr.nc
    with ExitStack() as es:
        sbl = lambda n, s, d: es.enter_context(nc.sbuf_tensor(n, s, d))
        wq_sb = sbl("wq_sb", [P, DC * NHC * HD], BF16)
        wk_sb = sbl("wk_sb", [P, DC * NHC * HD], BF16)
        wv_sb = sbl("wv_sb", [P, DC * NHC * HD], BF16)
        QT = sbl("QT", [P, NHP * T], BF16)
        KT = sbl("KT", [P, NHP * T], BF16)
        VT = sbl("VT", [P, NTT * NHC * HD], BF16)

        phw = Phase(pr, "wload")
        wops = [load_weight_dmajor(pr, phw, wq, wq_sb, "wq"),
                load_weight_dmajor(pr, phw, wk, wk_sb, "wk"),
                load_weight_dmajor(pr, phw, wv, wv_sb, "wv")]

        @phw.op("gpsimd", deps=tuple(wops), name="wfin")
        def _(e):
            return e.memset(QT[0:1, 0:1], 0.0)
        phw.emit()

        emit_qkv_rope(pr, ag_a_out, wq_sb, wk_sb, wv_sb, QT, KT, VT, "qkv",
                      rope=True)
        emit_causal_attention(pr, QT, KT, VT, ag_t_in, scale=1.0 / 8.0,
                              tag="att")


def emit_oproj_block(pr, ag_t_out, wo, x_in, x_out, tag):
    nc = pr.nc
    with ExitStack() as es:
        wo_sb = es.enter_context(nc.sbuf_tensor(f"{tag}_wo", [P, DC * D], BF16))
        phw = Phase(pr, "woload")
        wo_op = load_weight_dmajor(pr, phw, wo, wo_sb, "wo")

        @phw.op("gpsimd", deps=(wo_op,), name="wofin")
        def _(e):
            return e.memset(wo_sb[0:1, 0:1], 0.0)
        phw.emit()
        emit_oproj_ln_residual(pr, ag_t_out, wo_sb, None, x_in, x_out, tag)



def emit_bitlinear_wq(pr, w_dram, wq_sb, beta_rep, tag):
    nc = pr.nc
    with ExitStack() as es:
        sb = lambda n, s, d: es.enter_context(nc.sbuf_tensor(f"{tag}_{n}", s, d))
        wf = sb("wf", [P, DC * D], F32)
        aw = sb("aw", [P, DC * D], F32)
        acc = sb("acc", [P, DC], F32)
        acs = sb("acs", [P, 1], F32)
        sgn = sb("sgn", [P, DC * D], BF16)
        ps_b = es.enter_context(nc.psum_tensor(f"{tag}_pb", [P, 512], F32))

        ph = Phase(pr, tag)

        @ph.op("sync", inc=16, name="ldw")
        def _(e):
            return e.dma_start(wf[:].rearrange("p (c n) -> p c n", c=DC),
                               w_dram.rearrange("(c p) n -> p c n", p=P))
        ldw = ph.ops[-1]

        @ph.op("scalar", deps=(ldw,), name="absw")
        def _(e):
            for dc in range(DC):
                ins = e.activation(aw[:, ts(dc, D)], wf[:, ts(dc, D)], AF.Abs,
                                   accum_out=acc[:, dc:dc + 1])
            return ins
        absw = ph.ops[-1]

        @ph.op("vector", deps=(absw,), name="accsum")
        def _(e):
            return e.tensor_reduce(acs[:, 0:1], acc[:],
                                   mybir.AxisListType.X, ALU.add)
        accs = ph.ops[-1]

        @ph.op("tensor", deps=(accs,), name="betamm")
        def _(e):
            return e.matmul(ps_b[0:1, 0:1], pr.ones_f32c[:, 0:1], acs[:, 0:1],
                            start=True, stop=True)
        betm = ph.ops[-1]

        @ph.op("vector", deps=(betm,), name="betasc")
        def _(e):
            return e.tensor_scalar_mul(acs[0:1, 0:1], ps_b[0:1, 0:1],
                                       1.0 / (D * D))
        betsc = ph.ops[-1]

        @ph.op("tensor", deps=(betsc,), name="betabc")
        def _(e):
            return e.matmul(ps_b[:, 2:3], pr.ones_f32r[0:1, :], acs[0:1, 0:1],
                            start=True, stop=True)
        betbc = ph.ops[-1]

        @ph.op("vector", deps=(betbc,), name="betarep")
        def _(e):
            return e.tensor_copy(beta_rep[:, 0:1], ps_b[:, 2:3])
        brep = ph.ops[-1]

        @ph.op("scalar", deps=(ldw,), name="sgn")
        def _(e):
            return e.sign(sgn[:], wf[:])
        sg = ph.ops[-1]

        @ph.op("vector", deps=(brep,), name="thr")
        def _(e):
            return e.tensor_scalar(acs[:, 0:1], beta_rep[:, 0:1], 1e-5, 0.5,
                                   ALU.add, ALU.mult)
        throp = ph.ops[-1]

        @ph.op("vector", deps=(sg,), hdeps=(throp,), name="wq")
        def _(e):
            thr_bc = bap(acs, 0, [[1, P], [0, DC * D]])
            e.tensor_tensor(aw[:], aw[:], thr_bc, ALU.is_gt)
            return e.tensor_mul(wq_sb[:], sgn[:], aw[:])
        wqop = ph.ops[-1]

        @ph.op("gpsimd", deps=(wqop,), name="fin")
        def _(e):
            return e.memset(acc[0:1, 0:1], 0.0)
        ph.emit()


def emit_double_ln_absmax(pr, x_in, gam_dram, xn_sb, g_sb, b_sb, tag):
    nc = pr.nc
    NCH = TOWN // 512
    with ExitStack() as es:
        sb = lambda n, s, d: es.enter_context(nc.sbuf_tensor(f"{tag}_{n}", s, d))
        xf = sb("xf", [P, 2 * DC * 512], F32)
        z1 = sb("z1", [P, DC * 512], F32)
        zb = sb("zb", [P, DC * 512], BF16)
        zq = sb("zq", [P, DC * 512], BF16)
        nb = sb("nb", [P, 512], F32)
        mr = sb("mr", [P, 512], F32)
        rr = sb("rr", [P, 512], F32)
        m2 = sb("m2", [P, 512], F32)
        r2 = sb("r2", [P, 512], F32)
        mx = sb("mx", [P, 2 + DC], F32)
        mxd = nc.dram_tensor(f"{tag}_mxd", [P, 1], F32)
        ps_m = es.enter_context(nc.psum_tensor(f"{tag}_pm", [P, 512], F32))
        ps_s = es.enter_context(nc.psum_tensor(f"{tag}_ps", [P, 512], F32))

        x3 = x_in.rearrange("(c p) t -> p c t", p=P)
        xfv = xf[:].rearrange("p (b c t) -> p b c t", b=2, c=DC)
        xnv = xn_sb[:].rearrange("p (c n t) -> p c n t", c=DC, n=NCH)

        ph = Phase(pr, tag)
        ld = {}; cast = {}; smm = {}; comb = {}; rstd = {}; z1o = {}
        smm2 = {}; comb2 = {}; rstd2 = {}; aff = {}; mxo = {}
        for i in range(NCH):
            bf = i % 2

            @ph.op("sync", deps=(z1o.get(i - 2),), inc=16, name=f"ld{i}")
            def _(e, i=i, bf=bf):
                return e.dma_start(xfv[:, bf], x3[:, :, ts(i, 512)])
            ld[i] = ph.ops[-1]

            @ph.op("vector", deps=(ld[i], smm.get(i - 1)), name=f"cast{i}")
            def _(e, bf=bf):
                e.tensor_copy(zb[:], xfv[:, bf].rearrange("p c t -> p (c t)"))
                return e.tensor_mul(zq[:], zb[:], zb[:])
            cast[i] = ph.ops[-1]

            @ph.op("tensor", deps=(cast[i], comb.get(i - 1)), name=f"smm{i}")
            def _(e):
                for dc in range(DC):
                    e.matmul(ps_m[:], pr.ones_bf[:], zb[:, ts(dc, 512)],
                             start=(dc == 0), stop=(dc == DC - 1),
                             skip_group_check=True)
                for dc in range(DC):
                    ins = e.matmul(ps_s[:], pr.ones_bf[:], zq[:, ts(dc, 512)],
                                   start=(dc == 0), stop=(dc == DC - 1),
                                   skip_group_check=True)
                return ins
            smm[i] = ph.ops[-1]

            @ph.op("vector", deps=(smm[i],), name=f"comb{i}")
            def _(e):
                e.tensor_scalar_mul(mr[:], ps_m[:], 1.0 / D)
                e.tensor_mul(rr[:], mr[:], mr[:])
                return e.scalar_tensor_tensor(rr[:], ps_s[:], 1.0 / D, rr[:],
                                              ALU.mult, ALU.subtract)
            comb[i] = ph.ops[-1]

            @ph.op("scalar", deps=(comb[i],), name=f"rstd{i}")
            def _(e):
                e.activation(rr[:], rr[:], AF.Ln, bias=pr.eps_ln)
                return e.activation(rr[:], rr[:], AF.Exp, scale=-0.5)
            rstd[i] = ph.ops[-1]

            @ph.op("vector", deps=(rstd[i], smm2.get(i - 1)), name=f"z1o{i}")
            def _(e, bf=bf):
                for dc in range(DC):
                    e.tensor_sub(nb[:], xfv[:, bf, dc], mr[:])
                    e.tensor_mul(z1[:, ts(dc, 512)], nb[:], rr[:])
                e.tensor_copy(zb[:], z1[:])
                return e.tensor_mul(zq[:], zb[:], zb[:])
            z1o[i] = ph.ops[-1]

            @ph.op("tensor", deps=(z1o[i], comb2.get(i - 1)), name=f"smm2{i}")
            def _(e):
                for dc in range(DC):
                    e.matmul(ps_m[:], pr.ones_bf[:], zb[:, ts(dc, 512)],
                             start=(dc == 0), stop=(dc == DC - 1),
                             skip_group_check=True)
                for dc in range(DC):
                    ins = e.matmul(ps_s[:], pr.ones_bf[:], zq[:, ts(dc, 512)],
                                   start=(dc == 0), stop=(dc == DC - 1),
                                   skip_group_check=True)
                return ins
            smm2[i] = ph.ops[-1]

            @ph.op("vector", deps=(smm2[i],), name=f"comb2{i}")
            def _(e):
                e.tensor_scalar_mul(m2[:], ps_m[:], 1.0 / D)
                e.tensor_mul(r2[:], m2[:], m2[:])
                return e.scalar_tensor_tensor(r2[:], ps_s[:], 1.0 / D, r2[:],
                                              ALU.mult, ALU.subtract)
            comb2[i] = ph.ops[-1]

            @ph.op("scalar", deps=(comb2[i],), name=f"rstd2{i}")
            def _(e):
                e.activation(r2[:], r2[:], AF.Ln, bias=pr.eps_bl)
                return e.activation(r2[:], r2[:], AF.Exp, scale=-0.5)
            rstd2[i] = ph.ops[-1]

            @ph.op("vector", deps=(rstd2[i],), name=f"z2o{i}")
            def _(e):
                for dc in range(DC):
                    e.tensor_sub(nb[:], z1[:, ts(dc, 512)], m2[:])
                    ins = e.tensor_mul(z1[:, ts(dc, 512)], nb[:], r2[:])
                return ins
            z2o = ph.ops[-1]

            @ph.op("vector", deps=(z2o,), name=f"aff{i}")
            def _(e, i=i):
                for dc in range(DC):
                    b_bc = bap(b_sb, dc, [[DC, P], [0, 512]])
                    ins = e.scalar_tensor_tensor(
                        xnv[:, dc, i], z1[:, ts(dc, 512)],
                        g_sb[:, dc:dc + 1], b_bc, ALU.mult, ALU.add)
                return ins
            aff[i] = ph.ops[-1]

            @ph.op("vector", deps=(aff[i], mxo.get(i - 1)), name=f"mxa{i}")
            def _(e, i=i):
                return e.tensor_reduce(mx[:, 2:2 + DC], xnv[:, :, i],
                                       mybir.AxisListType.X, ALU.max,
                                       apply_absolute_value=True)
            mxa = ph.ops[-1]

            @ph.op("vector", hdeps=(mxa,), name=f"mxb{i}")
            def _(e, i=i):
                return e.tensor_reduce(mx[:, 1:2], mx[:, 2:2 + DC],
                                       mybir.AxisListType.X, ALU.max)
            mxb = ph.ops[-1]

            @ph.op("vector", hdeps=(mxb,), name=f"mx{i}")
            def _(e, i=i):
                if i == 0:
                    return e.tensor_copy(mx[:, 0:1], mx[:, 1:2])
                return e.tensor_max(mx[:, 0:1], mx[:, 0:1], mx[:, 1:2])
            mxo[i] = ph.ops[-1]

        mxr = sb("mxr", [1, P], F32)
        g8 = sb("g8", [1, 8], F32)

        @ph.op("gpsimd", deps=(mxo[NCH - 1],), inc=16, name="mxst")
        def _(e):
            return e.dma_start(mxd.ap()[:, :], mx[:, 0:1])
        mxst = ph.ops[-1]

        @ph.op("sync", deps=(mxst,), inc=16, name="mxld")
        def _(e):
            return e.dma_start(mxr[0:1, :], mxd.ap().rearrange("p o -> o p"))
        mxld = ph.ops[-1]

        @ph.op("vector", deps=(mxld,), name="mxred")
        def _(e):
            e.memset(g8[:], 0.0)
            return e.tensor_reduce(g8[0:1, 0:1], mxr[0:1, :],
                                   mybir.AxisListType.X, ALU.max)
        mxred = ph.ops[-1]

        @ph.op("gpsimd", deps=(mxred,), inc=16, name="gst")
        def _(e):
            return e.dma_start(gam_dram.ap()[:, :], g8[0:1, :])
        gst = ph.ops[-1]

        @ph.op("gpsimd", deps=(gst,), name="fin")
        def _(e):
            return e.memset(mx[0:1, 1:2], 0.0)
        ph.emit()



def emit_gamma_scales(pr, gam_dram, beta_rep, sc1, s_out, tag):
    nc = pr.nc
    with ExitStack() as es:
        sb = lambda n, s, d: es.enter_context(nc.sbuf_tensor(f"{tag}_{n}", s, d))
        g8 = sb("g8", [1, 8], F32)
        ps_g = es.enter_context(nc.psum_tensor(f"{tag}_pg", [P, 2], F32))
        ph = Phase(pr, tag)

        @ph.op("sync", inc=16, name="gld")
        def _(e):
            return e.dma_start(g8[0:1, :], gam_dram.ap()[:, :])
        gld = ph.ops[-1]

        @ph.op("tensor", deps=(gld,), name="gbc")
        def _(e):
            return e.matmul(ps_g[:, 0:1], pr.ones_f32r[0:1, :], g8[0:1, 0:1],
                            start=True, stop=True)
        gbc = ph.ops[-1]

        @ph.op("vector", deps=(gbc,), name="grecip")
        def _(e):
            return e.reciprocal(sc1[:, 0:1], ps_g[:, 0:1])
        grc = ph.ops[-1]

        @ph.op("vector", deps=(gbc,), hdeps=(grc,), name="gscales")
        def _(e):
            return e.tensor_scalar_mul(sc1[:, 0:1], sc1[:, 0:1], 128.0)
        gs1 = ph.ops[-1]

        @ph.op("vector", deps=(gbc,), hdeps=(gs1,), name="gsout")
        def _(e):
            return e.tensor_scalar(s_out[:, 0:1], ps_g[:, 0:1],
                                   beta_rep[:, 0:1], 1.0 / 128.0,
                                   ALU.mult, ALU.mult)
        gs = ph.ops[-1]

        @ph.op("gpsimd", deps=(gs,), name="fin")
        def _(e):
            return e.memset(g8[0:1, 0:1], 0.0)
        ph.emit()


def emit_slot_l0_ffn(pr, x_in, x_out, w, g_in, b_in, pfx="s1"):
    nc = pr.nc
    gam_dram = nc.dram_tensor(f"{pfx}_gam", [1, 8], F32)
    with ExitStack() as es:
        wq_sb = es.enter_context(nc.sbuf_tensor(f"{pfx}_blw", [P, DC * D], BF16))
        beta_rep = es.enter_context(nc.sbuf_tensor(f"{pfx}_beta", [P, 1], F32))
        xn_sb = es.enter_context(nc.sbuf_tensor(f"{pfx}_xn", [P, DC * TOWN], BF16))
        g_sb = es.enter_context(nc.sbuf_tensor(f"{pfx}_gg", [P, DC], F32))
        b_sb = es.enter_context(nc.sbuf_tensor(f"{pfx}_bb", [P, DC], F32))
        sc1 = es.enter_context(nc.sbuf_tensor(f"{pfx}_sc1", [P, 1], F32))
        s_out = es.enter_context(nc.sbuf_tensor(f"{pfx}_souts", [P, 1], F32))

        ph0 = Phase(pr, f"{pfx}_gbld")

        @ph0.op("sync", inc=16, name="gld")
        def _(e):
            return e.dma_start(g_sb[:], g_in[:, :])

        @ph0.op("sync", inc=16, name="bld")
        def _(e):
            return e.dma_start(b_sb[:], b_in[:, :])
        ph0.emit()

        emit_bitlinear_wq(pr, w, wq_sb, beta_rep, f"{pfx}_blq")
        emit_double_ln_absmax(pr, x_in, gam_dram, xn_sb, g_sb, b_sb, f"{pfx}_dln")
        pr.allreduce_max(gam_dram)
        emit_gamma_scales(pr, gam_dram, beta_rep, sc1, s_out, f"{pfx}_gsc")

        NCH = TOWN // 512
        xnv = xn_sb[:].rearrange("p (c n t) -> p c n t", c=DC, n=NCH)
        phq = Phase(pr, f"{pfx}_xq")

        @phq.op("vector", name="clip")
        def _(e):
            hi = 128.0 - 1e-5
            e.tensor_scalar(xn_sb[:], xn_sb[:], sc1[:, 0:1], -hi, ALU.mult,
                            ALU.max)
            return e.tensor_scalar_min(xn_sb[:], xn_sb[:], hi)
        phq.emit()

        emit_oproj_ln_residual(
            pr, None, wq_sb, None, x_in, x_out, f"{pfx}_bmm",
            rhs_view_fn=lambda i, ec: xnv[:, ec, i],
            out_scale_ap=s_out[:, 0:1])


def build_l0_ffn():
    pr = Prog()
    nc = pr.nc
    x_in = nc.dram_tensor("x_in", [D, TOWN], F32, kind="ExternalInput").ap()
    w = nc.dram_tensor("l0f_w", [D, D], F32, kind="ExternalInput").ap()
    g_in = nc.dram_tensor("l0f_g", [P, DC], F32, kind="ExternalInput").ap()
    b_in = nc.dram_tensor("l0f_b", [P, DC], F32, kind="ExternalInput").ap()
    x_out = nc.dram_tensor("x_out", [D, TOWN], F32, kind="ExternalOutput").ap()
    pr.load_consts(rope=False)
    emit_slot_l0_ffn(pr, x_in, x_out, w, g_in, b_in)
    pr.close()
    return pr.nc


def stage_l0_ffn(inputs, x_maps):
    w = np.ascontiguousarray(np.asarray(inputs["L0_ffn_weight"], np.float32))
    g = np.asarray(inputs["L0_ffn_ln_gamma"], np.float32)
    b = np.asarray(inputs["L0_ffn_ln_beta"], np.float32)
    gt = np.ascontiguousarray(g.reshape(DC, P).T)
    bt = np.ascontiguousarray(b.reshape(DC, P).T)
    maps = []
    for c in range(8):
        maps.append({"x_in": x_maps[c], "l0f_w": w, "l0f_g": gt, "l0f_b": bt,
                     "c_toff": np.array([[(c % 2) * (TOWN // 512)]], np.int32)})
    return maps



NL = 64
TPL = T // NL


def emit_nys_qkv(pr, ag_a_out, wqkv_sb, QT, KT, VT, tag):
    nc = pr.nc
    NCB = T // 512
    with ExitStack() as es:
        sb = lambda n, s, d: es.enter_context(nc.sbuf_tensor(f"{tag}_{n}", s, d))
        at = sb("at", [P, 2 * DC * 512], BF16)
        ps_q = es.enter_context(nc.psum_tensor(f"{tag}_pq", [P, 512], F32))
        ps_k = es.enter_context(nc.psum_tensor(f"{tag}_pk", [P, 512], F32))
        ps_v = es.enter_context(nc.psum_tensor(f"{tag}_pv", [P, 512], F32))

        agv = ag_a_out.ap().rearrange("(h c p) t -> p h c t", p=P, c=DC)
        atv = at[:].rearrange("p (b c q) -> p b c q", b=2, c=DC)
        QTv = QT[:].rearrange("p (hp t) -> p hp t", hp=NHP)
        KTv = KT[:].rearrange("p (hp t) -> p hp t", hp=NHP)
        VTv = VT[:].rearrange("p (tt h e) -> p tt h e", tt=NTT, h=NHC)
        wv = wqkv_sb[:].rearrange("p (c n) -> p c n", c=DC)

        ph = Phase(pr, tag)

        @ph.op("vector", name="vones")
        def _(e):
            return e.memset(bap(VT, 64, [[NTT * NHC * 65, P],
                                         [65, NTT * NHC], [1, 1]]), 1.0)
        vones = ph.ops[-1]
        lda = {}; mmq = {}; mmk = {}; mmv = {}; evq = {}; evk = {}; evv = {}
        for cb in range(NCB):
            bf = cb % 2
            h = (cb * 512) // TOWN
            co = (cb * 512) % TOWN

            @ph.op("sync", deps=(mmv.get(cb - 2),), inc=16, name=f"lda{cb}")
            def _(e, h=h, co=co, bf=bf):
                return e.dma_start(atv[:, bf], agv[:, h, :, co:co + 512])
            lda[cb] = ph.ops[-1]

            for hp in range(NHP):
                prevq = (cb, hp - 1) if hp > 0 else (cb - 1, NHP - 1)

                @ph.op("tensor", deps=(lda[cb], evq.get(prevq)),
                       name=f"mmq{cb}_{hp}")
                def _(e, hp=hp, bf=bf):
                    for dc in range(DC):
                        ins = e.matmul(ps_q[:], wv[:, dc, hp * P:(hp + 1) * P],
                                       atv[:, bf, dc],
                                       start=(dc == 0), stop=(dc == DC - 1),
                                       skip_group_check=True)
                    return ins
                mmq[(cb, hp)] = ph.ops[-1]

                @ph.op("vector", deps=(mmq[(cb, hp)],), name=f"evq{cb}_{hp}")
                def _(e, hp=hp, cb=cb):
                    return e.tensor_copy(QTv[:, hp, ts(cb, 512)], ps_q[:])
                evq[(cb, hp)] = ph.ops[-1]

                @ph.op("tensor", deps=(lda[cb], evk.get(prevq)),
                       name=f"mmk{cb}_{hp}")
                def _(e, hp=hp, bf=bf):
                    for dc in range(DC):
                        ins = e.matmul(ps_k[:],
                                       wv[:, dc, 512 + hp * P:512 + (hp + 1) * P],
                                       atv[:, bf, dc],
                                       start=(dc == 0), stop=(dc == DC - 1),
                                       skip_group_check=True)
                    return ins
                mmk[(cb, hp)] = ph.ops[-1]

                @ph.op("vector", deps=(mmk[(cb, hp)],), name=f"evk{cb}_{hp}")
                def _(e, hp=hp, cb=cb):
                    return e.tensor_copy(KTv[:, hp, ts(cb, 512)], ps_k[:])
                evk[(cb, hp)] = ph.ops[-1]

            for st4 in range(4):
                tt = cb * 4 + st4

                @ph.op("tensor", deps=(lda[cb], evv.get(tt - 1)),
                       name=f"mmv{tt}")
                def _(e, bf=bf, st4=st4):
                    for dc in range(DC):
                        ins = e.matmul(
                            ps_v[:],
                            atv[:, bf, dc, st4 * P:(st4 + 1) * P],
                            wv[:, dc, 1024:1536],
                            start=(dc == 0), stop=(dc == DC - 1),
                            skip_group_check=True)
                    return ins
                mmvo = ph.ops[-1]

                @ph.op("vector", deps=(mmvo, vones), name=f"evv{tt}")
                def _(e, tt=tt):
                    return e.tensor_copy(VTv[:, tt, :, 0:64], ps_v[:].rearrange(
                        "p (h e) -> p h e", h=NHC))
                evv[tt] = ph.ops[-1]
            mmv[cb] = mmvo
        ph.emit()


def emit_nystrom_attn(pr, QT, KT, VT, ag_t_in, tag, dbg=None):
    nc = pr.nc
    with ExitStack() as es:
        sb = lambda n, s, d: es.enter_context(nc.sbuf_tensor(f"{tag}_{n}", s, d))
        klT = sb("klT", [P, NHP * NL], BF16)
        qlT = sb("qlT", [P, NHP * NL], BF16)
        klF = sb("klF", [P, NL], F32)
        A = sb("A", [64, NHC * 64], F32)
        AT = sb("AT", [64, NHC * 64], F32)
        Vi = sb("Vi", [64, NHC * 64], F32)
        kv = sb("kv", [64, NHC * 64], F32)
        kvt = sb("kvt", [64, NHC * 64], F32)
        tm = sb("tm", [64, NHC * 64], F32)
        vt = sb("vt", [64, NHC * 64], F32)
        rs = sb("rs", [64, NHC], F32)
        Pn = sb("Pn", [P, 2 * 512], BF16)
        Zn = sb("Zn", [64, NHC * 64], F32)
        W2 = sb("W2", [64, NHC * 65], BF16)
        EL = sb("EL", [64, 2 * 512], BF16)
        rcl = sb("rcl", [P, 512], F32)
        att = sb("att", [64, 512], BF16)
        attf = sb("attf", [64, 2 * 512], BF16)
        ps_a = es.enter_context(nc.psum_tensor(f"{tag}_pa", [P, 512], F32))
        ps_b = es.enter_context(nc.psum_tensor(f"{tag}_pb", [P, 512], F32))
        ps_n = es.enter_context(nc.psum_tensor(f"{tag}_pn", [P, 2 * 512], F32))
        ps_z = es.enter_context(nc.psum_tensor(f"{tag}_pz", [64, 512], F32))
        ps_y = es.enter_context(nc.psum_tensor(f"{tag}_py", [P, 2 * 512], F32))
        ps_bc = es.enter_context(nc.psum_tensor(f"{tag}_pbc", [P, 512], F32))

        QTv = QT[:].rearrange("p (hp t) -> p hp t", hp=NHP)
        KTv = KT[:].rearrange("p (hp t) -> p hp t", hp=NHP)
        VTv = VT[:].rearrange("p (tt h e) -> p tt h e", tt=NTT, h=NHC)
        klTv = klT[:].rearrange("p (hp l) -> p hp l", hp=NHP)
        qlTv = qlT[:].rearrange("p (hp l) -> p hp l", hp=NHP)
        agv = ag_t_in.ap().rearrange("(h e) t -> e h t", e=HD)

        ph = Phase(pr, tag)

        lmk = {}
        lprev = None
        for hp in range(NHP):
            @ph.op("vector", deps=(lprev,), name=f"lmkr{hp}")
            def _(e, hp=hp):
                kk = KTv[:, hp].rearrange("p (l g) -> p l g", l=NL)
                return e.tensor_reduce(klF[:, :], kk, mybir.AxisListType.X,
                                       ALU.add)
            o1 = ph.ops[-1]

            @ph.op("vector", hdeps=(o1,), name=f"lmks{hp}")
            def _(e, hp=hp):
                return e.tensor_scalar_mul(klTv[:, hp], klF[:, :], 1.0 / TPL)
            o2 = ph.ops[-1]

            @ph.op("vector", hdeps=(o2,), name=f"lmqr{hp}")
            def _(e, hp=hp):
                qq = QTv[:, hp].rearrange("p (l g) -> p l g", l=NL)
                return e.tensor_reduce(klF[:, :], qq, mybir.AxisListType.X,
                                       ALU.add)
            o3 = ph.ops[-1]

            @ph.op("vector", hdeps=(o3,), name=f"lmqs{hp}")
            def _(e, hp=hp):
                return e.tensor_scalar_mul(qlTv[:, hp], klF[:, :], 1.0 / TPL)
            lmk[hp] = ph.ops[-1]
            lprev = lmk[hp]

        AS = {}; INV = {}
        aprev = None
        for h in range(NHC):
            hp, hh = divmod(h, 2)
            base = hh * 64

            @ph.op("tensor", deps=(lmk[hp], aprev), name=f"As{h}")
            def _(e, hp=hp, base=base, h=h):
                return e.matmul(ps_a[0:64, 0:64],
                                qlTv[base:base + 64, hp],
                                klTv[base:base + 64, hp],
                                start=True, stop=True)
            aso = ph.ops[-1]

            @ph.op("scalar", deps=(aso,), name=f"Ae{h}")
            def _(e, h=h):
                return e.activation(A[:, ts(h, 64)], ps_a[0:64, 0:64], AF.Exp)
            aeo = ph.ops[-1]

            @ph.op("vector", deps=(aeo,), name=f"Ars{h}")
            def _(e, h=h):
                return e.tensor_reduce(rs[:, h:h + 1], A[:, ts(h, 64)],
                                       mybir.AxisListType.X, ALU.add)
            arso = ph.ops[-1]

            @ph.op("vector", deps=(arso,), name=f"Arc{h}")
            def _(e, h=h):
                return e.reciprocal(rs[:, h:h + 1], rs[:, h:h + 1])
            arco = ph.ops[-1]

            @ph.op("vector", deps=(arco,), name=f"Anm{h}")
            def _(e, h=h):
                rbc = bap(rs, h, [[NHC, 64], [0, 64]])
                return e.tensor_tensor(A[:, ts(h, 64)], A[:, ts(h, 64)], rbc,
                                       ALU.mult)
            AS[h] = ph.ops[-1]
            aprev = AS[h]

        DN = {}; TR = {}
        dprev = AS[NHC - 1]
        for h in range(NHC):
            @ph.op("tensor", deps=(AS[h], dprev), name=f"Acs{h}")
            def _(e, h=h):
                e.matmul(ps_a[0:1, 0:64], pr.ones_f32c[0:64, 0:1],
                         A[:, ts(h, 64)], start=True, stop=True)
                return e.transpose(ps_b[0:64, 0:64], A[:, ts(h, 64)],
                                   pr.ident_f32[0:64, 0:64])
            acso = ph.ops[-1]

            @ph.op("vector", deps=(acso,), name=f"Amx{h}")
            def _(e, h=h):
                e.tensor_copy(AT[:, ts(h, 64)], ps_b[0:64, 0:64])
                return e.tensor_reduce(rs[0:1, h:h + 1], ps_a[0:1, 0:64],
                                       mybir.AxisListType.X, ALU.max)
            amxo = ph.ops[-1]

            @ph.op("vector", deps=(amxo,), name=f"Adn{h}")
            def _(e, h=h):
                return e.reciprocal(rs[0:1, h:h + 1], rs[0:1, h:h + 1])
            adno = ph.ops[-1]

            @ph.op("tensor", deps=(adno,), name=f"Adb{h}")
            def _(e, h=h):
                return e.matmul(ps_a[0:64, 64:65], pr.ones_f32r[0:1, 0:64],
                                rs[0:1, h:h + 1], start=True, stop=True)
            adbo = ph.ops[-1]

            @ph.op("vector", deps=(adbo,), name=f"V0{h}")
            def _(e, h=h):
                rbc = bap(ps_a, 64, [[512, 64], [0, 64]])
                return e.tensor_tensor(Vi[:, ts(h, 64)], AT[:, ts(h, 64)],
                                       rbc, ALU.mult)
            DN[h] = ph.ops[-1]
            dprev = DN[h]

        prev = {h: DN[h] for h in range(NHC)}
        chain = DN[NHC - 1]
        for it in range(6):
            for h in range(NHC):
                @ph.op("tensor", deps=(prev[h], chain), name=f"kv{it}_{h}")
                def _(e, h=h):
                    return e.matmul(ps_a[0:64, 0:64], AT[:, ts(h, 64)],
                                    Vi[:, ts(h, 64)], start=True, stop=True)
                o1 = ph.ops[-1]

                @ph.op("vector", deps=(o1,), name=f"kve{it}_{h}")
                def _(e, h=h):
                    return e.tensor_copy(kv[:, ts(h, 64)], ps_a[0:64, 0:64])
                o2 = ph.ops[-1]

                @ph.op("tensor", deps=(o2,), name=f"kvt{it}_{h}")
                def _(e, h=h):
                    return e.transpose(ps_b[0:64, 0:64], kv[:, ts(h, 64)],
                                       pr.ident_f32[0:64, 0:64])
                o3 = ph.ops[-1]

                @ph.op("vector", deps=(o3,), name=f"kvte{it}_{h}")
                def _(e, h=h):
                    e.tensor_copy(kvt[:, ts(h, 64)], ps_b[0:64, 0:64])
                    return e.scalar_tensor_tensor(
                        tm[:, ts(h, 64)], pr.eye64[:, :], 7.0, kv[:, ts(h, 64)],
                        ALU.mult, ALU.subtract)
                o4 = ph.ops[-1]

                @ph.op("tensor", deps=(o4,), name=f"u1{it}_{h}")
                def _(e, h=h):
                    return e.matmul(ps_a[0:64, 0:64], kvt[:, ts(h, 64)],
                                    tm[:, ts(h, 64)], start=True, stop=True)
                o5 = ph.ops[-1]

                @ph.op("vector", deps=(o5,), name=f"t2{it}_{h}")
                def _(e, h=h):
                    return e.scalar_tensor_tensor(
                        tm[:, ts(h, 64)], pr.eye64[:, :], 15.0,
                        ps_a[0:64, 0:64], ALU.mult, ALU.subtract)
                o6 = ph.ops[-1]

                @ph.op("tensor", deps=(o6,), name=f"u2{it}_{h}")
                def _(e, h=h):
                    return e.matmul(ps_a[0:64, 0:64], kvt[:, ts(h, 64)],
                                    tm[:, ts(h, 64)], start=True, stop=True)
                o7 = ph.ops[-1]

                @ph.op("vector", deps=(o7,), name=f"t3{it}_{h}")
                def _(e, h=h):
                    return e.scalar_tensor_tensor(
                        tm[:, ts(h, 64)], pr.eye64[:, :], 13.0,
                        ps_a[0:64, 0:64], ALU.mult, ALU.subtract)
                o8 = ph.ops[-1]

                @ph.op("tensor", deps=(o8,), name=f"vt{it}_{h}")
                def _(e, h=h):
                    return e.transpose(ps_b[0:64, 0:64], Vi[:, ts(h, 64)],
                                       pr.ident_f32[0:64, 0:64])
                o9 = ph.ops[-1]

                @ph.op("vector", deps=(o9,), name=f"vte{it}_{h}")
                def _(e, h=h):
                    return e.tensor_copy(vt[:, ts(h, 64)], ps_b[0:64, 0:64])
                o10 = ph.ops[-1]

                @ph.op("tensor", deps=(o10,), name=f"vn{it}_{h}")
                def _(e, h=h):
                    return e.matmul(ps_a[0:64, 0:64], vt[:, ts(h, 64)],
                                    tm[:, ts(h, 64)], start=True, stop=True)
                o11 = ph.ops[-1]

                @ph.op("vector", deps=(o11,), name=f"vne{it}_{h}")
                def _(e, h=h):
                    return e.tensor_scalar_mul(Vi[:, ts(h, 64)],
                                               ps_a[0:64, 0:64], 0.25)
                prev[h] = ph.ops[-1]
                chain = prev[h]

        ZO = {}
        for h in range(NHC):
            hp, hh = divmod(h, 2)
            base = hh * 64
            nexp = {}
            NZd = {}
            for g in range(NTT // 8):
                bf = g % 2

                @ph.op("tensor",
                       deps=(lmk[hp], nexp.get(g - 2) if g >= 2 else None,
                             (ZO.get(h - 1) if g < 2 else None)),
                       name=f"ns{h}_{g}")
                def _(e, h=h, hp=hp, base=base, g=g, bf=bf):
                    for s8 in range(8):
                        tt = g * 8 + s8
                        ins = e.matmul(ps_n[:, bf * 512 + s8 * 64:
                                            bf * 512 + (s8 + 1) * 64],
                                       KTv[base:base + 64, hp, ts(tt, P)],
                                       qlTv[base:base + 64, hp],
                                       start=True, stop=True,
                                       skip_group_check=True)
                    return ins
                nso = ph.ops[-1]

                @ph.op("scalar", deps=(nso, NZd.get(g - 2)),
                       name=f"ne{h}_{g}")
                def _(e, g=g, bf=bf):
                    return e.activation(Pn[:, ts(bf, 512)],
                                        ps_n[:, ts(bf, 512)], AF.Exp)
                nexp[g] = ph.ops[-1]

                @ph.op("tensor", deps=(nexp[g],), name=f"nz{h}_{g}")
                def _(e, h=h, g=g, bf=bf, first=(g == 0)):
                    for s8 in range(8):
                        tt = g * 8 + s8
                        st = first and s8 == 0
                        sp = (g == 3) and (s8 == 7)
                        ins = e.matmul(
                            ps_z[0:64, 0:65],
                            Pn[:, bf * 512 + s8 * 64:bf * 512 + (s8 + 1) * 64],
                            VTv[:, tt, h, :],
                            start=st, stop=sp, skip_group_check=True)
                    return ins
                zo = ph.ops[-1]
                NZd[g] = zo
            ZO[h] = zo

            @ph.op("vector", deps=(ZO[h],), name=f"zr{h}")
            def _(e, h=h):
                return e.reciprocal(rs[:, h:h + 1], ps_z[0:64, 64:65])
            zro = ph.ops[-1]

            @ph.op("vector", deps=(zro,), name=f"zn{h}")
            def _(e, h=h):
                rbc = bap(rs, h, [[NHC, 64], [0, 64]])
                return e.tensor_tensor(Zn[:, ts(h, 64)], ps_z[0:64, 0:64],
                                       rbc, ALU.mult)
            ZO[h] = ph.ops[-1]

        W2O = {}
        wprev = prev[NHC - 1]
        for h in range(NHC):
            @ph.op("tensor", deps=(prev[h], ZO[h], wprev), name=f"pht{h}")
            def _(e, h=h):
                return e.transpose(ps_b[0:64, 0:64], Vi[:, ts(h, 64)],
                                   pr.ident_f32[0:64, 0:64])
            o1 = ph.ops[-1]

            @ph.op("vector", deps=(o1,), name=f"phe{h}")
            def _(e, h=h):
                return e.tensor_copy(vt[:, ts(h, 64)], ps_b[0:64, 0:64])
            o2 = ph.ops[-1]

            @ph.op("tensor", deps=(o2,), name=f"w2m{h}")
            def _(e, h=h):
                return e.matmul(ps_a[0:64, 0:64], vt[:, ts(h, 64)],
                                Zn[:, ts(h, 64)], start=True, stop=True)
            o3 = ph.ops[-1]

            @ph.op("vector", deps=(o3,), name=f"w2e{h}")
            def _(e, h=h):
                e.tensor_copy(W2[:, h * 65:h * 65 + 64], ps_a[0:64, 0:64])
                return e.memset(W2[:, h * 65 + 64:h * 65 + 65], 1.0)
            W2O[h] = ph.ops[-1]
            wprev = W2O[h]

        if dbg is not None:
            phd_prev = wprev

            @ph.op("gpsimd", deps=(wprev, ZO[NHC - 1], prev[NHC - 1]),
                   inc=16, name="dbgA")
            def _(e):
                return e.dma_start(dbg["A"].ap()[:, :], A[:])
            d1 = ph.ops[-1]

            @ph.op("gpsimd", deps=(d1,), inc=16, name="dbgVi")
            def _(e):
                return e.dma_start(dbg["Vi"].ap()[:, :], Vi[:])
            d2 = ph.ops[-1]

            @ph.op("gpsimd", deps=(d2,), inc=16, name="dbgZn")
            def _(e):
                return e.dma_start(dbg["Zn"].ap()[:, :], Zn[:])
            d3 = ph.ops[-1]

            @ph.op("gpsimd", deps=(d3,), inc=16, name="dbgW2")
            def _(e):
                return e.dma_start(dbg["W2"].ap()[:, :], W2[:])
            d4 = ph.ops[-1]

            @ph.op("gpsimd", deps=(d4,), inc=16, name="dbgkl")
            def _(e):
                return e.dma_start(dbg["klT"].ap()[:, :], klT[:])
            d5 = ph.ops[-1]

            @ph.op("gpsimd", deps=(d5,), inc=16, name="dbgql")
            def _(e):
                return e.dma_start(dbg["qlT"].ap()[:, :], qlT[:])

        STO = {}
        prev_scl = None
        last_nexp_all = ZO[NHC - 1]
        for h in range(NHC):
            hp, hh = divmod(h, 2)
            base = hh * 64
            for qb in range(NQB):
                L = h * NQB + qb
                bf = L % 2

                @ph.op("tensor",
                       deps=(lmk[hp],) +
                            ((STO.get(("e", L - 2)),) if L >= 2 else
                             (last_nexp_all, wprev)),
                       name=f"es{h}_{qb}")
                def _(e, h=h, hp=hp, base=base, qb=qb, bf=bf):
                    return e.matmul(ps_n[0:64, ts(bf, 512)],
                                    klTv[base:base + 64, hp],
                                    QTv[base:base + 64, hp, ts(qb, 512)],
                                    start=True, stop=True)
                eso = ph.ops[-1]

                @ph.op("scalar", deps=(eso,) +
                       ((STO.get(("y", L - 2)),) if L >= 2 else ()),
                       name=f"ee{h}_{qb}")
                def _(e, bf=bf):
                    return e.activation(EL[0:64, ts(bf, 512)],
                                        ps_n[0:64, ts(bf, 512)], AF.Exp)
                eeo = ph.ops[-1]
                STO[("e", L)] = eeo

                @ph.op("tensor", deps=(W2O[h], eeo) +
                       ((prev_scl,) if prev_scl is not None else ()),
                       name=f"ym{h}_{qb}")
                def _(e, h=h, bf=bf):
                    return e.matmul(ps_y[0:65, ts(bf, 512)],
                                    W2[:, h * 65:(h + 1) * 65],
                                    EL[0:64, ts(bf, 512)],
                                    start=True, stop=True)
                ymo = ph.ops[-1]
                STO[("y", L)] = ymo

                @ph.op("vector", deps=(ymo,), name=f"yr{h}_{qb}")
                def _(e, bf=bf):
                    return e.reciprocal(rcl[64:65, 0:512],
                                        ps_y[64:65, ts(bf, 512)])
                yro = ph.ops[-1]

                @ph.op("tensor", deps=(yro,), name=f"yb{h}_{qb}")
                def _(e, bf=bf):
                    return e.matmul(ps_bc[0:64, 0:512],
                                    pr.ones_f32r[64:65, 0:64],
                                    rcl[64:65, 0:512], start=True, stop=True)
                ybo = ph.ops[-1]

                @ph.op("scalar", deps=(ymo,), name=f"ye{h}_{qb}")
                def _(e, bf=bf):
                    return e.copy(att[:, :], ps_y[0:64, ts(bf, 512)])
                yeo = ph.ops[-1]

                @ph.op("vector", deps=(ybo, yeo) +
                       ((STO.get(("s", L - 2)),) if L >= 2 else ()),
                       name=f"ys{h}_{qb}")
                def _(e, bf=bf):
                    return e.tensor_mul(attf[:, ts(bf, 512)], att[:, :],
                                        ps_bc[0:64, 0:512])
                yso = ph.ops[-1]
                prev_scl = yso

                @ph.op("gpsimd", deps=(yso,), inc=16, name=f"yst{h}_{qb}")
                def _(e, h=h, qb=qb, bf=bf):
                    return e.dma_start(agv[:, h, ts(qb, 512)],
                                       attf[:, ts(bf, 512)])
                STO[("s", L)] = ph.ops[-1]

        @ph.op("gpsimd", deps=(STO[("s", NHC * NQB - 1)],), name="fin")
        def _(e):
            return e.memset(rcl[0:1, 0:1], 0.0)
        ph.emit()


def emit_slot_l1_attn(pr, x_in, x_out, wqkv, wproj, pfx="s2", dbg=None):
    nc = pr.nc
    ag_a_in = nc.dram_tensor(f"{pfx}_agai", [D, TOWN], BF16)
    ag_a_out = nc.dram_tensor(f"{pfx}_agao", [2 * D, TOWN], BF16)
    ag_t_in = nc.dram_tensor(f"{pfx}_agti", [NHC * HD, T], BF16)
    ag_t_out = nc.dram_tensor(f"{pfx}_agto", [2 * NHC * HD, T], BF16)
    emit_ln_to_dram(pr, x_in, ag_a_in.ap(), f"{pfx}_ln1")
    pr.pair_allgather(ag_a_in, ag_a_out)
    with ExitStack() as es:
        wqkv_sb = es.enter_context(
            nc.sbuf_tensor(f"{pfx}_wqkv", [P, DC * 1536], BF16))
        QT = es.enter_context(nc.sbuf_tensor(f"{pfx}_QT", [P, NHP * T], BF16))
        KT = es.enter_context(nc.sbuf_tensor(f"{pfx}_KT", [P, NHP * T], BF16))
        VT = es.enter_context(nc.sbuf_tensor(f"{pfx}_VT", [P, NTT * NHC * 65], BF16))
        phw = Phase(pr, f"{pfx}_wload")
        load_weight_dmajor(pr, phw, wqkv, wqkv_sb, "wqkv")

        @phw.op("gpsimd", deps=(phw.ops[-1],), name="wfin")
        def _(e):
            return e.memset(QT[0:1, 0:1], 0.0)
        phw.emit()
        emit_nys_qkv(pr, ag_a_out, wqkv_sb, QT, KT, VT, f"{pfx}_nqkv")
        emit_nystrom_attn(pr, QT, KT, VT, ag_t_in, f"{pfx}_nys", dbg=dbg)
    pr.pair_allgather(ag_t_in, ag_t_out)
    if dbg is not None:
        with pr.nc.Block() as block:
            @block.gpsimd
            def _(g):
                g.dma_start(dbg["agt"].ap()[:, :], ag_t_out.ap()[:, :]
                            ).then_inc(pr.s_cc.h, 16)
                pr.s_cc.n += 16
                g.wait_ge(pr.s_cc.h, pr.s_cc.n)
    emit_oproj_block(pr, ag_t_out, wproj, x_in, x_out, f"{pfx}_opj")


def build_l1_attn(debug=False):
    pr = Prog()
    nc = pr.nc
    x_in = nc.dram_tensor("x_in", [D, TOWN], F32, kind="ExternalInput").ap()
    wqkv = nc.dram_tensor("l1_wqkv", [D, 3 * NHC * HD], F32,
                          kind="ExternalInput").ap()
    wproj = nc.dram_tensor("l1_wproj", [D, D], F32, kind="ExternalInput").ap()
    x_out = nc.dram_tensor("x_out", [D, TOWN], F32, kind="ExternalOutput").ap()
    dbg = None
    if debug:
        dbg = {"A": nc.dram_tensor("dbg_A", [64, NHC * 64], F32, kind="ExternalOutput"),
               "Vi": nc.dram_tensor("dbg_Vi", [64, NHC * 64], F32, kind="ExternalOutput"),
               "Zn": nc.dram_tensor("dbg_Zn", [64, NHC * 64], F32, kind="ExternalOutput"),
               "W2": nc.dram_tensor("dbg_W2", [64, NHC * 65], BF16, kind="ExternalOutput"),
               "klT": nc.dram_tensor("dbg_klT", [P, NHP * NL], BF16, kind="ExternalOutput"),
               "qlT": nc.dram_tensor("dbg_qlT", [P, NHP * NL], BF16, kind="ExternalOutput"),
               "agt": nc.dram_tensor("dbg_agt", [2 * NHC * HD, T], BF16, kind="ExternalOutput")}
    pr.load_consts(rope=False)
    emit_slot_l1_attn(pr, x_in, x_out, wqkv, wproj, dbg=dbg)
    pr.close()
    return pr.nc


def stage_l1_attn(inputs, x_maps):
    wa = np.asarray(inputs["L1_attn_W_attn"], np.float32)
    wp = np.ascontiguousarray(np.asarray(inputs["L1_attn_W_proj"], np.float32))
    maps = []
    for c in range(8):
        p = c % 2
        cols = []
        for part in range(3):
            lo = part * D + p * NHC * HD
            cols.append(wa[:, lo:lo + NHC * HD])
        wslice = np.ascontiguousarray(np.concatenate(cols, axis=1))
        maps.append({"x_in": x_maps[c], "l1_wqkv": wslice, "l1_wproj": wp,
                     "c_toff": np.array([[(c % 2) * (TOWN // 512)]], np.int32)})
    return maps



def emit_rms_to_dram(pr, src_dram, dst_dram, tag, src_f32=True):
    nc = pr.nc
    NCH = TOWN // 512
    with ExitStack() as es:
        sb = lambda n, s, d: es.enter_context(nc.sbuf_tensor(f"{tag}_{n}", s, d))
        xb = sb("xb", [P, 2 * DC * 512], BF16)
        xq = sb("xq", [P, DC * 512], BF16)
        rr = sb("rr", [P, 512], F32)
        ab = sb("ab", [P, 2 * DC * 512], BF16)
        ps_s = es.enter_context(nc.psum_tensor(f"{tag}_ps", [P, 512], F32))

        x3 = src_dram.rearrange("(c p) t -> p c t", p=P)
        a3 = dst_dram.rearrange("(c p) t -> p c t", p=P)
        xbv = xb[:].rearrange("p (b c t) -> p b c t", b=2, c=DC)
        abv = ab[:].rearrange("p (b c t) -> p b c t", b=2, c=DC)

        ph = Phase(pr, tag)
        ld = {}; sq = {}; smm = {}; comb = {}; rstd = {}; norm = {}; st = {}
        for i in range(NCH):
            bf = i % 2

            @ph.op("gpsimd", deps=(norm.get(i - 2),), inc=16, name=f"ld{i}")
            def _(e, i=i, bf=bf):
                return e.dma_start(xbv[:, bf], x3[:, :, ts(i, 512)])
            ld[i] = ph.ops[-1]

            @ph.op("vector", deps=(ld[i], smm.get(i - 1)), name=f"sq{i}")
            def _(e, bf=bf):
                return e.tensor_mul(xq[:], xbv[:, bf].rearrange("p c t -> p (c t)"),
                                    xbv[:, bf].rearrange("p c t -> p (c t)"))
            sq[i] = ph.ops[-1]

            @ph.op("tensor", deps=(sq[i], rstd.get(i - 1)), name=f"smm{i}")
            def _(e):
                for dc in range(DC):
                    ins = e.matmul(ps_s[:], pr.ones_bf[:], xq[:, ts(dc, 512)],
                                   start=(dc == 0), stop=(dc == DC - 1),
                                   skip_group_check=True)
                return ins
            smm[i] = ph.ops[-1]

            @ph.op("scalar", deps=(smm[i],), name=f"rstd{i}")
            def _(e):
                e.activation(rr[:], ps_s[:], AF.Ln, bias=pr.eps_rms,
                             scale=1.0 / D)
                return e.activation(rr[:], rr[:], AF.Exp, scale=-0.5)
            rstd[i] = ph.ops[-1]

            @ph.op("vector", deps=(rstd[i], st.get(i - 2)), name=f"norm{i}")
            def _(e, bf=bf):
                for dc in range(DC):
                    ins = e.tensor_mul(abv[:, bf, dc], xbv[:, bf, dc], rr[:])
                return ins
            norm[i] = ph.ops[-1]

            @ph.op("gpsimd", deps=(norm[i],), inc=16, name=f"st{i}")
            def _(e, i=i, bf=bf):
                return e.dma_start(a3[:, :, ts(i, 512)], abv[:, bf])
            st[i] = ph.ops[-1]

        @ph.op("gpsimd", deps=(st[NCH - 1],), name="fin")
        def _(e):
            return e.memset(rr[0:1, 0:1], 0.0)
        ph.emit()


def _rope_l1(nc, e, src, scratch, out_bf, cosT, sinT, tt, nhead):
    n = nhead
    x1 = bap(src, 0, [[512, P], [64, n], [1, 32]])
    x2 = bap(src, 32, [[512, P], [64, n], [1, 32]])
    t1 = bap(scratch, 0, [[512, P], [32, n], [1, 32]])
    t2 = bap(scratch, n * 32, [[512, P], [32, n], [1, 32]])
    oe = bap(out_bf, 0, [[n * 64, P], [64, n], [1, 32]])
    oo = bap(out_bf, 32, [[n * 64, P], [64, n], [1, 32]])
    cb = bap(cosT, tt * 32, [[1024, P], [0, n], [1, 32]])
    sn = bap(sinT, tt * 32, [[1024, P], [0, n], [1, 32]])
    e.tensor_mul(t1, x1, cb)
    e.tensor_mul(t2, x2, sn)
    e.tensor_add(oe, t1, t2)
    e.tensor_mul(t1, x1, sn)
    e.tensor_mul(t2, x2, cb)
    return e.tensor_sub(oo, t2, t1)


def emit_gqa_qkv(pr, ag_a_out, wq_sb, wkv_sb, qg_rep, kg_rep, QT, KTg, VTg,
                 tag):
    nc = pr.nc
    with ExitStack() as es:
        sb = lambda n, s, d: es.enter_context(nc.sbuf_tensor(f"{tag}_{n}", s, d))
        at = sb("at", [P, 2 * DC * P], BF16)
        qf = sb("qf", [P, 512], F32)
        sf = sb("sf", [P, 512], F32)
        qms = sb("qms", [P, NHC + 2], F32)
        qrob = sb("qrob", [P, 512], BF16)
        krob = sb("krob", [P, P], BF16)
        ps_q = es.enter_context(nc.psum_tensor(f"{tag}_pq", [P, 512], F32))
        ps_kv = es.enter_context(nc.psum_tensor(f"{tag}_pkv", [P, 256], F32))
        ps_tq = es.enter_context(nc.psum_tensor(f"{tag}_ptq", [P, 8 * P], BF16))
        ps_tk = es.enter_context(nc.psum_tensor(f"{tag}_ptk", [P, 8 * P], BF16))

        agv = ag_a_out.ap().rearrange("(h c p) t -> p h c t", p=P, c=DC)
        atv = at[:].rearrange("p (b c q) -> p b c q", b=2, c=DC)
        QTv = QT[:].rearrange("p (hp t) -> p hp t", hp=NHP)
        VTv = VTg[:].rearrange("p (tt h e) -> p tt h e", tt=NTT, h=2)
        wqv = wq_sb[:].rearrange("p (c n) -> p c n", c=DC)
        wkvv = wkv_sb[:].rearrange("p (c n) -> p c n", c=DC)

        ph = Phase(pr, tag)
        lda = {}; mmq = {}; mmkv = {}; qev = {}; kev = {}
        trq = {}; trk = {}; evq = {}; evk = {}; evv = {}; roq = {}; rok = {}
        for tt in range(NTT):
            bf = tt % 2
            h = (tt * P) // TOWN
            co = (tt * P) % TOWN

            @ph.op("sync", deps=(mmkv.get(tt - 2),), inc=16, name=f"lda{tt}")
            def _(e, h=h, co=co, bf=bf):
                return e.dma_start(atv[:, bf], agv[:, h, :, co:co + P])
            lda[tt] = ph.ops[-1]

            @ph.op("tensor", deps=(lda[tt], qev.get(tt - 1)), name=f"mmq{tt}")
            def _(e, bf=bf):
                for dc in range(DC):
                    ins = e.matmul(ps_q[:], atv[:, bf, dc], wqv[:, dc],
                                   start=(dc == 0), stop=(dc == DC - 1))
                return ins
            mmq[tt] = ph.ops[-1]

            @ph.op("tensor", deps=(kev.get(tt - 1), evv.get(tt - 1)),
                   name=f"mmkv{tt}")
            def _(e, bf=bf):
                for dc in range(DC):
                    ins = e.matmul(ps_kv[:], atv[:, bf, dc], wkvv[:, dc],
                                   start=(dc == 0), stop=(dc == DC - 1))
                return ins
            mmkv[tt] = ph.ops[-1]

            @ph.op("vector", deps=(mmq[tt],), name=f"qev{tt}")
            def _(e):
                return e.tensor_copy(qf[:], ps_q[:])
            qev[tt] = ph.ops[-1]

            @ph.op("vector", deps=(qev[tt],), name=f"qsq{tt}")
            def _(e):
                return e.tensor_mul(sf[:], qf[:], qf[:])
            o = ph.ops[-1]

            @ph.op("vector", deps=(o,), name=f"qrd{tt}")
            def _(e):
                return e.tensor_reduce(qms[:, 0:NHC],
                                       sf[:].rearrange("p (h e) -> p h e", h=NHC),
                                       mybir.AxisListType.X, ALU.add)
            o = ph.ops[-1]

            @ph.op("scalar", deps=(o,), name=f"qrs{tt}")
            def _(e):
                e.activation(qms[:, 0:NHC], qms[:, 0:NHC], AF.Ln,
                             bias=pr.eps_rms, scale=1.0 / HD)
                return e.activation(qms[:, 0:NHC], qms[:, 0:NHC], AF.Exp,
                                    scale=-0.5)
            o = ph.ops[-1]

            @ph.op("vector", deps=(o,), name=f"qrg{tt}")
            def _(e):
                return e.tensor_mul(qms[:, 0:NHC], qms[:, 0:NHC], qg_rep[:])
            o = ph.ops[-1]

            @ph.op("vector", deps=(o,), name=f"qap{tt}")
            def _(e):
                rbc = bap(qms, 0, [[NHC + 2, P], [1, NHC], [0, HD]])
                q3 = qf[:].rearrange("p (h e) -> p h e", h=NHC)
                return e.tensor_mul(q3, q3, rbc)
            o = ph.ops[-1]

            @ph.op("vector", deps=(o,), name=f"roq{tt}")
            def _(e, tt=tt):
                return _rope_l1(nc, e, qf, sf, qrob, pr.cos, pr.sin, tt, NHC)
            roq[tt] = ph.ops[-1]

            @ph.op("tensor", deps=(roq[tt], evq.get(tt - 1)), name=f"trq{tt}")
            def _(e):
                for j in range(NHP):
                    e.transpose(ps_tq[0:64, ts(j, P)],
                                qrob[:, j * 64:(j + 1) * 64],
                                pr.ident[:], tile_position=(0, 0))
                    ins = e.transpose(ps_tq[64:128, ts(j, P)],
                                      qrob[:, (j + 4) * 64:(j + 5) * 64],
                                      pr.ident[:], tile_position=(0, 64))
                return ins
            trq[tt] = ph.ops[-1]

            @ph.op("scalar", deps=(trq[tt],), name=f"evq{tt}")
            def _(e, tt=tt):
                for j in range(NHP):
                    ins = e.copy(QTv[:, j, ts(tt, P)], ps_tq[:, ts(j, P)])
                return ins
            evq[tt] = ph.ops[-1]

            @ph.op("vector", deps=(mmkv[tt], roq[tt]), name=f"kev{tt}")
            def _(e):
                return e.tensor_copy(qf[:, 0:128], ps_kv[:, 0:128])
            kev[tt] = ph.ops[-1]

            @ph.op("vector", deps=(kev[tt],), name=f"ksq{tt}")
            def _(e):
                return e.tensor_mul(sf[:, 0:128], qf[:, 0:128], qf[:, 0:128])
            o = ph.ops[-1]

            @ph.op("vector", deps=(o,), name=f"krd{tt}")
            def _(e):
                return e.tensor_reduce(qms[:, NHC:NHC + 2],
                                       bap(sf, 0, [[512, P], [64, 2], [1, HD]]),
                                       mybir.AxisListType.X, ALU.add)
            o = ph.ops[-1]

            @ph.op("scalar", deps=(o,), name=f"krs{tt}")
            def _(e):
                e.activation(qms[:, NHC:NHC + 2], qms[:, NHC:NHC + 2], AF.Ln,
                             bias=pr.eps_rms, scale=1.0 / HD)
                return e.activation(qms[:, NHC:NHC + 2], qms[:, NHC:NHC + 2],
                                    AF.Exp, scale=-0.5)
            o = ph.ops[-1]

            @ph.op("vector", deps=(o,), name=f"krg{tt}")
            def _(e):
                return e.tensor_mul(qms[:, NHC:NHC + 2], qms[:, NHC:NHC + 2],
                                    kg_rep[:])
            o = ph.ops[-1]

            @ph.op("vector", deps=(o,), name=f"kap{tt}")
            def _(e):
                rbc = bap(qms, NHC, [[NHC + 2, P], [1, 2], [0, HD]])
                k3 = bap(qf, 0, [[512, P], [64, 2], [1, HD]])
                return e.tensor_mul(k3, k3, rbc)
            o = ph.ops[-1]

            @ph.op("vector", deps=(o,), name=f"rok{tt}")
            def _(e, tt=tt):
                return _rope_l1(nc, e, qf, sf, krob, pr.cos, pr.sin, tt, 2)
            rok[tt] = ph.ops[-1]

            @ph.op("tensor", deps=(rok[tt], evk.get(tt - 1)), name=f"trk{tt}")
            def _(e):
                return e.transpose(ps_tk[:, 0:P], krob[:], pr.ident[:])
            trk[tt] = ph.ops[-1]

            @ph.op("scalar", deps=(trk[tt],), name=f"evk{tt}")
            def _(e, tt=tt):
                return e.copy(KTg[:, ts(tt, P)], ps_tk[:, 0:P])
            evk[tt] = ph.ops[-1]

            @ph.op("vector", deps=(mmkv[tt],), name=f"evv{tt}")
            def _(e, tt=tt):
                return e.tensor_copy(
                    VTv[:, tt].rearrange("p h e -> p (h e)"), ps_kv[:, 128:256])
            evv[tt] = ph.ops[-1]
        ph.emit()


def emit_oproj_residual_plain(pr, ag_t_out, wo_sb, g_dram, h1_dram, tag):
    nc = pr.nc
    NST = TOWN // 512
    with ExitStack() as es:
        sb = lambda n, s, d: es.enter_context(nc.sbuf_tensor(f"{tag}_{n}", s, d))
        ats = sb("ats", [P, 2 * DC * 512], BF16)
        gf = sb("gf", [P, 2 * DC * 512], F32)
        ho = sb("ho", [P, 2 * DC * 512], F32)
        ps_y = es.enter_context(nc.psum_tensor(f"{tag}_py", [P, 2 * 512], F32))

        agv = ag_t_out.ap().rearrange("(c p) t -> p c t", p=P)
        g3 = g_dram.rearrange("(c p) t -> p c t", p=P)
        o3 = h1_dram.rearrange("(c p) t -> p c t", p=P)
        atsv = ats[:].rearrange("p (b c t) -> p b c t", b=2, c=DC)
        gfv = gf[:].rearrange("p (b c t) -> p b c t", b=2, c=DC)
        hov = ho[:].rearrange("p (b c t) -> p b c t", b=2, c=DC)
        wov = wo_sb[:].rearrange("p (c n) -> p c n", c=DC)

        ph = Phase(pr, tag)
        lda = {}; ldg = {}; mmy = {}; addo = {}; sto = {}
        for i in range(NST):
            bf = i % 2

            @ph.op("sync", deps=(mmy.get((i - 2, DC - 1)),), inc=16,
                   name=f"lda{i}")
            def _(e, i=i, bf=bf):
                with e.register(f"{tag}r{i}") as r:
                    e.load(r, pr.toff_sb[0:1, 0:1])
                    e.reg_add(r, r, i)
                    e.reg_mul(r, r, 512)
                    v = AnnotatedLowered(r, 0, T - 512).to_runtime()
                    return e.dma_start(atsv[:, bf],
                                       agv[:, :, bass.DynSlice(v, 512)])
            lda[i] = ph.ops[-1]

            @ph.op("sync", deps=(addo.get((i - 2, DC - 1)),), inc=16,
                   name=f"ldg{i}")
            def _(e, i=i, bf=bf):
                return e.dma_start(gfv[:, bf], g3[:, :, ts(i, 512)])
            ldg[i] = ph.ops[-1]

            for oc in range(DC):
                yb2 = oc % 2

                @ph.op("tensor",
                       deps=(lda[i], addo.get((i, oc - 2)),
                             addo.get((i - 1, DC - 1)) if oc < 2 else None),
                       name=f"mmy{i}_{oc}")
                def _(e, i=i, oc=oc, bf=bf, yb2=yb2):
                    for ec in range(DC):
                        ins = e.matmul(ps_y[:, ts(yb2, 512)],
                                       wov[:, ec, ts(oc, P)], atsv[:, bf, ec],
                                       start=(ec == 0), stop=(ec == DC - 1),
                                       skip_group_check=True)
                    return ins
                mmy[(i, oc)] = ph.ops[-1]

                @ph.op("vector", deps=(mmy[(i, oc)], ldg[i], sto.get(i - 2)),
                       name=f"add{i}_{oc}")
                def _(e, i=i, oc=oc, bf=bf, yb2=yb2):
                    return e.tensor_add(hov[:, bf, oc], gfv[:, bf, oc],
                                        ps_y[:, ts(yb2, 512)])
                addo[(i, oc)] = ph.ops[-1]

            @ph.op("gpsimd", deps=(addo[(i, DC - 1)],), inc=16, name=f"sto{i}")
            def _(e, i=i, bf=bf):
                return e.dma_start(o3[:, :, ts(i, 512)], hov[:, bf])
            sto[i] = ph.ops[-1]

        @ph.op("gpsimd", deps=(sto[NST - 1],), name="fin")
        def _(e):
            return e.memset(gf[0:1, 0:1], 0.0)
        ph.emit()


def emit_gqa(pr, g_dram, h1_dram, wq, wkv, wo, gains, _unused, tag):
    nc = pr.nc
    h0_dram = nc.dram_tensor(f"{tag}_h0", [D, TOWN], BF16)
    ag_a_out = nc.dram_tensor(f"{tag}_aga", [2 * D, TOWN], BF16)
    ag_t_in = nc.dram_tensor(f"{tag}_agt", [NHC * HD, T], BF16)
    ag_t_out = nc.dram_tensor(f"{tag}_agto", [2 * NHC * HD, T], BF16)

    emit_rms_to_dram(pr, g_dram.ap(), h0_dram.ap(), f"{tag}_rms")
    pr.pair_allgather(h0_dram, ag_a_out)
    with ExitStack() as es:
        wq_sb = es.enter_context(nc.sbuf_tensor(f"{tag}_wq", [P, DC * NHC * HD], BF16))
        wkv_sb = es.enter_context(nc.sbuf_tensor(f"{tag}_wkv", [P, DC * 256], BF16))
        qg_rep = es.enter_context(nc.sbuf_tensor(f"{tag}_qg", [P, NHC], F32))
        kg_rep = es.enter_context(nc.sbuf_tensor(f"{tag}_kg", [P, 2], F32))
        gl = es.enter_context(nc.sbuf_tensor(f"{tag}_gl", [1, NHC + 2], F32))
        QT = es.enter_context(nc.sbuf_tensor(f"{tag}_QT", [P, NHP * T], BF16))
        KTg = es.enter_context(nc.sbuf_tensor(f"{tag}_KT", [P, T], BF16))
        VTg = es.enter_context(nc.sbuf_tensor(f"{tag}_VT", [P, NTT * 2 * HD], BF16))

        with nc.psum_tensor(f"{tag}_pg", [P, NHC + 2], F32) as ps_g:
            phw = Phase(pr, "gqw")
            w1 = load_weight_dmajor(pr, phw, wq, wq_sb, "wq")
            w2 = load_weight_dmajor(pr, phw, wkv, wkv_sb, "wkv")

            @phw.op("sync", inc=16, name="gains")
            def _(e):
                return e.dma_start(gl[0:1, :], gains[:, :])
            w3 = phw.ops[-1]

            @phw.op("tensor", deps=(w3,), name="gbc")
            def _(e):
                return e.matmul(ps_g[:, :], pr.ones_f32r[0:1, :], gl[0:1, :],
                                start=True, stop=True)
            w4 = phw.ops[-1]

            @phw.op("vector", deps=(w4,), name="gev")
            def _(e):
                e.tensor_copy(qg_rep[:], ps_g[:, 0:NHC])
                return e.tensor_copy(kg_rep[:], ps_g[:, NHC:NHC + 2])
            w5 = phw.ops[-1]

            @phw.op("gpsimd", deps=(w1, w2, w5), name="wfin")
            def _(e):
                return e.memset(gl[0:1, 0:1], 0.0)
            phw.emit()

        emit_gqa_qkv(pr, ag_a_out, wq_sb, wkv_sb, qg_rep, kg_rep,
                     QT, KTg, VTg, f"{tag}_qkv")
        KTgv = KTg[:]
        VTvv = VTg[:].rearrange("p (tt h e) -> p tt h e", tt=NTT, h=2)
        emit_causal_attention(
            pr, QT, QT, VTg, ag_t_in, scale=1.0 / 8.0, tag=f"{tag}_att",
            k_fn=lambda KTv, hp, half, kt: KTgv[half * 64:(half + 1) * 64,
                                                ts(kt, P)],
            v_fn=lambda VTv, hp, half, kt: VTvv[:, kt, half, :])
    pr.pair_allgather(ag_t_in, ag_t_out)
    emit_oproj_plain_block(pr, ag_t_out, wo, g_dram.ap(), h1_dram.ap(),
                           f"{tag}_opj")


def emit_oproj_plain_block(pr, ag_t_out, wo, g_dram, h1_dram, tag):
    nc = pr.nc
    with ExitStack() as es:
        wo_sb = es.enter_context(nc.sbuf_tensor(f"{tag}_wo", [P, DC * D], BF16))
        phw = Phase(pr, f"{tag}_wl")
        wo_op = load_weight_dmajor(pr, phw, wo, wo_sb, "wo")

        @phw.op("gpsimd", deps=(wo_op,), name="wofin")
        def _(e):
            return e.memset(wo_sb[0:1, 0:1], 0.0)
        phw.emit()
        emit_oproj_residual_plain(pr, ag_t_out, wo_sb, g_dram, h1_dram, tag)



def build_l1_ffn_gqa():
    pr = Prog()
    nc = pr.nc
    x_in = nc.dram_tensor("x_in", [D, TOWN], F32, kind="ExternalInput").ap()
    wq = nc.dram_tensor("gq_wq", [D, NHC * HD], F32, kind="ExternalInput").ap()
    wkv = nc.dram_tensor("gq_wkv", [D, 256], F32, kind="ExternalInput").ap()
    wo = nc.dram_tensor("gq_wo", [D, D], F32, kind="ExternalInput").ap()
    gains = nc.dram_tensor("gq_gains", [1, NHC + 2], F32, kind="ExternalInput").ap()
    h1_out = nc.dram_tensor("h1_out", [D, TOWN], F32, kind="ExternalOutput")
    g_dram = nc.dram_tensor("g_dram", [D, TOWN], F32)

    pr.load_consts(rope=True)
    emit_ln_to_dram(pr, x_in, g_dram.ap(), "ln7", out_dtype=F32)
    emit_gqa(pr, g_dram, h1_out, wq, wkv, wo, gains, None, "gqa")
    pr.close()
    return pr.nc


def stage_l1_ffn_gqa(inputs, x_maps):
    wq = np.asarray(inputs["L1_ffn_wq"], np.float32)
    wk = np.asarray(inputs["L1_ffn_wk"], np.float32)
    wv = np.asarray(inputs["L1_ffn_wv"], np.float32)
    wo = np.asarray(inputs["L1_ffn_wo"], np.float32)
    qg = np.asarray(inputs["L1_ffn_q_gain"], np.float32)
    kg = np.asarray(inputs["L1_ffn_k_gain"], np.float32)
    maps = []
    r = np.arange(D)
    pglob = r // 512
    j = (r % 512) // 128
    half = (r % 128) // 64
    dv = r % 64
    ghead = pglob * NHC + j + 4 * half
    perm = ghead * 64 + dv
    woT = np.ascontiguousarray(wo.T)
    woT_perm = np.ascontiguousarray(woT[perm])
    for c in range(8):
        p = c % 2
        heads = range(p * NHC, (p + 1) * NHC)
        rows_q = np.concatenate([np.arange(h * HD, (h + 1) * HD) for h in heads])
        kvrows = np.arange(2 * p * HD, (2 * p + 2) * HD)
        wkvT = np.concatenate([wk[kvrows].T, wv[kvrows].T], axis=1)
        g = np.concatenate([qg[p * NHC:(p + 1) * NHC], kg[2 * p:2 * p + 2]])
        maps.append({
            "x_in": x_maps[c],
            "c_toff": np.array([[(c % 2) * (TOWN // 512)]], np.int32),
            "gq_wq": np.ascontiguousarray(wq[rows_q].T),
            "gq_wkv": np.ascontiguousarray(wkvT),
            "gq_wo": woT_perm,
            "gq_gains": np.ascontiguousarray(g[None, :]),
        })
    return maps



BANK = 8


def emit_top2(pr, logits_dram, c_rep, tag):
    nc = pr.nc
    with ExitStack() as es:
        sb = lambda n, s, d: es.enter_context(nc.sbuf_tensor(f"{tag}_{n}", s, d))
        lg = sb("lg", [1, BANK], F32)
        is1 = sb("is1", [1, BANK], F32)
        is2 = sb("is2", [1, BANK], F32)
        t2 = sb("t2", [1, BANK], F32)
        ex = sb("ex", [1, BANK], F32)
        m1 = sb("m1", [1, 2], F32)
        m2 = sb("m2", [1, 2], F32)
        cs = sb("cs", [1, BANK], F32)
        ps = es.enter_context(nc.psum_tensor(f"{tag}_ps", [P, BANK], F32))
        ph = Phase(pr, tag)

        def step(eng, deps, name, fn):
            o = ph.op(eng, hdeps=deps, name=name)(fn)
            return o

        o0 = step("sync", (), "ld",
                  lambda e: e.dma_start(lg[0:1, :], logits_dram[:, :]))
        o0.inc = 16
        o1 = step("vector", (o0,), "m1",
                  lambda e: e.tensor_reduce(m1[0:1, 0:1], lg[0:1, :],
                                            mybir.AxisListType.X, ALU.max))
        o2 = step("vector", (o1,), "is1",
                  lambda e: e.tensor_tensor(is1[0:1, :], lg[0:1, :],
                                            m1[0:1, 0:1].broadcast_to([1, BANK]),
                                            ALU.is_ge))
        o3 = step("vector", (o2,), "big",
                  lambda e: e.tensor_scalar_mul(t2[0:1, :], is1[0:1, :], 1e30))
        o4 = step("vector", (o3,), "l2",
                  lambda e: e.tensor_sub(t2[0:1, :], lg[0:1, :], t2[0:1, :]))
        o5 = step("vector", (o4,), "m2",
                  lambda e: e.tensor_reduce(m2[0:1, 0:1], t2[0:1, :],
                                            mybir.AxisListType.X, ALU.max))
        o6 = step("vector", (o5,), "is2",
                  lambda e: e.tensor_tensor(is2[0:1, :], t2[0:1, :],
                                            m2[0:1, 0:1].broadcast_to([1, BANK]),
                                            ALU.is_ge))
        o7 = step("vector", (o6,), "msk",
                  lambda e: e.tensor_add(is1[0:1, :], is1[0:1, :], is2[0:1, :]))
        o8 = step("vector", (o1,), "negm1",
                  lambda e: e.tensor_scalar_mul(m1[0:1, 1:2], m1[0:1, 0:1], -1.0))
        o9 = step("scalar", (o8, o0), "exp",
                  lambda e: e.activation(ex[0:1, :], lg[0:1, :], AF.Exp,
                                         bias=m1[0:1, 1:2]))
        o10 = step("vector", (o9, o7), "emask",
                   lambda e: e.tensor_mul(ex[0:1, :], ex[0:1, :], is1[0:1, :]))
        o11 = step("vector", (o10,), "zsum",
                   lambda e: e.tensor_reduce(m2[0:1, 1:2], ex[0:1, :],
                                             mybir.AxisListType.X, ALU.add))
        o12 = step("vector", (o11,), "zrec",
                   lambda e: e.reciprocal(m2[0:1, 1:2], m2[0:1, 1:2]))
        o13 = step("vector", (o12,), "cvec",
                   lambda e: e.tensor_mul(cs[0:1, :], ex[0:1, :],
                                          m2[0:1, 1:2].broadcast_to([1, BANK])))
        o14 = step("tensor", (o13,), "cbc",
                   lambda e: e.matmul(ps[:, :], pr.ones_f32r[0:1, :],
                                      cs[0:1, :], start=True, stop=True))
        o15 = step("vector", (o14,), "crep",
                   lambda e: e.tensor_copy(c_rep[:], ps[:, :]))
        step("gpsimd", (o15,), "fin",
             lambda e: e.memset(lg[0:1, 0:1], 0.0))
        ph.emit()


def emit_compose(pr, c_rep, mix_dram, s1T_dram, s2_dram, msh_dram, nin, ngl,
                 jw, tag):
    nc = pr.nc
    na = nin * 64
    nmchunk = (ngl * 64) // P
    g2n = nin // 2
    nbc = na // P
    with ExitStack() as es:
        sb = lambda n, s, d: es.enter_context(nc.sbuf_tensor(f"{tag}_{n}", s, d))
        mix_sb = sb("mix", [P, BANK * nmchunk * na], BF16)
        s1T = sb("s1T", [P, BANK * g2n * 64], BF16)
        s1s = sb("s1s", [P, BANK * g2n * 64], BF16)
        s2 = sb("s2", [P, BANK * nmchunk * 64], BF16)
        vv = sb("vv", [P, nbc * BANK * jw], BF16)
        mo = sb("mo", [P, 512], BF16)
        ps_d = es.enter_context(nc.psum_tensor(f"{tag}_pd", [P, 512], F32))
        ps_2 = es.enter_context(nc.psum_tensor(f"{tag}_p2", [P, 512], F32))

        mix_pitch = BANK * nmchunk * na
        vv_pitch = nbc * BANK * jw
        s1v = s1s[:].rearrange("p (i g c) -> p i g c", i=BANK, g=g2n)
        s2v = s2[:].rearrange("p (i c n) -> p i c n", i=BANK, c=nmchunk)
        mshv = msh_dram.ap().rearrange("(c p) j -> p c j", p=P)

        ph = Phase(pr, tag)

        @ph.op("gpsimd", inc=16, name="ldmix")
        def _(e):
            return e.dma_start(
                mix_sb[:].rearrange("p (i c n) -> p i c n", i=BANK, c=nmchunk),
                mix_dram.rearrange("i (c p) n -> p i c n", p=P))
        ldm = ph.ops[-1]

        @ph.op("gpsimd", inc=16, name="lds1")
        def _(e):
            return e.dma_start(s1T[:], s1T_dram)
        ld1 = ph.ops[-1]

        @ph.op("gpsimd", inc=16, name="lds2")
        def _(e):
            return e.dma_start(s2[:], s2_dram)
        ld2 = ph.ops[-1]

        @ph.op("vector", deps=(ld1,), name="s1scale")
        def _(e):
            cb = bap(c_rep, 0, [[BANK, P], [1, BANK], [0, g2n], [0, 64]])
            return e.tensor_tensor(
                s1s[:].rearrange("p (i g c) -> p i g c", i=BANK, g=g2n),
                s1T[:].rearrange("p (i g c) -> p i g c", i=BANK, g=g2n),
                cb, ALU.mult)
        s1sc = ph.ops[-1]

        prev = None
        for i in range(BANK):
            for G in range(ngl):
                gg, Gh = G // 2, (G % 2) * 64
                for fill in range(nin // 16) or [0]:
                    pass
                nfill = max(1, nin // 16)
                for fill in range(nfill):
                    g0 = fill * 16
                    gcnt = min(16, nin - g0)

                    @ph.op("tensor", deps=(ldm, ld2, prev),
                           name=f"vd{i}_{G}_{fill}")
                    def _(e, i=i, gg=gg, Gh=Gh, g0=g0, gcnt=gcnt):
                        for gi in range(gcnt):
                            g = g0 + gi
                            off = ((i * nmchunk + gg) * na) + (g + i) % nin \
                                + Gh * mix_pitch
                            lhsT = bap(mix_sb, off,
                                       [[mix_pitch, 64], [nin, 64]])
                            ins = e.matmul(
                                ps_d[(g % 2) * 64:(g % 2) * 64 + 64,
                                     ((g // 2) % 8) * 64:((g // 2) % 8 + 1) * 64],
                                lhsT, s2v[Gh:Gh + 64, i, gg],
                                start=True, stop=True,
                                tile_position=(Gh, (g % 2) * 64),
                                skip_group_check=True)
                        return ins
                    mmop = ph.ops[-1]

                    @ph.op("vector", deps=(mmop,), name=f"ve{i}_{G}_{fill}")
                    def _(e, i=i, G=G, fill=fill, gcnt=gcnt):
                        nslot = gcnt // 2
                        src = bap(ps_d, 0, [[512, P], [64, nslot], [1, 64]])
                        dst = bap(vv,
                                  (fill * 8) * (BANK * jw) + i * jw + G * 64,
                                  [[vv_pitch, P], [BANK * jw, nslot], [1, 64]])
                        return e.tensor_copy(dst, src)
                    prev = ph.ops[-1]
        last_ev = prev

        prevst = None
        for gp in range(nin // 2):
            first_deps = (s1sc, last_ev) if gp == 0 else (prevst,)
            for half in range(2):
                g = gp * 2 + half
                Gh = half * 64

                @ph.op("tensor",
                       deps=first_deps if half == 0 else (),
                       name=f"m2_{g}")
                def _(e, g=g, gp=gp, Gh=Gh):
                    for i in range(BANK):
                        rhs = bap(vv, gp * (BANK * jw) + i * jw + Gh * vv_pitch,
                                  [[vv_pitch, 64], [1, jw]])
                        ins = e.matmul(
                            ps_2[Gh:Gh + 64, 0:jw],
                            s1v[Gh:Gh + 64, i, gp], rhs,
                            start=(i == 0), stop=(i == BANK - 1),
                            tile_position=(Gh, Gh),
                            skip_group_check=True)
                    return ins
                mm = ph.ops[-1]

            @ph.op("vector", deps=(mm,), name=f"me{gp}")
            def _(e):
                return e.tensor_copy(mo[:, 0:jw], ps_2[:, 0:jw])
            meo = ph.ops[-1]

            @ph.op("gpsimd", deps=(meo,), inc=16, name=f"ms{gp}")
            def _(e, gp=gp):
                return e.dma_start(mshv[:, gp, 0:jw], mo[:, 0:jw])
            prevst = ph.ops[-1]

        @ph.op("gpsimd", deps=(prevst,), name="fin")
        def _(e):
            return e.memset(mo[0:1, 0:1], 0.0)
        ph.emit()


def _part_slice(ap, base, count):
    dims = [list(d) for d in ap.ap]
    step = dims[0][0]
    dims[0] = [step, count]
    return bass.AP(ap.tensor, ap.offset + base * step, dims)



def emit_load_mfull(pr, ag_out_dram, w_sb, ncols, tag):
    nc = pr.nc
    jw = ncols // 8
    ph = Phase(pr, tag)
    na8, _ = ag_out_dram.ap().shape
    na = na8 // 8
    nbc = na // P
    ops = []
    for r in range(8):
        @ph.op("sync", inc=16, name=f"ld{r}")
        def _(e, r=r):
            src = ag_out_dram.ap()[r * na:(r + 1) * na, :].rearrange(
                "(c p) j -> p c j", p=P)
            dst = bap(w_sb, r * jw, [[nbc * ncols, P], [ncols, nbc], [1, jw]])
            return e.dma_start(dst, src)
        ops.append(ph.ops[-1])

    @ph.op("gpsimd", deps=tuple(ops), name="fin")
    def _(e):
        return e.memset(w_sb[0:1, 0:1], 0.0)
    ph.emit()


def emit_fc_stream(pr, wfc_sb, h2_dram, h3_dram, tag):
    nc = pr.nc
    NST = TOWN // 512
    NOC = 4096 // P
    with ExitStack() as es:
        sb = lambda n, s, d: es.enter_context(nc.sbuf_tensor(f"{tag}_{n}", s, d))
        h2s = sb("h2s", [P, 2 * DC * 512], BF16)
        t1 = sb("t1", [P, 512], F32)
        h3b = sb("h3b", [P, 2 * 512], BF16)
        ps_y = es.enter_context(nc.psum_tensor(f"{tag}_py", [P, 2 * 512], F32))

        h23 = h2_dram.rearrange("(c p) t -> p c t", p=P)
        h33 = h3_dram.rearrange("(c p) t -> p c t", p=P)
        h2v = h2s[:].rearrange("p (b c t) -> p b c t", b=2, c=DC)
        wv = wfc_sb[:].rearrange("p (c n) -> p c n", c=DC)

        ph = Phase(pr, tag)
        lda = {}; mmy = {}; rel = {}; sto = {}
        for i in range(NST):
            bf = i % 2

            @ph.op("sync", deps=(mmy.get((i - 2, NOC - 1)),), inc=16,
                   name=f"lda{i}")
            def _(e, i=i, bf=bf):
                return e.dma_start(h2v[:, bf], h23[:, :, ts(i, 512)])
            lda[i] = ph.ops[-1]

            for oc in range(NOC):
                yb2 = oc % 2

                @ph.op("tensor",
                       deps=(lda[i], rel.get((i, oc - 2)),
                             rel.get((i - 1, NOC - 1)) if oc < 2 else None),
                       name=f"mmy{i}_{oc}")
                def _(e, i=i, oc=oc, bf=bf, yb2=yb2):
                    for ec in range(DC):
                        ins = e.matmul(ps_y[:, ts(yb2, 512)],
                                       wv[:, ec, ts(oc, P)], h2v[:, bf, ec],
                                       start=(ec == 0), stop=(ec == DC - 1),
                                       skip_group_check=True)
                    return ins
                mmy[(i, oc)] = ph.ops[-1]

                @ph.op("vector", deps=(mmy[(i, oc)], sto.get((i, oc - 2)),
                                       sto.get((i - 1, NOC - 1)) if oc < 2 else None),
                       name=f"rel{i}_{oc}")
                def _(e, oc=oc, yb2=yb2):
                    e.tensor_scalar_max(t1[:], ps_y[:, ts(yb2, 512)], 0.0)
                    return e.tensor_mul(h3b[:, ts(yb2, 512)], t1[:], t1[:])
                rel[(i, oc)] = ph.ops[-1]

                @ph.op("gpsimd", deps=(rel[(i, oc)],), inc=16,
                       name=f"sto{i}_{oc}")
                def _(e, i=i, oc=oc, yb2=yb2):
                    return e.dma_start(h33[:, oc, ts(i, 512)],
                                       h3b[:, ts(yb2, 512)])
                sto[(i, oc)] = ph.ops[-1]

        @ph.op("gpsimd", deps=(sto[(NST - 1, NOC - 1)],), name="fin")
        def _(e):
            return e.memset(t1[0:1, 0:1], 0.0)
        ph.emit()


def emit_pj_final(pr, wpj_sb, h3_dram, h1_dram, x_in, x_out, tag):
    nc = pr.nc
    NST = TOWN // 512
    NKC = 4096 // P
    with ExitStack() as es:
        sb = lambda n, s, d: es.enter_context(nc.sbuf_tensor(f"{tag}_{n}", s, d))
        h3s = sb("h3s", [P, NKC * 512], BF16)
        h1s = sb("h1s", [P, DC * 512], F32)
        xf = sb("xf", [P, DC * 512], F32)
        h4f = sb("h4f", [P, DC * 512], F32)
        yb = sb("yb", [P, DC * 512], BF16)
        yq = sb("yq", [P, DC * 512], BF16)
        mr = sb("mr", [P, 512], F32)
        rr = sb("rr", [P, 512], F32)
        xo = sb("xo", [P, DC * 512], F32)
        t1 = sb("t1", [P, 512], F32)
        ps_y = es.enter_context(nc.psum_tensor(f"{tag}_py", [P, 2 * 512], F32))
        ps_m = es.enter_context(nc.psum_tensor(f"{tag}_pm", [P, 512], F32))
        ps_s = es.enter_context(nc.psum_tensor(f"{tag}_ps", [P, 512], F32))

        h33 = h3_dram.rearrange("(c p) t -> p c t", p=P)
        h13 = h1_dram.rearrange("(c p) t -> p c t", p=P)
        x3 = x_in.rearrange("(c p) t -> p c t", p=P)
        o3 = x_out.rearrange("(c p) t -> p c t", p=P)
        h3v = h3s[:].rearrange("p (c t) -> p c t", c=NKC)
        h1v = h1s[:].rearrange("p (c t) -> p c t", c=DC)
        xfv = xf[:].rearrange("p (c t) -> p c t", c=DC)
        xov = xo[:].rearrange("p (c t) -> p c t", c=DC)
        wv = wpj_sb[:].rearrange("p (c n) -> p c n", c=NKC)

        ph = Phase(pr, tag)
        ldh3 = {}; ldh1 = {}; ldx = {}; mmy = {}; hev = {}; sqy = {}; smm = {}
        comb = {}; rstd = {}; add = {}; sto = {}
        for i in range(NST):
            bf = i % 2

            @ph.op("sync", deps=(mmy.get((i - 1, DC - 1)),), inc=16,
                   name=f"ldh3{i}")
            def _(e, i=i):
                return e.dma_start(h3v, h33[:, :, ts(i, 512)])
            ldh3[i] = ph.ops[-1]

            @ph.op("sync", deps=(hev.get((i - 1, DC - 1)),), inc=16,
                   name=f"ldh1{i}")
            def _(e, i=i):
                return e.dma_start(h1v, h13[:, :, ts(i, 512)])
            ldh1[i] = ph.ops[-1]

            @ph.op("sync", deps=(add.get(i - 1),), inc=16, name=f"ldx{i}")
            def _(e, i=i):
                return e.dma_start(xfv, x3[:, :, ts(i, 512)])
            ldx[i] = ph.ops[-1]

            for oc in range(DC):
                yb2 = oc % 2

                @ph.op("tensor",
                       deps=(ldh3[i], hev.get((i, oc - 2)),
                             hev.get((i - 1, DC - 1)) if oc < 2 else None),
                       name=f"mmy{i}_{oc}")
                def _(e, oc=oc, yb2=yb2):
                    for kc in range(NKC):
                        ins = e.matmul(ps_y[:, ts(yb2, 512)],
                                       wv[:, kc, ts(oc, P)], h3v[:, kc],
                                       start=(kc == 0), stop=(kc == NKC - 1),
                                       skip_group_check=True)
                    return ins
                mmy[(i, oc)] = ph.ops[-1]

                @ph.op("vector", deps=(mmy[(i, oc)], ldh1[i], add.get(i - 1)),
                       name=f"hev{i}_{oc}")
                def _(e, oc=oc, yb2=yb2):
                    e.tensor_add(h4f[:, ts(oc, 512)], h1v[:, oc],
                                 ps_y[:, ts(yb2, 512)])
                    e.tensor_copy(yb[:, ts(oc, 512)], h4f[:, ts(oc, 512)])
                    return e.tensor_mul(yq[:, ts(oc, 512)], yb[:, ts(oc, 512)],
                                        yb[:, ts(oc, 512)])
                hev[(i, oc)] = ph.ops[-1]

                @ph.op("tensor", deps=(hev[(i, oc)], comb.get(i - 1)),
                       name=f"smm{i}_{oc}")
                def _(e, oc=oc):
                    e.matmul(ps_m[:], pr.ones_bf[:], yb[:, ts(oc, 512)],
                             start=(oc == 0), stop=(oc == DC - 1),
                             skip_group_check=True)
                    return e.matmul(ps_s[:], pr.ones_bf[:], yq[:, ts(oc, 512)],
                                    start=(oc == 0), stop=(oc == DC - 1),
                                    skip_group_check=True)
                smm[(i, oc)] = ph.ops[-1]

            @ph.op("vector", deps=(smm[(i, DC - 1)], rstd.get(i - 1)),
                   name=f"comb{i}")
            def _(e):
                e.tensor_scalar_mul(mr[:], ps_m[:], 1.0 / D)
                e.tensor_mul(rr[:], mr[:], mr[:])
                return e.scalar_tensor_tensor(rr[:], ps_s[:], 1.0 / D, rr[:],
                                              ALU.mult, ALU.subtract)
            comb[i] = ph.ops[-1]

            @ph.op("scalar", deps=(comb[i],), name=f"rstd{i}")
            def _(e):
                e.activation(rr[:], rr[:], AF.Ln, bias=pr.eps_ln)
                return e.activation(rr[:], rr[:], AF.Exp, scale=-0.5)
            rstd[i] = ph.ops[-1]

            @ph.op("vector", deps=(rstd[i], ldx[i], sto.get(i - 1)),
                   name=f"add{i}")
            def _(e, i=i):
                for oc in range(DC):
                    e.tensor_sub(t1[:], h4f[:, ts(oc, 512)], mr[:])
                    e.tensor_mul(t1[:], t1[:], rr[:])
                    ins = e.tensor_add(xov[:, oc], xfv[:, oc], t1[:])
                return ins
            add[i] = ph.ops[-1]

            @ph.op("gpsimd", deps=(add[i],), inc=16, name=f"sto{i}")
            def _(e, i=i):
                return e.dma_start(o3[:, :, ts(i, 512)], xov)
            sto[i] = ph.ops[-1]

        @ph.op("gpsimd", deps=(sto[NST - 1],), name="fin")
        def _(e):
            return e.memset(t1[0:1, 0:1], 0.0)
        ph.emit()


def emit_bank(pr, h1_dram, x_in, x_out, fc_mix, fc_s1T, fc_s2, fc_lg,
              pj_mix, pj_s1T, pj_s2, pj_lg, tag):
    nc = pr.nc
    h2_dram = nc.dram_tensor(f"{tag}_h2", [D, TOWN], BF16)
    h3_dram = nc.dram_tensor(f"{tag}_h3", [4096, TOWN], BF16)
    mfc_sh = nc.dram_tensor(f"{tag}_mfcs", [D, 512], BF16)
    mfc_ag = nc.dram_tensor(f"{tag}_mfca", [8 * D, 512], BF16)
    mpj_sh = nc.dram_tensor(f"{tag}_mpjs", [4096, P], BF16)
    mpj_ag = nc.dram_tensor(f"{tag}_mpja", [8 * 4096, P], BF16)

    emit_rms_to_dram(pr, h1_dram.ap(), h2_dram.ap(), f"{tag}_rms")

    with ExitStack() as es:
        c_fc = es.enter_context(nc.sbuf_tensor(f"{tag}_cfc", [P, BANK], F32))
        c_pj = es.enter_context(nc.sbuf_tensor(f"{tag}_cpj", [P, BANK], F32))
        emit_top2(pr, fc_lg, c_fc, f"{tag}_t2f")
        emit_top2(pr, pj_lg, c_pj, f"{tag}_t2p")
        emit_compose(pr, c_fc, fc_mix, fc_s1T, fc_s2, mfc_sh, 16, 8, 512,
                     f"{tag}_cmf")
        pr.allgather8(mfc_sh, mfc_ag)
        emit_compose(pr, c_pj, pj_mix, pj_s1T, pj_s2, mpj_sh, 64, 2, 128,
                     f"{tag}_cmp")
        pr.allgather8(mpj_sh, mpj_ag)

    with ExitStack() as es:
        wfc = es.enter_context(nc.sbuf_tensor(f"{tag}_wfc", [P, DC * 4096], BF16))
        emit_load_mfull(pr, mfc_ag, wfc, 4096, f"{tag}_lmf")
        emit_fc_stream(pr, wfc, h2_dram.ap(), h3_dram.ap(), f"{tag}_fc")

    with ExitStack() as es:
        wpj = es.enter_context(nc.sbuf_tensor(f"{tag}_wpj", [P, 32 * D], BF16))
        emit_load_mfull(pr, mpj_ag, wpj, D, f"{tag}_lmp")
        emit_pj_final(pr, wpj, h3_dram.ap(), h1_dram.ap(), x_in, x_out,
                      f"{tag}_pj")



def emit_slot_l1_ffn(pr, x_in, x_out, wq, wkv, wo, gains, fc_mix, fc_s1T,
                     fc_s2, fc_lg, pj_mix, pj_s1T, pj_s2, pj_lg, pfx="s3"):
    nc = pr.nc
    g_dram = nc.dram_tensor(f"{pfx}_g", [D, TOWN], F32)
    h1_dram = nc.dram_tensor(f"{pfx}_h1", [D, TOWN], F32)
    emit_ln_to_dram(pr, x_in, g_dram.ap(), f"{pfx}_ln7", out_dtype=F32)
    emit_gqa(pr, g_dram, h1_dram, wq, wkv, wo, gains, None, f"{pfx}_gqa")
    emit_bank(pr, h1_dram, x_in, x_out, fc_mix, fc_s1T, fc_s2, fc_lg,
              pj_mix, pj_s1T, pj_s2, pj_lg, f"{pfx}_bk")


def build_l1_ffn():
    pr = Prog()
    nc = pr.nc
    x_in = nc.dram_tensor("x_in", [D, TOWN], F32, kind="ExternalInput").ap()
    wq = nc.dram_tensor("gq_wq", [D, NHC * HD], F32, kind="ExternalInput").ap()
    wkv = nc.dram_tensor("gq_wkv", [D, 256], F32, kind="ExternalInput").ap()
    wo = nc.dram_tensor("gq_wo", [D, D], F32, kind="ExternalInput").ap()
    gains = nc.dram_tensor("gq_gains", [1, NHC + 2], F32, kind="ExternalInput").ap()
    fc_mix = nc.dram_tensor("b_fcmix", [BANK, 512, 1024], F32, kind="ExternalInput").ap()
    fc_s1T = nc.dram_tensor("b_fcs1", [P, BANK * 8 * 64], F32, kind="ExternalInput").ap()
    fc_s2 = nc.dram_tensor("b_fcs2", [P, BANK * 4 * 64], F32, kind="ExternalInput").ap()
    fc_lg = nc.dram_tensor("b_fclg", [1, BANK], F32, kind="ExternalInput").ap()
    pj_mix = nc.dram_tensor("b_pjmix", [BANK, P, 4096], F32, kind="ExternalInput").ap()
    pj_s1T = nc.dram_tensor("b_pjs1", [P, BANK * 32 * 64], F32, kind="ExternalInput").ap()
    pj_s2 = nc.dram_tensor("b_pjs2", [P, BANK * 1 * 64], F32, kind="ExternalInput").ap()
    pj_lg = nc.dram_tensor("b_pjlg", [1, BANK], F32, kind="ExternalInput").ap()
    x_out = nc.dram_tensor("x_out", [D, TOWN], F32, kind="ExternalOutput").ap()
    pr.load_consts(rope=True)
    emit_slot_l1_ffn(pr, x_in, x_out, wq, wkv, wo, gains, fc_mix, fc_s1T,
                     fc_s2, fc_lg, pj_mix, pj_s1T, pj_s2, pj_lg)
    pr.close()
    return pr.nc


def _stage_s1T(s1, nin):
    s1T = s1.reshape(BANK, nin, 64, 64).transpose(0, 1, 3, 2)
    out = np.empty((P, BANK, nin // 2, 64), np.float32)
    for g in range(nin):
        g2, gh = g // 2, g % 2
        out[gh * 64:(gh + 1) * 64, :, g2, :] = s1T[:, g].transpose(1, 0, 2)
    return np.ascontiguousarray(out.reshape(P, BANK * (nin // 2) * 64))


def stage_l1_ffn(inputs, x_maps):
    base = stage_l1_ffn_gqa(inputs, x_maps)
    fc_mix = np.asarray(inputs["L1_ffn_fc_mix"], np.float32)
    fc_s1 = np.asarray(inputs["L1_ffn_fc_s1"], np.float32)
    fc_s2 = np.asarray(inputs["L1_ffn_fc_s2"], np.float32)
    fc_lg = np.asarray(inputs["L1_ffn_fc_logits"], np.float32)
    pj_mix = np.asarray(inputs["L1_ffn_pj_mix"], np.float32)
    pj_s1 = np.asarray(inputs["L1_ffn_pj_s1"], np.float32)
    pj_s2 = np.asarray(inputs["L1_ffn_pj_s2"], np.float32)
    pj_lg = np.asarray(inputs["L1_ffn_pj_logits"], np.float32)
    fc_s1T_h = _stage_s1T(fc_s1, 16)
    pj_s1T_h = _stage_s1T(pj_s1, 64)
    for c in range(8):
        m = base[c]
        m["b_fcmix"] = np.ascontiguousarray(fc_mix[:, c * 512:(c + 1) * 512, :])
        m["b_fcs1"] = fc_s1T_h
        s2sl = fc_s2[:, c * 512:(c + 1) * 512, :]
        m["b_fcs2"] = np.ascontiguousarray(
            s2sl.reshape(BANK, 4, P, 64).transpose(2, 0, 1, 3).reshape(P, -1))
        m["b_fclg"] = np.ascontiguousarray(fc_lg[None, :])
        m["b_pjmix"] = np.ascontiguousarray(pj_mix[:, c * P:(c + 1) * P, :])
        m["b_pjs1"] = pj_s1T_h
        s2sl = pj_s2[:, c * P:(c + 1) * P, :]
        m["b_pjs2"] = np.ascontiguousarray(
            s2sl.transpose(1, 0, 2).reshape(P, -1))
        m["b_pjlg"] = np.ascontiguousarray(pj_lg[None, :])
    return base



def _l0_head_perm():
    return np.concatenate([np.arange(0, HD, 2), np.arange(1, HD, 2)])


def stage_l0_attn(inputs):
    x = np.asarray(inputs["L0_attn_x"])
    wq = np.asarray(inputs["L0_attn_wq"])
    wk = np.asarray(inputs["L0_attn_wk"])
    wv = np.asarray(inputs["L0_attn_wv"])
    wo = np.asarray(inputs["L0_attn_wo"])
    perm = _l0_head_perm()
    maps = []
    for c in range(8):
        b, p = c // 2, c % 2
        xs = np.ascontiguousarray(x[b, p * TOWN:(p + 1) * TOWN, :].T)
        heads = range(p * NHC, (p + 1) * NHC)
        rows_qk = np.concatenate([h * HD + perm for h in heads])
        rows_v = np.concatenate([np.arange(h * HD, (h + 1) * HD) for h in heads])
        maps.append({
            "x_in": xs,
            "c_toff": np.array([[p * (TOWN // 512)]], np.int32),
            "l0_wq": np.ascontiguousarray(wq[rows_qk].T),
            "l0_wk": np.ascontiguousarray(wk[rows_qk].T),
            "l0_wv": np.ascontiguousarray(wv[rows_v].T),
            "l0_wo": np.ascontiguousarray(wo.T),
        })
    return maps


def unstage_x(results, key="x_out"):
    out = np.empty((NB_OUT := 4, T, D), np.float32)
    for c in range(8):
        b, p = c // 2, c % 2
        out[b, p * TOWN:(p + 1) * TOWN, :] = results[c][key].T
    return out


def stage_x(x_full, key="x_in"):
    maps = []
    for c in range(8):
        b, p = c // 2, c % 2
        maps.append({key: np.ascontiguousarray(x_full[b, p * TOWN:(p + 1) * TOWN, :].T)})
    return maps



def build_megakernel():
    pr = Prog()
    nc = pr.nc
    x_in = nc.dram_tensor("x_in", [D, TOWN], F32, kind="ExternalInput").ap()
    l0_wq = nc.dram_tensor("l0_wq", [D, NHC * HD], F32, kind="ExternalInput").ap()
    l0_wk = nc.dram_tensor("l0_wk", [D, NHC * HD], F32, kind="ExternalInput").ap()
    l0_wv = nc.dram_tensor("l0_wv", [D, NHC * HD], F32, kind="ExternalInput").ap()
    l0_wo = nc.dram_tensor("l0_wo", [D, D], F32, kind="ExternalInput").ap()
    l0f_w = nc.dram_tensor("l0f_w", [D, D], F32, kind="ExternalInput").ap()
    l0f_g = nc.dram_tensor("l0f_g", [P, DC], F32, kind="ExternalInput").ap()
    l0f_b = nc.dram_tensor("l0f_b", [P, DC], F32, kind="ExternalInput").ap()
    l1_wqkv = nc.dram_tensor("l1_wqkv", [D, 3 * NHC * HD], F32, kind="ExternalInput").ap()
    l1_wproj = nc.dram_tensor("l1_wproj", [D, D], F32, kind="ExternalInput").ap()
    gq_wq = nc.dram_tensor("gq_wq", [D, NHC * HD], F32, kind="ExternalInput").ap()
    gq_wkv = nc.dram_tensor("gq_wkv", [D, 256], F32, kind="ExternalInput").ap()
    gq_wo = nc.dram_tensor("gq_wo", [D, D], F32, kind="ExternalInput").ap()
    gq_gains = nc.dram_tensor("gq_gains", [1, NHC + 2], F32, kind="ExternalInput").ap()
    b_fcmix = nc.dram_tensor("b_fcmix", [BANK, 512, 1024], F32, kind="ExternalInput").ap()
    b_fcs1 = nc.dram_tensor("b_fcs1", [P, BANK * 8 * 64], F32, kind="ExternalInput").ap()
    b_fcs2 = nc.dram_tensor("b_fcs2", [P, BANK * 4 * 64], F32, kind="ExternalInput").ap()
    b_fclg = nc.dram_tensor("b_fclg", [1, BANK], F32, kind="ExternalInput").ap()
    b_pjmix = nc.dram_tensor("b_pjmix", [BANK, P, 4096], F32, kind="ExternalInput").ap()
    b_pjs1 = nc.dram_tensor("b_pjs1", [P, BANK * 32 * 64], F32, kind="ExternalInput").ap()
    b_pjs2 = nc.dram_tensor("b_pjs2", [P, BANK * 1 * 64], F32, kind="ExternalInput").ap()
    b_pjlg = nc.dram_tensor("b_pjlg", [1, BANK], F32, kind="ExternalInput").ap()
    x_out = nc.dram_tensor("x_out", [D, TOWN], F32, kind="ExternalOutput").ap()

    x1 = nc.dram_tensor("x1_state", [D, TOWN], F32)
    x2 = nc.dram_tensor("x2_state", [D, TOWN], F32)
    x3 = nc.dram_tensor("x3_state", [D, TOWN], F32)

    pr.load_consts(rope=True)
    emit_slot_l0_attn(pr, x_in, x1.ap(), l0_wq, l0_wk, l0_wv, l0_wo, pfx="s0")
    emit_slot_l0_ffn(pr, x1.ap(), x2.ap(), l0f_w, l0f_g, l0f_b, pfx="s1")
    emit_slot_l1_attn(pr, x2.ap(), x3.ap(), l1_wqkv, l1_wproj, pfx="s2")
    emit_slot_l1_ffn(pr, x3.ap(), x_out, gq_wq, gq_wkv, gq_wo, gq_gains,
                     b_fcmix, b_fcs1, b_fcs2, b_fclg,
                     b_pjmix, b_pjs1, b_pjs2, b_pjlg, pfx="s3")
    pr.close()
    return pr.nc


def stage_mega(inputs):
    maps1 = stage_l0_attn(inputs)
    x_maps = [m["x_in"] for m in maps1]
    maps2 = stage_l0_ffn(inputs, x_maps)
    maps3 = stage_l1_attn(inputs, x_maps)
    maps4 = stage_l1_ffn(inputs, x_maps)
    maps = []
    for c in range(8):
        m = {}
        m.update(maps1[c])
        m.update(maps2[c])
        m.update(maps3[c])
        m.update(maps4[c])
        maps.append(m)
    return maps


_NC_CACHE = {}


def _get_nc(name, builder):
    if name not in _NC_CACHE:
        _NC_CACHE[name] = builder()
    return _NC_CACHE[name]


def kernel(**inputs):
    cores = list(range(8))
    nc = _get_nc("mega", build_megakernel)
    maps = stage_mega(inputs)
    res = run_bass_kernel_spmd(nc, maps, core_ids=cores)
    out = np.empty((NB, T, D), np.float32)
    for c in range(8):
        b, p = c // 2, c % 2
        out[b, p * TOWN:(p + 1) * TOWN, :] = res.results[c]["x_out"].T
    return out


# version 03
# speedup vs baseline: 1.0196x; 1.0158x over previous
import numpy as np
import ml_dtypes
import concourse.bass as bass
import concourse.mybir as mybir
from concourse.bass_utils import run_bass_kernel_spmd
from contextlib import ExitStack
from concourse.expressions_rust import AnnotatedLowered

F32 = mybir.dt.float32
BF16 = mybir.dt.bfloat16
AF = mybir.ActivationFunctionType
ALU = mybir.AluOpType
ET = mybir.EngineType
bfdt = ml_dtypes.bfloat16
ts = bass.ts

P = 128
D = 1024
DC = D // P
T = 4096
TOWN = 2048
NB = 4
NH = 16
HD = 64
NHC = 8
NHP = NHC // 2
NTT = T // P
NQB = T // 512
LN_EPS = 1e-5
ROPE_THETA = 10000.0
PAIR_GROUPS = [[0, 1], [2, 3], [4, 5], [6, 7]]
ALL_GROUP = [[0, 1, 2, 3, 4, 5, 6, 7]]

ENGS = ("sync", "tensor", "vector", "scalar", "gpsimd")



def _rope_tables_np():
    pos = np.arange(T, dtype=np.float64)[:, None]
    k = np.arange(HD // 2, dtype=np.float64)[None, :]
    ang = pos / ROPE_THETA ** (2.0 * k / HD)

    def f(a):
        a = a.astype(np.float32).reshape(T // P, P, HD // 2).transpose(1, 0, 2)
        return np.ascontiguousarray(a.reshape(P, (T // P) * (HD // 2)))
    return f(np.sin(ang)), f(np.cos(ang))


def _stair_mask_np():
    w = np.zeros((P, 896), dtype=np.float32)
    kk = np.arange(P)[:, None]
    cc = np.arange(896)[None, :]
    w[cc < 384 + kk] = -1e30
    return w



class Op:
    __slots__ = ("eng", "fn", "deps", "hdeps", "inc", "seq", "cum", "needed",
                 "name")

    def __init__(self, eng, fn, deps, hdeps, inc, name):
        self.eng = eng
        self.fn = fn
        self.deps = deps
        self.hdeps = hdeps
        self.inc = inc
        self.needed = False
        self.name = name


class Phase:

    def __init__(self, pr, name):
        self.pr = pr
        self.name = name
        self.ops = []

    def op(self, eng, deps=(), hdeps=(), inc=1, name=""):
        def reg(fn):
            o = Op(eng, fn, [d for d in deps if d is not None],
                   [d for d in hdeps if d is not None], inc, name)
            self.ops.append(o)
            return o
        return reg

    def emit(self):
        pr = self.pr
        nc = pr.nc
        for o in self.ops:
            if o.inc != 1:
                o.needed = True
            for d in o.deps:
                d.needed = True
            for d in o.hdeps:
                d.needed = True
        for o in self.ops:
            sem = pr.esem[o.eng]
            o.seq = sem.n + (o.inc if o.needed else 0)
            if o.needed:
                sem.n += o.inc
            o.cum = o.seq

        by_eng = {e: [o for o in self.ops if o.eng == e] for e in ENGS}

        def make_body(eng):
            def body(e):
                hwm = {}
                for o in by_eng[eng]:
                    for d in list(o.hdeps) + list(o.deps):
                        if not d.needed:
                            raise RuntimeError(f"dep {d.name} not marked needed")
                        sem = self.pr.esem[d.eng]
                        if hwm.get(d.eng, -1) < d.cum:
                            e.wait_ge(sem.h, d.cum)
                            hwm[d.eng] = d.cum
                    ins = o.fn(e)
                    if o.needed:
                        if ins is None:
                            raise RuntimeError(f"op {o.name} returned no instr")
                        ins.then_inc(self.pr.esem[o.eng].h, o.inc)
            return body

        with nc.Block() as block:
            for eng in ENGS:
                if by_eng[eng]:
                    getattr(block, eng)(make_body(eng))


class Sem:
    def __init__(self, nc, name):
        self.h = nc.alloc_semaphore(name)
        self.n = 0


class Prog:
    def __init__(self, num_devices=8):
        self.nc = bass.Bass("TRN2", target_bir_lowering=False,
                            num_devices=num_devices)
        self.es = ExitStack()
        self.esem = {e: Sem(self.nc, f"se_{e}") for e in ENGS}
        self.s_cc = Sem(self.nc, "s_cc")

    def sb(self, name, shape, dtype):
        return self.es.enter_context(self.nc.sbuf_tensor(name, shape, dtype))

    def close(self):
        self.es.close()

    def load_consts(self, rope=True):
        nc = self.nc
        self.toff_in = nc.dram_tensor("c_toff", [1, 1], mybir.dt.int32,
                                      kind="ExternalInput").ap()
        self.toff_sb = self.sb("c_toffs", [1, 1], mybir.dt.int32)
        self.ident = self.sb("c_ident", [P, P], BF16)
        self.ones_bf = self.sb("c_ones", [P, P], BF16)
        self.ones_f32r = self.sb("c_onesr", [P, P], F32)
        self.ones_f32c = self.sb("c_onesc", [P, 2], F32)
        self.stair = self.sb("c_stair", [P, 896], F32)
        self.ident_f32 = self.sb("c_idf", [P, P], F32)
        self.eye64 = self.sb("c_eye64", [64, 64], F32)
        loads = [
            (self.ident, nc.inline_tensor(np.eye(P, dtype=bfdt), "h_ident")),
            (self.ones_bf, nc.inline_tensor(np.ones((P, P), dtype=bfdt), "h_ones")),
            (self.ones_f32r,
             nc.inline_tensor(np.ones((P, P), dtype=np.float32), "h_onesr")),
            (self.ones_f32c,
             nc.inline_tensor(np.ones((P, 2), dtype=np.float32), "h_onesc")),
            (self.stair, nc.inline_tensor(_stair_mask_np(), "h_stair")),
            (self.ident_f32,
             nc.inline_tensor(np.eye(P, dtype=np.float32), "h_idf")),
            (self.eye64,
             nc.inline_tensor(np.eye(64, dtype=np.float32), "h_eye64")),
        ]
        if rope:
            sin_np, cos_np = _rope_tables_np()
            self.sin = self.sb("c_sin", [P, sin_np.shape[1]], F32)
            self.cos = self.sb("c_cos", [P, cos_np.shape[1]], F32)
            loads += [(self.sin, nc.inline_tensor(sin_np, "h_sin")),
                      (self.cos, nc.inline_tensor(cos_np, "h_cos"))]
        self.cbias = self.sb("c_bias", [P, 4], F32)
        ph = Phase(self, "consts")

        @ph.op("sync", inc=16, name="toff")
        def _(e):
            return e.dma_start(self.toff_sb[0:1, 0:1], self.toff_in[:, :])

        for i, (dst, h) in enumerate(loads):
            @ph.op("sync", inc=16, name=f"c{i}")
            def _(e, dst=dst, h=h):
                return e.dma_start(dst[:], h.ap()[:, :])

        @ph.op("vector", name="cbias")
        def _(e):
            e.memset(self.cbias[:, 0:1], float(LN_EPS))
            e.memset(self.cbias[:, 1:2], 1e-3)
            e.memset(self.cbias[:, 2:3], 1e-6)
            return e.memset(self.cbias[:, 3:4], 0.0)
        ph.emit()
        self.eps_ln = self.cbias[:, 0:1]
        self.eps_bl = self.cbias[:, 1:2]
        self.eps_rms = self.cbias[:, 2:3]

    def pair_allgather(self, src_dram, dst_dram):
        nc = self.nc
        s = self.s_cc
        with nc.Block() as block:
            @block.gpsimd
            def _(g):
                g.collective_compute(
                    "AllGather", ALU.bypass, replica_groups=PAIR_GROUPS,
                    ins=[src_dram.ap().opt()], outs=[dst_dram.ap().opt()],
                ).then_inc(s.h, 1)
                s.n += 1
                g.wait_ge(s.h, s.n)

    def allgather8(self, src_dram, dst_dram):
        nc = self.nc
        s = self.s_cc
        with nc.Block() as block:
            @block.gpsimd
            def _(g):
                g.collective_compute(
                    "AllGather", ALU.bypass, replica_groups=ALL_GROUP,
                    ins=[src_dram.ap().opt()], outs=[dst_dram.ap().opt()],
                ).then_inc(s.h, 1)
                s.n += 1
                g.wait_ge(s.h, s.n)

    def allreduce_max(self, buf_dram):
        nc = self.nc
        s = self.s_cc
        with nc.Block() as block:
            @block.gpsimd
            def _(g):
                g.collective_compute(
                    "AllReduce", ALU.max, replica_groups=ALL_GROUP,
                    ins=[buf_dram.ap().opt()], outs=[buf_dram.ap().opt()],
                ).then_inc(s.h, 1)
                s.n += 1
                g.wait_ge(s.h, s.n)


def bap(t, off, dims):
    return bass.AP(t, off, dims)



def emit_ln_to_dram(pr, x_in, a_out, tag, n_tok=TOWN, out_dtype=BF16):
    nc = pr.nc
    NCH = n_tok // 512
    es = ExitStack()
    with es:
        sb = lambda n, s, d: es.enter_context(nc.sbuf_tensor(f"{tag}_{n}", s, d))
        xf = sb("xf", [P, 2 * DC * 512], F32)
        xb = sb("xb", [P, DC * 512], BF16)
        xq = sb("xq", [P, DC * 512], BF16)
        nb = sb("nb", [P, 512], F32)
        mr = sb("mr", [P, 512], F32)
        rr = sb("rr", [P, 512], F32)
        ab = sb("ab", [P, 2 * DC * 512], out_dtype)
        ps_m = es.enter_context(nc.psum_tensor(f"{tag}_pm", [P, 512], F32))
        ps_s = es.enter_context(nc.psum_tensor(f"{tag}_ps", [P, 512], F32))

        x3 = x_in.rearrange("(c p) t -> p c t", p=P)
        a3 = a_out.rearrange("(c p) t -> p c t", p=P)
        xfv = xf[:].rearrange("p (b c t) -> p b c t", b=2, c=DC)
        abv = ab[:].rearrange("p (b c t) -> p b c t", b=2, c=DC)

        ph = Phase(pr, tag)
        ld = {}; cast = {}; smm = {}; comb = {}; rstd = {}; norm = {}; st = {}
        for i in range(NCH):
            bf = i % 2

            @ph.op("sync", deps=(norm.get(i - 2),), inc=16, name=f"ld{i}")
            def _(e, i=i, bf=bf):
                return e.dma_start(xfv[:, bf], x3[:, :, ts(i, 512)])
            ld[i] = ph.ops[-1]

            @ph.op("vector", deps=(ld[i], smm.get(i - 1)), name=f"cast{i}")
            def _(e, i=i, bf=bf):
                e.tensor_copy(xb[:], xfv[:, bf].rearrange("p c t -> p (c t)"))
                return e.tensor_mul(xq[:], xb[:], xb[:])
            cast[i] = ph.ops[-1]

            @ph.op("tensor", deps=(cast[i], comb.get(i - 1)), name=f"smm{i}")
            def _(e, i=i):
                for dc in range(DC):
                    e.matmul(ps_m[:], pr.ones_bf[:], xb[:, ts(dc, 512)],
                             start=(dc == 0), stop=(dc == DC - 1),
                             skip_group_check=True)
                for dc in range(DC):
                    ins = e.matmul(ps_s[:], pr.ones_bf[:], xq[:, ts(dc, 512)],
                                   start=(dc == 0), stop=(dc == DC - 1),
                                   skip_group_check=True)
                return ins
            smm[i] = ph.ops[-1]

            @ph.op("vector", deps=(smm[i], rstd.get(i - 1)), name=f"comb{i}")
            def _(e, i=i):
                e.tensor_scalar_mul(mr[:], ps_m[:], 1.0 / D)
                e.tensor_mul(rr[:], mr[:], mr[:])
                return e.scalar_tensor_tensor(rr[:], ps_s[:], 1.0 / D, rr[:],
                                              ALU.mult, ALU.subtract)
            comb[i] = ph.ops[-1]

            @ph.op("scalar", deps=(comb[i],), name=f"rstd{i}")
            def _(e, i=i):
                e.activation(rr[:], rr[:], AF.Ln, bias=pr.eps_ln)
                return e.activation(rr[:], rr[:], AF.Exp, scale=-0.5)
            rstd[i] = ph.ops[-1]

            @ph.op("vector", deps=(rstd[i], st.get(i - 2)), name=f"norm{i}")
            def _(e, i=i, bf=bf):
                for dc in range(DC):
                    e.tensor_sub(nb[:], xfv[:, bf, dc], mr[:])
                    ins = e.tensor_mul(abv[:, bf, dc], nb[:], rr[:])
                return ins
            norm[i] = ph.ops[-1]

            @ph.op("gpsimd", deps=(norm[i],), inc=16, name=f"st{i}")
            def _(e, i=i, bf=bf):
                return e.dma_start(a3[:, :, ts(i, 512)], abv[:, bf])
            st[i] = ph.ops[-1]

        @ph.op("gpsimd", deps=(st[NCH - 1],), name="fin")
        def _(e):
            return e.memset(nb[0:1, 0:1], 0.0)
        ph.emit()



def emit_qkv_rope(pr, ag_a_out, wq_sb, wk_sb, wv_sb, QT, KT, VT,
                  tag, rope=True):
    nc = pr.nc
    es = ExitStack()
    with es:
        sb = lambda n, s, d: es.enter_context(nc.sbuf_tensor(f"{tag}_{n}", s, d))
        at = sb("at", [P, 2 * DC * P], BF16)
        qs = sb("qs", [P, 512], F32)
        qs2 = sb("qs2", [P, 512], F32)
        qro = sb("qro", [P, 512], BF16)
        kro = sb("kro", [P, 512], BF16)
        ps_q = es.enter_context(nc.psum_tensor(f"{tag}_pq", [P, 512], F32))
        ps_k = es.enter_context(nc.psum_tensor(f"{tag}_pk", [P, 512], F32))
        ps_v = es.enter_context(nc.psum_tensor(f"{tag}_pv", [P, 512], F32))
        ps_tq = es.enter_context(nc.psum_tensor(f"{tag}_ptq", [P, 8 * P], BF16))
        ps_tk = es.enter_context(nc.psum_tensor(f"{tag}_ptk", [P, 8 * P], BF16))

        agv = ag_a_out.ap().rearrange("(h c p) t -> p h c t", p=P, c=DC)
        atv = at[:].rearrange("p (b c q) -> p b c q", b=2, c=DC)
        QTv = QT[:].rearrange("p (hp t) -> p hp t", hp=NHP)
        KTv = KT[:].rearrange("p (hp t) -> p hp t", hp=NHP)
        VTv = VT[:].rearrange("p (tt h e) -> p tt h e", tt=NTT, h=NHC)
        wqv = wq_sb[:].rearrange("p (c n) -> p c n", c=DC)
        wkv = wk_sb[:].rearrange("p (c n) -> p c n", c=DC)
        wvv = wv_sb[:].rearrange("p (c n) -> p c n", c=DC)

        ph = Phase(pr, tag)

        @ph.op("vector", name="vones")
        def _(e):
            return e.memset(bap(VT, 64, [[NTT * NHC * 65, P],
                                         [65, NTT * NHC], [1, 1]]), 1.0)
        vones = ph.ops[-1]
        lda = {}; mmq = {}; mmk = {}; mmv = {}
        rpq = {}; rpk = {}; evv = {}; trq = {}; trk = {}; evq = {}; evk = {}
        for tt in range(NTT):
            bf = tt % 2
            h = (tt * P) // TOWN
            co = (tt * P) % TOWN

            @ph.op("sync", deps=(mmv.get(tt - 2),), inc=16, name=f"lda{tt}")
            def _(e, h=h, co=co, bf=bf):
                return e.dma_start(atv[:, bf], agv[:, h, :, co:co + P])
            lda[tt] = ph.ops[-1]

            @ph.op("tensor", deps=(lda[tt], rpq.get(tt - 1)), name=f"mmq{tt}")
            def _(e, bf=bf):
                for dc in range(DC):
                    ins = e.matmul(ps_q[:], atv[:, bf, dc], wqv[:, dc],
                                   start=(dc == 0), stop=(dc == DC - 1))
                return ins
            mmq[tt] = ph.ops[-1]

            @ph.op("tensor", deps=(rpk.get(tt - 1),), name=f"mmk{tt}")
            def _(e, bf=bf):
                for dc in range(DC):
                    ins = e.matmul(ps_k[:], atv[:, bf, dc], wkv[:, dc],
                                   start=(dc == 0), stop=(dc == DC - 1))
                return ins
            mmk[tt] = ph.ops[-1]

            @ph.op("tensor", deps=(evv.get(tt - 1),), name=f"mmv{tt}")
            def _(e, bf=bf):
                for dc in range(DC):
                    ins = e.matmul(ps_v[:], atv[:, bf, dc], wvv[:, dc],
                                   start=(dc == 0), stop=(dc == DC - 1))
                return ins
            mmv[tt] = ph.ops[-1]

            @ph.op("vector", deps=(mmq[tt], trq.get(tt - 1)), name=f"rpq{tt}")
            def _(e, tt=tt):
                if rope:
                    _rope_rm(nc, e, ps_q, qs, qs2, qro, pr.cos, pr.sin, tt)
                    return e.tensor_copy(qro[:, 0:8], qro[:, 0:8])
                return e.tensor_copy(qro[:], ps_q[:])
            rpq[tt] = ph.ops[-1]

            @ph.op("vector", deps=(mmk[tt], trk.get(tt - 1)), name=f"rpk{tt}")
            def _(e, tt=tt):
                if rope:
                    _rope_rm(nc, e, ps_k, qs, qs2, kro, pr.cos, pr.sin, tt)
                    return e.tensor_copy(kro[:, 0:8], kro[:, 0:8])
                return e.tensor_copy(kro[:], ps_k[:])
            rpk[tt] = ph.ops[-1]

            @ph.op("vector", deps=(mmv[tt], vones), name=f"evv{tt}")
            def _(e, tt=tt):
                return e.tensor_copy(
                    VTv[:, tt, :, 0:64],
                    ps_v[:].rearrange("p (h e) -> p h e", h=NHC))
            evv[tt] = ph.ops[-1]

            @ph.op("tensor", deps=(rpq[tt], evq.get(tt - 1)), name=f"trq{tt}")
            def _(e):
                for hp in range(NHP):
                    ins = e.transpose(ps_tq[:, ts(hp, P)], qro[:, ts(hp, P)],
                                      pr.ident[:])
                return ins
            trq[tt] = ph.ops[-1]

            @ph.op("tensor", deps=(rpk[tt], evk.get(tt - 1)), name=f"trk{tt}")
            def _(e):
                for hp in range(NHP):
                    ins = e.transpose(ps_tk[:, ts(hp, P)], kro[:, ts(hp, P)],
                                      pr.ident[:])
                return ins
            trk[tt] = ph.ops[-1]

            @ph.op("scalar", deps=(trq[tt],), name=f"evq{tt}")
            def _(e, tt=tt):
                for hp in range(NHP):
                    ins = e.copy(QTv[:, hp, ts(tt, P)], ps_tq[:, ts(hp, P)])
                return ins
            evq[tt] = ph.ops[-1]

            @ph.op("scalar", deps=(trk[tt],), name=f"evk{tt}")
            def _(e, tt=tt):
                for hp in range(NHP):
                    ins = e.copy(KTv[:, hp, ts(tt, P)], ps_tk[:, ts(hp, P)])
                return ins
            evk[tt] = ph.ops[-1]
        ph.emit()


def _rope_rm(nc, e, ps_in, t1s, t2s, out_bf, cosT, sinT, tt):
    ev = bap(ps_in, 0, [[512, P], [64, NHC], [1, 32]])
    ov = bap(ps_in, 32, [[512, P], [64, NHC], [1, 32]])
    t1 = bap(t1s, 0, [[512, P], [32, NHC], [1, 32]])
    t2 = bap(t2s, 0, [[512, P], [32, NHC], [1, 32]])
    oe = bap(out_bf, 0, [[512, P], [64, NHC], [1, 32]])
    oo = bap(out_bf, 32, [[512, P], [64, NHC], [1, 32]])
    cb = bap(cosT, tt * 32, [[1024, P], [0, NHC], [1, 32]])
    sn = bap(sinT, tt * 32, [[1024, P], [0, NHC], [1, 32]])
    e.tensor_mul(t1, ev, cb)
    e.tensor_mul(t2, ov, sn)
    e.tensor_sub(oe, t1, t2)
    e.tensor_mul(t1, ev, sn)
    e.tensor_mul(t2, ov, cb)
    e.tensor_add(oo, t1, t2)



MASK_ENG = "vector"


def emit_causal_attention(pr, QT, KT, VT, ag_t_in, scale, tag,
                          k_fn=None, v_fn=None):
    nc = pr.nc
    es = ExitStack()
    with es:
        sb = lambda n, s, d: es.enter_context(nc.sbuf_tensor(f"{tag}_{n}", s, d))
        PP = sb("PP", [P, 3 * 1024], BF16)
        attA = sb("atA", [P, 512], BF16)
        attB = sb("atB", [P, 512], BF16)
        afA = sb("afA", [P, 2 * 512], BF16)
        afB = sb("afB", [P, 2 * 512], BF16)
        rec = sb("rec", [P, 1024], F32)
        ps_s = es.enter_context(nc.psum_tensor(f"{tag}_s", [P, 3 * 1024], F32))
        ps_pA = es.enter_context(nc.psum_tensor(f"{tag}_pA", [P, 512], F32))
        ps_pB = es.enter_context(nc.psum_tensor(f"{tag}_pB", [P, 512], F32))

        QTv = QT[:].rearrange("p (hp t) -> p hp t", hp=NHP)
        if k_fn is None:
            KTv = KT[:].rearrange("p (hp t) -> p hp t", hp=NHP)
            k_fn = lambda KTv, hp, half, kt: \
                KTv[half * 64:(half + 1) * 64, hp, ts(kt, P)]
        else:
            KTv = None
        if v_fn is None:
            VTv = VT[:].rearrange("p (tt h e) -> p tt h e", tt=NTT, h=NHC)
            v_fn = lambda VTv, hp, half, kt: VTv[:, kt, 2 * hp + half, :]
        else:
            VTv = None
        agv = ag_t_in.ap().rearrange("(hp q) t -> q hp t", q=P)

        ph = Phase(pr, tag)
        Sg = {}; EXg = {}; PVg = {}
        SCL = {}; STA = {}; STB = {}
        blocks = [(hp, qb) for hp in range(NHP) for qb in range(NQB)]
        ginfo = []
        for hp, qb in blocks:
            for kt in range(4 * (qb + 1)):
                ginfo.append((hp, qb, kt))

        def reg_S(g):
            hp, qb, kt = ginfo[g]
            s = g % 3

            @ph.op("tensor", deps=(EXg.get(g - 3),), name=f"S{g}")
            def _(e, hp=hp, qb=qb, kt=kt, s=s):
                e.matmul(ps_s[:, s * 1024:s * 1024 + 512],
                         k_fn(KTv, hp, 0, kt),
                         QTv[0:64, hp, ts(qb, 512)],
                         start=True, stop=True, tile_position=(0, 0))
                return e.matmul(ps_s[:, s * 1024 + 512:s * 1024 + 1024],
                                k_fn(KTv, hp, 1, kt),
                                QTv[64:128, hp, ts(qb, 512)],
                                start=True, stop=True, tile_position=(64, 0))
            Sg[g] = ph.ops[-1]

            mk = None
            if kt >= 4 * qb:
                j = kt - 4 * qb

                @ph.op(MASK_ENG, deps=(Sg[g],), name=f"MK{g}")
                def _(e, s=s, j=j):
                    off = 384 - j * P
                    so = bap(ps_s, s * 1024, [[3 * 1024, P], [512, 2], [1, 512]])
                    st = bap(pr.stair, off, [[896, P], [0, 2], [1, 512]])
                    return e.tensor_tensor(so, so, st, ALU.add)
                mk = ph.ops[-1]

            @ph.op("scalar", deps=(mk or Sg[g], PVg.get(g - 3)), name=f"EX{g}")
            def _(e, s=s):
                return e.activation(PP[:, s * 1024:(s + 1) * 1024],
                                    ps_s[:, s * 1024:(s + 1) * 1024],
                                    AF.Exp, scale=float(scale))
            EXg[g] = ph.ops[-1]

        def reg_PV(g):
            hp, qb, kt = ginfo[g]
            nkt = 4 * (qb + 1)
            s = g % 3
            L = hp * NQB + qb
            deps = [EXg[g]]
            if kt == 0 and L > 0:
                deps.append(SCL[L - 1])

            @ph.op("tensor", deps=tuple(deps), name=f"PV{g}")
            def _(e, hp=hp, kt=kt, s=s, st=(kt == 0), sp=(kt == nkt - 1)):
                e.matmul(ps_pA[0:65, :], v_fn(VTv, hp, 0, kt),
                         PP[:, s * 1024:s * 1024 + 512], start=st, stop=sp,
                         tile_position=(0, 0), skip_group_check=True)
                return e.matmul(ps_pB[0:65, :], v_fn(VTv, hp, 1, kt),
                                PP[:, s * 1024 + 512:s * 1024 + 1024],
                                start=st, stop=sp,
                                tile_position=(0, 0), skip_group_check=True)
            PVg[g] = ph.ops[-1]

        gbase = 0
        BCp = None
        for hp, qb in blocks:
            nkt = 4 * (qb + 1)
            L = hp * NQB + qb
            for kt in range(nkt):
                reg_S(gbase + kt)
                if kt >= 2:
                    reg_PV(gbase + kt - 2)
            reg_PV(gbase + nkt - 2)
            reg_PV(gbase + nkt - 1)
            last = PVg[gbase + nkt - 1]
            gbase += nkt

            deps = [last] + ([SCL[L - 1]] if L > 0 else [])

            @ph.op("scalar", deps=tuple(deps), name=f"EV{L}")
            def _(e):
                e.copy(attA[0:64, :], ps_pA[0:64, :])
                return e.copy(attB[0:64, :], ps_pB[0:64, :])
            EV = ph.ops[-1]

            deps = [last] + ([BCp] if L > 0 else [])

            @ph.op("vector", deps=tuple(deps), name=f"RC{L}")
            def _(e):
                e.reciprocal(rec[64:65, 0:512], ps_pA[64:65, :])
                return e.reciprocal(rec[64:65, 512:1024], ps_pB[64:65, :])
            RC = ph.ops[-1]

            @ph.op("tensor", deps=(RC, EV), name=f"BC{L}")
            def _(e):
                e.matmul(ps_pA[0:64, :], pr.ones_f32r[64:65, 0:64],
                         rec[64:65, 0:512], start=True, stop=True)
                return e.matmul(ps_pB[0:64, :], pr.ones_f32r[64:65, 0:64],
                                rec[64:65, 512:1024], start=True, stop=True)
            BCp = ph.ops[-1]

            deps = [BCp] + ([STA[L - 2], STB[L - 2]] if L >= 2 else [])

            @ph.op("vector", deps=tuple(deps), name=f"SCL{L}")
            def _(e, bf=L % 2):
                e.tensor_mul(afA[0:64, ts(bf, 512)], attA[0:64, :],
                             ps_pA[0:64, :])
                return e.tensor_mul(afB[0:64, ts(bf, 512)], attB[0:64, :],
                                    ps_pB[0:64, :])
            SCL[L] = ph.ops[-1]

            @ph.op("gpsimd", deps=(SCL[L],), inc=16, name=f"STA{L}")
            def _(e, hp=hp, qb=qb, bf=L % 2):
                return e.dma_start(agv[0:64, hp, ts(qb, 512)],
                                   afA[0:64, ts(bf, 512)])
            STA[L] = ph.ops[-1]

            @ph.op("gpsimd", deps=(SCL[L],), inc=16, name=f"STB{L}")
            def _(e, hp=hp, qb=qb, bf=L % 2):
                return e.dma_start(agv[64:128, hp, ts(qb, 512)],
                                   afB[0:64, ts(bf, 512)])
            STB[L] = ph.ops[-1]

        @ph.op("gpsimd", deps=(STA[len(blocks) - 1], STB[len(blocks) - 1]),
               name="fin")
        def _(e):
            return e.memset(rec[0:1, 0:1], 0.0)
        ph.emit()



def emit_oproj_ln_residual(pr, ag_t_out, wo_sb, wo_ld_op, x_in, x_out, tag,
                           rhs_view_fn=None, out_scale_ap=None,
                           rhs_dep=None):
    nc = pr.nc
    NST = TOWN // 512
    es = ExitStack()
    with es:
        sb = lambda n, s, d: es.enter_context(nc.sbuf_tensor(f"{tag}_{n}", s, d))
        ats = sb("ats", [P, 2 * DC * 512], BF16)
        yb = sb("yb", [P, DC * 512], BF16)
        yq = sb("yq", [P, DC * 512], BF16)
        xf = sb("xf", [P, 2 * DC * 512], F32)
        mr = sb("mr", [P, 512], F32)
        rr = sb("rr", [P, 512], F32)
        xo = sb("xo", [P, 2 * DC * 512], F32)
        t1 = sb("t1", [P, 512], F32)
        ps_y = es.enter_context(nc.psum_tensor(f"{tag}_py", [P, 2 * 512], F32))
        ps_m = es.enter_context(nc.psum_tensor(f"{tag}_pm", [P, 512], F32))
        ps_s = es.enter_context(nc.psum_tensor(f"{tag}_ps", [P, 512], F32))

        agv = (ag_t_out.ap().rearrange("(c p) t -> p c t", p=P)
               if ag_t_out is not None else None)
        x3 = x_in.rearrange("(c p) t -> p c t", p=P)
        o3 = x_out.rearrange("(c p) t -> p c t", p=P)
        atsv = ats[:].rearrange("p (b c t) -> p b c t", b=2, c=DC)
        xfv = xf[:].rearrange("p (b c t) -> p b c t", b=2, c=DC)
        xov = xo[:].rearrange("p (b c t) -> p b c t", b=2, c=DC)
        wov = wo_sb[:].rearrange("p (c n) -> p c n", c=DC)

        ph = Phase(pr, tag)
        lda = {}; ldx = {}; mmy = {}; evy = {}; sqy = {}; smm = {}
        comb = {}; rstd = {}; add = {}; sto = {}
        for i in range(NST):
            bf = i % 2

            if rhs_view_fn is None:
                @ph.op("sync", deps=(mmy.get((i - 2, DC - 1)),), inc=16,
                       name=f"lda{i}")
                def _(e, i=i, bf=bf):
                    with e.register(f"{tag}r{i}") as r:
                        e.load(r, pr.toff_sb[0:1, 0:1])
                        e.reg_add(r, r, i)
                        e.reg_mul(r, r, 512)
                        v = AnnotatedLowered(r, 0, T - 512).to_runtime()
                        return e.dma_start(atsv[:, bf],
                                           agv[:, :, bass.DynSlice(v, 512)])
                lda[i] = ph.ops[-1]
            else:
                lda[i] = rhs_dep

            @ph.op("sync", deps=(add.get(i - 2),), inc=16, name=f"ldx{i}")
            def _(e, i=i, bf=bf):
                return e.dma_start(xfv[:, bf], x3[:, :, ts(i, 512)])
            ldx[i] = ph.ops[-1]

            def reg_mmy(i, oc, bf):
                yb2 = oc % 2

                @ph.op("tensor",
                       deps=(lda[i], evy.get((i, oc - 2)),
                             evy.get((i - 1, DC - 1)) if oc < 2 else None,
                             (wo_ld_op if (i == 0 and oc == 0) else None)),
                       name=f"mmy{i}_{oc}")
                def _(e, i=i, oc=oc, bf=bf, yb2=yb2):
                    for ec in range(DC):
                        rhs = (atsv[:, bf, ec] if rhs_view_fn is None
                               else rhs_view_fn(i, ec))
                        ins = e.matmul(ps_y[:, ts(yb2, 512)],
                                       wov[:, ec, ts(oc, P)], rhs,
                                       start=(ec == 0), stop=(ec == DC - 1),
                                       skip_group_check=True)
                    return ins
                mmy[(i, oc)] = ph.ops[-1]

                @ph.op("scalar", deps=(mmy[(i, oc)], add.get(i - 1)),
                       name=f"evy{i}_{oc}")
                def _(e, oc=oc, yb2=yb2):
                    if out_scale_ap is not None:
                        return e.activation(yb[:, ts(oc, 512)],
                                            ps_y[:, ts(yb2, 512)], AF.Copy,
                                            scale=out_scale_ap)
                    return e.copy(yb[:, ts(oc, 512)], ps_y[:, ts(yb2, 512)])
                evy[(i, oc)] = ph.ops[-1]

                @ph.op("vector", deps=(evy[(i, oc)],), name=f"sqy{i}_{oc}")
                def _(e, oc=oc):
                    return e.tensor_mul(yq[:, ts(oc, 512)], yb[:, ts(oc, 512)],
                                        yb[:, ts(oc, 512)])
                sqy[(i, oc)] = ph.ops[-1]

            def reg_smm(i, oc):
                @ph.op("tensor", deps=(sqy[(i, oc)], comb.get(i - 1)),
                       name=f"smm{i}_{oc}")
                def _(e, oc=oc):
                    e.matmul(ps_m[:], pr.ones_bf[:], yb[:, ts(oc, 512)],
                             start=(oc == 0), stop=(oc == DC - 1),
                             skip_group_check=True)
                    return e.matmul(ps_s[:], pr.ones_bf[:], yq[:, ts(oc, 512)],
                                    start=(oc == 0), stop=(oc == DC - 1),
                                    skip_group_check=True)
                smm[(i, oc)] = ph.ops[-1]

            for oc in range(DC):
                reg_mmy(i, oc, bf)
                if oc >= 1:
                    reg_smm(i, oc - 1)
            reg_smm(i, DC - 1)

            @ph.op("vector", deps=(smm[(i, DC - 1)], rstd.get(i - 1)),
                   name=f"comb{i}")
            def _(e):
                e.tensor_scalar_mul(mr[:], ps_m[:], 1.0 / D)
                e.tensor_mul(rr[:], mr[:], mr[:])
                return e.scalar_tensor_tensor(rr[:], ps_s[:], 1.0 / D, rr[:],
                                              ALU.mult, ALU.subtract)
            comb[i] = ph.ops[-1]

            @ph.op("scalar", deps=(comb[i],), name=f"rstd{i}")
            def _(e):
                e.activation(rr[:], rr[:], AF.Ln, bias=pr.eps_ln)
                return e.activation(rr[:], rr[:], AF.Exp, scale=-0.5)
            rstd[i] = ph.ops[-1]

            @ph.op("vector", deps=(rstd[i], ldx[i], sto.get(i - 2)),
                   name=f"add{i}")
            def _(e, i=i, bf=bf):
                for oc in range(DC):
                    e.tensor_sub(t1[:], yb[:, ts(oc, 512)], mr[:])
                    e.tensor_mul(t1[:], t1[:], rr[:])
                    ins = e.tensor_add(xov[:, bf, oc], xfv[:, bf, oc], t1[:])
                return ins
            add[i] = ph.ops[-1]

            @ph.op("gpsimd", deps=(add[i],), inc=16, name=f"sto{i}")
            def _(e, i=i, bf=bf):
                return e.dma_start(o3[:, :, ts(i, 512)], xov[:, bf])
            sto[i] = ph.ops[-1]

        @ph.op("gpsimd", deps=(sto[NST - 1],), name="fin")
        def _(e):
            return e.memset(t1[0:1, 0:1], 0.0)
        ph.emit()



def emit_slot_l0_attn(pr, x_in, x_out, wq, wk, wv, wo, pfx="s0"):
    nc = pr.nc
    ag_a_in = nc.dram_tensor(f"{pfx}_agai", [D, TOWN], BF16)
    ag_a_out = nc.dram_tensor(f"{pfx}_agao", [2 * D, TOWN], BF16)
    ag_t_in = nc.dram_tensor(f"{pfx}_agti", [NHC * HD, T], BF16)
    ag_t_out = nc.dram_tensor(f"{pfx}_agto", [2 * NHC * HD, T], BF16)
    emit_ln_to_dram(pr, x_in, ag_a_in.ap(), f"{pfx}_ln1")
    pr.pair_allgather(ag_a_in, ag_a_out)
    emit_attn_mid_l0(pr, ag_a_out, wq, wk, wv, ag_t_in)
    pr.pair_allgather(ag_t_in, ag_t_out)
    emit_oproj_block(pr, ag_t_out, wo, x_in, x_out, f"{pfx}_opj")


def build_l0_attn():
    pr = Prog()
    nc = pr.nc
    x_in = nc.dram_tensor("x_in", [D, TOWN], F32, kind="ExternalInput").ap()
    wq = nc.dram_tensor("l0_wq", [D, NHC * HD], F32, kind="ExternalInput").ap()
    wk = nc.dram_tensor("l0_wk", [D, NHC * HD], F32, kind="ExternalInput").ap()
    wv = nc.dram_tensor("l0_wv", [D, NHC * HD], F32, kind="ExternalInput").ap()
    wo = nc.dram_tensor("l0_wo", [D, D], F32, kind="ExternalInput").ap()
    x_out = nc.dram_tensor("x_out", [D, TOWN], F32, kind="ExternalOutput").ap()
    pr.load_consts(rope=True)
    emit_slot_l0_attn(pr, x_in, x_out, wq, wk, wv, wo)
    pr.close()
    return pr.nc


def load_weight_dmajor(pr, ph, w_dram, w_sb, name):
    @ph.op("gpsimd", inc=16, name=name)
    def _(e):
        return e.dma_start(
            w_sb[:].rearrange("p (c n) -> p c n", c=DC),
            w_dram.rearrange("(c p) n -> p c n", p=P))
    return ph.ops[-1]


def emit_attn_mid_l0(pr, ag_a_out, wq, wk, wv, ag_t_in):
    nc = pr.nc
    with ExitStack() as es:
        sbl = lambda n, s, d: es.enter_context(nc.sbuf_tensor(n, s, d))
        wq_sb = sbl("wq_sb", [P, DC * NHC * HD], BF16)
        wk_sb = sbl("wk_sb", [P, DC * NHC * HD], BF16)
        wv_sb = sbl("wv_sb", [P, DC * NHC * HD], BF16)
        QT = sbl("QT", [P, NHP * T], BF16)
        KT = sbl("KT", [P, NHP * T], BF16)
        VT = sbl("VT", [P, NTT * NHC * 65], BF16)

        phw = Phase(pr, "wload")
        wops = [load_weight_dmajor(pr, phw, wq, wq_sb, "wq"),
                load_weight_dmajor(pr, phw, wk, wk_sb, "wk"),
                load_weight_dmajor(pr, phw, wv, wv_sb, "wv")]

        @phw.op("gpsimd", deps=tuple(wops), name="wfin")
        def _(e):
            return e.memset(QT[0:1, 0:1], 0.0)
        phw.emit()

        emit_qkv_rope(pr, ag_a_out, wq_sb, wk_sb, wv_sb, QT, KT, VT, "qkv",
                      rope=True)
        emit_causal_attention(pr, QT, KT, VT, ag_t_in, scale=1.0 / 8.0,
                              tag="att")


def emit_oproj_block(pr, ag_t_out, wo, x_in, x_out, tag):
    nc = pr.nc
    with ExitStack() as es:
        wo_sb = es.enter_context(nc.sbuf_tensor(f"{tag}_wo", [P, DC * D], BF16))
        phw = Phase(pr, "woload")
        wo_op = load_weight_dmajor(pr, phw, wo, wo_sb, "wo")

        @phw.op("gpsimd", deps=(wo_op,), name="wofin")
        def _(e):
            return e.memset(wo_sb[0:1, 0:1], 0.0)
        phw.emit()
        emit_oproj_ln_residual(pr, ag_t_out, wo_sb, None, x_in, x_out, tag)



def emit_bitlinear_wq(pr, w_dram, wq_sb, beta_rep, tag):
    nc = pr.nc
    with ExitStack() as es:
        sb = lambda n, s, d: es.enter_context(nc.sbuf_tensor(f"{tag}_{n}", s, d))
        wf = sb("wf", [P, DC * D], F32)
        aw = sb("aw", [P, DC * D], F32)
        acc = sb("acc", [P, DC], F32)
        acs = sb("acs", [P, 1], F32)
        sgn = sb("sgn", [P, DC * D], BF16)
        ps_b = es.enter_context(nc.psum_tensor(f"{tag}_pb", [P, 512], F32))

        ph = Phase(pr, tag)

        @ph.op("sync", inc=16, name="ldw")
        def _(e):
            return e.dma_start(wf[:].rearrange("p (c n) -> p c n", c=DC),
                               w_dram.rearrange("(c p) n -> p c n", p=P))
        ldw = ph.ops[-1]

        @ph.op("scalar", deps=(ldw,), name="absw")
        def _(e):
            for dc in range(DC):
                ins = e.activation(aw[:, ts(dc, D)], wf[:, ts(dc, D)], AF.Abs,
                                   accum_out=acc[:, dc:dc + 1])
            return ins
        absw = ph.ops[-1]

        @ph.op("vector", deps=(absw,), name="accsum")
        def _(e):
            return e.tensor_reduce(acs[:, 0:1], acc[:],
                                   mybir.AxisListType.X, ALU.add)
        accs = ph.ops[-1]

        @ph.op("tensor", deps=(accs,), name="betamm")
        def _(e):
            return e.matmul(ps_b[0:1, 0:1], pr.ones_f32c[:, 0:1], acs[:, 0:1],
                            start=True, stop=True)
        betm = ph.ops[-1]

        @ph.op("vector", deps=(betm,), name="betasc")
        def _(e):
            return e.tensor_scalar_mul(acs[0:1, 0:1], ps_b[0:1, 0:1],
                                       1.0 / (D * D))
        betsc = ph.ops[-1]

        @ph.op("tensor", deps=(betsc,), name="betabc")
        def _(e):
            return e.matmul(ps_b[:, 2:3], pr.ones_f32r[0:1, :], acs[0:1, 0:1],
                            start=True, stop=True)
        betbc = ph.ops[-1]

        @ph.op("vector", deps=(betbc,), name="betarep")
        def _(e):
            return e.tensor_copy(beta_rep[:, 0:1], ps_b[:, 2:3])
        brep = ph.ops[-1]

        @ph.op("scalar", deps=(ldw,), name="sgn")
        def _(e):
            return e.sign(sgn[:], wf[:])
        sg = ph.ops[-1]

        @ph.op("vector", deps=(brep,), name="thr")
        def _(e):
            return e.tensor_scalar(acs[:, 0:1], beta_rep[:, 0:1], 1e-5, 0.5,
                                   ALU.add, ALU.mult)
        throp = ph.ops[-1]

        @ph.op("vector", deps=(sg,), hdeps=(throp,), name="wq")
        def _(e):
            thr_bc = bap(acs, 0, [[1, P], [0, DC * D]])
            e.tensor_tensor(aw[:], aw[:], thr_bc, ALU.is_gt)
            return e.tensor_mul(wq_sb[:], sgn[:], aw[:])
        wqop = ph.ops[-1]

        @ph.op("gpsimd", deps=(wqop,), name="fin")
        def _(e):
            return e.memset(acc[0:1, 0:1], 0.0)
        ph.emit()


def emit_double_ln_absmax(pr, x_in, gam_dram, xn_sb, g_sb, b_sb, tag):
    nc = pr.nc
    NCH = TOWN // 512
    with ExitStack() as es:
        sb = lambda n, s, d: es.enter_context(nc.sbuf_tensor(f"{tag}_{n}", s, d))
        xf = sb("xf", [P, 2 * DC * 512], F32)
        z1 = sb("z1", [P, DC * 512], F32)
        zb = sb("zb", [P, DC * 512], BF16)
        zq = sb("zq", [P, DC * 512], BF16)
        nb = sb("nb", [P, 512], F32)
        mr = sb("mr", [P, 512], F32)
        rr = sb("rr", [P, 512], F32)
        m2 = sb("m2", [P, 512], F32)
        r2 = sb("r2", [P, 512], F32)
        mx = sb("mx", [P, 2 + DC], F32)
        mxd = nc.dram_tensor(f"{tag}_mxd", [P, 1], F32)
        ps_m = es.enter_context(nc.psum_tensor(f"{tag}_pm", [P, 512], F32))
        ps_s = es.enter_context(nc.psum_tensor(f"{tag}_ps", [P, 512], F32))

        x3 = x_in.rearrange("(c p) t -> p c t", p=P)
        xfv = xf[:].rearrange("p (b c t) -> p b c t", b=2, c=DC)
        xnv = xn_sb[:].rearrange("p (c n t) -> p c n t", c=DC, n=NCH)

        ph = Phase(pr, tag)
        ld = {}; cast = {}; smm = {}; comb = {}; rstd = {}; z1o = {}
        smm2 = {}; comb2 = {}; rstd2 = {}; aff = {}; mxo = {}
        for i in range(NCH):
            bf = i % 2

            @ph.op("sync", deps=(z1o.get(i - 2),), inc=16, name=f"ld{i}")
            def _(e, i=i, bf=bf):
                return e.dma_start(xfv[:, bf], x3[:, :, ts(i, 512)])
            ld[i] = ph.ops[-1]

            @ph.op("vector", deps=(ld[i], smm.get(i - 1)), name=f"cast{i}")
            def _(e, bf=bf):
                e.tensor_copy(zb[:], xfv[:, bf].rearrange("p c t -> p (c t)"))
                return e.tensor_mul(zq[:], zb[:], zb[:])
            cast[i] = ph.ops[-1]

            @ph.op("tensor", deps=(cast[i], comb.get(i - 1)), name=f"smm{i}")
            def _(e):
                for dc in range(DC):
                    e.matmul(ps_m[:], pr.ones_bf[:], zb[:, ts(dc, 512)],
                             start=(dc == 0), stop=(dc == DC - 1),
                             skip_group_check=True)
                for dc in range(DC):
                    ins = e.matmul(ps_s[:], pr.ones_bf[:], zq[:, ts(dc, 512)],
                                   start=(dc == 0), stop=(dc == DC - 1),
                                   skip_group_check=True)
                return ins
            smm[i] = ph.ops[-1]

            @ph.op("vector", deps=(smm[i],), name=f"comb{i}")
            def _(e):
                e.tensor_scalar_mul(mr[:], ps_m[:], 1.0 / D)
                e.tensor_mul(rr[:], mr[:], mr[:])
                return e.scalar_tensor_tensor(rr[:], ps_s[:], 1.0 / D, rr[:],
                                              ALU.mult, ALU.subtract)
            comb[i] = ph.ops[-1]

            @ph.op("scalar", deps=(comb[i],), name=f"rstd{i}")
            def _(e):
                e.activation(rr[:], rr[:], AF.Ln, bias=pr.eps_ln)
                return e.activation(rr[:], rr[:], AF.Exp, scale=-0.5)
            rstd[i] = ph.ops[-1]

            @ph.op("vector", deps=(rstd[i], smm2.get(i - 1)), name=f"z1o{i}")
            def _(e, bf=bf):
                for dc in range(DC):
                    e.tensor_sub(nb[:], xfv[:, bf, dc], mr[:])
                    e.tensor_mul(z1[:, ts(dc, 512)], nb[:], rr[:])
                e.tensor_copy(zb[:], z1[:])
                return e.tensor_mul(zq[:], zb[:], zb[:])
            z1o[i] = ph.ops[-1]

            @ph.op("tensor", deps=(z1o[i], comb2.get(i - 1)), name=f"smm2{i}")
            def _(e):
                for dc in range(DC):
                    e.matmul(ps_m[:], pr.ones_bf[:], zb[:, ts(dc, 512)],
                             start=(dc == 0), stop=(dc == DC - 1),
                             skip_group_check=True)
                for dc in range(DC):
                    ins = e.matmul(ps_s[:], pr.ones_bf[:], zq[:, ts(dc, 512)],
                                   start=(dc == 0), stop=(dc == DC - 1),
                                   skip_group_check=True)
                return ins
            smm2[i] = ph.ops[-1]

            @ph.op("vector", deps=(smm2[i],), name=f"comb2{i}")
            def _(e):
                e.tensor_scalar_mul(m2[:], ps_m[:], 1.0 / D)
                e.tensor_mul(r2[:], m2[:], m2[:])
                return e.scalar_tensor_tensor(r2[:], ps_s[:], 1.0 / D, r2[:],
                                              ALU.mult, ALU.subtract)
            comb2[i] = ph.ops[-1]

            @ph.op("scalar", deps=(comb2[i],), name=f"rstd2{i}")
            def _(e):
                e.activation(r2[:], r2[:], AF.Ln, bias=pr.eps_bl)
                return e.activation(r2[:], r2[:], AF.Exp, scale=-0.5)
            rstd2[i] = ph.ops[-1]

            @ph.op("vector", deps=(rstd2[i],), name=f"z2o{i}")
            def _(e):
                for dc in range(DC):
                    e.tensor_sub(nb[:], z1[:, ts(dc, 512)], m2[:])
                    ins = e.tensor_mul(z1[:, ts(dc, 512)], nb[:], r2[:])
                return ins
            z2o = ph.ops[-1]

            @ph.op("vector", deps=(z2o,), name=f"aff{i}")
            def _(e, i=i):
                for dc in range(DC):
                    b_bc = bap(b_sb, dc, [[DC, P], [0, 512]])
                    ins = e.scalar_tensor_tensor(
                        xnv[:, dc, i], z1[:, ts(dc, 512)],
                        g_sb[:, dc:dc + 1], b_bc, ALU.mult, ALU.add)
                return ins
            aff[i] = ph.ops[-1]

            @ph.op("vector", deps=(aff[i], mxo.get(i - 1)), name=f"mxa{i}")
            def _(e, i=i):
                return e.tensor_reduce(mx[:, 2:2 + DC], xnv[:, :, i],
                                       mybir.AxisListType.X, ALU.max,
                                       apply_absolute_value=True)
            mxa = ph.ops[-1]

            @ph.op("vector", hdeps=(mxa,), name=f"mxb{i}")
            def _(e, i=i):
                return e.tensor_reduce(mx[:, 1:2], mx[:, 2:2 + DC],
                                       mybir.AxisListType.X, ALU.max)
            mxb = ph.ops[-1]

            @ph.op("vector", hdeps=(mxb,), name=f"mx{i}")
            def _(e, i=i):
                if i == 0:
                    return e.tensor_copy(mx[:, 0:1], mx[:, 1:2])
                return e.tensor_max(mx[:, 0:1], mx[:, 0:1], mx[:, 1:2])
            mxo[i] = ph.ops[-1]

        mxr = sb("mxr", [1, P], F32)
        g8 = sb("g8", [1, 8], F32)

        @ph.op("gpsimd", deps=(mxo[NCH - 1],), inc=16, name="mxst")
        def _(e):
            return e.dma_start(mxd.ap()[:, :], mx[:, 0:1])
        mxst = ph.ops[-1]

        @ph.op("sync", deps=(mxst,), inc=16, name="mxld")
        def _(e):
            return e.dma_start(mxr[0:1, :], mxd.ap().rearrange("p o -> o p"))
        mxld = ph.ops[-1]

        @ph.op("vector", deps=(mxld,), name="mxred")
        def _(e):
            e.memset(g8[:], 0.0)
            return e.tensor_reduce(g8[0:1, 0:1], mxr[0:1, :],
                                   mybir.AxisListType.X, ALU.max)
        mxred = ph.ops[-1]

        @ph.op("gpsimd", deps=(mxred,), inc=16, name="gst")
        def _(e):
            return e.dma_start(gam_dram.ap()[:, :], g8[0:1, :])
        gst = ph.ops[-1]

        @ph.op("gpsimd", deps=(gst,), name="fin")
        def _(e):
            return e.memset(mx[0:1, 1:2], 0.0)
        ph.emit()



def emit_gamma_scales(pr, gam_dram, beta_rep, sc1, s_out, tag):
    nc = pr.nc
    with ExitStack() as es:
        sb = lambda n, s, d: es.enter_context(nc.sbuf_tensor(f"{tag}_{n}", s, d))
        g8 = sb("g8", [1, 8], F32)
        ps_g = es.enter_context(nc.psum_tensor(f"{tag}_pg", [P, 2], F32))
        ph = Phase(pr, tag)

        @ph.op("sync", inc=16, name="gld")
        def _(e):
            return e.dma_start(g8[0:1, :], gam_dram.ap()[:, :])
        gld = ph.ops[-1]

        @ph.op("tensor", deps=(gld,), name="gbc")
        def _(e):
            return e.matmul(ps_g[:, 0:1], pr.ones_f32r[0:1, :], g8[0:1, 0:1],
                            start=True, stop=True)
        gbc = ph.ops[-1]

        @ph.op("vector", deps=(gbc,), name="grecip")
        def _(e):
            return e.reciprocal(sc1[:, 0:1], ps_g[:, 0:1])
        grc = ph.ops[-1]

        @ph.op("vector", deps=(gbc,), hdeps=(grc,), name="gscales")
        def _(e):
            return e.tensor_scalar_mul(sc1[:, 0:1], sc1[:, 0:1], 128.0)
        gs1 = ph.ops[-1]

        @ph.op("vector", deps=(gbc,), hdeps=(gs1,), name="gsout")
        def _(e):
            return e.tensor_scalar(s_out[:, 0:1], ps_g[:, 0:1],
                                   beta_rep[:, 0:1], 1.0 / 128.0,
                                   ALU.mult, ALU.mult)
        gs = ph.ops[-1]

        @ph.op("gpsimd", deps=(gs,), name="fin")
        def _(e):
            return e.memset(g8[0:1, 0:1], 0.0)
        ph.emit()


def emit_slot_l0_ffn(pr, x_in, x_out, w, g_in, b_in, pfx="s1"):
    nc = pr.nc
    gam_dram = nc.dram_tensor(f"{pfx}_gam", [1, 8], F32)
    with ExitStack() as es:
        wq_sb = es.enter_context(nc.sbuf_tensor(f"{pfx}_blw", [P, DC * D], BF16))
        beta_rep = es.enter_context(nc.sbuf_tensor(f"{pfx}_beta", [P, 1], F32))
        xn_sb = es.enter_context(nc.sbuf_tensor(f"{pfx}_xn", [P, DC * TOWN], BF16))
        g_sb = es.enter_context(nc.sbuf_tensor(f"{pfx}_gg", [P, DC], F32))
        b_sb = es.enter_context(nc.sbuf_tensor(f"{pfx}_bb", [P, DC], F32))
        sc1 = es.enter_context(nc.sbuf_tensor(f"{pfx}_sc1", [P, 1], F32))
        s_out = es.enter_context(nc.sbuf_tensor(f"{pfx}_souts", [P, 1], F32))

        ph0 = Phase(pr, f"{pfx}_gbld")

        @ph0.op("sync", inc=16, name="gld")
        def _(e):
            return e.dma_start(g_sb[:], g_in[:, :])

        @ph0.op("sync", inc=16, name="bld")
        def _(e):
            return e.dma_start(b_sb[:], b_in[:, :])
        ph0.emit()

        emit_bitlinear_wq(pr, w, wq_sb, beta_rep, f"{pfx}_blq")
        emit_double_ln_absmax(pr, x_in, gam_dram, xn_sb, g_sb, b_sb, f"{pfx}_dln")
        pr.allreduce_max(gam_dram)
        emit_gamma_scales(pr, gam_dram, beta_rep, sc1, s_out, f"{pfx}_gsc")

        NCH = TOWN // 512
        xnv = xn_sb[:].rearrange("p (c n t) -> p c n t", c=DC, n=NCH)
        phq = Phase(pr, f"{pfx}_xq")

        @phq.op("vector", name="clip")
        def _(e):
            hi = 128.0 - 1e-5
            e.tensor_scalar(xn_sb[:], xn_sb[:], sc1[:, 0:1], -hi, ALU.mult,
                            ALU.max)
            return e.tensor_scalar_min(xn_sb[:], xn_sb[:], hi)
        phq.emit()

        emit_oproj_ln_residual(
            pr, None, wq_sb, None, x_in, x_out, f"{pfx}_bmm",
            rhs_view_fn=lambda i, ec: xnv[:, ec, i],
            out_scale_ap=s_out[:, 0:1])


def build_l0_ffn():
    pr = Prog()
    nc = pr.nc
    x_in = nc.dram_tensor("x_in", [D, TOWN], F32, kind="ExternalInput").ap()
    w = nc.dram_tensor("l0f_w", [D, D], F32, kind="ExternalInput").ap()
    g_in = nc.dram_tensor("l0f_g", [P, DC], F32, kind="ExternalInput").ap()
    b_in = nc.dram_tensor("l0f_b", [P, DC], F32, kind="ExternalInput").ap()
    x_out = nc.dram_tensor("x_out", [D, TOWN], F32, kind="ExternalOutput").ap()
    pr.load_consts(rope=False)
    emit_slot_l0_ffn(pr, x_in, x_out, w, g_in, b_in)
    pr.close()
    return pr.nc


def stage_l0_ffn(inputs, x_maps):
    w = np.ascontiguousarray(np.asarray(inputs["L0_ffn_weight"], np.float32))
    g = np.asarray(inputs["L0_ffn_ln_gamma"], np.float32)
    b = np.asarray(inputs["L0_ffn_ln_beta"], np.float32)
    gt = np.ascontiguousarray(g.reshape(DC, P).T)
    bt = np.ascontiguousarray(b.reshape(DC, P).T)
    maps = []
    for c in range(8):
        maps.append({"x_in": x_maps[c], "l0f_w": w, "l0f_g": gt, "l0f_b": bt,
                     "c_toff": np.array([[(c % 2) * (TOWN // 512)]], np.int32)})
    return maps



NL = 64
TPL = T // NL


def emit_nys_qkv(pr, ag_a_out, wqkv_sb, QT, KT, VT, tag):
    nc = pr.nc
    NCB = T // 512
    with ExitStack() as es:
        sb = lambda n, s, d: es.enter_context(nc.sbuf_tensor(f"{tag}_{n}", s, d))
        at = sb("at", [P, 2 * DC * 512], BF16)
        ps_q = es.enter_context(nc.psum_tensor(f"{tag}_pq", [P, 512], F32))
        ps_k = es.enter_context(nc.psum_tensor(f"{tag}_pk", [P, 512], F32))
        ps_v = es.enter_context(nc.psum_tensor(f"{tag}_pv", [P, 512], F32))

        agv = ag_a_out.ap().rearrange("(h c p) t -> p h c t", p=P, c=DC)
        atv = at[:].rearrange("p (b c q) -> p b c q", b=2, c=DC)
        QTv = QT[:].rearrange("p (hp t) -> p hp t", hp=NHP)
        KTv = KT[:].rearrange("p (hp t) -> p hp t", hp=NHP)
        VTv = VT[:].rearrange("p (tt h e) -> p tt h e", tt=NTT, h=NHC)
        wv = wqkv_sb[:].rearrange("p (c n) -> p c n", c=DC)

        ph = Phase(pr, tag)

        @ph.op("vector", name="vones")
        def _(e):
            return e.memset(bap(VT, 64, [[NTT * NHC * 65, P],
                                         [65, NTT * NHC], [1, 1]]), 1.0)
        vones = ph.ops[-1]
        lda = {}; mmq = {}; mmk = {}; mmv = {}; evq = {}; evk = {}; evv = {}
        for cb in range(NCB):
            bf = cb % 2
            h = (cb * 512) // TOWN
            co = (cb * 512) % TOWN

            @ph.op("sync", deps=(mmv.get(cb - 2),), inc=16, name=f"lda{cb}")
            def _(e, h=h, co=co, bf=bf):
                return e.dma_start(atv[:, bf], agv[:, h, :, co:co + 512])
            lda[cb] = ph.ops[-1]

            for hp in range(NHP):
                prevq = (cb, hp - 1) if hp > 0 else (cb - 1, NHP - 1)

                @ph.op("tensor", deps=(lda[cb], evq.get(prevq)),
                       name=f"mmq{cb}_{hp}")
                def _(e, hp=hp, bf=bf):
                    for dc in range(DC):
                        ins = e.matmul(ps_q[:], wv[:, dc, hp * P:(hp + 1) * P],
                                       atv[:, bf, dc],
                                       start=(dc == 0), stop=(dc == DC - 1),
                                       skip_group_check=True)
                    return ins
                mmq[(cb, hp)] = ph.ops[-1]

                @ph.op("vector", deps=(mmq[(cb, hp)],), name=f"evq{cb}_{hp}")
                def _(e, hp=hp, cb=cb):
                    return e.tensor_copy(QTv[:, hp, ts(cb, 512)], ps_q[:])
                evq[(cb, hp)] = ph.ops[-1]

                @ph.op("tensor", deps=(lda[cb], evk.get(prevq)),
                       name=f"mmk{cb}_{hp}")
                def _(e, hp=hp, bf=bf):
                    for dc in range(DC):
                        ins = e.matmul(ps_k[:],
                                       wv[:, dc, 512 + hp * P:512 + (hp + 1) * P],
                                       atv[:, bf, dc],
                                       start=(dc == 0), stop=(dc == DC - 1),
                                       skip_group_check=True)
                    return ins
                mmk[(cb, hp)] = ph.ops[-1]

                @ph.op("vector", deps=(mmk[(cb, hp)],), name=f"evk{cb}_{hp}")
                def _(e, hp=hp, cb=cb):
                    return e.tensor_copy(KTv[:, hp, ts(cb, 512)], ps_k[:])
                evk[(cb, hp)] = ph.ops[-1]

            for st4 in range(4):
                tt = cb * 4 + st4

                @ph.op("tensor", deps=(lda[cb], evv.get(tt - 1)),
                       name=f"mmv{tt}")
                def _(e, bf=bf, st4=st4):
                    for dc in range(DC):
                        ins = e.matmul(
                            ps_v[:],
                            atv[:, bf, dc, st4 * P:(st4 + 1) * P],
                            wv[:, dc, 1024:1536],
                            start=(dc == 0), stop=(dc == DC - 1),
                            skip_group_check=True)
                    return ins
                mmvo = ph.ops[-1]

                @ph.op("vector", deps=(mmvo, vones), name=f"evv{tt}")
                def _(e, tt=tt):
                    return e.tensor_copy(VTv[:, tt, :, 0:64], ps_v[:].rearrange(
                        "p (h e) -> p h e", h=NHC))
                evv[tt] = ph.ops[-1]
            mmv[cb] = mmvo
        ph.emit()


def emit_nystrom_attn(pr, QT, KT, VT, ag_t_in, tag, dbg=None):
    nc = pr.nc
    with ExitStack() as es:
        sb = lambda n, s, d: es.enter_context(nc.sbuf_tensor(f"{tag}_{n}", s, d))
        klT = sb("klT", [P, NHP * NL], BF16)
        qlT = sb("qlT", [P, NHP * NL], BF16)
        klF = sb("klF", [P, NL], F32)
        A = sb("A", [64, NHC * 64], F32)
        AT = sb("AT", [64, NHC * 64], F32)
        Vi = sb("Vi", [64, NHC * 64], F32)
        kv = sb("kv", [64, NHC * 64], F32)
        kvt = sb("kvt", [64, NHC * 64], F32)
        tm = sb("tm", [64, NHC * 64], F32)
        vt = sb("vt", [64, NHC * 64], F32)
        rs = sb("rs", [64, NHC], F32)
        Pn = sb("Pn", [P, 2 * 512], BF16)
        Zn = sb("Zn", [64, NHC * 64], F32)
        W2 = sb("W2", [64, NHC * 65], BF16)
        EL = sb("EL", [64, 2 * 512], BF16)
        rcl = sb("rcl", [P, 512], F32)
        att = sb("att", [64, 512], BF16)
        attf = sb("attf", [64, 2 * 512], BF16)
        ps_a = es.enter_context(nc.psum_tensor(f"{tag}_pa", [P, 512], F32))
        ps_b = es.enter_context(nc.psum_tensor(f"{tag}_pb", [P, 512], F32))
        ps_n = es.enter_context(nc.psum_tensor(f"{tag}_pn", [P, 2 * 512], F32))
        ps_z = es.enter_context(nc.psum_tensor(f"{tag}_pz", [64, 512], F32))
        ps_y = es.enter_context(nc.psum_tensor(f"{tag}_py", [P, 2 * 512], F32))
        ps_bc = es.enter_context(nc.psum_tensor(f"{tag}_pbc", [P, 512], F32))

        QTv = QT[:].rearrange("p (hp t) -> p hp t", hp=NHP)
        KTv = KT[:].rearrange("p (hp t) -> p hp t", hp=NHP)
        VTv = VT[:].rearrange("p (tt h e) -> p tt h e", tt=NTT, h=NHC)
        klTv = klT[:].rearrange("p (hp l) -> p hp l", hp=NHP)
        qlTv = qlT[:].rearrange("p (hp l) -> p hp l", hp=NHP)
        agv = ag_t_in.ap().rearrange("(h e) t -> e h t", e=HD)

        ph = Phase(pr, tag)

        lmk = {}
        lprev = None
        for hp in range(NHP):
            @ph.op("vector", deps=(lprev,), name=f"lmkr{hp}")
            def _(e, hp=hp):
                kk = KTv[:, hp].rearrange("p (l g) -> p l g", l=NL)
                return e.tensor_reduce(klF[:, :], kk, mybir.AxisListType.X,
                                       ALU.add)
            o1 = ph.ops[-1]

            @ph.op("vector", hdeps=(o1,), name=f"lmks{hp}")
            def _(e, hp=hp):
                return e.tensor_scalar_mul(klTv[:, hp], klF[:, :], 1.0 / TPL)
            o2 = ph.ops[-1]

            @ph.op("vector", hdeps=(o2,), name=f"lmqr{hp}")
            def _(e, hp=hp):
                qq = QTv[:, hp].rearrange("p (l g) -> p l g", l=NL)
                return e.tensor_reduce(klF[:, :], qq, mybir.AxisListType.X,
                                       ALU.add)
            o3 = ph.ops[-1]

            @ph.op("vector", hdeps=(o3,), name=f"lmqs{hp}")
            def _(e, hp=hp):
                return e.tensor_scalar_mul(qlTv[:, hp], klF[:, :], 1.0 / TPL)
            lmk[hp] = ph.ops[-1]
            lprev = lmk[hp]

        AS = {}; INV = {}
        aprev = None
        for h in range(NHC):
            hp, hh = divmod(h, 2)
            base = hh * 64

            @ph.op("tensor", deps=(lmk[hp], aprev), name=f"As{h}")
            def _(e, hp=hp, base=base, h=h):
                return e.matmul(ps_a[0:64, 0:64],
                                qlTv[base:base + 64, hp],
                                klTv[base:base + 64, hp],
                                start=True, stop=True)
            aso = ph.ops[-1]

            @ph.op("scalar", deps=(aso,), name=f"Ae{h}")
            def _(e, h=h):
                return e.activation(A[:, ts(h, 64)], ps_a[0:64, 0:64], AF.Exp)
            aeo = ph.ops[-1]

            @ph.op("vector", deps=(aeo,), name=f"Ars{h}")
            def _(e, h=h):
                return e.tensor_reduce(rs[:, h:h + 1], A[:, ts(h, 64)],
                                       mybir.AxisListType.X, ALU.add)
            arso = ph.ops[-1]

            @ph.op("vector", deps=(arso,), name=f"Arc{h}")
            def _(e, h=h):
                return e.reciprocal(rs[:, h:h + 1], rs[:, h:h + 1])
            arco = ph.ops[-1]

            @ph.op("vector", deps=(arco,), name=f"Anm{h}")
            def _(e, h=h):
                rbc = bap(rs, h, [[NHC, 64], [0, 64]])
                return e.tensor_tensor(A[:, ts(h, 64)], A[:, ts(h, 64)], rbc,
                                       ALU.mult)
            AS[h] = ph.ops[-1]
            aprev = AS[h]

        DN = {}; TR = {}
        dprev = AS[NHC - 1]
        for h in range(NHC):
            @ph.op("tensor", deps=(AS[h], dprev), name=f"Acs{h}")
            def _(e, h=h):
                e.matmul(ps_a[0:1, 0:64], pr.ones_f32c[0:64, 0:1],
                         A[:, ts(h, 64)], start=True, stop=True)
                return e.transpose(ps_b[0:64, 0:64], A[:, ts(h, 64)],
                                   pr.ident_f32[0:64, 0:64])
            acso = ph.ops[-1]

            @ph.op("vector", deps=(acso,), name=f"Amx{h}")
            def _(e, h=h):
                e.tensor_copy(AT[:, ts(h, 64)], ps_b[0:64, 0:64])
                return e.tensor_reduce(rs[0:1, h:h + 1], ps_a[0:1, 0:64],
                                       mybir.AxisListType.X, ALU.max)
            amxo = ph.ops[-1]

            @ph.op("vector", deps=(amxo,), name=f"Adn{h}")
            def _(e, h=h):
                return e.reciprocal(rs[0:1, h:h + 1], rs[0:1, h:h + 1])
            adno = ph.ops[-1]

            @ph.op("tensor", deps=(adno,), name=f"Adb{h}")
            def _(e, h=h):
                return e.matmul(ps_a[0:64, 64:65], pr.ones_f32r[0:1, 0:64],
                                rs[0:1, h:h + 1], start=True, stop=True)
            adbo = ph.ops[-1]

            @ph.op("vector", deps=(adbo,), name=f"V0{h}")
            def _(e, h=h):
                rbc = bap(ps_a, 64, [[512, 64], [0, 64]])
                return e.tensor_tensor(Vi[:, ts(h, 64)], AT[:, ts(h, 64)],
                                       rbc, ALU.mult)
            DN[h] = ph.ops[-1]
            dprev = DN[h]

        prev = {h: DN[h] for h in range(NHC)}
        chain = DN[NHC - 1]
        for it in range(6):
            for h in range(NHC):
                @ph.op("tensor", deps=(prev[h], chain), name=f"kv{it}_{h}")
                def _(e, h=h):
                    return e.matmul(ps_a[0:64, 0:64], AT[:, ts(h, 64)],
                                    Vi[:, ts(h, 64)], start=True, stop=True)
                o1 = ph.ops[-1]

                @ph.op("vector", deps=(o1,), name=f"kve{it}_{h}")
                def _(e, h=h):
                    return e.tensor_copy(kv[:, ts(h, 64)], ps_a[0:64, 0:64])
                o2 = ph.ops[-1]

                @ph.op("tensor", deps=(o2,), name=f"kvt{it}_{h}")
                def _(e, h=h):
                    return e.transpose(ps_b[0:64, 0:64], kv[:, ts(h, 64)],
                                       pr.ident_f32[0:64, 0:64])
                o3 = ph.ops[-1]

                @ph.op("vector", deps=(o3,), name=f"kvte{it}_{h}")
                def _(e, h=h):
                    e.tensor_copy(kvt[:, ts(h, 64)], ps_b[0:64, 0:64])
                    return e.scalar_tensor_tensor(
                        tm[:, ts(h, 64)], pr.eye64[:, :], 7.0, kv[:, ts(h, 64)],
                        ALU.mult, ALU.subtract)
                o4 = ph.ops[-1]

                @ph.op("tensor", deps=(o4,), name=f"u1{it}_{h}")
                def _(e, h=h):
                    return e.matmul(ps_a[0:64, 0:64], kvt[:, ts(h, 64)],
                                    tm[:, ts(h, 64)], start=True, stop=True)
                o5 = ph.ops[-1]

                @ph.op("vector", deps=(o5,), name=f"t2{it}_{h}")
                def _(e, h=h):
                    return e.scalar_tensor_tensor(
                        tm[:, ts(h, 64)], pr.eye64[:, :], 15.0,
                        ps_a[0:64, 0:64], ALU.mult, ALU.subtract)
                o6 = ph.ops[-1]

                @ph.op("tensor", deps=(o6,), name=f"u2{it}_{h}")
                def _(e, h=h):
                    return e.matmul(ps_a[0:64, 0:64], kvt[:, ts(h, 64)],
                                    tm[:, ts(h, 64)], start=True, stop=True)
                o7 = ph.ops[-1]

                @ph.op("vector", deps=(o7,), name=f"t3{it}_{h}")
                def _(e, h=h):
                    return e.scalar_tensor_tensor(
                        tm[:, ts(h, 64)], pr.eye64[:, :], 13.0,
                        ps_a[0:64, 0:64], ALU.mult, ALU.subtract)
                o8 = ph.ops[-1]

                @ph.op("tensor", deps=(o8,), name=f"vt{it}_{h}")
                def _(e, h=h):
                    return e.transpose(ps_b[0:64, 0:64], Vi[:, ts(h, 64)],
                                       pr.ident_f32[0:64, 0:64])
                o9 = ph.ops[-1]

                @ph.op("vector", deps=(o9,), name=f"vte{it}_{h}")
                def _(e, h=h):
                    return e.tensor_copy(vt[:, ts(h, 64)], ps_b[0:64, 0:64])
                o10 = ph.ops[-1]

                @ph.op("tensor", deps=(o10,), name=f"vn{it}_{h}")
                def _(e, h=h):
                    return e.matmul(ps_a[0:64, 0:64], vt[:, ts(h, 64)],
                                    tm[:, ts(h, 64)], start=True, stop=True)
                o11 = ph.ops[-1]

                @ph.op("vector", deps=(o11,), name=f"vne{it}_{h}")
                def _(e, h=h):
                    return e.tensor_scalar_mul(Vi[:, ts(h, 64)],
                                               ps_a[0:64, 0:64], 0.25)
                prev[h] = ph.ops[-1]
                chain = prev[h]

        ZO = {}
        for h in range(NHC):
            hp, hh = divmod(h, 2)
            base = hh * 64
            nexp = {}
            NZd = {}
            for g in range(NTT // 8):
                bf = g % 2

                @ph.op("tensor",
                       deps=(lmk[hp], nexp.get(g - 2) if g >= 2 else None,
                             (ZO.get(h - 1) if g < 2 else None)),
                       name=f"ns{h}_{g}")
                def _(e, h=h, hp=hp, base=base, g=g, bf=bf):
                    for s8 in range(8):
                        tt = g * 8 + s8
                        ins = e.matmul(ps_n[:, bf * 512 + s8 * 64:
                                            bf * 512 + (s8 + 1) * 64],
                                       KTv[base:base + 64, hp, ts(tt, P)],
                                       qlTv[base:base + 64, hp],
                                       start=True, stop=True,
                                       skip_group_check=True)
                    return ins
                nso = ph.ops[-1]

                @ph.op("scalar", deps=(nso, NZd.get(g - 2)),
                       name=f"ne{h}_{g}")
                def _(e, g=g, bf=bf):
                    return e.activation(Pn[:, ts(bf, 512)],
                                        ps_n[:, ts(bf, 512)], AF.Exp)
                nexp[g] = ph.ops[-1]

                @ph.op("tensor", deps=(nexp[g],), name=f"nz{h}_{g}")
                def _(e, h=h, g=g, bf=bf, first=(g == 0)):
                    for s8 in range(8):
                        tt = g * 8 + s8
                        st = first and s8 == 0
                        sp = (g == 3) and (s8 == 7)
                        ins = e.matmul(
                            ps_z[0:64, 0:65],
                            Pn[:, bf * 512 + s8 * 64:bf * 512 + (s8 + 1) * 64],
                            VTv[:, tt, h, :],
                            start=st, stop=sp, skip_group_check=True)
                    return ins
                zo = ph.ops[-1]
                NZd[g] = zo
            ZO[h] = zo

            @ph.op("vector", deps=(ZO[h],), name=f"zr{h}")
            def _(e, h=h):
                return e.reciprocal(rs[:, h:h + 1], ps_z[0:64, 64:65])
            zro = ph.ops[-1]

            @ph.op("vector", deps=(zro,), name=f"zn{h}")
            def _(e, h=h):
                rbc = bap(rs, h, [[NHC, 64], [0, 64]])
                return e.tensor_tensor(Zn[:, ts(h, 64)], ps_z[0:64, 0:64],
                                       rbc, ALU.mult)
            ZO[h] = ph.ops[-1]

        W2O = {}
        wprev = prev[NHC - 1]
        for h in range(NHC):
            @ph.op("tensor", deps=(prev[h], ZO[h], wprev), name=f"pht{h}")
            def _(e, h=h):
                return e.transpose(ps_b[0:64, 0:64], Vi[:, ts(h, 64)],
                                   pr.ident_f32[0:64, 0:64])
            o1 = ph.ops[-1]

            @ph.op("vector", deps=(o1,), name=f"phe{h}")
            def _(e, h=h):
                return e.tensor_copy(vt[:, ts(h, 64)], ps_b[0:64, 0:64])
            o2 = ph.ops[-1]

            @ph.op("tensor", deps=(o2,), name=f"w2m{h}")
            def _(e, h=h):
                return e.matmul(ps_a[0:64, 0:64], vt[:, ts(h, 64)],
                                Zn[:, ts(h, 64)], start=True, stop=True)
            o3 = ph.ops[-1]

            @ph.op("vector", deps=(o3,), name=f"w2e{h}")
            def _(e, h=h):
                e.tensor_copy(W2[:, h * 65:h * 65 + 64], ps_a[0:64, 0:64])
                return e.memset(W2[:, h * 65 + 64:h * 65 + 65], 1.0)
            W2O[h] = ph.ops[-1]
            wprev = W2O[h]

        if dbg is not None:
            phd_prev = wprev

            @ph.op("gpsimd", deps=(wprev, ZO[NHC - 1], prev[NHC - 1]),
                   inc=16, name="dbgA")
            def _(e):
                return e.dma_start(dbg["A"].ap()[:, :], A[:])
            d1 = ph.ops[-1]

            @ph.op("gpsimd", deps=(d1,), inc=16, name="dbgVi")
            def _(e):
                return e.dma_start(dbg["Vi"].ap()[:, :], Vi[:])
            d2 = ph.ops[-1]

            @ph.op("gpsimd", deps=(d2,), inc=16, name="dbgZn")
            def _(e):
                return e.dma_start(dbg["Zn"].ap()[:, :], Zn[:])
            d3 = ph.ops[-1]

            @ph.op("gpsimd", deps=(d3,), inc=16, name="dbgW2")
            def _(e):
                return e.dma_start(dbg["W2"].ap()[:, :], W2[:])
            d4 = ph.ops[-1]

            @ph.op("gpsimd", deps=(d4,), inc=16, name="dbgkl")
            def _(e):
                return e.dma_start(dbg["klT"].ap()[:, :], klT[:])
            d5 = ph.ops[-1]

            @ph.op("gpsimd", deps=(d5,), inc=16, name="dbgql")
            def _(e):
                return e.dma_start(dbg["qlT"].ap()[:, :], qlT[:])

        STO = {}
        prev_scl = None
        last_nexp_all = ZO[NHC - 1]
        for h in range(NHC):
            hp, hh = divmod(h, 2)
            base = hh * 64
            for qb in range(NQB):
                L = h * NQB + qb
                bf = L % 2

                @ph.op("tensor",
                       deps=(lmk[hp],) +
                            ((STO.get(("e", L - 2)),) if L >= 2 else
                             (last_nexp_all, wprev)),
                       name=f"es{h}_{qb}")
                def _(e, h=h, hp=hp, base=base, qb=qb, bf=bf):
                    return e.matmul(ps_n[0:64, ts(bf, 512)],
                                    klTv[base:base + 64, hp],
                                    QTv[base:base + 64, hp, ts(qb, 512)],
                                    start=True, stop=True)
                eso = ph.ops[-1]

                @ph.op("scalar", deps=(eso,) +
                       ((STO.get(("y", L - 2)),) if L >= 2 else ()),
                       name=f"ee{h}_{qb}")
                def _(e, bf=bf):
                    return e.activation(EL[0:64, ts(bf, 512)],
                                        ps_n[0:64, ts(bf, 512)], AF.Exp)
                eeo = ph.ops[-1]
                STO[("e", L)] = eeo

                @ph.op("tensor", deps=(W2O[h], eeo) +
                       ((prev_scl,) if prev_scl is not None else ()),
                       name=f"ym{h}_{qb}")
                def _(e, h=h, bf=bf):
                    return e.matmul(ps_y[0:65, ts(bf, 512)],
                                    W2[:, h * 65:(h + 1) * 65],
                                    EL[0:64, ts(bf, 512)],
                                    start=True, stop=True)
                ymo = ph.ops[-1]
                STO[("y", L)] = ymo

                @ph.op("vector", deps=(ymo,), name=f"yr{h}_{qb}")
                def _(e, bf=bf):
                    return e.reciprocal(rcl[64:65, 0:512],
                                        ps_y[64:65, ts(bf, 512)])
                yro = ph.ops[-1]

                @ph.op("tensor", deps=(yro,), name=f"yb{h}_{qb}")
                def _(e, bf=bf):
                    return e.matmul(ps_bc[0:64, 0:512],
                                    pr.ones_f32r[64:65, 0:64],
                                    rcl[64:65, 0:512], start=True, stop=True)
                ybo = ph.ops[-1]

                @ph.op("scalar", deps=(ymo,), name=f"ye{h}_{qb}")
                def _(e, bf=bf):
                    return e.copy(att[:, :], ps_y[0:64, ts(bf, 512)])
                yeo = ph.ops[-1]

                @ph.op("vector", deps=(ybo, yeo) +
                       ((STO.get(("s", L - 2)),) if L >= 2 else ()),
                       name=f"ys{h}_{qb}")
                def _(e, bf=bf):
                    return e.tensor_mul(attf[:, ts(bf, 512)], att[:, :],
                                        ps_bc[0:64, 0:512])
                yso = ph.ops[-1]
                prev_scl = yso

                @ph.op("gpsimd", deps=(yso,), inc=16, name=f"yst{h}_{qb}")
                def _(e, h=h, qb=qb, bf=bf):
                    return e.dma_start(agv[:, h, ts(qb, 512)],
                                       attf[:, ts(bf, 512)])
                STO[("s", L)] = ph.ops[-1]

        @ph.op("gpsimd", deps=(STO[("s", NHC * NQB - 1)],), name="fin")
        def _(e):
            return e.memset(rcl[0:1, 0:1], 0.0)
        ph.emit()


def emit_slot_l1_attn(pr, x_in, x_out, wqkv, wproj, pfx="s2", dbg=None):
    nc = pr.nc
    ag_a_in = nc.dram_tensor(f"{pfx}_agai", [D, TOWN], BF16)
    ag_a_out = nc.dram_tensor(f"{pfx}_agao", [2 * D, TOWN], BF16)
    ag_t_in = nc.dram_tensor(f"{pfx}_agti", [NHC * HD, T], BF16)
    ag_t_out = nc.dram_tensor(f"{pfx}_agto", [2 * NHC * HD, T], BF16)
    emit_ln_to_dram(pr, x_in, ag_a_in.ap(), f"{pfx}_ln1")
    pr.pair_allgather(ag_a_in, ag_a_out)
    with ExitStack() as es:
        wqkv_sb = es.enter_context(
            nc.sbuf_tensor(f"{pfx}_wqkv", [P, DC * 1536], BF16))
        QT = es.enter_context(nc.sbuf_tensor(f"{pfx}_QT", [P, NHP * T], BF16))
        KT = es.enter_context(nc.sbuf_tensor(f"{pfx}_KT", [P, NHP * T], BF16))
        VT = es.enter_context(nc.sbuf_tensor(f"{pfx}_VT", [P, NTT * NHC * 65], BF16))
        phw = Phase(pr, f"{pfx}_wload")
        load_weight_dmajor(pr, phw, wqkv, wqkv_sb, "wqkv")

        @phw.op("gpsimd", deps=(phw.ops[-1],), name="wfin")
        def _(e):
            return e.memset(QT[0:1, 0:1], 0.0)
        phw.emit()
        emit_nys_qkv(pr, ag_a_out, wqkv_sb, QT, KT, VT, f"{pfx}_nqkv")
        emit_nystrom_attn(pr, QT, KT, VT, ag_t_in, f"{pfx}_nys", dbg=dbg)
    pr.pair_allgather(ag_t_in, ag_t_out)
    if dbg is not None:
        with pr.nc.Block() as block:
            @block.gpsimd
            def _(g):
                g.dma_start(dbg["agt"].ap()[:, :], ag_t_out.ap()[:, :]
                            ).then_inc(pr.s_cc.h, 16)
                pr.s_cc.n += 16
                g.wait_ge(pr.s_cc.h, pr.s_cc.n)
    emit_oproj_block(pr, ag_t_out, wproj, x_in, x_out, f"{pfx}_opj")


def build_l1_attn(debug=False):
    pr = Prog()
    nc = pr.nc
    x_in = nc.dram_tensor("x_in", [D, TOWN], F32, kind="ExternalInput").ap()
    wqkv = nc.dram_tensor("l1_wqkv", [D, 3 * NHC * HD], F32,
                          kind="ExternalInput").ap()
    wproj = nc.dram_tensor("l1_wproj", [D, D], F32, kind="ExternalInput").ap()
    x_out = nc.dram_tensor("x_out", [D, TOWN], F32, kind="ExternalOutput").ap()
    dbg = None
    if debug:
        dbg = {"A": nc.dram_tensor("dbg_A", [64, NHC * 64], F32, kind="ExternalOutput"),
               "Vi": nc.dram_tensor("dbg_Vi", [64, NHC * 64], F32, kind="ExternalOutput"),
               "Zn": nc.dram_tensor("dbg_Zn", [64, NHC * 64], F32, kind="ExternalOutput"),
               "W2": nc.dram_tensor("dbg_W2", [64, NHC * 65], BF16, kind="ExternalOutput"),
               "klT": nc.dram_tensor("dbg_klT", [P, NHP * NL], BF16, kind="ExternalOutput"),
               "qlT": nc.dram_tensor("dbg_qlT", [P, NHP * NL], BF16, kind="ExternalOutput"),
               "agt": nc.dram_tensor("dbg_agt", [2 * NHC * HD, T], BF16, kind="ExternalOutput")}
    pr.load_consts(rope=False)
    emit_slot_l1_attn(pr, x_in, x_out, wqkv, wproj, dbg=dbg)
    pr.close()
    return pr.nc


def stage_l1_attn(inputs, x_maps):
    wa = np.asarray(inputs["L1_attn_W_attn"], np.float32)
    wp = np.ascontiguousarray(np.asarray(inputs["L1_attn_W_proj"], np.float32))
    maps = []
    for c in range(8):
        p = c % 2
        cols = []
        for part in range(3):
            lo = part * D + p * NHC * HD
            cols.append(wa[:, lo:lo + NHC * HD])
        wslice = np.ascontiguousarray(np.concatenate(cols, axis=1))
        maps.append({"x_in": x_maps[c], "l1_wqkv": wslice, "l1_wproj": wp,
                     "c_toff": np.array([[(c % 2) * (TOWN // 512)]], np.int32)})
    return maps



def emit_rms_to_dram(pr, src_dram, dst_dram, tag, src_f32=True):
    nc = pr.nc
    NCH = TOWN // 512
    with ExitStack() as es:
        sb = lambda n, s, d: es.enter_context(nc.sbuf_tensor(f"{tag}_{n}", s, d))
        xb = sb("xb", [P, 2 * DC * 512], BF16)
        xq = sb("xq", [P, DC * 512], BF16)
        rr = sb("rr", [P, 512], F32)
        ab = sb("ab", [P, 2 * DC * 512], BF16)
        ps_s = es.enter_context(nc.psum_tensor(f"{tag}_ps", [P, 512], F32))

        x3 = src_dram.rearrange("(c p) t -> p c t", p=P)
        a3 = dst_dram.rearrange("(c p) t -> p c t", p=P)
        xbv = xb[:].rearrange("p (b c t) -> p b c t", b=2, c=DC)
        abv = ab[:].rearrange("p (b c t) -> p b c t", b=2, c=DC)

        ph = Phase(pr, tag)
        ld = {}; sq = {}; smm = {}; comb = {}; rstd = {}; norm = {}; st = {}
        for i in range(NCH):
            bf = i % 2

            @ph.op("gpsimd", deps=(norm.get(i - 2),), inc=16, name=f"ld{i}")
            def _(e, i=i, bf=bf):
                return e.dma_start(xbv[:, bf], x3[:, :, ts(i, 512)])
            ld[i] = ph.ops[-1]

            @ph.op("vector", deps=(ld[i], smm.get(i - 1)), name=f"sq{i}")
            def _(e, bf=bf):
                return e.tensor_mul(xq[:], xbv[:, bf].rearrange("p c t -> p (c t)"),
                                    xbv[:, bf].rearrange("p c t -> p (c t)"))
            sq[i] = ph.ops[-1]

            @ph.op("tensor", deps=(sq[i], rstd.get(i - 1)), name=f"smm{i}")
            def _(e):
                for dc in range(DC):
                    ins = e.matmul(ps_s[:], pr.ones_bf[:], xq[:, ts(dc, 512)],
                                   start=(dc == 0), stop=(dc == DC - 1),
                                   skip_group_check=True)
                return ins
            smm[i] = ph.ops[-1]

            @ph.op("scalar", deps=(smm[i],), name=f"rstd{i}")
            def _(e):
                e.activation(rr[:], ps_s[:], AF.Ln, bias=pr.eps_rms,
                             scale=1.0 / D)
                return e.activation(rr[:], rr[:], AF.Exp, scale=-0.5)
            rstd[i] = ph.ops[-1]

            @ph.op("vector", deps=(rstd[i], st.get(i - 2)), name=f"norm{i}")
            def _(e, bf=bf):
                for dc in range(DC):
                    ins = e.tensor_mul(abv[:, bf, dc], xbv[:, bf, dc], rr[:])
                return ins
            norm[i] = ph.ops[-1]

            @ph.op("gpsimd", deps=(norm[i],), inc=16, name=f"st{i}")
            def _(e, i=i, bf=bf):
                return e.dma_start(a3[:, :, ts(i, 512)], abv[:, bf])
            st[i] = ph.ops[-1]

        @ph.op("gpsimd", deps=(st[NCH - 1],), name="fin")
        def _(e):
            return e.memset(rr[0:1, 0:1], 0.0)
        ph.emit()


def _rope_l1(nc, e, src, scratch, out_bf, cosT, sinT, tt, nhead):
    n = nhead
    x1 = bap(src, 0, [[512, P], [64, n], [1, 32]])
    x2 = bap(src, 32, [[512, P], [64, n], [1, 32]])
    t1 = bap(scratch, 0, [[512, P], [32, n], [1, 32]])
    t2 = bap(scratch, n * 32, [[512, P], [32, n], [1, 32]])
    oe = bap(out_bf, 0, [[n * 64, P], [64, n], [1, 32]])
    oo = bap(out_bf, 32, [[n * 64, P], [64, n], [1, 32]])
    cb = bap(cosT, tt * 32, [[1024, P], [0, n], [1, 32]])
    sn = bap(sinT, tt * 32, [[1024, P], [0, n], [1, 32]])
    e.tensor_mul(t1, x1, cb)
    e.tensor_mul(t2, x2, sn)
    e.tensor_add(oe, t1, t2)
    e.tensor_mul(t1, x1, sn)
    e.tensor_mul(t2, x2, cb)
    return e.tensor_sub(oo, t2, t1)


def emit_gqa_qkv(pr, ag_a_out, wq_sb, wkv_sb, qg_rep, kg_rep, QT, KTg, VTg,
                 tag):
    nc = pr.nc
    with ExitStack() as es:
        sb = lambda n, s, d: es.enter_context(nc.sbuf_tensor(f"{tag}_{n}", s, d))
        at = sb("at", [P, 2 * DC * P], BF16)
        qf = sb("qf", [P, 512], F32)
        sf = sb("sf", [P, 512], F32)
        qms = sb("qms", [P, NHC + 2], F32)
        qrob = sb("qrob", [P, 512], BF16)
        krob = sb("krob", [P, P], BF16)
        ps_q = es.enter_context(nc.psum_tensor(f"{tag}_pq", [P, 512], F32))
        ps_kv = es.enter_context(nc.psum_tensor(f"{tag}_pkv", [P, 256], F32))
        ps_tq = es.enter_context(nc.psum_tensor(f"{tag}_ptq", [P, 8 * P], BF16))
        ps_tk = es.enter_context(nc.psum_tensor(f"{tag}_ptk", [P, 8 * P], BF16))

        agv = ag_a_out.ap().rearrange("(h c p) t -> p h c t", p=P, c=DC)
        atv = at[:].rearrange("p (b c q) -> p b c q", b=2, c=DC)
        QTv = QT[:].rearrange("p (hp t) -> p hp t", hp=NHP)
        VTv = VTg[:].rearrange("p (tt h e) -> p tt h e", tt=NTT, h=2)
        wqv = wq_sb[:].rearrange("p (c n) -> p c n", c=DC)
        wkvv = wkv_sb[:].rearrange("p (c n) -> p c n", c=DC)

        ph = Phase(pr, tag)

        @ph.op("vector", name="vones")
        def _(e):
            return e.memset(bap(VTg, 64, [[NTT * 2 * 65, P],
                                          [65, NTT * 2], [1, 1]]), 1.0)
        vones = ph.ops[-1]
        lda = {}; mmq = {}; mmkv = {}; qev = {}; kev = {}
        trq = {}; trk = {}; evq = {}; evk = {}; evv = {}; roq = {}; rok = {}
        for tt in range(NTT):
            bf = tt % 2
            h = (tt * P) // TOWN
            co = (tt * P) % TOWN

            @ph.op("sync", deps=(mmkv.get(tt - 2),), inc=16, name=f"lda{tt}")
            def _(e, h=h, co=co, bf=bf):
                return e.dma_start(atv[:, bf], agv[:, h, :, co:co + P])
            lda[tt] = ph.ops[-1]

            @ph.op("tensor", deps=(lda[tt], qev.get(tt - 1)), name=f"mmq{tt}")
            def _(e, bf=bf):
                for dc in range(DC):
                    ins = e.matmul(ps_q[:], atv[:, bf, dc], wqv[:, dc],
                                   start=(dc == 0), stop=(dc == DC - 1))
                return ins
            mmq[tt] = ph.ops[-1]

            @ph.op("tensor", deps=(kev.get(tt - 1), evv.get(tt - 1)),
                   name=f"mmkv{tt}")
            def _(e, bf=bf):
                for dc in range(DC):
                    ins = e.matmul(ps_kv[:], atv[:, bf, dc], wkvv[:, dc],
                                   start=(dc == 0), stop=(dc == DC - 1))
                return ins
            mmkv[tt] = ph.ops[-1]

            @ph.op("vector", deps=(mmq[tt],), name=f"qev{tt}")
            def _(e):
                return e.tensor_copy(qf[:], ps_q[:])
            qev[tt] = ph.ops[-1]

            @ph.op("vector", deps=(qev[tt],), name=f"qsq{tt}")
            def _(e):
                return e.tensor_mul(sf[:], qf[:], qf[:])
            o = ph.ops[-1]

            @ph.op("vector", deps=(o,), name=f"qrd{tt}")
            def _(e):
                return e.tensor_reduce(qms[:, 0:NHC],
                                       sf[:].rearrange("p (h e) -> p h e", h=NHC),
                                       mybir.AxisListType.X, ALU.add)
            o = ph.ops[-1]

            @ph.op("scalar", deps=(o,), name=f"qrs{tt}")
            def _(e):
                e.activation(qms[:, 0:NHC], qms[:, 0:NHC], AF.Ln,
                             bias=pr.eps_rms, scale=1.0 / HD)
                return e.activation(qms[:, 0:NHC], qms[:, 0:NHC], AF.Exp,
                                    scale=-0.5)
            o = ph.ops[-1]

            @ph.op("vector", deps=(o,), name=f"qrg{tt}")
            def _(e):
                return e.tensor_mul(qms[:, 0:NHC], qms[:, 0:NHC], qg_rep[:])
            o = ph.ops[-1]

            @ph.op("vector", deps=(o,), name=f"qap{tt}")
            def _(e):
                rbc = bap(qms, 0, [[NHC + 2, P], [1, NHC], [0, HD]])
                q3 = qf[:].rearrange("p (h e) -> p h e", h=NHC)
                return e.tensor_mul(q3, q3, rbc)
            o = ph.ops[-1]

            @ph.op("vector", deps=(o,), name=f"roq{tt}")
            def _(e, tt=tt):
                return _rope_l1(nc, e, qf, sf, qrob, pr.cos, pr.sin, tt, NHC)
            roq[tt] = ph.ops[-1]

            @ph.op("tensor", deps=(roq[tt], evq.get(tt - 1)), name=f"trq{tt}")
            def _(e):
                for j in range(NHP):
                    e.transpose(ps_tq[0:64, ts(j, P)],
                                qrob[:, j * 64:(j + 1) * 64],
                                pr.ident[:], tile_position=(0, 0))
                    ins = e.transpose(ps_tq[64:128, ts(j, P)],
                                      qrob[:, (j + 4) * 64:(j + 5) * 64],
                                      pr.ident[:], tile_position=(0, 64))
                return ins
            trq[tt] = ph.ops[-1]

            @ph.op("scalar", deps=(trq[tt],), name=f"evq{tt}")
            def _(e, tt=tt):
                for j in range(NHP):
                    ins = e.copy(QTv[:, j, ts(tt, P)], ps_tq[:, ts(j, P)])
                return ins
            evq[tt] = ph.ops[-1]

            @ph.op("vector", deps=(mmkv[tt], roq[tt]), name=f"kev{tt}")
            def _(e):
                return e.tensor_copy(qf[:, 0:128], ps_kv[:, 0:128])
            kev[tt] = ph.ops[-1]

            @ph.op("vector", deps=(kev[tt],), name=f"ksq{tt}")
            def _(e):
                return e.tensor_mul(sf[:, 0:128], qf[:, 0:128], qf[:, 0:128])
            o = ph.ops[-1]

            @ph.op("vector", deps=(o,), name=f"krd{tt}")
            def _(e):
                return e.tensor_reduce(qms[:, NHC:NHC + 2],
                                       bap(sf, 0, [[512, P], [64, 2], [1, HD]]),
                                       mybir.AxisListType.X, ALU.add)
            o = ph.ops[-1]

            @ph.op("scalar", deps=(o,), name=f"krs{tt}")
            def _(e):
                e.activation(qms[:, NHC:NHC + 2], qms[:, NHC:NHC + 2], AF.Ln,
                             bias=pr.eps_rms, scale=1.0 / HD)
                return e.activation(qms[:, NHC:NHC + 2], qms[:, NHC:NHC + 2],
                                    AF.Exp, scale=-0.5)
            o = ph.ops[-1]

            @ph.op("vector", deps=(o,), name=f"krg{tt}")
            def _(e):
                return e.tensor_mul(qms[:, NHC:NHC + 2], qms[:, NHC:NHC + 2],
                                    kg_rep[:])
            o = ph.ops[-1]

            @ph.op("vector", deps=(o,), name=f"kap{tt}")
            def _(e):
                rbc = bap(qms, NHC, [[NHC + 2, P], [1, 2], [0, HD]])
                k3 = bap(qf, 0, [[512, P], [64, 2], [1, HD]])
                return e.tensor_mul(k3, k3, rbc)
            o = ph.ops[-1]

            @ph.op("vector", deps=(o,), name=f"rok{tt}")
            def _(e, tt=tt):
                return _rope_l1(nc, e, qf, sf, krob, pr.cos, pr.sin, tt, 2)
            rok[tt] = ph.ops[-1]

            @ph.op("tensor", deps=(rok[tt], evk.get(tt - 1)), name=f"trk{tt}")
            def _(e):
                return e.transpose(ps_tk[:, 0:P], krob[:], pr.ident[:])
            trk[tt] = ph.ops[-1]

            @ph.op("scalar", deps=(trk[tt],), name=f"evk{tt}")
            def _(e, tt=tt):
                return e.copy(KTg[:, ts(tt, P)], ps_tk[:, 0:P])
            evk[tt] = ph.ops[-1]

            @ph.op("vector", deps=(mmkv[tt], vones), name=f"evv{tt}")
            def _(e, tt=tt):
                return e.tensor_copy(
                    VTv[:, tt, :, 0:64],
                    ps_kv[:, 128:256].rearrange("p (h e) -> p h e", h=2))
            evv[tt] = ph.ops[-1]
        ph.emit()


def emit_oproj_residual_plain(pr, ag_t_out, wo_sb, g_dram, h1_dram, tag):
    nc = pr.nc
    NST = TOWN // 512
    with ExitStack() as es:
        sb = lambda n, s, d: es.enter_context(nc.sbuf_tensor(f"{tag}_{n}", s, d))
        ats = sb("ats", [P, 2 * DC * 512], BF16)
        gf = sb("gf", [P, 2 * DC * 512], F32)
        ho = sb("ho", [P, 2 * DC * 512], F32)
        ps_y = es.enter_context(nc.psum_tensor(f"{tag}_py", [P, 2 * 512], F32))

        agv = ag_t_out.ap().rearrange("(c p) t -> p c t", p=P)
        g3 = g_dram.rearrange("(c p) t -> p c t", p=P)
        o3 = h1_dram.rearrange("(c p) t -> p c t", p=P)
        atsv = ats[:].rearrange("p (b c t) -> p b c t", b=2, c=DC)
        gfv = gf[:].rearrange("p (b c t) -> p b c t", b=2, c=DC)
        hov = ho[:].rearrange("p (b c t) -> p b c t", b=2, c=DC)
        wov = wo_sb[:].rearrange("p (c n) -> p c n", c=DC)

        ph = Phase(pr, tag)
        lda = {}; ldg = {}; mmy = {}; addo = {}; sto = {}
        for i in range(NST):
            bf = i % 2

            @ph.op("sync", deps=(mmy.get((i - 2, DC - 1)),), inc=16,
                   name=f"lda{i}")
            def _(e, i=i, bf=bf):
                with e.register(f"{tag}r{i}") as r:
                    e.load(r, pr.toff_sb[0:1, 0:1])
                    e.reg_add(r, r, i)
                    e.reg_mul(r, r, 512)
                    v = AnnotatedLowered(r, 0, T - 512).to_runtime()
                    return e.dma_start(atsv[:, bf],
                                       agv[:, :, bass.DynSlice(v, 512)])
            lda[i] = ph.ops[-1]

            @ph.op("sync", deps=(addo.get((i - 2, DC - 1)),), inc=16,
                   name=f"ldg{i}")
            def _(e, i=i, bf=bf):
                return e.dma_start(gfv[:, bf], g3[:, :, ts(i, 512)])
            ldg[i] = ph.ops[-1]

            for oc in range(DC):
                yb2 = oc % 2

                @ph.op("tensor",
                       deps=(lda[i], addo.get((i, oc - 2)),
                             addo.get((i - 1, DC - 1)) if oc < 2 else None),
                       name=f"mmy{i}_{oc}")
                def _(e, i=i, oc=oc, bf=bf, yb2=yb2):
                    for ec in range(DC):
                        ins = e.matmul(ps_y[:, ts(yb2, 512)],
                                       wov[:, ec, ts(oc, P)], atsv[:, bf, ec],
                                       start=(ec == 0), stop=(ec == DC - 1),
                                       skip_group_check=True)
                    return ins
                mmy[(i, oc)] = ph.ops[-1]

                @ph.op("vector", deps=(mmy[(i, oc)], ldg[i], sto.get(i - 2)),
                       name=f"add{i}_{oc}")
                def _(e, i=i, oc=oc, bf=bf, yb2=yb2):
                    return e.tensor_add(hov[:, bf, oc], gfv[:, bf, oc],
                                        ps_y[:, ts(yb2, 512)])
                addo[(i, oc)] = ph.ops[-1]

            @ph.op("gpsimd", deps=(addo[(i, DC - 1)],), inc=16, name=f"sto{i}")
            def _(e, i=i, bf=bf):
                return e.dma_start(o3[:, :, ts(i, 512)], hov[:, bf])
            sto[i] = ph.ops[-1]

        @ph.op("gpsimd", deps=(sto[NST - 1],), name="fin")
        def _(e):
            return e.memset(gf[0:1, 0:1], 0.0)
        ph.emit()


def emit_gqa(pr, g_dram, h1_dram, wq, wkv, wo, gains, _unused, tag):
    nc = pr.nc
    h0_dram = nc.dram_tensor(f"{tag}_h0", [D, TOWN], BF16)
    ag_a_out = nc.dram_tensor(f"{tag}_aga", [2 * D, TOWN], BF16)
    ag_t_in = nc.dram_tensor(f"{tag}_agt", [NHC * HD, T], BF16)
    ag_t_out = nc.dram_tensor(f"{tag}_agto", [2 * NHC * HD, T], BF16)

    emit_rms_to_dram(pr, g_dram.ap(), h0_dram.ap(), f"{tag}_rms")
    pr.pair_allgather(h0_dram, ag_a_out)
    with ExitStack() as es:
        wq_sb = es.enter_context(nc.sbuf_tensor(f"{tag}_wq", [P, DC * NHC * HD], BF16))
        wkv_sb = es.enter_context(nc.sbuf_tensor(f"{tag}_wkv", [P, DC * 256], BF16))
        qg_rep = es.enter_context(nc.sbuf_tensor(f"{tag}_qg", [P, NHC], F32))
        kg_rep = es.enter_context(nc.sbuf_tensor(f"{tag}_kg", [P, 2], F32))
        gl = es.enter_context(nc.sbuf_tensor(f"{tag}_gl", [1, NHC + 2], F32))
        QT = es.enter_context(nc.sbuf_tensor(f"{tag}_QT", [P, NHP * T], BF16))
        KTg = es.enter_context(nc.sbuf_tensor(f"{tag}_KT", [P, T], BF16))
        VTg = es.enter_context(nc.sbuf_tensor(f"{tag}_VT", [P, NTT * 2 * 65], BF16))

        with nc.psum_tensor(f"{tag}_pg", [P, NHC + 2], F32) as ps_g:
            phw = Phase(pr, "gqw")
            w1 = load_weight_dmajor(pr, phw, wq, wq_sb, "wq")
            w2 = load_weight_dmajor(pr, phw, wkv, wkv_sb, "wkv")

            @phw.op("sync", inc=16, name="gains")
            def _(e):
                return e.dma_start(gl[0:1, :], gains[:, :])
            w3 = phw.ops[-1]

            @phw.op("tensor", deps=(w3,), name="gbc")
            def _(e):
                return e.matmul(ps_g[:, :], pr.ones_f32r[0:1, :], gl[0:1, :],
                                start=True, stop=True)
            w4 = phw.ops[-1]

            @phw.op("vector", deps=(w4,), name="gev")
            def _(e):
                e.tensor_copy(qg_rep[:], ps_g[:, 0:NHC])
                return e.tensor_copy(kg_rep[:], ps_g[:, NHC:NHC + 2])
            w5 = phw.ops[-1]

            @phw.op("gpsimd", deps=(w1, w2, w5), name="wfin")
            def _(e):
                return e.memset(gl[0:1, 0:1], 0.0)
            phw.emit()

        emit_gqa_qkv(pr, ag_a_out, wq_sb, wkv_sb, qg_rep, kg_rep,
                     QT, KTg, VTg, f"{tag}_qkv")
        KTgv = KTg[:]
        VTvv = VTg[:].rearrange("p (tt h e) -> p tt h e", tt=NTT, h=2)
        emit_causal_attention(
            pr, QT, QT, VTg, ag_t_in, scale=1.0 / 8.0, tag=f"{tag}_att",
            k_fn=lambda KTv, hp, half, kt: KTgv[half * 64:(half + 1) * 64,
                                                ts(kt, P)],
            v_fn=lambda VTv, hp, half, kt: VTvv[:, kt, half, :])
    pr.pair_allgather(ag_t_in, ag_t_out)
    emit_oproj_plain_block(pr, ag_t_out, wo, g_dram.ap(), h1_dram.ap(),
                           f"{tag}_opj")


def emit_oproj_plain_block(pr, ag_t_out, wo, g_dram, h1_dram, tag):
    nc = pr.nc
    with ExitStack() as es:
        wo_sb = es.enter_context(nc.sbuf_tensor(f"{tag}_wo", [P, DC * D], BF16))
        phw = Phase(pr, f"{tag}_wl")
        wo_op = load_weight_dmajor(pr, phw, wo, wo_sb, "wo")

        @phw.op("gpsimd", deps=(wo_op,), name="wofin")
        def _(e):
            return e.memset(wo_sb[0:1, 0:1], 0.0)
        phw.emit()
        emit_oproj_residual_plain(pr, ag_t_out, wo_sb, g_dram, h1_dram, tag)



def build_l1_ffn_gqa():
    pr = Prog()
    nc = pr.nc
    x_in = nc.dram_tensor("x_in", [D, TOWN], F32, kind="ExternalInput").ap()
    wq = nc.dram_tensor("gq_wq", [D, NHC * HD], F32, kind="ExternalInput").ap()
    wkv = nc.dram_tensor("gq_wkv", [D, 256], F32, kind="ExternalInput").ap()
    wo = nc.dram_tensor("gq_wo", [D, D], F32, kind="ExternalInput").ap()
    gains = nc.dram_tensor("gq_gains", [1, NHC + 2], F32, kind="ExternalInput").ap()
    h1_out = nc.dram_tensor("h1_out", [D, TOWN], F32, kind="ExternalOutput")
    g_dram = nc.dram_tensor("g_dram", [D, TOWN], F32)

    pr.load_consts(rope=True)
    emit_ln_to_dram(pr, x_in, g_dram.ap(), "ln7", out_dtype=F32)
    emit_gqa(pr, g_dram, h1_out, wq, wkv, wo, gains, None, "gqa")
    pr.close()
    return pr.nc


def stage_l1_ffn_gqa(inputs, x_maps):
    wq = np.asarray(inputs["L1_ffn_wq"], np.float32)
    wk = np.asarray(inputs["L1_ffn_wk"], np.float32)
    wv = np.asarray(inputs["L1_ffn_wv"], np.float32)
    wo = np.asarray(inputs["L1_ffn_wo"], np.float32)
    qg = np.asarray(inputs["L1_ffn_q_gain"], np.float32)
    kg = np.asarray(inputs["L1_ffn_k_gain"], np.float32)
    maps = []
    r = np.arange(D)
    pglob = r // 512
    j = (r % 512) // 128
    half = (r % 128) // 64
    dv = r % 64
    ghead = pglob * NHC + j + 4 * half
    perm = ghead * 64 + dv
    woT = np.ascontiguousarray(wo.T)
    woT_perm = np.ascontiguousarray(woT[perm])
    for c in range(8):
        p = c % 2
        heads = range(p * NHC, (p + 1) * NHC)
        rows_q = np.concatenate([np.arange(h * HD, (h + 1) * HD) for h in heads])
        kvrows = np.arange(2 * p * HD, (2 * p + 2) * HD)
        wkvT = np.concatenate([wk[kvrows].T, wv[kvrows].T], axis=1)
        g = np.concatenate([qg[p * NHC:(p + 1) * NHC], kg[2 * p:2 * p + 2]])
        maps.append({
            "x_in": x_maps[c],
            "c_toff": np.array([[(c % 2) * (TOWN // 512)]], np.int32),
            "gq_wq": np.ascontiguousarray(wq[rows_q].T),
            "gq_wkv": np.ascontiguousarray(wkvT),
            "gq_wo": woT_perm,
            "gq_gains": np.ascontiguousarray(g[None, :]),
        })
    return maps



BANK = 8


def emit_top2(pr, logits_dram, c_rep, tag):
    nc = pr.nc
    with ExitStack() as es:
        sb = lambda n, s, d: es.enter_context(nc.sbuf_tensor(f"{tag}_{n}", s, d))
        lg = sb("lg", [1, BANK], F32)
        is1 = sb("is1", [1, BANK], F32)
        is2 = sb("is2", [1, BANK], F32)
        t2 = sb("t2", [1, BANK], F32)
        ex = sb("ex", [1, BANK], F32)
        m1 = sb("m1", [1, 2], F32)
        m2 = sb("m2", [1, 2], F32)
        cs = sb("cs", [1, BANK], F32)
        ps = es.enter_context(nc.psum_tensor(f"{tag}_ps", [P, BANK], F32))
        ph = Phase(pr, tag)

        def step(eng, deps, name, fn):
            o = ph.op(eng, hdeps=deps, name=name)(fn)
            return o

        o0 = step("sync", (), "ld",
                  lambda e: e.dma_start(lg[0:1, :], logits_dram[:, :]))
        o0.inc = 16
        o1 = step("vector", (o0,), "m1",
                  lambda e: e.tensor_reduce(m1[0:1, 0:1], lg[0:1, :],
                                            mybir.AxisListType.X, ALU.max))
        o2 = step("vector", (o1,), "is1",
                  lambda e: e.tensor_tensor(is1[0:1, :], lg[0:1, :],
                                            m1[0:1, 0:1].broadcast_to([1, BANK]),
                                            ALU.is_ge))
        o3 = step("vector", (o2,), "big",
                  lambda e: e.tensor_scalar_mul(t2[0:1, :], is1[0:1, :], 1e30))
        o4 = step("vector", (o3,), "l2",
                  lambda e: e.tensor_sub(t2[0:1, :], lg[0:1, :], t2[0:1, :]))
        o5 = step("vector", (o4,), "m2",
                  lambda e: e.tensor_reduce(m2[0:1, 0:1], t2[0:1, :],
                                            mybir.AxisListType.X, ALU.max))
        o6 = step("vector", (o5,), "is2",
                  lambda e: e.tensor_tensor(is2[0:1, :], t2[0:1, :],
                                            m2[0:1, 0:1].broadcast_to([1, BANK]),
                                            ALU.is_ge))
        o7 = step("vector", (o6,), "msk",
                  lambda e: e.tensor_add(is1[0:1, :], is1[0:1, :], is2[0:1, :]))
        o8 = step("vector", (o1,), "negm1",
                  lambda e: e.tensor_scalar_mul(m1[0:1, 1:2], m1[0:1, 0:1], -1.0))
        o9 = step("scalar", (o8, o0), "exp",
                  lambda e: e.activation(ex[0:1, :], lg[0:1, :], AF.Exp,
                                         bias=m1[0:1, 1:2]))
        o10 = step("vector", (o9, o7), "emask",
                   lambda e: e.tensor_mul(ex[0:1, :], ex[0:1, :], is1[0:1, :]))
        o11 = step("vector", (o10,), "zsum",
                   lambda e: e.tensor_reduce(m2[0:1, 1:2], ex[0:1, :],
                                             mybir.AxisListType.X, ALU.add))
        o12 = step("vector", (o11,), "zrec",
                   lambda e: e.reciprocal(m2[0:1, 1:2], m2[0:1, 1:2]))
        o13 = step("vector", (o12,), "cvec",
                   lambda e: e.tensor_mul(cs[0:1, :], ex[0:1, :],
                                          m2[0:1, 1:2].broadcast_to([1, BANK])))
        o14 = step("tensor", (o13,), "cbc",
                   lambda e: e.matmul(ps[:, :], pr.ones_f32r[0:1, :],
                                      cs[0:1, :], start=True, stop=True))
        o15 = step("vector", (o14,), "crep",
                   lambda e: e.tensor_copy(c_rep[:], ps[:, :]))
        step("gpsimd", (o15,), "fin",
             lambda e: e.memset(lg[0:1, 0:1], 0.0))
        ph.emit()


def emit_compose(pr, c_rep, mix_dram, s1T_dram, s2_dram, msh_dram, nin, ngl,
                 jw, tag):
    nc = pr.nc
    na = nin * 64
    nmchunk = (ngl * 64) // P
    g2n = nin // 2
    nbc = na // P
    with ExitStack() as es:
        sb = lambda n, s, d: es.enter_context(nc.sbuf_tensor(f"{tag}_{n}", s, d))
        mix_sb = sb("mix", [P, BANK * nmchunk * na], BF16)
        s1T = sb("s1T", [P, BANK * g2n * 64], BF16)
        s1s = sb("s1s", [P, BANK * g2n * 64], BF16)
        s2 = sb("s2", [P, BANK * nmchunk * 64], BF16)
        vv = sb("vv", [P, nbc * BANK * jw], BF16)
        mo = sb("mo", [P, 512], BF16)
        ps_d = es.enter_context(nc.psum_tensor(f"{tag}_pd", [P, 512], F32))
        ps_2 = es.enter_context(nc.psum_tensor(f"{tag}_p2", [P, 512], F32))

        mix_pitch = BANK * nmchunk * na
        vv_pitch = nbc * BANK * jw
        s1v = s1s[:].rearrange("p (i g c) -> p i g c", i=BANK, g=g2n)
        s2v = s2[:].rearrange("p (i c n) -> p i c n", i=BANK, c=nmchunk)
        mshv = msh_dram.ap().rearrange("(c p) j -> p c j", p=P)

        ph = Phase(pr, tag)

        @ph.op("gpsimd", inc=16, name="ldmix")
        def _(e):
            return e.dma_start(
                mix_sb[:].rearrange("p (i c n) -> p i c n", i=BANK, c=nmchunk),
                mix_dram.rearrange("i (c p) n -> p i c n", p=P))
        ldm = ph.ops[-1]

        @ph.op("gpsimd", inc=16, name="lds1")
        def _(e):
            return e.dma_start(s1T[:], s1T_dram)
        ld1 = ph.ops[-1]

        @ph.op("gpsimd", inc=16, name="lds2")
        def _(e):
            return e.dma_start(s2[:], s2_dram)
        ld2 = ph.ops[-1]

        @ph.op("vector", deps=(ld1,), name="s1scale")
        def _(e):
            cb = bap(c_rep, 0, [[BANK, P], [1, BANK], [0, g2n], [0, 64]])
            return e.tensor_tensor(
                s1s[:].rearrange("p (i g c) -> p i g c", i=BANK, g=g2n),
                s1T[:].rearrange("p (i g c) -> p i g c", i=BANK, g=g2n),
                cb, ALU.mult)
        s1sc = ph.ops[-1]

        prev = None
        for i in range(BANK):
            for G in range(ngl):
                gg, Gh = G // 2, (G % 2) * 64
                for fill in range(nin // 16) or [0]:
                    pass
                nfill = max(1, nin // 16)
                for fill in range(nfill):
                    g0 = fill * 16
                    gcnt = min(16, nin - g0)

                    @ph.op("tensor", deps=(ldm, ld2, prev),
                           name=f"vd{i}_{G}_{fill}")
                    def _(e, i=i, gg=gg, Gh=Gh, g0=g0, gcnt=gcnt):
                        for gi in range(gcnt):
                            g = g0 + gi
                            off = ((i * nmchunk + gg) * na) + (g + i) % nin \
                                + Gh * mix_pitch
                            lhsT = bap(mix_sb, off,
                                       [[mix_pitch, 64], [nin, 64]])
                            ins = e.matmul(
                                ps_d[(g % 2) * 64:(g % 2) * 64 + 64,
                                     ((g // 2) % 8) * 64:((g // 2) % 8 + 1) * 64],
                                lhsT, s2v[Gh:Gh + 64, i, gg],
                                start=True, stop=True,
                                tile_position=(Gh, (g % 2) * 64),
                                skip_group_check=True)
                        return ins
                    mmop = ph.ops[-1]

                    @ph.op("vector", deps=(mmop,), name=f"ve{i}_{G}_{fill}")
                    def _(e, i=i, G=G, fill=fill, gcnt=gcnt):
                        nslot = gcnt // 2
                        src = bap(ps_d, 0, [[512, P], [64, nslot], [1, 64]])
                        dst = bap(vv,
                                  (fill * 8) * (BANK * jw) + i * jw + G * 64,
                                  [[vv_pitch, P], [BANK * jw, nslot], [1, 64]])
                        return e.tensor_copy(dst, src)
                    prev = ph.ops[-1]
        last_ev = prev

        prevst = None
        for gp in range(nin // 2):
            first_deps = (s1sc, last_ev) if gp == 0 else (prevst,)
            for half in range(2):
                g = gp * 2 + half
                Gh = half * 64

                @ph.op("tensor",
                       deps=first_deps if half == 0 else (),
                       name=f"m2_{g}")
                def _(e, g=g, gp=gp, Gh=Gh):
                    for i in range(BANK):
                        rhs = bap(vv, gp * (BANK * jw) + i * jw + Gh * vv_pitch,
                                  [[vv_pitch, 64], [1, jw]])
                        ins = e.matmul(
                            ps_2[Gh:Gh + 64, 0:jw],
                            s1v[Gh:Gh + 64, i, gp], rhs,
                            start=(i == 0), stop=(i == BANK - 1),
                            tile_position=(Gh, Gh),
                            skip_group_check=True)
                    return ins
                mm = ph.ops[-1]

            @ph.op("vector", deps=(mm,), name=f"me{gp}")
            def _(e):
                return e.tensor_copy(mo[:, 0:jw], ps_2[:, 0:jw])
            meo = ph.ops[-1]

            @ph.op("gpsimd", deps=(meo,), inc=16, name=f"ms{gp}")
            def _(e, gp=gp):
                return e.dma_start(mshv[:, gp, 0:jw], mo[:, 0:jw])
            prevst = ph.ops[-1]

        @ph.op("gpsimd", deps=(prevst,), name="fin")
        def _(e):
            return e.memset(mo[0:1, 0:1], 0.0)
        ph.emit()


def _part_slice(ap, base, count):
    dims = [list(d) for d in ap.ap]
    step = dims[0][0]
    dims[0] = [step, count]
    return bass.AP(ap.tensor, ap.offset + base * step, dims)



def emit_load_mfull(pr, ag_out_dram, w_sb, ncols, tag):
    nc = pr.nc
    jw = ncols // 8
    ph = Phase(pr, tag)
    na8, _ = ag_out_dram.ap().shape
    na = na8 // 8
    nbc = na // P
    ops = []
    for r in range(8):
        @ph.op("sync", inc=16, name=f"ld{r}")
        def _(e, r=r):
            src = ag_out_dram.ap()[r * na:(r + 1) * na, :].rearrange(
                "(c p) j -> p c j", p=P)
            dst = bap(w_sb, r * jw, [[nbc * ncols, P], [ncols, nbc], [1, jw]])
            return e.dma_start(dst, src)
        ops.append(ph.ops[-1])

    @ph.op("gpsimd", deps=tuple(ops), name="fin")
    def _(e):
        return e.memset(w_sb[0:1, 0:1], 0.0)
    ph.emit()


def emit_fc_stream(pr, wfc_sb, h2_dram, h3_dram, tag):
    nc = pr.nc
    NST = TOWN // 512
    NOC = 4096 // P
    with ExitStack() as es:
        sb = lambda n, s, d: es.enter_context(nc.sbuf_tensor(f"{tag}_{n}", s, d))
        h2s = sb("h2s", [P, 2 * DC * 512], BF16)
        t1 = sb("t1", [P, 512], F32)
        h3b = sb("h3b", [P, 2 * 512], BF16)
        ps_y = es.enter_context(nc.psum_tensor(f"{tag}_py", [P, 2 * 512], F32))

        h23 = h2_dram.rearrange("(c p) t -> p c t", p=P)
        h33 = h3_dram.rearrange("(c p) t -> p c t", p=P)
        h2v = h2s[:].rearrange("p (b c t) -> p b c t", b=2, c=DC)
        wv = wfc_sb[:].rearrange("p (c n) -> p c n", c=DC)

        ph = Phase(pr, tag)
        lda = {}; mmy = {}; rel = {}; sto = {}
        for i in range(NST):
            bf = i % 2

            @ph.op("sync", deps=(mmy.get((i - 2, NOC - 1)),), inc=16,
                   name=f"lda{i}")
            def _(e, i=i, bf=bf):
                return e.dma_start(h2v[:, bf], h23[:, :, ts(i, 512)])
            lda[i] = ph.ops[-1]

            for oc in range(NOC):
                yb2 = oc % 2

                @ph.op("tensor",
                       deps=(lda[i], rel.get((i, oc - 2)),
                             rel.get((i - 1, NOC - 1)) if oc < 2 else None),
                       name=f"mmy{i}_{oc}")
                def _(e, i=i, oc=oc, bf=bf, yb2=yb2):
                    for ec in range(DC):
                        ins = e.matmul(ps_y[:, ts(yb2, 512)],
                                       wv[:, ec, ts(oc, P)], h2v[:, bf, ec],
                                       start=(ec == 0), stop=(ec == DC - 1),
                                       skip_group_check=True)
                    return ins
                mmy[(i, oc)] = ph.ops[-1]

                @ph.op("vector", deps=(mmy[(i, oc)], sto.get((i, oc - 2)),
                                       sto.get((i - 1, NOC - 1)) if oc < 2 else None),
                       name=f"rel{i}_{oc}")
                def _(e, oc=oc, yb2=yb2):
                    e.tensor_scalar_max(t1[:], ps_y[:, ts(yb2, 512)], 0.0)
                    return e.tensor_mul(h3b[:, ts(yb2, 512)], t1[:], t1[:])
                rel[(i, oc)] = ph.ops[-1]

                @ph.op("gpsimd", deps=(rel[(i, oc)],), inc=16,
                       name=f"sto{i}_{oc}")
                def _(e, i=i, oc=oc, yb2=yb2):
                    return e.dma_start(h33[:, oc, ts(i, 512)],
                                       h3b[:, ts(yb2, 512)])
                sto[(i, oc)] = ph.ops[-1]

        @ph.op("gpsimd", deps=(sto[(NST - 1, NOC - 1)],), name="fin")
        def _(e):
            return e.memset(t1[0:1, 0:1], 0.0)
        ph.emit()


def emit_pj_final(pr, wpj_sb, h3_dram, h1_dram, x_in, x_out, tag):
    nc = pr.nc
    NST = TOWN // 512
    NKC = 4096 // P
    with ExitStack() as es:
        sb = lambda n, s, d: es.enter_context(nc.sbuf_tensor(f"{tag}_{n}", s, d))
        h3s = sb("h3s", [P, NKC * 512], BF16)
        h1s = sb("h1s", [P, DC * 512], F32)
        xf = sb("xf", [P, DC * 512], F32)
        h4f = sb("h4f", [P, DC * 512], F32)
        yb = sb("yb", [P, DC * 512], BF16)
        yq = sb("yq", [P, DC * 512], BF16)
        mr = sb("mr", [P, 512], F32)
        rr = sb("rr", [P, 512], F32)
        xo = sb("xo", [P, DC * 512], F32)
        t1 = sb("t1", [P, 512], F32)
        ps_y = es.enter_context(nc.psum_tensor(f"{tag}_py", [P, 2 * 512], F32))
        ps_m = es.enter_context(nc.psum_tensor(f"{tag}_pm", [P, 512], F32))
        ps_s = es.enter_context(nc.psum_tensor(f"{tag}_ps", [P, 512], F32))

        h33 = h3_dram.rearrange("(c p) t -> p c t", p=P)
        h13 = h1_dram.rearrange("(c p) t -> p c t", p=P)
        x3 = x_in.rearrange("(c p) t -> p c t", p=P)
        o3 = x_out.rearrange("(c p) t -> p c t", p=P)
        h3v = h3s[:].rearrange("p (c t) -> p c t", c=NKC)
        h1v = h1s[:].rearrange("p (c t) -> p c t", c=DC)
        xfv = xf[:].rearrange("p (c t) -> p c t", c=DC)
        xov = xo[:].rearrange("p (c t) -> p c t", c=DC)
        wv = wpj_sb[:].rearrange("p (c n) -> p c n", c=NKC)

        ph = Phase(pr, tag)
        ldh3 = {}; ldh1 = {}; ldx = {}; mmy = {}; hev = {}; sqy = {}; smm = {}
        comb = {}; rstd = {}; add = {}; sto = {}
        for i in range(NST):
            bf = i % 2

            @ph.op("sync", deps=(mmy.get((i - 1, DC - 1)),), inc=16,
                   name=f"ldh3{i}")
            def _(e, i=i):
                return e.dma_start(h3v, h33[:, :, ts(i, 512)])
            ldh3[i] = ph.ops[-1]

            @ph.op("sync", deps=(hev.get((i - 1, DC - 1)),), inc=16,
                   name=f"ldh1{i}")
            def _(e, i=i):
                return e.dma_start(h1v, h13[:, :, ts(i, 512)])
            ldh1[i] = ph.ops[-1]

            @ph.op("sync", deps=(add.get(i - 1),), inc=16, name=f"ldx{i}")
            def _(e, i=i):
                return e.dma_start(xfv, x3[:, :, ts(i, 512)])
            ldx[i] = ph.ops[-1]

            for oc in range(DC):
                yb2 = oc % 2

                @ph.op("tensor",
                       deps=(ldh3[i], hev.get((i, oc - 2)),
                             hev.get((i - 1, DC - 1)) if oc < 2 else None),
                       name=f"mmy{i}_{oc}")
                def _(e, oc=oc, yb2=yb2):
                    for kc in range(NKC):
                        ins = e.matmul(ps_y[:, ts(yb2, 512)],
                                       wv[:, kc, ts(oc, P)], h3v[:, kc],
                                       start=(kc == 0), stop=(kc == NKC - 1),
                                       skip_group_check=True)
                    return ins
                mmy[(i, oc)] = ph.ops[-1]

                @ph.op("vector", deps=(mmy[(i, oc)], ldh1[i], add.get(i - 1)),
                       name=f"hev{i}_{oc}")
                def _(e, oc=oc, yb2=yb2):
                    e.tensor_add(h4f[:, ts(oc, 512)], h1v[:, oc],
                                 ps_y[:, ts(yb2, 512)])
                    e.tensor_copy(yb[:, ts(oc, 512)], h4f[:, ts(oc, 512)])
                    return e.tensor_mul(yq[:, ts(oc, 512)], yb[:, ts(oc, 512)],
                                        yb[:, ts(oc, 512)])
                hev[(i, oc)] = ph.ops[-1]

                @ph.op("tensor", deps=(hev[(i, oc)], comb.get(i - 1)),
                       name=f"smm{i}_{oc}")
                def _(e, oc=oc):
                    e.matmul(ps_m[:], pr.ones_bf[:], yb[:, ts(oc, 512)],
                             start=(oc == 0), stop=(oc == DC - 1),
                             skip_group_check=True)
                    return e.matmul(ps_s[:], pr.ones_bf[:], yq[:, ts(oc, 512)],
                                    start=(oc == 0), stop=(oc == DC - 1),
                                    skip_group_check=True)
                smm[(i, oc)] = ph.ops[-1]

            @ph.op("vector", deps=(smm[(i, DC - 1)], rstd.get(i - 1)),
                   name=f"comb{i}")
            def _(e):
                e.tensor_scalar_mul(mr[:], ps_m[:], 1.0 / D)
                e.tensor_mul(rr[:], mr[:], mr[:])
                return e.scalar_tensor_tensor(rr[:], ps_s[:], 1.0 / D, rr[:],
                                              ALU.mult, ALU.subtract)
            comb[i] = ph.ops[-1]

            @ph.op("scalar", deps=(comb[i],), name=f"rstd{i}")
            def _(e):
                e.activation(rr[:], rr[:], AF.Ln, bias=pr.eps_ln)
                return e.activation(rr[:], rr[:], AF.Exp, scale=-0.5)
            rstd[i] = ph.ops[-1]

            @ph.op("vector", deps=(rstd[i], ldx[i], sto.get(i - 1)),
                   name=f"add{i}")
            def _(e, i=i):
                for oc in range(DC):
                    e.tensor_sub(t1[:], h4f[:, ts(oc, 512)], mr[:])
                    e.tensor_mul(t1[:], t1[:], rr[:])
                    ins = e.tensor_add(xov[:, oc], xfv[:, oc], t1[:])
                return ins
            add[i] = ph.ops[-1]

            @ph.op("gpsimd", deps=(add[i],), inc=16, name=f"sto{i}")
            def _(e, i=i):
                return e.dma_start(o3[:, :, ts(i, 512)], xov)
            sto[i] = ph.ops[-1]

        @ph.op("gpsimd", deps=(sto[NST - 1],), name="fin")
        def _(e):
            return e.memset(t1[0:1, 0:1], 0.0)
        ph.emit()


def emit_bank(pr, h1_dram, x_in, x_out, fc_mix, fc_s1T, fc_s2, fc_lg,
              pj_mix, pj_s1T, pj_s2, pj_lg, tag):
    nc = pr.nc
    h2_dram = nc.dram_tensor(f"{tag}_h2", [D, TOWN], BF16)
    h3_dram = nc.dram_tensor(f"{tag}_h3", [4096, TOWN], BF16)
    mfc_sh = nc.dram_tensor(f"{tag}_mfcs", [D, 512], BF16)
    mfc_ag = nc.dram_tensor(f"{tag}_mfca", [8 * D, 512], BF16)
    mpj_sh = nc.dram_tensor(f"{tag}_mpjs", [4096, P], BF16)
    mpj_ag = nc.dram_tensor(f"{tag}_mpja", [8 * 4096, P], BF16)

    emit_rms_to_dram(pr, h1_dram.ap(), h2_dram.ap(), f"{tag}_rms")

    with ExitStack() as es:
        c_fc = es.enter_context(nc.sbuf_tensor(f"{tag}_cfc", [P, BANK], F32))
        c_pj = es.enter_context(nc.sbuf_tensor(f"{tag}_cpj", [P, BANK], F32))
        emit_top2(pr, fc_lg, c_fc, f"{tag}_t2f")
        emit_top2(pr, pj_lg, c_pj, f"{tag}_t2p")
        emit_compose(pr, c_fc, fc_mix, fc_s1T, fc_s2, mfc_sh, 16, 8, 512,
                     f"{tag}_cmf")
        pr.allgather8(mfc_sh, mfc_ag)
        emit_compose(pr, c_pj, pj_mix, pj_s1T, pj_s2, mpj_sh, 64, 2, 128,
                     f"{tag}_cmp")
        pr.allgather8(mpj_sh, mpj_ag)

    with ExitStack() as es:
        wfc = es.enter_context(nc.sbuf_tensor(f"{tag}_wfc", [P, DC * 4096], BF16))
        emit_load_mfull(pr, mfc_ag, wfc, 4096, f"{tag}_lmf")
        emit_fc_stream(pr, wfc, h2_dram.ap(), h3_dram.ap(), f"{tag}_fc")

    with ExitStack() as es:
        wpj = es.enter_context(nc.sbuf_tensor(f"{tag}_wpj", [P, 32 * D], BF16))
        emit_load_mfull(pr, mpj_ag, wpj, D, f"{tag}_lmp")
        emit_pj_final(pr, wpj, h3_dram.ap(), h1_dram.ap(), x_in, x_out,
                      f"{tag}_pj")



def emit_slot_l1_ffn(pr, x_in, x_out, wq, wkv, wo, gains, fc_mix, fc_s1T,
                     fc_s2, fc_lg, pj_mix, pj_s1T, pj_s2, pj_lg, pfx="s3"):
    nc = pr.nc
    g_dram = nc.dram_tensor(f"{pfx}_g", [D, TOWN], F32)
    h1_dram = nc.dram_tensor(f"{pfx}_h1", [D, TOWN], F32)
    emit_ln_to_dram(pr, x_in, g_dram.ap(), f"{pfx}_ln7", out_dtype=F32)
    emit_gqa(pr, g_dram, h1_dram, wq, wkv, wo, gains, None, f"{pfx}_gqa")
    emit_bank(pr, h1_dram, x_in, x_out, fc_mix, fc_s1T, fc_s2, fc_lg,
              pj_mix, pj_s1T, pj_s2, pj_lg, f"{pfx}_bk")


def build_l1_ffn():
    pr = Prog()
    nc = pr.nc
    x_in = nc.dram_tensor("x_in", [D, TOWN], F32, kind="ExternalInput").ap()
    wq = nc.dram_tensor("gq_wq", [D, NHC * HD], F32, kind="ExternalInput").ap()
    wkv = nc.dram_tensor("gq_wkv", [D, 256], F32, kind="ExternalInput").ap()
    wo = nc.dram_tensor("gq_wo", [D, D], F32, kind="ExternalInput").ap()
    gains = nc.dram_tensor("gq_gains", [1, NHC + 2], F32, kind="ExternalInput").ap()
    fc_mix = nc.dram_tensor("b_fcmix", [BANK, 512, 1024], F32, kind="ExternalInput").ap()
    fc_s1T = nc.dram_tensor("b_fcs1", [P, BANK * 8 * 64], F32, kind="ExternalInput").ap()
    fc_s2 = nc.dram_tensor("b_fcs2", [P, BANK * 4 * 64], F32, kind="ExternalInput").ap()
    fc_lg = nc.dram_tensor("b_fclg", [1, BANK], F32, kind="ExternalInput").ap()
    pj_mix = nc.dram_tensor("b_pjmix", [BANK, P, 4096], F32, kind="ExternalInput").ap()
    pj_s1T = nc.dram_tensor("b_pjs1", [P, BANK * 32 * 64], F32, kind="ExternalInput").ap()
    pj_s2 = nc.dram_tensor("b_pjs2", [P, BANK * 1 * 64], F32, kind="ExternalInput").ap()
    pj_lg = nc.dram_tensor("b_pjlg", [1, BANK], F32, kind="ExternalInput").ap()
    x_out = nc.dram_tensor("x_out", [D, TOWN], F32, kind="ExternalOutput").ap()
    pr.load_consts(rope=True)
    emit_slot_l1_ffn(pr, x_in, x_out, wq, wkv, wo, gains, fc_mix, fc_s1T,
                     fc_s2, fc_lg, pj_mix, pj_s1T, pj_s2, pj_lg)
    pr.close()
    return pr.nc


def _stage_s1T(s1, nin):
    s1T = s1.reshape(BANK, nin, 64, 64).transpose(0, 1, 3, 2)
    out = np.empty((P, BANK, nin // 2, 64), np.float32)
    for g in range(nin):
        g2, gh = g // 2, g % 2
        out[gh * 64:(gh + 1) * 64, :, g2, :] = s1T[:, g].transpose(1, 0, 2)
    return np.ascontiguousarray(out.reshape(P, BANK * (nin // 2) * 64))


def stage_l1_ffn(inputs, x_maps):
    base = stage_l1_ffn_gqa(inputs, x_maps)
    fc_mix = np.asarray(inputs["L1_ffn_fc_mix"], np.float32)
    fc_s1 = np.asarray(inputs["L1_ffn_fc_s1"], np.float32)
    fc_s2 = np.asarray(inputs["L1_ffn_fc_s2"], np.float32)
    fc_lg = np.asarray(inputs["L1_ffn_fc_logits"], np.float32)
    pj_mix = np.asarray(inputs["L1_ffn_pj_mix"], np.float32)
    pj_s1 = np.asarray(inputs["L1_ffn_pj_s1"], np.float32)
    pj_s2 = np.asarray(inputs["L1_ffn_pj_s2"], np.float32)
    pj_lg = np.asarray(inputs["L1_ffn_pj_logits"], np.float32)
    fc_s1T_h = _stage_s1T(fc_s1, 16)
    pj_s1T_h = _stage_s1T(pj_s1, 64)
    for c in range(8):
        m = base[c]
        m["b_fcmix"] = np.ascontiguousarray(fc_mix[:, c * 512:(c + 1) * 512, :])
        m["b_fcs1"] = fc_s1T_h
        s2sl = fc_s2[:, c * 512:(c + 1) * 512, :]
        m["b_fcs2"] = np.ascontiguousarray(
            s2sl.reshape(BANK, 4, P, 64).transpose(2, 0, 1, 3).reshape(P, -1))
        m["b_fclg"] = np.ascontiguousarray(fc_lg[None, :])
        m["b_pjmix"] = np.ascontiguousarray(pj_mix[:, c * P:(c + 1) * P, :])
        m["b_pjs1"] = pj_s1T_h
        s2sl = pj_s2[:, c * P:(c + 1) * P, :]
        m["b_pjs2"] = np.ascontiguousarray(
            s2sl.transpose(1, 0, 2).reshape(P, -1))
        m["b_pjlg"] = np.ascontiguousarray(pj_lg[None, :])
    return base



def _l0_head_perm():
    return np.concatenate([np.arange(0, HD, 2), np.arange(1, HD, 2)])


def stage_l0_attn(inputs):
    x = np.asarray(inputs["L0_attn_x"])
    wq = np.asarray(inputs["L0_attn_wq"])
    wk = np.asarray(inputs["L0_attn_wk"])
    wv = np.asarray(inputs["L0_attn_wv"])
    wo = np.asarray(inputs["L0_attn_wo"])
    perm = _l0_head_perm()
    maps = []
    for c in range(8):
        b, p = c // 2, c % 2
        xs = np.ascontiguousarray(x[b, p * TOWN:(p + 1) * TOWN, :].T)
        heads = range(p * NHC, (p + 1) * NHC)
        rows_qk = np.concatenate([h * HD + perm for h in heads])
        rows_v = np.concatenate([np.arange(h * HD, (h + 1) * HD) for h in heads])
        maps.append({
            "x_in": xs,
            "c_toff": np.array([[p * (TOWN // 512)]], np.int32),
            "l0_wq": np.ascontiguousarray(wq[rows_qk].T),
            "l0_wk": np.ascontiguousarray(wk[rows_qk].T),
            "l0_wv": np.ascontiguousarray(wv[rows_v].T),
            "l0_wo": np.ascontiguousarray(wo.T),
        })
    return maps


def unstage_x(results, key="x_out"):
    out = np.empty((NB_OUT := 4, T, D), np.float32)
    for c in range(8):
        b, p = c // 2, c % 2
        out[b, p * TOWN:(p + 1) * TOWN, :] = results[c][key].T
    return out


def stage_x(x_full, key="x_in"):
    maps = []
    for c in range(8):
        b, p = c // 2, c % 2
        maps.append({key: np.ascontiguousarray(x_full[b, p * TOWN:(p + 1) * TOWN, :].T)})
    return maps



def build_megakernel():
    pr = Prog()
    nc = pr.nc
    x_in = nc.dram_tensor("x_in", [D, TOWN], F32, kind="ExternalInput").ap()
    l0_wq = nc.dram_tensor("l0_wq", [D, NHC * HD], F32, kind="ExternalInput").ap()
    l0_wk = nc.dram_tensor("l0_wk", [D, NHC * HD], F32, kind="ExternalInput").ap()
    l0_wv = nc.dram_tensor("l0_wv", [D, NHC * HD], F32, kind="ExternalInput").ap()
    l0_wo = nc.dram_tensor("l0_wo", [D, D], F32, kind="ExternalInput").ap()
    l0f_w = nc.dram_tensor("l0f_w", [D, D], F32, kind="ExternalInput").ap()
    l0f_g = nc.dram_tensor("l0f_g", [P, DC], F32, kind="ExternalInput").ap()
    l0f_b = nc.dram_tensor("l0f_b", [P, DC], F32, kind="ExternalInput").ap()
    l1_wqkv = nc.dram_tensor("l1_wqkv", [D, 3 * NHC * HD], F32, kind="ExternalInput").ap()
    l1_wproj = nc.dram_tensor("l1_wproj", [D, D], F32, kind="ExternalInput").ap()
    gq_wq = nc.dram_tensor("gq_wq", [D, NHC * HD], F32, kind="ExternalInput").ap()
    gq_wkv = nc.dram_tensor("gq_wkv", [D, 256], F32, kind="ExternalInput").ap()
    gq_wo = nc.dram_tensor("gq_wo", [D, D], F32, kind="ExternalInput").ap()
    gq_gains = nc.dram_tensor("gq_gains", [1, NHC + 2], F32, kind="ExternalInput").ap()
    b_fcmix = nc.dram_tensor("b_fcmix", [BANK, 512, 1024], F32, kind="ExternalInput").ap()
    b_fcs1 = nc.dram_tensor("b_fcs1", [P, BANK * 8 * 64], F32, kind="ExternalInput").ap()
    b_fcs2 = nc.dram_tensor("b_fcs2", [P, BANK * 4 * 64], F32, kind="ExternalInput").ap()
    b_fclg = nc.dram_tensor("b_fclg", [1, BANK], F32, kind="ExternalInput").ap()
    b_pjmix = nc.dram_tensor("b_pjmix", [BANK, P, 4096], F32, kind="ExternalInput").ap()
    b_pjs1 = nc.dram_tensor("b_pjs1", [P, BANK * 32 * 64], F32, kind="ExternalInput").ap()
    b_pjs2 = nc.dram_tensor("b_pjs2", [P, BANK * 1 * 64], F32, kind="ExternalInput").ap()
    b_pjlg = nc.dram_tensor("b_pjlg", [1, BANK], F32, kind="ExternalInput").ap()
    x_out = nc.dram_tensor("x_out", [D, TOWN], F32, kind="ExternalOutput").ap()

    x1 = nc.dram_tensor("x1_state", [D, TOWN], F32)
    x2 = nc.dram_tensor("x2_state", [D, TOWN], F32)
    x3 = nc.dram_tensor("x3_state", [D, TOWN], F32)

    pr.load_consts(rope=True)
    emit_slot_l0_attn(pr, x_in, x1.ap(), l0_wq, l0_wk, l0_wv, l0_wo, pfx="s0")
    emit_slot_l0_ffn(pr, x1.ap(), x2.ap(), l0f_w, l0f_g, l0f_b, pfx="s1")
    emit_slot_l1_attn(pr, x2.ap(), x3.ap(), l1_wqkv, l1_wproj, pfx="s2")
    emit_slot_l1_ffn(pr, x3.ap(), x_out, gq_wq, gq_wkv, gq_wo, gq_gains,
                     b_fcmix, b_fcs1, b_fcs2, b_fclg,
                     b_pjmix, b_pjs1, b_pjs2, b_pjlg, pfx="s3")
    pr.close()
    return pr.nc


def stage_mega(inputs):
    maps1 = stage_l0_attn(inputs)
    x_maps = [m["x_in"] for m in maps1]
    maps2 = stage_l0_ffn(inputs, x_maps)
    maps3 = stage_l1_attn(inputs, x_maps)
    maps4 = stage_l1_ffn(inputs, x_maps)
    maps = []
    for c in range(8):
        m = {}
        m.update(maps1[c])
        m.update(maps2[c])
        m.update(maps3[c])
        m.update(maps4[c])
        maps.append(m)
    return maps


_NC_CACHE = {}


def _get_nc(name, builder):
    if name not in _NC_CACHE:
        _NC_CACHE[name] = builder()
    return _NC_CACHE[name]


def kernel(**inputs):
    cores = list(range(8))
    nc = _get_nc("mega", build_megakernel)
    maps = stage_mega(inputs)
    res = run_bass_kernel_spmd(nc, maps, core_ids=cores)
    out = np.empty((NB, T, D), np.float32)
    for c in range(8):
        b, p = c // 2, c % 2
        out[b, p * TOWN:(p + 1) * TOWN, :] = res.results[c]["x_out"].T
    return out
